# version 01
# baseline (speedup 1.0000x reference)
import numpy as np
import ml_dtypes
import concourse.bass as bass
import concourse.mybir as mybir
from concourse.bass_utils import run_bass_kernel_spmd

F32 = mybir.dt.float32
BF16 = mybir.dt.bfloat16
U8 = mybir.dt.uint8
U32 = mybir.dt.uint32
AF = mybir.ActivationFunctionType
ALU = mybir.AluOpType
AX = mybir.AxisListType.X
ET = mybir.EngineType
NPBF16 = ml_dtypes.bfloat16

TC = 4096
D = 1024
H = 16
HD = 64
NB = 8
TT = 512
NT = TC // TT
CH = 128
NCH = TC // CH
DFF = 4096
MAGIC1 = 0x5A17C0DE
MAGIC2 = 0x7E57B1D5
SB_BYTES = 224 * 1024
DETECT_RACES = False

ENGS = ["tensor", "vector", "scalar", "sync", "gpsimd"]
ECTR = {"tensor": "cT", "vector": "cV", "scalar": "cS", "sync": "cY", "gpsimd": "cG"}


class Prog:

    def __init__(self, nc):
        self.nc = nc
        self.items = {e: [] for e in ENGS}
        self.sems = {}
        self.cnt = {}
        self.lastwait = {e: {} for e in ENGS}

    def sem(self, name):
        if name not in self.sems:
            self.sems[name] = self.nc.alloc_semaphore(name)
            self.cnt[name] = 0
        return self.sems[name]

    def _waits(self, eng, waits):
        out = []
        for s, v in waits:
            self.sem(s)
            if v <= 0:
                continue
            if self.lastwait[eng].get(s, -1) >= v:
                continue
            self.lastwait[eng][s] = v
            out.append((s, v))
        return out

    def op(self, eng, fn, wait=None, sig=True, chain=None):
        ctr = ECTR[eng]
        self.sem(ctr)
        if chain is None:
            chain = eng in ("vector", "scalar", "gpsimd")
        waits = list(wait or [])
        if chain and self.cnt[ctr] > 0:
            waits.append((ctr, self.cnt[ctr]))
        waits = self._waits(eng, waits)
        if sig:
            self.cnt[ctr] += 1
        val = self.cnt[ctr]

        def closure(e, waits=waits, fn=fn, sig=sig, ctr=ctr):
            for s, v in waits:
                e.wait_ge(self.sems[s], v)
            ins = fn(e)
            if sig:
                if ins is None:
                    ins = e.nop()
                ins.then_inc(self.sems[ctr], 1)
            return ins
        self.items[eng].append(closure)
        return val

    def mark(self, eng):
        self.sem(ECTR[eng])
        return self.cnt[ECTR[eng]]

    def ctr(self, eng):
        return ECTR[eng]

    def now(self, semname):
        self.sem(semname)
        return self.cnt[semname]

    def dma(self, eng, out, in_, semname, wait=None, chain=None, **kw):
        self.sem(semname)
        ctr = ECTR[eng]
        if chain is None:
            chain = eng in ("vector", "scalar", "gpsimd")
        waits = list(wait or [])
        if chain and self.cnt[ctr] > 0:
            waits.append((ctr, self.cnt[ctr]))
        waits = self._waits(eng, waits)
        self.cnt[semname] += 16
        def closure(e, waits=waits, out=out, in_=in_, kw=kw, s=semname):
            for sn, v in waits:
                e.wait_ge(self.sems[sn], v)
            return e.dma_start(out=out, in_=in_, **kw).then_inc(self.sems[s], 16)
        self.items[eng].append(closure)
        return self.cnt[semname]

    def wait(self, eng, semname, value=None):
        if value is None:
            value = self.now(semname)
        ws = self._waits(eng, [(semname, value)])
        if ws:
            def closure(e, ws=ws):
                for s, v in ws:
                    e.wait_ge(self.sems[s], v)
                return None
            self.items[eng].append(closure)

    def barrier(self, tag=""):
        marks = {e: self.mark(e) for e in ENGS}
        ctrs = set(ECTR.values())
        dmas = [(s, v) for s, v in self.cnt.items() if s not in ctrs and v > 0]
        for e in ENGS:
            for e2 in ENGS:
                if e2 != e:
                    self.wait(e, ECTR[e2], marks[e2])
            for s, v in dmas:
                self.wait(e, s, v)
        for e in ENGS:
            self.op(e, lambda x: x.nop(), chain=(e in ("vector", "scalar", "gpsimd")))

    def replay(self):
        with self.nc.Block() as blk:
            for e in ENGS:
                getattr(blk, e)(lambda eng, e=e: [f(eng) for f in self.items[e]])


class PsumPool:
    def __init__(self, prog, tiles):
        self.p = prog
        self.tiles = tiles
        self.state = [None] * len(tiles)

    def acquire(self, i, eng):
        if self.state[i] is not None:
            s, v = self.state[i]
            self.p.wait(eng, s, v)
        return self.tiles[i]

    def release(self, i, eng):
        self.state[i] = (ECTR[eng], self.p.mark(eng))


def rowap(t, a, b):
    return t[0:1, a:b]


def build(stages, dev=False, s2_nwin=None):
    nc = bass.Bass("TRN2", target_bir_lowering=False,
                   detect_race_conditions=DETECT_RACES)
    P = Prog(nc)

    def inp(name, shape, dt=F32):
        return nc.dram_tensor(name, shape, dt, kind="ExternalInput")

    x_sl = inp("x_sl", [D, TC + 1])
    wq = inp("wq", [D, D]); wk = inp("wk", [D, D]); wv = inp("wv", [D, D])
    wbeta = inp("wbeta", [D, H])
    wo = inp("wo", [D, D]); w1 = inp("w1", [D, DFF]); w2 = inp("w2", [DFF, D])
    wrT = inp("wrT", [D, D]); wkT = inp("wkT", [D, D]); wvT = inp("wvT", [D, D])
    wqkv = inp("wqkv", [D, 3 * D]); wproj = inp("wproj", [D, D])
    k0u = inp("k0u", [2 * D, DFF], U8)
    k1u = inp("k1u", [DFF, DFF], U8)
    k2u = inp("k2u", [DFF, D], U8)
    vecs = inp("vecs", [128, 80])
    b1v = inp("b1v", [128, 32])
    bqkvv = inp("bqkvv", [128, 24])
    sclv = inp("sclv", [128, 4])
    bk_bc = inp("bk_bc", [128, D]); bv_bc = inp("bv_bc", [128, D])
    bqkv_k_bc = inp("bqkv_k_bc", [128, D]); bqkv_v_bc = inp("bqkv_v_bc", [128, D])
    bbeta_bc = inp("bbeta_bc", [128, H])
    i64p = inp("i64p", [128, 64])
    maskln = inp("maskln", [128, CH])
    masku = inp("masku", [128, CH])
    identb = inp("identb", [128, 128], BF16)
    onesbc = inp("onesbc", [1, 128], BF16)
    onesbf = inp("onesbf", [1, 128])
    onescb = inp("onescb", [128, 1], BF16)
    onescf = inp("onescf", [128, 1])
    sel16 = inp("sel16", [16, 8 * 128])

    outs = []

    def outp(name, shape, dt):
        outs.append(name)
        return nc.dram_tensor(name, shape, dt, kind="ExternalOutput")

    qT = nc.dram_tensor("qT", [D, TC], BF16)
    ktokN = nc.dram_tensor("ktokN", [TC, D], BF16)
    vtok = nc.dram_tensor("vtok", [TC, D], BF16)
    OT = nc.dram_tensor("OTb", [D, TC], BF16)
    ABb = nc.dram_tensor("ABb", [NCH, 8, 128, 128], BF16)
    WTb = nc.dram_tensor("WTb", [NCH, 8, 128, CH], BF16)
    statsH = nc.dram_tensor("statsH", [4, TC + 8], F32)
    stats1H = nc.dram_tensor("stats1H", [2, TC + 8], F32)
    sTh = nc.dram_tensor("sTh", [D, TC], F32)
    xaTh = nc.dram_tensor("xaTh", [D, TC], F32)
    h2Th = nc.dram_tensor("h2Th", [D, TC], BF16)
    shst = nc.dram_tensor("shst", [8, 128, 512], BF16, addr_space="Shared")
    shkv = nc.dram_tensor("shkv", [8, 128, 8 * 65], F32, addr_space="Shared")
    shfl = nc.dram_tensor("shfl", [8, 2, 16], U32, addr_space="Shared")
    q1T = nc.dram_tensor("q1T", [D, TC], BF16)
    s1Th = nc.dram_tensor("s1Th", [D, TC], F32)
    h1fT = nc.dram_tensor("h1fT", [DFF, TC], BF16)
    h2fT = nc.dram_tensor("h2fT", [DFF, TC], BF16)
    d0T = nc.dram_tensor("d0T", [D, DFF], BF16)
    w1eT = nc.dram_tensor("w1eT", [DFF, DFF], BF16)
    w2eT = nc.dram_tensor("w2eT", [DFF, D], BF16)

    full = "l1" in stages or "t3c" in stages
    if "l1" in stages and "l0" not in stages:
        x1T = inp("x1T_in", [D, TC])
        xn1T_t = inp("xn1T_in", [D, TC], BF16)
    elif dev and ("s6" in stages or "l0" in stages) and not full:
        x1T = outp("x1T", [D, TC], F32)
        xn1T_t = outp("xn1T", [D, TC], BF16)
    else:
        x1T = nc.dram_tensor("x1T", [D, TC], F32)
        xn1T_t = nc.dram_tensor("xn1T", [D, TC], BF16)
    out_sl = outp("out_sl", [D, TC], F32) if full else None

    dev_out = {}
    if dev:
        for nm in stages:
            if nm == "s1":
                dev_out["d_q"] = outp("d_q", [D, TC], BF16)
                dev_out["d_k"] = outp("d_k", [TC, D], BF16)
                dev_out["d_v"] = outp("d_v", [TC, D], BF16)
                dev_out["d_beta"] = outp("d_beta", [128, 32 * H], F32)
                dev_out["d_stats"] = outp("d_stats", [4, TC + 8], F32)
            if nm == "s2":
                dev_out["d_stl"] = outp("d_stl", [128, 8 * HD], BF16)
                dev_out["d_ot"] = outp("d_ot", [D, TC], BF16)
                dev_out["d_ab"] = outp("d_ab", [NCH, 8, 128, 128], BF16)
                dev_out["d_wt"] = outp("d_wt", [NCH, 8, 128, CH], BF16)
            if nm == "s4":
                dev_out["d_ot2"] = outp("d_ot2", [D, TC], BF16)
            if nm == "s5":
                dev_out["d_s"] = outp("d_s", [D, TC], F32)
                dev_out["d_xa"] = outp("d_xa", [D, TC], F32)
                dev_out["d_h2"] = outp("d_h2", [D, TC], BF16)

            if nm == "t1":
                dev_out["d_kv"] = outp("d_kv", [128, 8 * 65], F32)
                dev_out["d_q1"] = outp("d_q1", [D, TC], BF16)
                dev_out["d_kt"] = outp("d_kt", [128, D], BF16)
                dev_out["d_vt"] = outp("d_vt", [128, D], BF16)
                dev_out["d_ktall"] = outp("d_ktall", [32, 128, D], BF16)
            if nm == "t3a":
                dev_out["d_s1"] = outp("d_s1", [D, TC], F32)
                dev_out["d_h1f"] = outp("d_h1f", [DFF, TC], BF16)
                dev_out["d_ob"] = outp("d_ob", [128, NB * TT], BF16)
                dev_out["d_z"] = outp("d_z", [128, TT], F32)
                dev_out["d_zr"] = outp("d_zr", [128, TT], F32)
                dev_out["d_ks"] = outp("d_ks", [128, 128], BF16)
                dev_out["d_d0"] = outp("d_d0", [D, DFF], BF16)
            if nm == "t3b":
                dev_out["d_h2f"] = outp("d_h2f", [DFF, TC], BF16)

    pid0 = nc.partition_id(engines=[ET.SP, ET.Pool])
    pid = nc.snap(pid0, engines=[ET.SP, ET.Pool], min_val=0, max_val=7)
    partner = (pid // 2) * 2 + (1 - (pid % 2))

    per = Arena = None
    off = [20480]

    def salloc(name, shape, dt, at=None):
        if at is not None:
            off[0] = at
        t = nc.alloc_sbuf_tensor_at(name, list(shape), dt, offset=off[0])
        per_part = int(np.prod(shape[1:])) * mybir.dt.size(dt)
        off[0] += (per_part + 31) // 32 * 32
        assert off[0] <= SB_BYTES, f"SBUF overflow at {name}: {off[0]}"
        return t

    sb_vecs = salloc("sb_vecs", [128, 80], F32)
    sb_b1 = salloc("sb_b1", [128, 32], F32)
    sb_bqkv = salloc("sb_bqkv", [128, 24], F32)
    sb_scl = salloc("sb_scl", [128, 4], F32)
    sb_bkbc = salloc("sb_bkbc", [128, D], F32)
    sb_bvbc = salloc("sb_bvbc", [128, D], F32)
    sb_bqkbc = sb_bkbc
    sb_bqvbc = sb_bvbc
    sb_bbeta = salloc("sb_bbeta", [128, H], F32)
    sb_i64 = salloc("sb_i64", [128, 64], F32)
    sb_mln = salloc("sb_mln", [128, CH], F32)
    sb_mu = salloc("sb_mu", [128, CH], F32)
    sb_idb = salloc("sb_idb", [128, 128], BF16)
    sb_ones = salloc("sb_ones", [1, 128], BF16)
    sb_onesf = salloc("sb_onesf", [1, 128], F32)
    sb_onescb = salloc("sb_onescb", [128, 1], BF16)
    sb_onescf = salloc("sb_onescf", [128, 1], F32)
    sb_beta = salloc("sb_beta", [128, 32 * H], F32)
    sb_stl = salloc("sb_stl", [128, 8 * HD], BF16)
    sb_stc = salloc("sb_stc", [128, 8 * HD], BF16)
    sb_rows = salloc("sb_rows", [1, 2048], F32)
    sb_eps = salloc("sb_eps", [128, 8], F32)
    sb_flag = salloc("sb_flag", [1, 16], U32)
    sb_magic = salloc("sb_magic", [1, 32], U32)
    sb_lastc = salloc("sb_lastc", [128, 16], F32)
    sb_zb = salloc("sb_zb", [128, 1], BF16)
    sb_kvloc = salloc("sb_kvloc", [128, 8 * 65], F32)
    sb_kvb = salloc("sb_kvb", [128, 8 * 65], BF16)
    sb_ks = salloc("sb_ks", [128, NB * H], BF16)
    PERSIST = off[0]

    s1_wq = salloc("s1_wq", [128, NB * D], BF16, at=PERSIST)
    s1_wk = salloc("s1_wk", [128, NB * D], BF16)
    s1_wv = salloc("s1_wv", [128, NB * D], BF16)
    s1_wbeta = salloc("s1_wbeta", [128, NB * H], BF16)
    W_S1_END = off[0]
    s5_wo = salloc("s5_wo", [128, NB * D], BF16, at=PERSIST)
    W_S5A_END = off[0]
    s5_w1 = salloc("s5_w1", [128, NB * DFF], BF16, at=PERSIST)
    s5_w2 = salloc("s5_w2", [128, 32 * D], BF16)
    W_S5B_END = off[0]
    s6_wr = salloc("s6_wr", [128, NB * D], BF16, at=PERSIST)
    s6_wkk = salloc("s6_wkk", [128, NB * D], BF16)
    s6_wvv = salloc("s6_wvv", [128, NB * D], BF16)
    W_S6_END = off[0]
    t1_wqkv = salloc("t1_wqkv", [128, NB * 3 * D], BF16, at=PERSIST)
    W_T1_END = off[0]
    t3_wproj = salloc("t3_wproj", [128, NB * D], BF16, at=PERSIST)
    t3_d0 = salloc("t3_d0", [128, NB * DFF], BF16)
    W_T3A_END = off[0]
    t3_w1s = [salloc(f"t3_w1s{i}", [128, 4 * 128], BF16, at=(PERSIST if i == 0 else None))
              for i in range(64)]
    W_T3B_END = off[0]
    t3_w2 = salloc("t3_w2", [128, 32 * D], BF16, at=PERSIST)
    W_T3C_END = off[0]
    WMAX = max(W_S1_END, W_S5A_END, W_S5B_END, W_S6_END, W_T1_END,
               W_T3A_END, W_T3B_END, W_T3C_END)

    def aalloc(name, shape, dt, at=None):
        return salloc(name, shape, dt, at=at)

    s1_x = aalloc("s1_x", [128, NB * TT], F32, at=W_S1_END)
    s1_h1 = aalloc("s1_h1", [128, NB * TT], BF16)
    s1_xsq = aalloc("s1_xsq", [128, NB * TT], BF16)
    s1_kt = aalloc("s1_kt", [128, D], F32)
    s1_ksq = aalloc("s1_ksq", [128, D], BF16)
    s1_rn = aalloc("s1_rn", [128, 4 * H], F32)
    s1_st = aalloc("s1_st", [128, NB * TT], BF16)
    s1_kst = aalloc("s1_kst", [128, 2 * D], BF16)
    s1_bc = aalloc("s1_bc", [128, 2 * TT], F32)
    S1_END = off[0]
    s2_qw = aalloc("s2_qw", [128, NB * TT], BF16, at=PERSIST)
    s2_kw = aalloc("s2_kw", [128, 4 * D], BF16)
    s2_vw = aalloc("s2_vw", [128, 4 * D], BF16)
    s2_kf = aalloc("s2_kf", [128, NB * CH], BF16)
    s2_m = aalloc("s2_m", [128, CH], BF16)
    s2_mt = aalloc("s2_mt", [128, CH], BF16)
    s2_m2 = aalloc("s2_m2", [128, CH], BF16)
    s2_mt2 = aalloc("s2_mt2", [128, CH], BF16)
    s2_sacc = aalloc("s2_sacc", [128, CH], BF16)
    s2_mo = aalloc("s2_mo", [128, CH], BF16)
    s2_r = aalloc("s2_r", [128, CH], BF16)
    s2_ab = aalloc("s2_ab", [128, 8 * 128], BF16)
    s2_wts = aalloc("s2_wts", [128, 8 * CH], BF16)
    s2_obs = aalloc("s2_obs", [128, NB * CH], BF16)
    S2_END = off[0]
    s4_ab = aalloc("s4_ab", [128, 8 * 128], BF16, at=PERSIST)
    s4_wt = aalloc("s4_wt", [128, 8 * CH], BF16)
    s4_o = aalloc("s4_o", [128, NB * CH], BF16)
    S4_END = off[0]
    s5_ot = aalloc("s5_ot", [128, NB * TT], BF16, at=W_S5A_END)
    s5_att = aalloc("s5_att", [128, NB * TT], F32)
    s5_x = aalloc("s5_x", [128, NB * TT], F32)
    s5_xa = aalloc("s5_xa", [128, NB * TT], F32)
    s5_sq = aalloc("s5_sq", [128, NB * TT], BF16)
    s5_h2 = aalloc("s5_h2", [128, NB * TT], BF16)
    s5_bc = aalloc("s5_bc", [128, 3 * TT], F32)
    S5A_END = off[0]
    s5_h2b = aalloc("s5_h2b", [128, NB * TT], BF16, at=W_S5B_END)
    s5_hf = aalloc("s5_hf", [128, 32 * TT], BF16)
    s5_gt = aalloc("s5_gt", [128, 2 * TT], F32)
    s5_xab = aalloc("s5_xab", [128, TT], F32)
    s5_s = aalloc("s5_s", [128, TT], F32)
    S5B_END = off[0]
    s6_x = aalloc("s6_x", [128, NB * TT], F32, at=W_S6_END)
    s6_xn = aalloc("s6_xn", [128, NB * (TT + 8)], BF16)
    s6_xr = aalloc("s6_xr", [128, NB * TT], BF16)
    s6_xk = aalloc("s6_xk", [128, NB * TT], BF16)
    s6_r = aalloc("s6_r", [128, NB * TT], F32)
    s6_kk = aalloc("s6_kk", [128, NB * TT], BF16)
    s6_sld = aalloc("s6_sld", [128, NB * TT], F32)
    s6_g = aalloc("s6_g", [128, 2 * TT], F32)
    s6_x1 = aalloc("s6_x1", [128, NB * TT], F32)
    s6_sq = aalloc("s6_sq", [128, NB * TT], BF16)
    s6_bc = aalloc("s6_bc", [128, 3 * TT], F32)
    s6_xn1 = aalloc("s6_xn1", [128, NB * TT], BF16)
    S6_END = off[0]
    t1_xn1 = aalloc("t1_xn1", [128, NB * TT], BF16, at=W_T1_END)
    t1_q = aalloc("t1_q", [128, NB * TT], BF16)
    t1_kt = aalloc("t1_kt", [128, D], BF16)
    t1_vt = aalloc("t1_vt", [128, D], BF16)
    t1_tmp = aalloc("t1_tmp", [128, 2 * TT], F32)
    sb_kvp = aalloc("sb_kvp", [128, 8 * 65], F32)
    T1_END = off[0]
    t3_q = aalloc("t3_q", [128, NB * TT], BF16, at=W_T3A_END)
    t3_xn1 = aalloc("t3_xn1", [128, NB * TT], BF16)
    t3_z = aalloc("t3_z", [128, TT], F32)
    t3_ob = aalloc("t3_ob", [128, NB * TT], BF16)
    t3_hf = aalloc("t3_hf", [128, 32 * TT], BF16)
    t3_tmp = aalloc("t3_tmp", [128, 2 * TT], F32)
    t3_den = aalloc("t3_den", [128, TT], F32)
    sb_sel = aalloc("sb_sel", [16, 8 * 128], F32)
    T3A_END = off[0]
    t3_h1 = aalloc("t3_h1", [128, 32 * 1024], BF16, at=W_T3B_END)
    t3_h2o = aalloc("t3_h2o", [128, 8 * 1024], BF16)
    t3b_tmp = aalloc("t3b_tmp", [128, TT], F32)
    T3B_END = off[0]
    t3_hr = aalloc("t3_hr", [128, 32 * TT], BF16, at=W_T3C_END)
    t3_g1 = aalloc("t3_g1", [128, 2 * TT], F32)
    t3_sl = aalloc("t3_sl", [128, NB * TT], F32)
    t3_x1l = aalloc("t3_x1l", [128, NB * TT], F32)
    t3_out = aalloc("t3_out", [128, NB * TT], F32)
    T3C_END = off[0]
    cv_a = aalloc("cv_a", [128, DFF], U8, at=PERSIST)
    cv_b = aalloc("cv_b", [128, DFF], U8)
    cv_o = aalloc("cv_o", [128, DFF], BF16)
    CV_END = off[0]
    for nm, v in [("S1", S1_END), ("S2", S2_END), ("S4", S4_END), ("S5A", S5A_END),
                  ("S5B", S5B_END), ("S6", S6_END), ("T1", T1_END), ("T3A", T3A_END),
                  ("T3B", T3B_END), ("T3C", T3C_END), ("CV", CV_END)]:
        assert v <= SB_BYTES, f"{nm} overflows SBUF: {v}"

    pbt = [nc.alloc_psum_tensor(f"pb{i}", [128, 512], F32) for i in range(8)]
    PS = PsumPool(P, pbt)
    PT = PS

    ctx = dict(locals())
    if s2_nwin is not None:
        ctx["S2_NWIN"] = s2_nwin

    emit_t0(ctx)
    if "s1" in stages or "l0" in stages:
        emit_s1(ctx)
    if "s2" in stages or "l0" in stages:
        emit_s2(ctx)
        if "nos34" not in stages:
            emit_s3(ctx)
            emit_s4(ctx)
    if "s5" in stages or "l0" in stages:
        emit_s5(ctx)
    if "s6" in stages or "l0" in stages:
        emit_s6(ctx)
    if "t1" in stages or "l1" in stages:
        emit_cv(ctx)
        emit_t1(ctx)
        emit_t2(ctx)
    if "t3a" in stages or "l1" in stages:
        emit_t3a(ctx)
    if "t3b" in stages or "l1" in stages:
        emit_t3b(ctx)
    if "t3c" in stages or "l1" in stages:
        emit_t3c(ctx)
    emit_dev_tail(ctx, stages, dev, dev_out)

    P.replay()
    return nc, outs


def emit_t0(c):
    P = c["P"]
    pairs = [(c["sb_vecs"], c["vecs"]), (c["sb_b1"], c["b1v"]),
             (c["sb_bqkv"], c["bqkvv"]), (c["sb_scl"], c["sclv"]),
             (c["sb_bkbc"], c["bk_bc"]), (c["sb_bvbc"], c["bv_bc"]),
             (c["sb_bbeta"], c["bbeta_bc"]), (c["sb_i64"], c["i64p"]),
             (c["sb_mln"], c["maskln"]), (c["sb_mu"], c["masku"]),
             (c["sb_idb"], c["identb"])]
    for dst, src_ in pairs:
        P.dma("sync", dst[:, :], src_[:, :], "cst")
    for dst, src_ in [(c["sb_ones"], c["onesbc"]), (c["sb_onesf"], c["onesbf"])]:
        P.dma("sync", dst[0:1, :], src_[0:1, :], "cst")
    for dst, src_ in [(c["sb_onescb"], c["onescb"]), (c["sb_onescf"], c["onescf"])]:
        P.dma("sync", dst[:, 0:1], src_[:, 0:1], "cst")
    eps = c["sb_eps"]
    for i, val in enumerate([1e-5, 1e-6, -0.5, 1.0 / D, -1.0, 1e-8, 0.0, 1.0]):
        P.op("vector", lambda v, i=i, val=val: v.memset(eps[:, i:i + 1], val))
    P.op("vector", lambda v: v.memset(c["sb_stl"][:, :], 0.0))
    P.op("vector", lambda v: v.memset(c["sb_lastc"][:, :], 0.0))
    P.op("vector", lambda v: v.memset(c["sb_zb"][:, :], 0.0))
    P.op("gpsimd", lambda g: g.memset(c["sb_magic"][0:1, 0:16], MAGIC1))
    P.op("gpsimd", lambda g: g.memset(c["sb_magic"][0:1, 16:32], MAGIC2))


def emit_s1(c):
    P, PS = c["P"], c["PS"]
    x_sl, statsH = c["x_sl"], c["statsH"]
    s1_x, s1_h1, s1_xsq = c["s1_x"], c["s1_h1"], c["s1_xsq"]
    s1_kt, s1_ksq, s1_rn, s1_st, s1_bc = (c["s1_kt"], c["s1_ksq"], c["s1_rn"],
                                          c["s1_st"], c["s1_bc"])
    s1_kst = c["s1_kst"]
    rows, eps, vecs = c["sb_rows"], c["sb_eps"], c["sb_vecs"]
    onescf, onescb, onesf = c["sb_onescf"], c["sb_onescb"], c["sb_onesf"]
    for w_h, w_s in [(c["wq"], c["s1_wq"]), (c["wk"], c["s1_wk"]),
                     (c["wv"], c["s1_wv"]), (c["wbeta"], c["s1_wbeta"])]:
        n = w_h.shape[1]
        for kb in range(NB):
            P.dma("gpsimd", w_s[:, kb * n:(kb + 1) * n],
                  w_h[kb * 128:(kb + 1) * 128, :], "wS1", chain=False)
    wS1 = P.now("wS1")

    prev_free = None
    for it in range(-1, NT):
        w = 1 if it < 0 else TT
        c0 = 0 if it < 0 else 1 + it * TT
        if prev_free is not None:
            P.wait("sync", prev_free[0], prev_free[1])
        for kb in range(NB):
            P.dma("sync", s1_x[:, kb * TT:kb * TT + w],
                  x_sl[kb * 128:(kb + 1) * 128, c0:c0 + w], "s1x",
                  **({"allow_slow_non_contiguous": True} if w == 1 else {}))
        xv = P.now("s1x")
        P.wait("scalar", "s1x", xv)
        for kb in range(NB):
            P.op("scalar", lambda e, kb=kb, w=w: e.activation(
                s1_xsq[:, kb * TT:kb * TT + w], s1_x[:, kb * TT:kb * TT + w],
                AF.Square))
        sqm = P.mark("scalar")
        P.wait("tensor", "cS", sqm)
        if it == -1:
            P.wait("tensor", "cst", P.now("cst"))
        p0 = PS.acquire(0, "tensor")
        p1 = PS.acquire(1, "tensor")
        for kb in range(NB):
            P.op("tensor", lambda e, kb=kb, w=w, p0=p0: e.matmul(
                p0[0:1, 0:w], onescf[:, 0:1],
                s1_x[:, kb * TT:kb * TT + w], start=(kb == 0), stop=(kb == NB - 1)),
                sig=(kb == NB - 1))
        for kb in range(NB):
            P.op("tensor", lambda e, kb=kb, w=w, p1=p1: e.matmul(
                p1[0:1, 0:w], onescb[:, 0:1],
                s1_xsq[:, kb * TT:kb * TT + w], start=(kb == 0), stop=(kb == NB - 1)),
                sig=(kb == NB - 1))
        mm1 = P.mark("tensor")
        r_mean = rowap(rows, 0, w)
        r_comb = rowap(rows, 512, 512 + w)
        r_rstd = rowap(rows, 1024, 1024 + w)
        r_t = rowap(rows, 1536, 1536 + w)
        P.wait("vector", "cT", mm1)
        P.wait("vector", "s1so", P.now("s1so"))
        P.op("vector", lambda v, p0=p0, r_mean=r_mean, w=w: v.tensor_scalar_mul(
            r_mean, p0[0:1, 0:w], eps[0:1, 3:4]))
        P.op("vector", lambda v, p1=p1, r_t=r_t, w=w: v.tensor_scalar_mul(
            r_t, p1[0:1, 0:w], eps[0:1, 3:4]))
        P.op("vector", lambda v, r_comb=r_comb, r_mean=r_mean: v.tensor_mul(
            r_comb, r_mean, r_mean))
        P.op("vector", lambda v, r_t=r_t, r_comb=r_comb: v.tensor_sub(
            r_t, r_t, r_comb))
        PS.release(0, "vector")
        PS.release(1, "vector")
        P.wait("scalar", "cV", P.mark("vector"))
        P.op("scalar", lambda e, r_comb=r_comb, r_t=r_t: e.activation(
            r_comb, r_t, AF.Ln, bias=eps[0:1, 0:1]))
        P.op("scalar", lambda e, r_rstd=r_rstd, r_comb=r_comb: e.activation(
            r_rstd, r_comb, AF.Exp, scale=-0.5))
        P.wait("vector", "cS", P.mark("scalar"))
        P.op("vector", lambda v, r_comb=r_comb, r_t=r_t: v.tensor_scalar_add(
            r_comb, r_t, eps[0:1, 0:1]))
        P.op("vector", lambda v, r_comb=r_comb: v.reciprocal(r_comb, r_comb))
        P.op("vector", lambda v, r_comb=r_comb, r_t=r_t: v.tensor_mul(
            r_comb, r_t, r_comb))
        P.wait("scalar", "cV", P.mark("vector"))
        P.op("scalar", lambda e, r_comb=r_comb: e.activation(
            r_comb, r_comb, AF.Ln, bias=eps[0:1, 1:2]))
        P.op("scalar", lambda e, r_comb=r_comb: e.activation(
            r_comb, r_comb, AF.Exp, scale=-0.5))
        P.wait("vector", "cS", P.mark("scalar"))
        P.op("vector", lambda v, r_comb=r_comb, r_rstd=r_rstd: v.tensor_mul(
            r_comb, r_comb, r_rstd))
        rv = P.mark("vector")
        P.wait("sync", "cV", rv)
        P.dma("sync", statsH[0:1, c0:c0 + w], rowap(rows, 0, w), "s1so",
              **({"allow_slow_non_contiguous": True} if w == 1 else {}))
        P.dma("sync", statsH[1:2, c0:c0 + w], rowap(rows, 1024, 1024 + w), "s1so",
              **({"allow_slow_non_contiguous": True} if w == 1 else {}))
        P.dma("sync", statsH[2:3, c0:c0 + w], rowap(rows, 512, 512 + w), "s1so",
              **({"allow_slow_non_contiguous": True} if w == 1 else {}))
        if it < 0:
            prev_free = ("cT", mm1)
            continue
        P.wait("tensor", "cV", rv)
        p2 = PS.acquire(2, "tensor")
        P.op("tensor", lambda e, p2=p2: e.matmul(
            p2[:, 0:TT], onesf[0:1, :], rowap(rows, 0, TT), start=True, stop=True),
            sig=False)
        p3 = PS.acquire(3, "tensor")
        P.op("tensor", lambda e, p3=p3: e.matmul(
            p3[:, 0:TT], onesf[0:1, :], rowap(rows, 512, 512 + TT),
            start=True, stop=True))
        P.wait("vector", "cT", P.mark("tensor"))
        P.op("vector", lambda v, p2=p2: v.tensor_copy(s1_bc[:, 0:TT], p2[:, 0:TT]))
        P.op("vector", lambda v, p3=p3: v.tensor_copy(s1_bc[:, TT:2 * TT], p3[:, 0:TT]))
        PS.release(2, "vector")
        PS.release(3, "vector")
        for kb in range(NB):
            P.op("vector", lambda v, kb=kb: v.tensor_sub(
                s1_x[:, kb * TT:(kb + 1) * TT], s1_x[:, kb * TT:(kb + 1) * TT],
                s1_bc[:, 0:TT]))
            P.op("vector", lambda v, kb=kb: v.scalar_tensor_tensor(
                s1_h1[:, kb * TT:(kb + 1) * TT], s1_x[:, kb * TT:(kb + 1) * TT],
                vecs[:, 40 + kb:41 + kb], s1_bc[:, TT:2 * TT], ALU.mult, ALU.mult))
        h1v = P.mark("vector")
        prev_free = ("cV", h1v)
        P.wait("tensor", "cV", h1v)
        if it == 0:
            P.wait("tensor", "wS1", wS1)
        P.wait("scalar", "s1qo", P.now("s1qo"))
        for mb in range(NB):
            pi = 4 + (mb % 2)
            pq = PS.acquire(pi, "tensor")
            for kb in range(NB):
                P.op("tensor", lambda e, kb=kb, mb=mb, pq=pq: e.matmul(
                    pq[:, 0:TT],
                    c["s1_wq"][:, kb * D + mb * 128:kb * D + (mb + 1) * 128],
                    s1_h1[:, kb * TT:(kb + 1) * TT],
                    start=(kb == 0), stop=(kb == NB - 1)), sig=(kb == NB - 1))
            P.wait("scalar", "cT", P.mark("tensor"))
            P.op("scalar", lambda e, mb=mb, pq=pq: e.activation(
                s1_st[:, mb * TT:(mb + 1) * TT], pq[:, 0:TT], AF.Identity,
                bias=vecs[:, mb:mb + 1]))
            PS.release(pi, "scalar")
        P.wait("gpsimd", "cS", P.mark("scalar"))
        for mb in range(NB):
            P.dma("gpsimd", c["qT"][mb * 128:(mb + 1) * 128, it * TT:(it + 1) * TT],
                  s1_st[:, mb * TT:(mb + 1) * TT], "s1qo", chain=False)
        qov = P.now("s1qo")
        for g in range(4):
            def lh(kb, g=g):
                return s1_h1[:, kb * TT + g * 128:kb * TT + (g + 1) * 128]
            for cc in range(2):
                pi = 4 + cc
                pk = PS.acquire(pi, "tensor")
                for kb in range(NB):
                    P.op("tensor", lambda e, kb=kb, cc=cc, pk=pk, g=g: e.matmul(
                        pk[:, 0:TT], lh(kb, g),
                        c["s1_wk"][:, kb * D + cc * TT:kb * D + (cc + 1) * TT],
                        start=(kb == 0), stop=(kb == NB - 1)), sig=(kb == NB - 1))
                P.wait("vector", "cT", P.mark("tensor"))
                P.op("vector", lambda v, cc=cc, pk=pk: v.tensor_add(
                    s1_kt[:, cc * TT:(cc + 1) * TT], pk[:, 0:TT],
                    c["sb_bkbc"][:, cc * TT:(cc + 1) * TT]))
                PS.release(pi, "vector")
            P.wait("scalar", "cV", P.mark("vector"))
            P.op("scalar", lambda e: e.activation(s1_ksq[:, :], s1_kt[:, :],
                                                  AF.Square))
            P.wait("vector", "cS", P.mark("scalar"))
            P.op("vector", lambda v: v.tensor_reduce(
                s1_rn[:, 0:H], s1_ksq[:, :].rearrange("p (h d) -> p h d", h=H),
                AX, ALU.add))
            P.wait("scalar", "cV", P.mark("vector"))
            P.op("scalar", lambda e: e.activation(s1_rn[:, H:2 * H],
                                                  s1_rn[:, 0:H], AF.Ln))
            P.op("scalar", lambda e: e.activation(s1_rn[:, H:2 * H],
                                                  s1_rn[:, H:2 * H], AF.Exp,
                                                  scale=0.5))
            P.wait("vector", "cS", P.mark("scalar"))
            P.op("vector", lambda v: v.tensor_scalar_add(
                s1_rn[:, H:2 * H], s1_rn[:, H:2 * H], eps[:, 1:2]))
            P.op("vector", lambda v: v.reciprocal(s1_rn[:, H:2 * H],
                                                  s1_rn[:, H:2 * H]))
            P.wait("vector", "s1ko", P.now("s1ko"))
            P.op("vector", lambda v: v.tensor_tensor(
                s1_kst[:, 0:D].rearrange("p (h d) -> p h d", h=H),
                s1_kt[:, :].rearrange("p (h d) -> p h d", h=H),
                s1_rn[:, H:2 * H].rearrange("p (h o) -> p h o", o=1)
                .broadcast_to((128, H, HD)), ALU.mult))
            P.wait("gpsimd", "cV", P.mark("vector"))
            P.dma("gpsimd", c["ktokN"][it * TT + g * 128:it * TT + (g + 1) * 128, :],
                  s1_kst[:, 0:D], "s1ko", chain=False)
            for cc in range(2):
                pi = 4 + cc
                pv = PS.acquire(pi, "tensor")
                for kb in range(NB):
                    P.op("tensor", lambda e, kb=kb, cc=cc, pv=pv, g=g: e.matmul(
                        pv[:, 0:TT], lh(kb, g),
                        c["s1_wv"][:, kb * D + cc * TT:kb * D + (cc + 1) * TT],
                        start=(kb == 0), stop=(kb == NB - 1)), sig=(kb == NB - 1))
                P.wait("vector", "cT", P.mark("tensor"))
                P.wait("vector", "s1ko", P.now("s1ko"))
                P.op("vector", lambda v, cc=cc, pv=pv: v.tensor_add(
                    s1_kst[:, D + cc * TT:D + (cc + 1) * TT], pv[:, 0:TT],
                    c["sb_bvbc"][:, cc * TT:(cc + 1) * TT]))
                PS.release(pi, "vector")
            P.wait("gpsimd", "cV", P.mark("vector"))
            P.dma("gpsimd", c["vtok"][it * TT + g * 128:it * TT + (g + 1) * 128, :],
                  s1_kst[:, D:2 * D], "s1ko", chain=False)
            pbeta = PS.acquire(4, "tensor")
            for kb in range(NB):
                P.op("tensor", lambda e, kb=kb, pbeta=pbeta, g=g: e.matmul(
                    pbeta[:, 0:H], lh(kb, g),
                    c["s1_wbeta"][:, kb * H:(kb + 1) * H],
                    start=(kb == 0), stop=(kb == NB - 1)), sig=(kb == NB - 1))
            P.wait("vector", "cT", P.mark("tensor"))
            gi = it * 4 + g
            P.op("vector", lambda v, pbeta=pbeta, gi=gi: v.tensor_add(
                c["sb_beta"][:, gi * H:(gi + 1) * H], pbeta[:, 0:H],
                c["sb_bbeta"][:, :]))
            PS.release(4, "vector")
            P.wait("scalar", "cV", P.mark("vector"))
            P.op("scalar", lambda e, gi=gi: e.activation(
                c["sb_beta"][:, gi * H:(gi + 1) * H],
                c["sb_beta"][:, gi * H:(gi + 1) * H], AF.Sigmoid))
    P.barrier()


def emit_s2(c):
    P, PS, PT = c["P"], c["PS"], c["PT"]
    s2_qw, s2_kw, s2_vw, s2_kf = c["s2_qw"], c["s2_kw"], c["s2_vw"], c["s2_kf"]
    s2_m, s2_mt, s2_m2, s2_mt2 = c["s2_m"], c["s2_mt"], c["s2_m2"], c["s2_mt2"]
    s2_sacc, s2_mo = c["s2_sacc"], c["s2_mo"]
    s2_ab, s2_wts, s2_obs = c["s2_ab"], c["s2_wts"], c["s2_obs"]
    beta, stl, i64, mln, mu, idb = (c["sb_beta"], c["sb_stl"], c["sb_i64"],
                                    c["sb_mln"], c["sb_mu"], c["sb_idb"])
    qT, ktokN, vtok, OT, ABb, WTb = (c["qT"], c["ktokN"], c["vtok"], c["OT"],
                                     c["ABb"], c["WTb"])
    nwin = c.get("S2_NWIN", NT)
    for wi in range(nwin):
        P.wait("sync", "cT", P.mark("tensor"))
        P.wait("sync", "s2sp", P.now("s2sp"))
        for mb in range(NB):
            P.dma("sync", s2_qw[:, mb * TT:(mb + 1) * TT],
                  qT[mb * 128:(mb + 1) * 128, wi * TT:(wi + 1) * TT], "s2w")
        for g in range(4):
            P.dma("sync", s2_kw[:, g * D:(g + 1) * D],
                  ktokN[wi * TT + g * 128:wi * TT + (g + 1) * 128, :], "s2w")
            P.dma("sync", s2_vw[:, g * D:(g + 1) * D],
                  vtok[wi * TT + g * 128:wi * TT + (g + 1) * 128, :], "s2w")
        wv_ = P.now("s2w")
        P.wait("tensor", "s2w", wv_)
        P.wait("vector", "s2w", wv_)
        for cp in range(4):
            ci = wi * 4 + cp
            P.wait("tensor", "cV", P.mark("vector"))
            for pb in range(NB):
                pti = 6 + pb % 2
                pt = PT.acquire(pti, "tensor")
                P.op("tensor", lambda e, pb=pb, cp=cp, pt=pt: e.transpose(
                    pt.bitcast(BF16)[:, 0:128],
                    s2_kw[:, cp * D + pb * 128:cp * D + (pb + 1) * 128],
                    idb[:, :]))
                P.wait("vector", "cT", P.mark("tensor"))
                P.op("vector", lambda v, pb=pb, pt=pt: v.tensor_copy(
                    s2_kf[:, pb * CH:(pb + 1) * CH], pt.bitcast(BF16)[:, 0:128]))
                PT.release(pti, "vector")
            kfv = P.mark("vector")
            P.wait("tensor", "cV", kfv)
            for h in range(H):
                pb, bh = h // 2, (h % 2) * 64
                tp = (lambda r, cc: None if (r == 0 and cc == 0) else (r, cc))
                kf_h = s2_kf[bh:bh + 64, pb * CH:(pb + 1) * CH]
                q_h = s2_qw[bh:bh + 64, pb * TT + cp * 128:pb * TT + (cp + 1) * 128]
                ktok_h = s2_kw[:, cp * D + h * HD:cp * D + (h + 1) * HD]
                vtok_h = s2_vw[:, cp * D + h * HD:cp * D + (h + 1) * HD]
                bcol = beta[:, ci % 32 * H + h:ci % 32 * H + h + 1] if False else \
                    beta[:, (ci % 32) * H + h:(ci % 32) * H + h + 1]
                bcol = beta[:, ci * H + h:ci * H + h + 1]
                p0 = PS.acquire(0, "tensor")
                P.op("tensor", lambda e, kf_h=kf_h, p0=p0, bh=bh: e.matmul(
                    p0[:, 0:CH], kf_h, kf_h, start=True, stop=True,
                    tile_position=tp(bh, 0)))
                P.wait("vector", "cT", P.mark("tensor"))
                P.op("vector", lambda v, p0=p0, bcol=bcol: v.scalar_tensor_tensor(
                    s2_m[:, :], p0[:, 0:CH], bcol, mln[:, :], ALU.mult, ALU.mult))
                PS.release(0, "vector")
                P.wait("tensor", "cV", P.mark("vector"))
                pt0 = PT.acquire(6, "tensor")
                P.op("tensor", lambda e, pt0=pt0: e.transpose(
                    pt0.bitcast(BF16)[:, 0:128], s2_m[:, :], idb[:, :]))
                P.wait("vector", "cT", P.mark("tensor"))
                P.op("vector", lambda v, pt0=pt0: v.tensor_copy(
                    s2_mt[:, :], pt0.bitcast(BF16)[:, 0:128]))
                PT.release(6, "vector")
                P.op("vector", lambda v, vtok_h=vtok_h, bcol=bcol: v.tensor_scalar_mul(
                    s2_sacc[:, 0:64], vtok_h, bcol))
                P.op("vector", lambda v, ktok_h=ktok_h, bcol=bcol: v.tensor_scalar_mul(
                    s2_sacc[:, 64:128], ktok_h, bcol))
                mbufs = [(s2_m, s2_mt), (s2_m2, s2_mt2)]
                for j in range(7):
                    mj, mtj = mbufs[j % 2]
                    mj1, mtj1 = mbufs[(j + 1) % 2]
                    P.wait("tensor", "cV", P.mark("vector"))
                    p1 = PS.acquire(1 + (j % 2), "tensor")
                    P.op("tensor", lambda e, mtj=mtj, p1=p1: e.matmul(
                        p1[:, 0:CH], mtj[:, :], s2_sacc[:, :],
                        start=True, stop=True))
                    if j < 6:
                        p3 = PS.acquire(3, "tensor")
                        P.op("tensor", lambda e, mtj=mtj, mj=mj, p3=p3: e.matmul(
                            p3[:, 0:CH], mtj[:, :], mj[:, :],
                            start=True, stop=True))
                        p4 = PS.acquire(4, "tensor")
                        P.op("tensor", lambda e, mj=mj, mtj=mtj, p4=p4: e.matmul(
                            p4[:, 0:CH], mj[:, :], mtj[:, :],
                            start=True, stop=True))
                    mmk = P.mark("tensor")
                    P.wait("vector", "cT", mmk)
                    P.op("vector", lambda v, p1=p1: v.tensor_add(
                        s2_sacc[:, :], s2_sacc[:, :], p1[:, 0:CH]))
                    PS.release(1 + (j % 2), "vector")
                    if j < 6:
                        P.op("vector", lambda v, mj1=mj1, p3=p3: v.tensor_copy(
                            mj1[:, :], p3[:, 0:CH]))
                        PS.release(3, "vector")
                        P.op("vector", lambda v, mtj1=mtj1, p4=p4: v.tensor_copy(
                            mtj1[:, :], p4[:, 0:CH]))
                        PS.release(4, "vector")
                P.wait("tensor", "cV", P.mark("vector"))
                p5 = PS.acquire(5, "tensor")
                P.op("tensor", lambda e, ktok_h=ktok_h, p5=p5, bh=bh: e.matmul(
                    p5[bh:bh + 64, 0:64], s2_sacc[:, 64:128], ktok_h,
                    start=True, stop=True, tile_position=tp(0, bh)), sig=False)
                P.op("tensor", lambda e, ktok_h=ktok_h, p5=p5, bh=bh: e.matmul(
                    p5[bh:bh + 64, 64:128], ktok_h, s2_sacc[:, 0:64],
                    start=True, stop=True, tile_position=tp(0, bh)))
                p0b = PS.acquire(0, "tensor")
                P.op("tensor", lambda e, kf_h=kf_h, q_h=q_h, p0b=p0b, bh=bh: e.matmul(
                    p0b[:, 0:CH], kf_h, q_h, start=True, stop=True,
                    tile_position=tp(bh, 0)))
                mmk = P.mark("tensor")
                P.wait("vector", "cT", mmk)
                P.op("vector", lambda v, p5=p5, bh=bh, pb=pb: v.tensor_sub(
                    s2_ab[bh:bh + 64, pb * 128:pb * 128 + 64],
                    i64[bh:bh + 64, :], p5[bh:bh + 64, 0:64]))
                P.op("vector", lambda v, p5=p5, bh=bh, pb=pb: v.tensor_copy(
                    s2_ab[bh:bh + 64, pb * 128 + 64:(pb + 1) * 128],
                    p5[bh:bh + 64, 64:128]))
                PS.release(5, "vector")
                P.op("vector", lambda v, p0b=p0b: v.tensor_mul(
                    s2_mo[:, :], p0b[:, 0:CH], mu[:, :]))
                PS.release(0, "vector")
                P.wait("tensor", "cV", P.mark("vector"))
                p1b = PS.acquire(1, "tensor")
                P.op("tensor", lambda e, p1b=p1b, bh=bh: e.matmul(
                    p1b[bh:bh + 64, 0:CH], s2_sacc[:, 0:64], s2_mo[:, :],
                    start=True, stop=True, tile_position=tp(0, bh)), sig=False)
                p2b = PS.acquire(2, "tensor")
                P.op("tensor", lambda e, p2b=p2b, bh=bh: e.matmul(
                    p2b[bh:bh + 64, 0:CH], s2_sacc[:, 64:128], s2_mo[:, :],
                    start=True, stop=True, tile_position=tp(0, bh)))
                a_sb = s2_ab[bh:bh + 64, pb * 128:pb * 128 + 64]
                bt_sb = s2_ab[bh:bh + 64, pb * 128 + 64:(pb + 1) * 128]
                st_sb = stl[bh:bh + 64, pb * 64:(pb + 1) * 64]
                P.wait("tensor", "cV", P.mark("vector"))
                p3b = PS.acquire(3, "tensor")
                P.op("tensor", lambda e, a_sb=a_sb, st_sb=st_sb, p3b=p3b, bh=bh:
                     e.matmul(p3b[bh:bh + 64, 0:64], a_sb, st_sb,
                              start=True, stop=True, tile_position=tp(bh, bh)))
                mmk = P.mark("tensor")
                P.wait("vector", "cT", mmk)
                P.op("vector", lambda v, p1b=p1b, bh=bh, pb=pb: v.tensor_copy(
                    s2_obs[bh:bh + 64, pb * CH:(pb + 1) * CH], p1b[bh:bh + 64, 0:CH]))
                PS.release(1, "vector")
                P.op("vector", lambda v, p2b=p2b, q_h=q_h, bh=bh, pb=pb: v.tensor_sub(
                    s2_wts[bh:bh + 64, pb * CH:(pb + 1) * CH], q_h,
                    p2b[bh:bh + 64, 0:CH]))
                PS.release(2, "vector")
                P.op("vector", lambda v, p3b=p3b, bt_sb=bt_sb, st_sb=st_sb, bh=bh:
                     v.tensor_add(st_sb, p3b[bh:bh + 64, 0:64], bt_sb))
                PS.release(3, "vector")
            sp = P.mark("vector")
            P.wait("gpsimd", "cV", sp)
            P.dma("gpsimd", ABb[ci, :, :, :].rearrange("p r c -> r p c"),
                  s2_ab[:, :].rearrange("q (p c) -> q p c", p=NB), "s2sp",
                  chain=False)
            P.dma("gpsimd", WTb[ci, :, :, :].rearrange("p r c -> r p c"),
                  s2_wts[:, :].rearrange("q (p c) -> q p c", p=NB), "s2sp",
                  chain=False)
            P.dma("gpsimd",
                  OT[:, ci * CH:(ci + 1) * CH].rearrange("(p r) c -> r p c", p=NB),
                  s2_obs[:, :].rearrange("q (p c) -> q p c", p=NB), "s2sp",
                  chain=False)
            P.wait("vector", "s2sp", P.now("s2sp"))
    P.barrier()


def emit_s3(c):
    P = c["P"]
    if "noxc" in c["stages"]:
        P.op("vector", lambda v: v.tensor_scalar_mul(
            c["sb_stc"][:, :], c["sb_stl"][:, :], c["sb_scl"][:, 3:4]))
        P.barrier()
        return
    nc, pid, partner = c["nc"], c["pid"], c["partner"]
    shst, shfl = c["shst"], c["shfl"]
    stl, stc = c["sb_stl"], c["sb_stc"]
    sb_flag, sb_magic, scl = c["sb_flag"], c["sb_magic"], c["sb_scl"]
    P.dma("gpsimd", shst[pid, :, :], stl[:, :], "s3a", chain=False)
    P.wait("gpsimd", "s3a", P.now("s3a"))
    P.dma("gpsimd", shfl[pid, 0, :], sb_magic[0:1, 0:16], "s3a", chain=False)
    P.wait("gpsimd", "s3a", P.now("s3a"))

    def poll(g):
        val = g.alloc_register("pollv")
        cnt = g.alloc_register("pollc")
        vals = bass.RegisterHandles((val,))
        cnts = bass.RegisterHandles((cnt,))
        nc.regs_mov(vals, 0)
        nc.regs_mov(cnts, P.now("s3p") + 16)
        loop, done = "s3_loop", "s3_done"
        nc.br(loop, engines=[ET.Pool])
        with nc.body(loop, valid_engines=[ET.Pool]):
            g.dma_start(out=sb_flag[:, :], in_=shfl[partner, 0, :][None, :]) \
                .then_inc(P.sems["s3p"], 16)
            g.wait_ge(P.sems["s3p"], cnt)
            nc.regs_alu(cnts, cnts, 16, op=ALU.add)
            nc.regs_load(vals, sb_flag[0:1, 0:1])
            nc.br_cmp(val, MAGIC1, on_true=done, on_false=loop,
                      comp_op="IS_EQ", engines=[ET.Pool])
        nc.switch_bb(done)
        return None
    P.sem("s3p")
    P.op("gpsimd", poll, sig=True, chain=False)
    P.dma("gpsimd", stc[:, :], shst[partner, :, :], "s3b", chain=False)
    P.wait("vector", "s3b", P.now("s3b"))
    P.op("vector", lambda v: v.tensor_scalar_mul(stc[:, :], stc[:, :],
                                                 scl[:, 3:4]))
    P.barrier()


def emit_s4(c):
    P, PS = c["P"], c["PS"]
    s4_ab, s4_wt, s4_o = c["s4_ab"], c["s4_wt"], c["s4_o"]
    stc = c["sb_stc"]
    OT, ABb, WTb = c["OT"], c["ABb"], c["WTb"]
    nch = c.get("S2_NWIN", NT) * 4
    for ci in range(nch):
        P.wait("sync", "s4o", P.now("s4o"))
        P.wait("sync", "cT", P.mark("tensor"))
        P.dma("sync", s4_ab[:, :].rearrange("q (p c) -> q p c", p=NB),
              ABb[ci, :, :, :].rearrange("p r c -> r p c"), "s4w")
        P.dma("sync", s4_wt[:, :].rearrange("q (p c) -> q p c", p=NB),
              WTb[ci, :, :, :].rearrange("p r c -> r p c"), "s4w")
        P.dma("sync", s4_o[:, :].rearrange("q (p c) -> q p c", p=NB),
              OT[:, ci * CH:(ci + 1) * CH].rearrange("(p r) c -> r p c", p=NB),
              "s4w")
        wv_ = P.now("s4w")
        P.wait("tensor", "s4w", wv_)
        P.wait("vector", "s4w", wv_)
        for h in range(H):
            pb, bh = h // 2, (h % 2) * 64
            tp = (lambda r, cc: None if (r == 0 and cc == 0) else (r, cc))
            st_sb = stc[bh:bh + 64, pb * 64:(pb + 1) * 64]
            a_sb = s4_ab[bh:bh + 64, pb * 128:pb * 128 + 64]
            bt_sb = s4_ab[bh:bh + 64, pb * 128 + 64:(pb + 1) * 128]
            wt_sb = s4_wt[bh:bh + 64, pb * CH:(pb + 1) * CH]
            p0 = PS.acquire(h % 2, "tensor")
            P.op("tensor", lambda e, st_sb=st_sb, wt_sb=wt_sb, p0=p0, bh=bh:
                 e.matmul(p0[bh:bh + 64, 0:CH], st_sb, wt_sb,
                          start=True, stop=True, tile_position=tp(bh, bh)))
            p1 = PS.acquire(2 + h % 2, "tensor")
            P.op("tensor", lambda e, a_sb=a_sb, st_sb=st_sb, p1=p1, bh=bh:
                 e.matmul(p1[bh:bh + 64, 0:64], a_sb, st_sb,
                          start=True, stop=True, tile_position=tp(bh, bh)))
            mmk = P.mark("tensor")
            P.wait("vector", "cT", mmk)
            P.op("vector", lambda v, p0=p0, bh=bh, pb=pb: v.tensor_add(
                s4_o[bh:bh + 64, pb * CH:(pb + 1) * CH],
                s4_o[bh:bh + 64, pb * CH:(pb + 1) * CH], p0[bh:bh + 64, 0:CH]))
            PS.release(h % 2, "vector")
            P.op("vector", lambda v, p1=p1, bt_sb=bt_sb, st_sb=st_sb, bh=bh:
                 v.tensor_add(st_sb, p1[bh:bh + 64, 0:64], bt_sb))
            PS.release(2 + h % 2, "vector")
        P.wait("gpsimd", "cV", P.mark("vector"))
        P.dma("gpsimd",
              OT[:, ci * CH:(ci + 1) * CH].rearrange("(p r) c -> r p c", p=NB),
              s4_o[:, :].rearrange("q (p c) -> q p c", p=NB), "s4o", chain=False)
    P.barrier()


def emit_s5(c):
    P, PS = c["P"], c["PS"]
    rows, eps, vecs = c["sb_rows"], c["sb_eps"], c["sb_vecs"]
    onescf, onescb, onesf = c["sb_onescf"], c["sb_onescb"], c["sb_onesf"]
    s5_ot, s5_att, s5_x, s5_xa = c["s5_ot"], c["s5_att"], c["s5_x"], c["s5_xa"]
    s5_sq, s5_h2, s5_bc = c["s5_sq"], c["s5_h2"], c["s5_bc"]
    OT, statsH, x_sl, xaTh, h2Th = (c["OT"], c["statsH"], c["x_sl"], c["xaTh"],
                                    c["h2Th"])
    for kb in range(NB):
        P.dma("gpsimd", c["s5_wo"][:, kb * D:(kb + 1) * D],
              c["wo"][kb * 128:(kb + 1) * 128, :], "wS5a", chain=False)
    P.wait("tensor", "wS5a", P.now("wS5a"))
    for it in range(NT):
        c0 = 1 + it * TT
        P.wait("sync", "cV", P.mark("vector"))
        P.wait("sync", "cS", P.mark("scalar"))
        for mb in range(NB):
            P.dma("sync", s5_ot[:, mb * TT:(mb + 1) * TT],
                  OT[mb * 128:(mb + 1) * 128, it * TT:(it + 1) * TT], "s5w")
            P.dma("sync", s5_x[:, mb * TT:(mb + 1) * TT],
                  x_sl[kb * 128:(kb + 1) * 128, c0:c0 + TT] if False else
                  x_sl[mb * 128:(mb + 1) * 128, c0:c0 + TT], "s5w")
        P.dma("sync", rowap(rows, 0, TT), statsH[0:1, c0:c0 + TT], "s5w")
        P.dma("sync", rowap(rows, 1024, 1024 + TT), statsH[1:2, c0:c0 + TT], "s5w")
        wv_ = P.now("s5w")
        P.wait("tensor", "s5w", wv_)
        p2 = PS.acquire(2, "tensor")
        P.op("tensor", lambda e, p2=p2: e.matmul(
            p2[:, 0:TT], onesf[0:1, :], rowap(rows, 0, TT), start=True, stop=True),
            sig=False)
        p3 = PS.acquire(3, "tensor")
        P.op("tensor", lambda e, p3=p3: e.matmul(
            p3[:, 0:TT], onesf[0:1, :], rowap(rows, 1024, 1024 + TT),
            start=True, stop=True))
        P.wait("vector", "cT", P.mark("tensor"))
        P.op("vector", lambda v, p2=p2: v.tensor_copy(s5_bc[:, 0:TT], p2[:, 0:TT]))
        P.op("vector", lambda v, p3=p3: v.tensor_copy(s5_bc[:, TT:2 * TT],
                                                      p3[:, 0:TT]))
        PS.release(2, "vector")
        PS.release(3, "vector")
        for mb in range(NB):
            pi = 4 + (mb % 2)
            pq = PS.acquire(pi, "tensor")
            for kb in range(NB):
                P.op("tensor", lambda e, kb=kb, mb=mb, pq=pq: e.matmul(
                    pq[:, 0:TT],
                    c["s5_wo"][:, kb * D + mb * 128:kb * D + (mb + 1) * 128],
                    s5_ot[:, kb * TT:(kb + 1) * TT],
                    start=(kb == 0), stop=(kb == NB - 1)), sig=(kb == NB - 1))
            P.wait("scalar", "cT", P.mark("tensor"))
            P.op("scalar", lambda e, mb=mb, pq=pq: e.activation(
                s5_att[:, mb * TT:(mb + 1) * TT], pq[:, 0:TT], AF.Identity,
                bias=vecs[:, 24 + mb:25 + mb]))
            PS.release(pi, "scalar")
        P.wait("vector", "cS", P.mark("scalar"))
        P.wait("vector", "s5o", P.now("s5o"))
        for mb in range(NB):
            sl = slice(mb * TT, (mb + 1) * TT)
            P.op("vector", lambda v, sl=sl: v.tensor_sub(
                s5_x[:, sl], s5_x[:, sl], s5_bc[:, 0:TT]))
            P.op("vector", lambda v, sl=sl: v.tensor_mul(
                s5_x[:, sl], s5_x[:, sl], s5_bc[:, TT:2 * TT]))
            P.op("vector", lambda v, sl=sl: v.tensor_add(
                s5_xa[:, sl], s5_x[:, sl], s5_att[:, sl]))
        xav = P.mark("vector")
        P.wait("gpsimd", "cV", xav)
        for mb in range(NB):
            P.dma("gpsimd", xaTh[mb * 128:(mb + 1) * 128, it * TT:(it + 1) * TT],
                  s5_xa[:, mb * TT:(mb + 1) * TT], "s5o", chain=False)
        P.wait("scalar", "cV", xav)
        for mb in range(NB):
            P.op("scalar", lambda e, mb=mb: e.activation(
                s5_sq[:, mb * TT:(mb + 1) * TT], s5_xa[:, mb * TT:(mb + 1) * TT],
                AF.Square))
        P.wait("tensor", "cS", P.mark("scalar"))
        p0 = PS.acquire(0, "tensor")
        for kb in range(NB):
            P.op("tensor", lambda e, kb=kb, p0=p0: e.matmul(
                p0[0:1, 0:TT], onescb[:, 0:1], s5_sq[:, kb * TT:(kb + 1) * TT],
                start=(kb == 0), stop=(kb == NB - 1)), sig=(kb == NB - 1))
        P.wait("vector", "cT", P.mark("tensor"))
        P.op("vector", lambda v, p0=p0: v.tensor_scalar_mul(
            rowap(rows, 1536, 1536 + TT), p0[0:1, 0:TT], eps[0:1, 3:4]))
        PS.release(0, "vector")
        P.wait("scalar", "cV", P.mark("vector"))
        P.op("scalar", lambda e: e.activation(rowap(rows, 1536, 1536 + TT),
                                              rowap(rows, 1536, 1536 + TT),
                                              AF.Ln, bias=eps[0:1, 1:2]))
        P.op("scalar", lambda e: e.activation(rowap(rows, 1536, 1536 + TT),
                                              rowap(rows, 1536, 1536 + TT),
                                              AF.Exp, scale=-0.5))
        P.wait("tensor", "cS", P.mark("scalar"))
        p1 = PS.acquire(1, "tensor")
        P.op("tensor", lambda e, p1=p1: e.matmul(
            p1[:, 0:TT], onesf[0:1, :], rowap(rows, 1536, 1536 + TT),
            start=True, stop=True))
        P.wait("vector", "cT", P.mark("tensor"))
        P.op("vector", lambda v, p1=p1: v.tensor_copy(s5_bc[:, 2 * TT:3 * TT],
                                                      p1[:, 0:TT]))
        PS.release(1, "vector")
        P.wait("vector", "s5o", P.now("s5o"))
        for mb in range(NB):
            P.op("vector", lambda v, mb=mb: v.scalar_tensor_tensor(
                s5_h2[:, mb * TT:(mb + 1) * TT], s5_xa[:, mb * TT:(mb + 1) * TT],
                vecs[:, 48 + mb:49 + mb], s5_bc[:, 2 * TT:3 * TT],
                ALU.mult, ALU.mult))
        P.wait("gpsimd", "cV", P.mark("vector"))
        for mb in range(NB):
            P.dma("gpsimd", h2Th[mb * 128:(mb + 1) * 128, it * TT:(it + 1) * TT],
                  s5_h2[:, mb * TT:(mb + 1) * TT], "s5o", chain=False)
    P.barrier()
    s5_h2b, s5_hf, s5_gt = c["s5_h2b"], c["s5_hf"], c["s5_gt"]
    s5_xab, s5_s, sb_b1 = c["s5_xab"], c["s5_s"], c["sb_b1"]
    sTh = c["sTh"]
    for kb in range(NB):
        P.dma("gpsimd", c["s5_w1"][:, kb * DFF:(kb + 1) * DFF],
              c["w1"][kb * 128:(kb + 1) * 128, :], "wS5b", chain=False)
    for ko in range(32):
        P.dma("gpsimd", c["s5_w2"][:, ko * D:(ko + 1) * D],
              c["w2"][ko * 128:(ko + 1) * 128, :], "wS5b", chain=False)
    P.wait("tensor", "wS5b", P.now("wS5b"))
    for it in range(NT):
        P.wait("sync", "cT", P.mark("tensor"))
        for mb in range(NB):
            P.dma("sync", s5_h2b[:, mb * TT:(mb + 1) * TT],
                  h2Th[mb * 128:(mb + 1) * 128, it * TT:(it + 1) * TT], "s5bw")
        wv_ = P.now("s5bw")
        P.wait("tensor", "s5bw", wv_)
        P.wait("vector", "cT", P.mark("tensor"))
        P.wait("scalar", "cT", P.mark("tensor"))
        for mo in range(32):
            pi = 4 + (mo % 2)
            pq = PS.acquire(pi, "tensor")
            for kb in range(NB):
                P.op("tensor", lambda e, kb=kb, mo=mo, pq=pq: e.matmul(
                    pq[:, 0:TT],
                    c["s5_w1"][:, kb * DFF + mo * 128:kb * DFF + (mo + 1) * 128],
                    s5_h2b[:, kb * TT:(kb + 1) * TT],
                    start=(kb == 0), stop=(kb == NB - 1)), sig=(kb == NB - 1))
            P.wait("scalar", "cT", P.mark("tensor"))
            P.wait("scalar", "cV", P.mark("vector"))
            t0 = s5_gt[:, 0:TT]
            t1 = s5_gt[:, TT:2 * TT]
            P.op("scalar", lambda e, mo=mo, pq=pq, t0=t0: e.activation(
                t0, pq[:, 0:TT], AF.Identity, bias=sb_b1[:, mo:mo + 1]))
            PS.release(pi, "scalar")
            P.op("scalar", lambda e, t0=t0, t1=t1: e.activation(t1, t0, AF.Square))
            P.wait("vector", "cS", P.mark("scalar"))
            P.op("vector", lambda v, t1=t1: v.tensor_scalar(
                t1, t1, 0.044715, 1.0, ALU.mult, ALU.add))
            P.op("vector", lambda v, t0=t0, t1=t1: v.tensor_mul(t1, t0, t1))
            P.wait("scalar", "cV", P.mark("vector"))
            P.op("scalar", lambda e, t1=t1: e.activation(
                t1, t1, AF.Tanh, scale=0.7978845608028654))
            P.wait("vector", "cS", P.mark("scalar"))
            P.op("vector", lambda v, t1=t1: v.tensor_scalar(
                t1, t1, 0.5, 0.5, ALU.mult, ALU.add))
            P.op("vector", lambda v, mo=mo, t0=t0, t1=t1: v.tensor_mul(
                s5_hf[:, mo * TT:(mo + 1) * TT], t0, t1))
        hfv = P.mark("vector")
        P.wait("tensor", "cV", hfv)
        for mb in range(NB):
            pi = 4 + (mb % 2)
            pq = PS.acquire(pi, "tensor")
            for ko in range(32):
                P.op("tensor", lambda e, ko=ko, mb=mb, pq=pq: e.matmul(
                    pq[:, 0:TT],
                    c["s5_w2"][:, ko * D + mb * 128:ko * D + (mb + 1) * 128],
                    s5_hf[:, ko * TT:(ko + 1) * TT],
                    start=(ko == 0), stop=(ko == 31)), sig=(ko == 31))
            P.dma("sync", s5_xab[:, 0:TT],
                  xaTh[mb * 128:(mb + 1) * 128, it * TT:(it + 1) * TT], "s5bx",
                  wait=[("cV", P.mark("vector"))])
            P.wait("scalar", "cT", P.mark("tensor"))
            P.wait("scalar", "s5so", P.now("s5so"))
            P.op("scalar", lambda e, mb=mb, pq=pq: e.activation(
                s5_s[:, 0:TT], pq[:, 0:TT], AF.Identity,
                bias=vecs[:, 32 + mb:33 + mb]))
            PS.release(pi, "scalar")
            P.wait("vector", "cS", P.mark("scalar"))
            P.wait("vector", "s5bx", P.now("s5bx"))
            P.op("vector", lambda v: v.tensor_add(s5_s[:, 0:TT], s5_s[:, 0:TT],
                                                  s5_xab[:, 0:TT]))
            P.wait("gpsimd", "cV", P.mark("vector"))
            P.dma("gpsimd", sTh[mb * 128:(mb + 1) * 128, it * TT:(it + 1) * TT],
                  s5_s[:, 0:TT], "s5so", chain=False)
    P.barrier()
    if "s5" in c["stages"] and c["dev"]:
        P.dma("sync", c["dev_out"]["d_s"][:, :], sTh[:, :], "dev")
        P.dma("sync", c["dev_out"]["d_xa"][:, :], xaTh[:, :], "dev")
        P.dma("sync", c["dev_out"]["d_h2"][:, :], h2Th[:, :], "dev")
        P.wait("sync", "dev", P.now("dev"))


def emit_s6(c):
    P, PS = c["P"], c["PS"]
    rows, eps, vecs = c["sb_rows"], c["sb_eps"], c["sb_vecs"]
    onescf, onescb, onesf = c["sb_onescf"], c["sb_onescb"], c["sb_onesf"]
    lastc = c["sb_lastc"]
    s6_x, s6_xn, s6_xr, s6_xk = c["s6_x"], c["s6_xn"], c["s6_xr"], c["s6_xk"]
    s6_r, s6_kk, s6_sld, s6_g = c["s6_r"], c["s6_kk"], c["s6_sld"], c["s6_g"]
    s6_x1, s6_sq, s6_bc, s6_xn1 = c["s6_x1"], c["s6_sq"], c["s6_bc"], c["s6_xn1"]
    x_sl, statsH, stats1H, sTh = c["x_sl"], c["statsH"], c["stats1H"], c["sTh"]
    x1T, xn1T = c["x1T"], c["xn1T_t"]
    WB = TT + 8
    for w_h, w_s in [(c["wrT"], c["s6_wr"]), (c["wkT"], c["s6_wkk"]),
                     (c["wvT"], c["s6_wvv"])]:
        for kb in range(NB):
            P.dma("gpsimd", w_s[:, kb * D:(kb + 1) * D],
                  w_h[kb * 128:(kb + 1) * 128, :], "wS6", chain=False)
    P.wait("tensor", "wS6", P.now("wS6"))
    P.dma("sync", rowap(rows, 0, 8), statsH[0:1, 0:8], "s6w",
          allow_slow_non_contiguous=True)
    P.dma("sync", rowap(rows, 1024, 1032), statsH[1:2, 0:8], "s6w",
          allow_slow_non_contiguous=True)
    for mb in range(NB):
        P.dma("sync", s6_x[:, mb * TT:mb * TT + 1],
              x_sl[mb * 128:(mb + 1) * 128, 0:1], "s6w",
              allow_slow_non_contiguous=True)
    P.wait("tensor", "s6w", P.now("s6w"))
    ph = PS.acquire(0, "tensor")
    P.op("tensor", lambda e, ph=ph: e.matmul(
        ph[:, 0:1], onesf[0:1, :], rowap(rows, 0, 1), start=True, stop=True),
        sig=False)
    P.op("tensor", lambda e, ph=ph: e.matmul(
        ph[:, 1:2], onesf[0:1, :], rowap(rows, 1024, 1025), start=True, stop=True))
    P.wait("vector", "cT", P.mark("tensor"))
    for mb in range(NB):
        P.op("vector", lambda v, mb=mb, ph=ph: v.tensor_sub(
            lastc[:, mb:mb + 1], s6_x[:, mb * TT:mb * TT + 1], ph[:, 0:1]))
        P.op("vector", lambda v, mb=mb, ph=ph: v.tensor_mul(
            lastc[:, mb:mb + 1], lastc[:, mb:mb + 1], ph[:, 1:2]))
    PS.release(0, "vector")
    for it in range(NT):
        c0 = 1 + it * TT
        P.wait("sync", "cV", P.mark("vector"))
        P.wait("sync", "cT", P.mark("tensor"))
        P.wait("sync", "s6so", P.now("s6so"))
        for mb in range(NB):
            P.dma("sync", s6_x[:, mb * TT:(mb + 1) * TT],
                  x_sl[mb * 128:(mb + 1) * 128, c0:c0 + TT], "s6w")
        P.dma("sync", rowap(rows, 0, TT), statsH[0:1, c0:c0 + TT], "s6w")
        P.dma("sync", rowap(rows, 1024, 1024 + TT), statsH[1:2, c0:c0 + TT], "s6w")
        wv_ = P.now("s6w")
        P.wait("tensor", "s6w", wv_)
        P.wait("vector", "cT", P.mark("tensor"))
        p2 = PS.acquire(2, "tensor")
        P.op("tensor", lambda e, p2=p2: e.matmul(
            p2[:, 0:TT], onesf[0:1, :], rowap(rows, 0, TT), start=True, stop=True),
            sig=False)
        p3 = PS.acquire(3, "tensor")
        P.op("tensor", lambda e, p3=p3: e.matmul(
            p3[:, 0:TT], onesf[0:1, :], rowap(rows, 1024, 1024 + TT),
            start=True, stop=True))
        P.wait("vector", "cT", P.mark("tensor"))
        P.op("vector", lambda v, p2=p2: v.tensor_copy(s6_bc[:, 0:TT], p2[:, 0:TT]))
        P.op("vector", lambda v, p3=p3: v.tensor_copy(s6_bc[:, TT:2 * TT],
                                                      p3[:, 0:TT]))
        PS.release(2, "vector")
        PS.release(3, "vector")
        for mb in range(NB):
            xb = slice(mb * WB + 1, mb * WB + 1 + TT)
            P.op("vector", lambda v, mb=mb: v.tensor_copy(
                s6_xn[:, mb * WB:mb * WB + 1], lastc[:, mb:mb + 1]))
            P.op("vector", lambda v, mb=mb, xb=xb: v.tensor_sub(
                s6_x[:, mb * TT:(mb + 1) * TT], s6_x[:, mb * TT:(mb + 1) * TT],
                s6_bc[:, 0:TT]))
            P.op("vector", lambda v, mb=mb, xb=xb: v.tensor_tensor(
                s6_xn[:, xb], s6_x[:, mb * TT:(mb + 1) * TT],
                s6_bc[:, TT:2 * TT], ALU.mult))
            P.op("vector", lambda v, mb=mb: v.tensor_copy(
                lastc[:, mb:mb + 1], s6_xn[:, mb * WB + TT:mb * WB + TT + 1]))
        for mb in range(NB):
            cur = s6_xn[:, mb * WB + 1:mb * WB + 1 + TT]
            prv = s6_xn[:, mb * WB:mb * WB + TT]
            dif = s6_g[:, 0:TT]
            P.op("vector", lambda v, cur=cur, prv=prv, dif=dif: v.tensor_sub(
                dif, cur, prv))
            P.op("vector", lambda v, mb=mb, dif=dif, prv=prv: v.scalar_tensor_tensor(
                s6_xr[:, mb * TT:(mb + 1) * TT], dif, vecs[:, 56 + mb:57 + mb],
                prv, ALU.mult, ALU.add))
            P.op("vector", lambda v, mb=mb, dif=dif, prv=prv: v.scalar_tensor_tensor(
                s6_xk[:, mb * TT:(mb + 1) * TT], dif, vecs[:, 64 + mb:65 + mb],
                prv, ALU.mult, ALU.add))
        mixv = P.mark("vector")
        P.wait("tensor", "cV", mixv)
        for mo in range(NB):
            pi = 4 + (mo % 2)
            pq = PS.acquire(pi, "tensor")
            for kb in range(NB):
                P.op("tensor", lambda e, kb=kb, mo=mo, pq=pq: e.matmul(
                    pq[:, 0:TT],
                    c["s6_wr"][:, kb * D + mo * 128:kb * D + (mo + 1) * 128],
                    s6_xr[:, kb * TT:(kb + 1) * TT],
                    start=(kb == 0), stop=(kb == NB - 1)), sig=(kb == NB - 1))
            P.wait("scalar", "cT", P.mark("tensor"))
            P.op("scalar", lambda e, mo=mo, pq=pq: e.activation(
                s6_r[:, mo * TT:(mo + 1) * TT], pq[:, 0:TT], AF.Sigmoid))
            PS.release(pi, "scalar")
        for mo in range(NB):
            pi = 4 + (mo % 2)
            pq = PS.acquire(pi, "tensor")
            for kb in range(NB):
                P.op("tensor", lambda e, kb=kb, mo=mo, pq=pq: e.matmul(
                    pq[:, 0:TT],
                    c["s6_wkk"][:, kb * D + mo * 128:kb * D + (mo + 1) * 128],
                    s6_xk[:, kb * TT:(kb + 1) * TT],
                    start=(kb == 0), stop=(kb == NB - 1)), sig=(kb == NB - 1))
            P.wait("scalar", "cT", P.mark("tensor"))
            P.op("scalar", lambda e, pq=pq: e.activation(
                s6_g[:, 0:TT], pq[:, 0:TT], AF.Relu))
            PS.release(pi, "scalar")
            P.op("scalar", lambda e, mo=mo: e.activation(
                s6_kk[:, mo * TT:(mo + 1) * TT], s6_g[:, 0:TT], AF.Square))
        kkv = P.mark("scalar")
        P.wait("tensor", "cS", kkv)
        P.wait("sync", "cV", P.mark("vector"))
        for mo in range(NB):
            P.dma("sync", s6_sld[:, mo * TT:(mo + 1) * TT],
                  sTh[mo * 128:(mo + 1) * 128, it * TT:(it + 1) * TT], "s6s")
        for mo in range(NB):
            pi = 4 + (mo % 2)
            pq = PS.acquire(pi, "tensor")
            for kb in range(NB):
                P.op("tensor", lambda e, kb=kb, mo=mo, pq=pq: e.matmul(
                    pq[:, 0:TT],
                    c["s6_wvv"][:, kb * D + mo * 128:kb * D + (mo + 1) * 128],
                    s6_kk[:, kb * TT:(kb + 1) * TT],
                    start=(kb == 0), stop=(kb == NB - 1)), sig=(kb == NB - 1))
            P.wait("vector", "cT", P.mark("tensor"))
            P.op("vector", lambda v, mo=mo, pq=pq: v.tensor_mul(
                s6_g[:, 0:TT], pq[:, 0:TT], s6_r[:, mo * TT:(mo + 1) * TT]))
            PS.release(pi, "vector")
            P.wait("scalar", "cV", P.mark("vector"))
            P.op("scalar", lambda e: e.activation(
                s6_g[:, TT:2 * TT], s6_g[:, 0:TT], AF.Sigmoid))
            P.wait("vector", "cS", P.mark("scalar"))
            P.op("vector", lambda v: v.tensor_mul(
                s6_g[:, 0:TT], s6_g[:, 0:TT], s6_g[:, TT:2 * TT]))
            P.wait("vector", "s6s", P.now("s6s"))
            P.op("vector", lambda v, mo=mo: v.tensor_mul(
                s6_g[:, 0:TT], s6_g[:, 0:TT], s6_sld[:, mo * TT:(mo + 1) * TT]))
            P.op("vector", lambda v, mo=mo: v.tensor_add(
                s6_g[:, 0:TT], s6_g[:, 0:TT], s6_bc[:, 0:TT]))
            P.op("vector", lambda v, mo=mo: v.tensor_add(
                s6_x1[:, mo * TT:(mo + 1) * TT], s6_x[:, mo * TT:(mo + 1) * TT],
                s6_g[:, 0:TT]))
        x1v = P.mark("vector")
        P.wait("gpsimd", "cV", x1v)
        P.wait("gpsimd", "s6o", P.now("s6o"))
        for mb in range(NB):
            P.dma("gpsimd", x1T[mb * 128:(mb + 1) * 128, it * TT:(it + 1) * TT],
                  s6_x1[:, mb * TT:(mb + 1) * TT], "s6o", chain=False)
        P.wait("scalar", "cV", x1v)
        for mb in range(NB):
            P.op("scalar", lambda e, mb=mb: e.activation(
                s6_sq[:, mb * TT:(mb + 1) * TT], s6_x1[:, mb * TT:(mb + 1) * TT],
                AF.Square))
        P.wait("tensor", "cS", P.mark("scalar"))
        P.wait("tensor", "cV", x1v)
        p0 = PS.acquire(0, "tensor")
        p1 = PS.acquire(1, "tensor")
        for kb in range(NB):
            P.op("tensor", lambda e, kb=kb, p0=p0: e.matmul(
                p0[0:1, 0:TT], onescf[:, 0:1], s6_x1[:, kb * TT:(kb + 1) * TT],
                start=(kb == 0), stop=(kb == NB - 1)), sig=(kb == NB - 1))
        for kb in range(NB):
            P.op("tensor", lambda e, kb=kb, p1=p1: e.matmul(
                p1[0:1, 0:TT], onescb[:, 0:1], s6_sq[:, kb * TT:(kb + 1) * TT],
                start=(kb == 0), stop=(kb == NB - 1)), sig=(kb == NB - 1))
        P.wait("vector", "cT", P.mark("tensor"))
        P.wait("vector", "s6so", P.now("s6so"))
        r_m = rowap(rows, 1536, 1536 + TT)
        r_v = rowap(rows, 512, 512 + TT)
        P.op("vector", lambda v, p0=p0, r_m=r_m: v.tensor_scalar_mul(
            r_m, p0[0:1, 0:TT], eps[0:1, 3:4]))
        P.op("vector", lambda v, p1=p1, r_v=r_v: v.tensor_scalar_mul(
            r_v, p1[0:1, 0:TT], eps[0:1, 3:4]))
        PS.release(0, "vector")
        PS.release(1, "vector")
        P.op("vector", lambda v, r_m=r_m: v.tensor_mul(
            rowap(rows, 0, TT), r_m, r_m))
        P.op("vector", lambda v, r_v=r_v: v.tensor_sub(
            rowap(rows, 0, TT), r_v, rowap(rows, 0, TT)))
        P.wait("scalar", "cV", P.mark("vector"))
        P.op("scalar", lambda e: e.activation(
            rowap(rows, 0, TT), rowap(rows, 0, TT), AF.Ln,
            bias=eps[0:1, 0:1]))
        P.op("scalar", lambda e: e.activation(
            rowap(rows, 0, TT), rowap(rows, 0, TT), AF.Exp,
            scale=-0.5))
        scv = P.mark("scalar")
        P.wait("sync", "cS", scv)
        P.dma("sync", stats1H[0:1, c0:c0 + TT], r_m, "s6so")
        P.dma("sync", stats1H[1:2, c0:c0 + TT], rowap(rows, 0, TT), "s6so")
        P.wait("tensor", "cS", scv)
        p2b = PS.acquire(2, "tensor")
        P.op("tensor", lambda e, p2b=p2b: e.matmul(
            p2b[:, 0:TT], onesf[0:1, :], r_m, start=True, stop=True), sig=False)
        p3b = PS.acquire(3, "tensor")
        P.op("tensor", lambda e, p3b=p3b: e.matmul(
            p3b[:, 0:TT], onesf[0:1, :], rowap(rows, 0, TT),
            start=True, stop=True))
        P.wait("vector", "cT", P.mark("tensor"))
        P.op("vector", lambda v, p2b=p2b: v.tensor_copy(s6_bc[:, 0:TT],
                                                        p2b[:, 0:TT]))
        P.op("vector", lambda v, p3b=p3b: v.tensor_copy(s6_bc[:, TT:2 * TT],
                                                        p3b[:, 0:TT]))
        PS.release(2, "vector")
        PS.release(3, "vector")
        P.wait("vector", "s6xo", P.now("s6xo"))
        P.wait("vector", "s6o", P.now("s6o"))
        for mb in range(NB):
            P.op("vector", lambda v, mb=mb: v.tensor_sub(
                s6_x1[:, mb * TT:(mb + 1) * TT], s6_x1[:, mb * TT:(mb + 1) * TT],
                s6_bc[:, 0:TT]))
            P.op("vector", lambda v, mb=mb: v.tensor_tensor(
                s6_xn1[:, mb * TT:(mb + 1) * TT], s6_x1[:, mb * TT:(mb + 1) * TT],
                s6_bc[:, TT:2 * TT], ALU.mult))
        P.wait("gpsimd", "cV", P.mark("vector"))
        for mb in range(NB):
            P.dma("gpsimd", xn1T[mb * 128:(mb + 1) * 128, it * TT:(it + 1) * TT],
                  s6_xn1[:, mb * TT:(mb + 1) * TT], "s6xo", chain=False)
    P.barrier()


def emit_cv(c):
    P = c["P"]
    cv_a, cv_b, cv_o, eps = c["cv_a"], c["cv_b"], c["cv_o"], c["sb_eps"]
    for kb in range(NB):
        P.wait("sync", "cV", P.mark("vector"))
        P.dma("sync", cv_a[:, :], c["k0u"][kb * 128:(kb + 1) * 128, :], "cvw")
        P.dma("sync", cv_b[:, :], c["k0u"][D + kb * 128:D + (kb + 1) * 128, :], "cvw")
        P.wait("vector", "cvw", P.now("cvw"))
        P.wait("vector", "cvo", P.now("cvo"))
        P.op("vector", lambda v: v.tensor_sub(cv_o[:, :], cv_a[:, :], cv_b[:, :]))
        P.wait("gpsimd", "cV", P.mark("vector"))
        P.dma("gpsimd", c["d0T"][kb * 128:(kb + 1) * 128, :], cv_o[:, :], "cvo",
              chain=False)
    for src_t, dst_t, n in [(c["k1u"], c["w1eT"], 32), (c["k2u"], c["w2eT"], 32)]:
        w = src_t.shape[1]
        for kb in range(n):
            P.wait("sync", "cS", P.mark("scalar"))
            P.wait("sync", "cV", P.mark("vector"))
            P.dma("sync", cv_a[:, 0:w], src_t[kb * 128:(kb + 1) * 128, :], "cvw")
            P.wait("scalar", "cvw", P.now("cvw"))
            P.wait("scalar", "cvo", P.now("cvo"))
            P.op("scalar", lambda e, w=w: e.activation(
                cv_o[:, 0:w], cv_a[:, 0:w], AF.Identity, bias=eps[:, 4:5],
                scale=2.0))
            P.wait("gpsimd", "cS", P.mark("scalar"))
            P.dma("gpsimd", dst_t[kb * 128:(kb + 1) * 128, :], cv_o[:, 0:w], "cvo",
                  chain=False)
    P.barrier()


def emit_t1(c):
    P, PS = c["P"], c["PS"]
    t1_xn1, t1_q, t1_kt, t1_vt, t1_tmp = (c["t1_xn1"], c["t1_q"], c["t1_kt"],
                                          c["t1_vt"], c["t1_tmp"])
    bqkv, eps = c["sb_bqkv"], c["sb_eps"]
    xn1T, q1T = c["xn1T_t"], c["q1T"]
    for kb in range(NB):
        P.dma("gpsimd", c["t1_wqkv"][:, kb * 3 * D:(kb + 1) * 3 * D],
              c["wqkv"][kb * 128:(kb + 1) * 128, :], "wT1", chain=False)
    P.dma("sync", c["sb_bqkbc"][:, :], c["bqkv_k_bc"][:, :], "wT1s")
    P.dma("sync", c["sb_bqvbc"][:, :], c["bqkv_v_bc"][:, :], "wT1s")
    P.wait("tensor", "wT1", P.now("wT1"))
    P.wait("vector", "wT1", P.now("wT1"))
    P.wait("vector", "wT1s", P.now("wT1s"))
    P.op("vector", lambda v: v.memset(c["sb_kvloc"][:, :], 0.0))
    for it in range(NT):
        P.wait("sync", "cT", P.mark("tensor"))
        P.wait("sync", "cV", P.mark("vector"))
        P.wait("sync", "cS", P.mark("scalar"))
        for mb in range(NB):
            P.dma("sync", t1_xn1[:, mb * TT:(mb + 1) * TT],
                  xn1T[mb * 128:(mb + 1) * 128, it * TT:(it + 1) * TT], "t1w")
        wv_ = P.now("t1w")
        P.wait("tensor", "t1w", wv_)
        P.wait("scalar", "t1qo", P.now("t1qo"))
        P.wait("vector", "t1qo", P.now("t1qo"))
        for mo in range(NB):
            pi = mo % 2
            pq = PS.acquire(pi, "tensor")
            for kb in range(NB):
                P.op("tensor", lambda e, kb=kb, mo=mo, pq=pq: e.matmul(
                    pq[:, 0:TT],
                    c["t1_wqkv"][:, kb * 3 * D + mo * 128:kb * 3 * D + (mo + 1) * 128],
                    t1_xn1[:, kb * TT:(kb + 1) * TT],
                    start=(kb == 0), stop=(kb == NB - 1)), sig=(kb == NB - 1))
            P.wait("scalar", "cT", P.mark("tensor"))
            P.wait("scalar", "cV", P.mark("vector"))
            u = t1_tmp[:, 0:TT]
            w2_ = t1_tmp[:, TT:2 * TT]
            P.op("scalar", lambda e, mo=mo, pq=pq, u=u: e.activation(
                u, pq[:, 0:TT], AF.Identity, bias=bqkv[:, mo:mo + 1]))
            PS.release(pi, "scalar")
            P.wait("vector", "cS", P.mark("scalar"))
            P.op("vector", lambda v, u=u, w2_=w2_: v.tensor_scalar_min(w2_, u, 0.0))
            P.wait("scalar", "cV", P.mark("vector"))
            P.op("scalar", lambda e, w2_=w2_: e.activation(w2_, w2_, AF.Exp))
            P.wait("vector", "cS", P.mark("scalar"))
            P.op("vector", lambda v, u=u: v.tensor_scalar_max(u, u, 0.0))
            P.op("vector", lambda v, mo=mo, u=u, w2_=w2_: v.tensor_add(
                t1_q[:, mo * TT:(mo + 1) * TT], u, w2_))
        P.wait("gpsimd", "cV", P.mark("vector"))
        for mo in range(NB):
            P.dma("gpsimd", q1T[mo * 128:(mo + 1) * 128, it * TT:(it + 1) * TT],
                  t1_q[:, mo * TT:(mo + 1) * TT], "t1qo", chain=False)
        for g in range(4):
            def lh(kb, g=g):
                return t1_xn1[:, kb * TT + g * 128:kb * TT + (g + 1) * 128]
            for cc in range(2):
                pi = cc
                pk = PS.acquire(pi, "tensor")
                for kb in range(NB):
                    P.op("tensor", lambda e, kb=kb, cc=cc, pk=pk, g=g: e.matmul(
                        pk[:, 0:TT], lh(kb, g),
                        c["t1_wqkv"][:, kb * 3 * D + D + cc * TT:
                                     kb * 3 * D + D + (cc + 1) * TT],
                        start=(kb == 0), stop=(kb == NB - 1)), sig=(kb == NB - 1))
                P.wait("vector", "cT", P.mark("tensor"))
                P.wait("vector", "cS", P.mark("scalar"))
                u = t1_tmp[:, 0:TT]
                w2_ = t1_tmp[:, TT:2 * TT]
                P.op("vector", lambda v, cc=cc, pk=pk, u=u: v.tensor_add(
                    u, pk[:, 0:TT], c["sb_bqkbc"][:, cc * TT:(cc + 1) * TT]))
                PS.release(pi, "vector")
                P.op("vector", lambda v, u=u, w2_=w2_: v.tensor_scalar_min(w2_, u, 0.0))
                P.wait("scalar", "cV", P.mark("vector"))
                P.op("scalar", lambda e, w2_=w2_: e.activation(w2_, w2_, AF.Exp))
                P.wait("vector", "cS", P.mark("scalar"))
                P.op("vector", lambda v, u=u: v.tensor_scalar_max(u, u, 0.0))
                P.op("vector", lambda v, cc=cc, u=u, w2_=w2_: v.tensor_tensor(
                    t1_kt[:, cc * TT:(cc + 1) * TT], u, w2_, ALU.add))
            for cc in range(2):
                pi = cc
                pv = PS.acquire(pi, "tensor")
                for kb in range(NB):
                    P.op("tensor", lambda e, kb=kb, cc=cc, pv=pv, g=g: e.matmul(
                        pv[:, 0:TT], lh(kb, g),
                        c["t1_wqkv"][:, kb * 3 * D + 2 * D + cc * TT:
                                     kb * 3 * D + 2 * D + (cc + 1) * TT],
                        start=(kb == 0), stop=(kb == NB - 1)), sig=(kb == NB - 1))
                P.wait("vector", "cT", P.mark("tensor"))
                P.op("vector", lambda v, cc=cc, pv=pv: v.tensor_add(
                    t1_vt[:, cc * TT:(cc + 1) * TT], pv[:, 0:TT],
                    c["sb_bqvbc"][:, cc * TT:(cc + 1) * TT]))
                PS.release(pi, "vector")
            kvv = P.mark("vector")
            if c["dev"] and "d_ktall" in c["dev_out"]:
                P.wait("gpsimd", "cV", kvv)
                P.dma("gpsimd", c["dev_out"]["d_ktall"][it * 4 + g, :, :],
                      c["t1_kt"][:, :], "dktg", chain=False)
                P.wait("vector", "dktg", P.now("dktg"))
            P.wait("tensor", "cV", kvv)
            pkv0 = PS.acquire(4, "tensor")
            pkv1 = PS.acquire(5, "tensor")
            for h in range(H):
                p_ = h // 2
                bh = (h % 2) * 64
                pp = pkv0 if p_ < 4 else pkv1
                co = (p_ % 4) * 65
                P.op("tensor", lambda e, h=h, bh=bh, pp=pp, co=co:
                     e.matmul(pp[bh:bh + 64, co:co + 64],
                              t1_kt[:, h * 64:(h + 1) * 64],
                              t1_vt[:, h * 64:(h + 1) * 64],
                              start=True, stop=True,
                              tile_position=(0, bh) if bh else None), sig=False)
                P.op("tensor", lambda e, h=h, bh=bh, pp=pp, co=co:
                     e.matmul(pp[bh:bh + 64, co + 64:co + 65],
                              t1_kt[:, h * 64:(h + 1) * 64],
                              c["sb_onescb"][:, 0:1],
                              start=True, stop=True,
                              tile_position=(0, bh) if bh else None),
                     sig=(h == H - 1))
            P.wait("vector", "cT", P.mark("tensor"))
            P.op("vector", lambda v, pkv0=pkv0: v.tensor_add(
                c["sb_kvloc"][:, 0:260], c["sb_kvloc"][:, 0:260], pkv0[:, 0:260]))
            P.op("vector", lambda v, pkv1=pkv1: v.tensor_add(
                c["sb_kvloc"][:, 260:520], c["sb_kvloc"][:, 260:520],
                pkv1[:, 0:260]))
            PS.release(4, "vector")
            PS.release(5, "vector")
    P.barrier()


def emit_t2(c):
    P = c["P"]
    if "noxc" in c["stages"]:
        kvl, kvb, ks = c["sb_kvloc"], c["sb_kvb"], c["sb_ks"]
        P.op("vector", lambda v: v.tensor_copy(kvb[:, :], kvl[:, :]))
        P.op("vector", lambda v: v.memset(ks[:, :], 0.0))
        for h in range(H):
            bh = (h % 2) * 64
            P.op("vector", lambda v, h=h, bh=bh: v.tensor_copy(
                ks[bh:bh + 64, (h // 2) * H + h:(h // 2) * H + h + 1],
                kvb[bh:bh + 64, (h // 2) * 65 + 64:(h // 2) * 65 + 65]))
        P.barrier()
        return
    nc, pid, partner = c["nc"], c["pid"], c["partner"]
    shkv, shfl = c["shkv"], c["shfl"]
    kvl, kvp, kvb, ks = c["sb_kvloc"], c["sb_kvp"], c["sb_kvb"], c["sb_ks"]
    sb_flag, sb_magic = c["sb_flag"], c["sb_magic"]
    P.dma("gpsimd", shkv[pid, :, :], kvl[:, :], "t2a", chain=False)
    P.wait("gpsimd", "t2a", P.now("t2a"))
    P.dma("gpsimd", shfl[pid, 1, :], sb_magic[0:1, 16:32], "t2a", chain=False)
    P.wait("gpsimd", "t2a", P.now("t2a"))

    def poll(g):
        val = g.alloc_register("pollv2")
        cnt = g.alloc_register("pollc2")
        vals = bass.RegisterHandles((val,))
        cnts = bass.RegisterHandles((cnt,))
        nc.regs_mov(vals, 0)
        nc.regs_mov(cnts, 16)
        loop, done = "t2_loop", "t2_done"
        nc.br(loop, engines=[ET.Pool])
        with nc.body(loop, valid_engines=[ET.Pool]):
            g.dma_start(out=sb_flag[:, :], in_=shfl[partner, 1, :][None, :]) \
                .then_inc(P.sems["t2p"], 16)
            g.wait_ge(P.sems["t2p"], cnt)
            nc.regs_alu(cnts, cnts, 16, op=ALU.add)
            nc.regs_load(vals, sb_flag[0:1, 0:1])
            nc.br_cmp(val, MAGIC2, on_true=done, on_false=loop,
                      comp_op="IS_EQ", engines=[ET.Pool])
        nc.switch_bb(done)
        return None
    P.sem("t2p")
    P.op("gpsimd", poll, sig=True, chain=False)
    P.dma("gpsimd", kvp[:, :], shkv[partner, :, :], "t2b", chain=False)
    P.wait("vector", "t2b", P.now("t2b"))
    P.op("vector", lambda v: v.tensor_add(kvl[:, :], kvl[:, :], kvp[:, :]))
    P.op("vector", lambda v: v.tensor_copy(kvb[:, :], kvl[:, :]))
    P.op("vector", lambda v: v.memset(ks[:, :], 0.0))
    for h in range(H):
        bh = (h % 2) * 64
        P.op("vector", lambda v, h=h, bh=bh: v.tensor_copy(
            ks[bh:bh + 64, (h // 2) * H + h:(h // 2) * H + h + 1],
            kvb[bh:bh + 64, (h // 2) * 65 + 64:(h // 2) * 65 + 65]))
    P.barrier()


def emit_t3a(c):
    P, PS = c["P"], c["PS"]
    t3_q, t3_xn1, t3_z, t3_ob = c["t3_q"], c["t3_xn1"], c["t3_z"], c["t3_ob"]
    t3_hf, t3_tmp = c["t3_hf"], c["t3_tmp"]
    kvb, ks, eps, vecs, scl, sel = (c["sb_kvb"], c["sb_ks"], c["sb_eps"],
                                    c["sb_vecs"], c["sb_scl"], c["sb_sel"])
    q1T, xn1T, s1Th, h1fT = c["q1T"], c["xn1T_t"], c["s1Th"], c["h1fT"]
    P.dma("sync", c["sb_sel"][:, :], c["sel16"][:, :], "wT3s")
    P.wait("tensor", "wT3s", P.now("wT3s"))
    for kb in range(NB):
        P.dma("gpsimd", c["t3_wproj"][:, kb * D:(kb + 1) * D],
              c["wproj"][kb * 128:(kb + 1) * 128, :], "wT3a", chain=False)
        P.dma("gpsimd", c["t3_d0"][:, kb * DFF:(kb + 1) * DFF],
              c["d0T"][kb * 128:(kb + 1) * 128, :], "wT3a", chain=False)
    P.wait("tensor", "wT3a", P.now("wT3a"))
    for it in range(NT):
        P.wait("sync", "cT", P.mark("tensor"))
        P.wait("sync", "cV", P.mark("vector"))
        for mb in range(NB):
            P.dma("sync", t3_q[:, mb * TT:(mb + 1) * TT],
                  q1T[mb * 128:(mb + 1) * 128, it * TT:(it + 1) * TT], "t3w")
            P.dma("sync", t3_xn1[:, mb * TT:(mb + 1) * TT],
                  xn1T[mb * 128:(mb + 1) * 128, it * TT:(it + 1) * TT], "t3w")
        wv_ = P.now("t3w")
        P.wait("tensor", "t3w", wv_)
        p6 = PS.acquire(6, "tensor")
        for kb in range(NB):
            P.op("tensor", lambda e, kb=kb, p6=p6: e.matmul(
                p6[0:16, 0:TT], ks[:, kb * H:(kb + 1) * H],
                t3_q[:, kb * TT:(kb + 1) * TT],
                start=(kb == 0), stop=(kb == NB - 1)), sig=(kb == NB - 1))
        P.wait("vector", "cT", P.mark("tensor"))
        P.op("vector", lambda v, p6=p6: v.tensor_scalar_add(
            c["t3_den"][0:16, 0:TT], p6[0:16, 0:TT], eps[0:16, 5:6]))
        PS.release(6, "vector")
        P.op("vector", lambda v: v.reciprocal(c["t3_den"][0:16, 0:TT],
                                              c["t3_den"][0:16, 0:TT]))
        zv = P.mark("vector")
        for pb in range(NB):
            pi = 4 + pb % 2
            po = PS.acquire(pi, "tensor")
            for sub in range(2):
                h = 2 * pb + sub
                bh = sub * 64
                P.op("tensor", lambda e, h=h, bh=bh, po=po, pb=pb: e.matmul(
                    po[bh:bh + 64, 0:TT],
                    kvb[bh:bh + 64, pb * 65:pb * 65 + 64],
                    t3_q[bh:bh + 64, pb * TT:(pb + 1) * TT],
                    start=True, stop=True,
                    tile_position=(bh, bh) if bh else None), sig=(sub == 1))
            P.wait("tensor", "cV", zv)
            p6b = PS.acquire(6, "tensor")
            P.op("tensor", lambda e, pb=pb, p6b=p6b: e.matmul(
                p6b[:, 0:TT], sel[0:16, pb * 128:(pb + 1) * 128],
                c["t3_den"][0:16, 0:TT], start=True, stop=True))
            P.wait("vector", "cT", P.mark("tensor"))
            P.op("vector", lambda v, p6b=p6b: v.tensor_copy(t3_z[:, 0:TT],
                                                            p6b[:, 0:TT]))
            PS.release(6, "vector")
            P.op("vector", lambda v, pb=pb, po=po: v.tensor_mul(
                t3_ob[:, pb * TT:(pb + 1) * TT], po[:, 0:TT], t3_z[:, 0:TT]))
            PS.release(pi, "vector")
        obv = P.mark("vector")
        P.wait("tensor", "cV", obv)
        P.wait("scalar", "t3so", P.now("t3so"))
        for mo in range(NB):
            pi = mo % 2
            pq = PS.acquire(pi, "tensor")
            for kb in range(NB):
                P.op("tensor", lambda e, kb=kb, mo=mo, pq=pq: e.matmul(
                    pq[:, 0:TT],
                    c["t3_wproj"][:, kb * D + mo * 128:kb * D + (mo + 1) * 128],
                    t3_ob[:, kb * TT:(kb + 1) * TT],
                    start=(kb == 0), stop=(kb == NB - 1)), sig=(kb == NB - 1))
            P.wait("scalar", "cT", P.mark("tensor"))
            P.op("scalar", lambda e, mo=mo, pq=pq: e.activation(
                t3_tmp[:, TT:2 * TT], pq[:, 0:TT], AF.Identity,
                bias=vecs[:, 72 + mo:73 + mo]))
            PS.release(pi, "scalar")
            P.wait("gpsimd", "cS", P.mark("scalar"))
            P.dma("gpsimd", s1Th[mo * 128:(mo + 1) * 128, it * TT:(it + 1) * TT],
                  t3_tmp[:, TT:2 * TT], "t3so", chain=False)
            P.wait("scalar", "t3so", P.now("t3so"))
        P.wait("vector", "t3ho", P.now("t3ho"))
        for mo in range(32):
            pi = 2 + mo % 2
            pq = PS.acquire(pi, "tensor")
            for kb in range(NB):
                P.op("tensor", lambda e, kb=kb, mo=mo, pq=pq: e.matmul(
                    pq[:, 0:TT],
                    c["t3_d0"][:, kb * DFF + mo * 128:kb * DFF + (mo + 1) * 128],
                    t3_xn1[:, kb * TT:(kb + 1) * TT],
                    start=(kb == 0), stop=(kb == NB - 1)), sig=(kb == NB - 1))
            P.wait("vector", "cT", P.mark("tensor"))
            P.wait("vector", "cS", P.mark("scalar"))
            P.op("vector", lambda v, pq=pq: v.tensor_scalar_mul(
                t3_tmp[:, 0:TT], pq[:, 0:TT], scl[:, 0:1]))
            PS.release(pi, "vector")
            P.wait("scalar", "cV", P.mark("vector"))
            P.op("scalar", lambda e, mo=mo: e.activation(
                t3_hf[:, mo * TT:(mo + 1) * TT], t3_tmp[:, 0:TT], AF.Relu,
                scale=4.0))
        P.wait("gpsimd", "cS", P.mark("scalar"))
        for mo in range(32):
            P.dma("gpsimd", h1fT[mo * 128:(mo + 1) * 128, it * TT:(it + 1) * TT],
                  t3_hf[:, mo * TT:(mo + 1) * TT], "t3ho", chain=False)
    P.barrier()


def emit_t3b(c):
    P, PS = c["P"], c["PS"]
    t3_h1, t3_h2o, scl = c["t3_h1"], c["t3_h2o"], c["sb_scl"]
    t3_tmp = c["t3b_tmp"]
    h1fT, h2fT, w1eT = c["h1fT"], c["h2fT"], c["w1eT"]
    w1s = c["t3_w1s"]
    NSUP = 4
    STT = 1024
    for st in range(NSUP):
        P.wait("sync", "cT", P.mark("tensor"))
        for ko in range(32):
            P.dma("sync", t3_h1[:, ko * STT:(ko + 1) * STT],
                  h1fT[ko * 128:(ko + 1) * 128, st * STT:(st + 1) * STT], "t3bw")
        hv = P.now("t3bw")
        P.wait("tensor", "t3bw", hv)
        for mog in range(8):
            bufbase = (mog % 2) * 32
            if mog >= 2 or st > 0:
                P.wait("sync", "cT", P.mark("tensor"))
            sname = f"t3bs{mog % 2}"
            for ko in range(32):
                P.dma("sync", w1s[bufbase + ko][:, :],
                      w1eT[ko * 128:(ko + 1) * 128,
                           mog * 512:(mog + 1) * 512], sname)
            sv = P.now(sname)
            P.wait("tensor", sname, sv)
            pqs = [PS.acquire(i, "tensor") for i in range(8)]
            for ko in range(32):
                for mb in range(4):
                    for nh in range(2):
                        P.op("tensor", lambda e, ko=ko, mb=mb, nh=nh, mog=mog,
                             pqs=pqs, bufbase=bufbase: e.matmul(
                                 pqs[mb * 2 + nh][:, 0:TT],
                                 w1s[bufbase + ko][:, mb * 128:(mb + 1) * 128],
                                 t3_h1[:, ko * STT + nh * TT:ko * STT + (nh + 1) * TT],
                                 start=(ko == 0), stop=(ko == 31)),
                             sig=(ko == 31 and mb == 3 and nh == 1))
            mmk = P.mark("tensor")
            P.wait("vector", "cT", mmk)
            P.wait("vector", "t3bo", P.now("t3bo"))
            P.wait("scalar", "t3bo", P.now("t3bo"))
            for mb in range(4):
                for nh in range(2):
                    P.wait("vector", "cS", P.mark("scalar"))
                    P.op("vector", lambda v, mb=mb, nh=nh, pqs=pqs: v.tensor_scalar_mul(
                        t3_tmp[:, 0:TT], pqs[mb * 2 + nh][:, 0:TT], scl[:, 1:2]))
                    P.wait("scalar", "cV", P.mark("vector"))
                    P.op("scalar", lambda e, mb=mb, nh=nh, mog=mog: e.activation(
                        t3_h2o[:, (mog % 2) * 4096 + mb * STT + nh * TT:
                               (mog % 2) * 4096 + mb * STT + (nh + 1) * TT],
                        t3_tmp[:, 0:TT], AF.Relu, scale=2.0))
            for i in range(8):
                PS.release(i, "scalar")
            P.wait("gpsimd", "cS", P.mark("scalar"))
            for mb in range(4):
                P.dma("gpsimd",
                      h2fT[mog * 512 + mb * 128:mog * 512 + (mb + 1) * 128,
                           st * STT:(st + 1) * STT],
                      t3_h2o[:, (mog % 2) * 4096 + mb * STT:
                             (mog % 2) * 4096 + (mb + 1) * STT], "t3bo",
                      chain=False)
    P.barrier()


def emit_t3c(c):
    P, PS = c["P"], c["PS"]
    t3_hr, t3_g1, t3_sl, t3_x1l, t3_out = (c["t3_hr"], c["t3_g1"], c["t3_sl"],
                                           c["t3_x1l"], c["t3_out"])
    scl = c["sb_scl"]
    h2fT, w2eT, s1Th, x1T, out_sl = (c["h2fT"], c["w2eT"], c["s1Th"], c["x1T"],
                                     c["out_sl"])
    for ko in range(32):
        P.dma("gpsimd", c["t3_w2"][:, ko * D:(ko + 1) * D],
              w2eT[ko * 128:(ko + 1) * 128, :], "wT3c", chain=False)
    P.wait("tensor", "wT3c", P.now("wT3c"))
    for it in range(NT):
        P.wait("sync", "cT", P.mark("tensor"))
        P.wait("sync", "cV", P.mark("vector"))
        for ko in range(32):
            P.dma("sync", t3_hr[:, ko * TT:(ko + 1) * TT],
                  h2fT[ko * 128:(ko + 1) * 128, it * TT:(it + 1) * TT], "t3cw")
        for mb in range(NB):
            P.dma("sync", t3_sl[:, mb * TT:(mb + 1) * TT],
                  s1Th[mb * 128:(mb + 1) * 128, it * TT:(it + 1) * TT], "t3cw")
            P.dma("sync", t3_x1l[:, mb * TT:(mb + 1) * TT],
                  x1T[mb * 128:(mb + 1) * 128, it * TT:(it + 1) * TT], "t3cw")
        wv_ = P.now("t3cw")
        P.wait("tensor", "t3cw", wv_)
        P.wait("vector", "t3cw", wv_)
        for mo in range(NB):
            pi = mo % 2
            pq = PS.acquire(pi, "tensor")
            for ko in range(32):
                P.op("tensor", lambda e, ko=ko, mo=mo, pq=pq: e.matmul(
                    pq[:, 0:TT],
                    c["t3_w2"][:, ko * D + mo * 128:ko * D + (mo + 1) * 128],
                    t3_hr[:, ko * TT:(ko + 1) * TT],
                    start=(ko == 0), stop=(ko == 31)), sig=(ko == 31))
            P.wait("vector", "cT", P.mark("tensor"))
            P.op("vector", lambda v, pq=pq: v.tensor_scalar_mul(
                t3_g1[:, 0:TT], pq[:, 0:TT], scl[:, 2:3]))
            PS.release(pi, "vector")
            P.wait("scalar", "cV", P.mark("vector"))
            P.op("scalar", lambda e: e.activation(
                t3_g1[:, TT:2 * TT], t3_g1[:, 0:TT], AF.Sigmoid))
            P.wait("vector", "cS", P.mark("scalar"))
            P.op("vector", lambda v: v.tensor_mul(
                t3_g1[:, 0:TT], t3_g1[:, 0:TT], t3_g1[:, TT:2 * TT]))
            P.op("vector", lambda v, mo=mo: v.tensor_mul(
                t3_g1[:, 0:TT], t3_g1[:, 0:TT], t3_sl[:, mo * TT:(mo + 1) * TT]))
            P.wait("vector", "t3out", P.now("t3out"))
            P.op("vector", lambda v, mo=mo: v.tensor_add(
                t3_out[:, mo * TT:(mo + 1) * TT], t3_x1l[:, mo * TT:(mo + 1) * TT],
                t3_g1[:, 0:TT]))
            P.wait("gpsimd", "cV", P.mark("vector"))
            P.dma("gpsimd", out_sl[mo * 128:(mo + 1) * 128, it * TT:(it + 1) * TT],
                  t3_out[:, mo * TT:(mo + 1) * TT], "t3out", chain=False)
    P.wait("gpsimd", "t3out", P.now("t3out"))
    P.barrier()


def emit_dev_tail(c, stages, dev, dev_out):
    if not dev:
        return
    P = c["P"]
    do = dev_out
    nch = c.get("S2_NWIN", NT) * 4
    if "s2" in stages:
        P.wait("sync", "s2sp", P.now("s2sp"))
        P.wait("sync", "cV", P.mark("vector"))
        P.dma("sync", do["d_stl"][:, :], c["sb_stl"][:, :], "dev")
        P.dma("sync", do["d_ab"][0:nch], c["ABb"][0:nch], "dev")
        P.dma("sync", do["d_wt"][0:nch], c["WTb"][0:nch], "dev")
        if "s4" not in stages:
            P.dma("sync", do["d_ot"][:, 0:nch * CH], c["OT"][:, 0:nch * CH], "dev")
    if "s4" in stages:
        P.wait("sync", "s4o", P.now("s4o"))
        P.dma("sync", do["d_ot2"][:, 0:nch * CH], c["OT"][:, 0:nch * CH], "dev")
    if "t1" in stages:
        P.wait("sync", "t1qo", P.now("t1qo"))
        P.wait("sync", "cV", P.mark("vector"))
        P.dma("sync", do["d_kv"][:, :], c["sb_kvloc"][:, :], "dev")
        P.dma("sync", do["d_q1"][:, :], c["q1T"][:, :], "dev")
        P.dma("sync", do["d_kt"][:, :], c["t1_kt"][:, :], "dev")
        P.dma("sync", do["d_vt"][:, :], c["t1_vt"][:, :], "dev")
    if "t3a" in stages:
        P.wait("sync", "t3so", P.now("t3so"))
        P.wait("sync", "t3ho", P.now("t3ho"))
        P.wait("sync", "cV", P.mark("vector"))
        P.dma("sync", do["d_s1"][:, :], c["s1Th"][:, :], "dev")
        P.dma("sync", do["d_h1f"][:, :], c["h1fT"][:, :], "dev")
        P.dma("sync", do["d_ob"][:, :], c["t3_ob"][:, :], "dev")
        P.dma("sync", do["d_z"][:, :], c["t3_z"][:, :], "dev")
        P.dma("sync", do["d_zr"][:, :], c["t3_den"][:, 0:TT], "dev")
        P.dma("sync", do["d_ks"][:, :], c["sb_ks"][:, :], "dev")
        P.dma("sync", do["d_d0"][:, :].rearrange("(kb p) c -> p kb c", p=128),
              c["t3_d0"][:, :].rearrange("p (kb c) -> p kb c", kb=NB), "dev")
    if "t3b" in stages:
        P.wait("sync", "t3bo", P.now("t3bo"))
        P.dma("sync", do["d_h2f"][:, :], c["h2fT"][:, :], "dev")
    P.wait("sync", "dev", P.now("dev"))
    if "s1" in stages:
        P.wait("sync", "s1qo", P.now("s1qo"))
        P.wait("sync", "s1ko", P.now("s1ko"))
        P.wait("sync", "s1so", P.now("s1so"))
        P.wait("sync", "cS", P.mark("scalar"))
        P.dma("sync", do["d_q"][:, :], c["qT"][:, :], "dev")
        P.dma("sync", do["d_k"][:, :], c["ktokN"][:, :], "dev")
        P.dma("sync", do["d_v"][:, :], c["vtok"][:, :], "dev")
        P.dma("sync", do["d_beta"][:, :], c["sb_beta"][:, :], "dev")
        P.dma("sync", do["d_stats"][0:3, 0:TC + 1], c["statsH"][0:3, 0:TC + 1], "dev")
        P.wait("sync", "dev", P.now("dev"))


def stage_inputs(inputs, core):
    b, hf = core // 2, core % 2
    x = np.asarray(inputs["L0_attn_x"])
    t0 = hf * TC
    sl = np.empty((TC + 1, D), np.float32)
    sl[1:] = x[b, t0:t0 + TC]
    sl[0] = 0.0 if hf == 0 else x[b, t0 - 1]
    m = {}
    m["x_sl"] = np.ascontiguousarray(sl.T)
    for nm, key in [("wq", "L0_attn_wq"), ("wk", "L0_attn_wk"), ("wv", "L0_attn_wv"),
                    ("wbeta", "L0_attn_wbeta"), ("wo", "L0_attn_wo"),
                    ("w1", "L0_attn_w1"), ("w2", "L0_attn_w2"),
                    ("wqkv", "L1_attn_w_qkv"), ("wproj", "L1_attn_w_proj")]:
        m[nm] = np.asarray(inputs[key], np.float32)
    for nm, key in [("wrT", "L0_ffn_W_r"), ("wkT", "L0_ffn_W_k"), ("wvT", "L0_ffn_W_v")]:
        m[nm] = np.ascontiguousarray(np.asarray(inputs[key], np.float32).T)
    m["k0u"] = np.asarray(inputs["L1_ffn_k0"]).view(np.uint8)
    m["k1u"] = np.asarray(inputs["L1_ffn_k1"]).view(np.uint8)
    m["k2u"] = np.asarray(inputs["L1_ffn_k2"]).view(np.uint8)

    def cols(vec, n):
        v = np.asarray(vec, np.float32).reshape(n, 128)
        return np.ascontiguousarray(v.T)
    vec_list = [cols(inputs["L0_attn_bq"], 8), cols(inputs["L0_attn_bk"], 8),
                cols(inputs["L0_attn_bv"], 8), cols(inputs["L0_attn_bo"], 8),
                cols(inputs["L0_attn_b2"], 8), cols(inputs["L0_attn_ln1_scale"], 8),
                cols(inputs["L0_attn_ln2_scale"], 8), cols(inputs["L0_ffn_mu_r"], 8),
                cols(inputs["L0_ffn_mu_k"], 8), cols(inputs["L1_attn_b_proj"], 8)]
    m["vecs"] = np.concatenate(vec_list, axis=1)
    m["b1v"] = cols(inputs["L0_attn_b1"], 32)
    m["bqkvv"] = cols(inputs["L1_attn_b_qkv"], 24)
    scl = np.zeros((128, 4), np.float32)
    scl[:, 0] = float(np.asarray(inputs["L1_ffn_s0"]))
    scl[:, 1] = float(np.asarray(inputs["L1_ffn_s1"]))
    scl[:, 2] = float(np.asarray(inputs["L1_ffn_s2"]))
    scl[:, 3] = float(hf)
    m["sclv"] = scl
    m["bk_bc"] = np.broadcast_to(np.asarray(inputs["L0_attn_bk"], np.float32),
                                 (128, D)).copy()
    m["bv_bc"] = np.broadcast_to(np.asarray(inputs["L0_attn_bv"], np.float32),
                                 (128, D)).copy()
    bq1 = np.asarray(inputs["L1_attn_b_qkv"], np.float32)
    m["bqkv_k_bc"] = np.broadcast_to(bq1[D:2 * D], (128, D)).copy()
    m["bqkv_v_bc"] = np.broadcast_to(bq1[2 * D:], (128, D)).copy()
    m["bbeta_bc"] = np.broadcast_to(np.asarray(inputs["L0_attn_bbeta"], np.float32),
                                    (128, H)).copy()
    i64 = np.zeros((128, 64), np.float32)
    i64[:64] = np.eye(64)
    i64[64:] = np.eye(64)
    m["i64p"] = i64
    m["maskln"] = -np.tril(np.ones((128, CH), np.float32), -1)
    m["masku"] = np.triu(np.ones((128, CH), np.float32), 0)
    m["identb"] = np.eye(128, dtype=NPBF16)
    m["onesbc"] = np.ones((1, 128), NPBF16)
    m["onesbf"] = np.ones((1, 128), np.float32)
    m["onescb"] = np.ones((128, 1), NPBF16)
    m["onescf"] = np.ones((128, 1), np.float32)
    sel = np.zeros((16, 8 * 128), np.float32)
    for pb in range(8):
        for sub in range(2):
            sel[2 * pb + sub, pb * 128 + sub * 64:pb * 128 + sub * 64 + 64] = 1.0
    m["sel16"] = sel
    return m


_BUILD_CACHE = {}


def kernel(**inputs):
    key = "full"
    stages = ["l0", "l1"]
    nc, outs = build(stages)
    in_maps = [stage_inputs(inputs, cc) for cc in range(8)]
    res = run_bass_kernel_spmd(nc, in_maps, core_ids=list(range(8)))
    out = np.empty((4, 2 * TC, D), np.float32)
    for cc in range(8):
        b, hf = cc // 2, cc % 2
        out[b, hf * TC:(hf + 1) * TC] = res.results[cc]["out_sl"].T
    return out


# version 02
# speedup vs baseline: 1.1771x; 1.1771x over previous
import numpy as np
import ml_dtypes
import concourse.bass as bass
import concourse.mybir as mybir
from concourse.bass_utils import run_bass_kernel_spmd

F32 = mybir.dt.float32
BF16 = mybir.dt.bfloat16
U8 = mybir.dt.uint8
U32 = mybir.dt.uint32
AF = mybir.ActivationFunctionType
ALU = mybir.AluOpType
AX = mybir.AxisListType.X
ET = mybir.EngineType
NPBF16 = ml_dtypes.bfloat16

TC = 4096
D = 1024
H = 16
HD = 64
NB = 8
TT = 512
NT = TC // TT
CH = 128
NCH = TC // CH
DFF = 4096
MAGIC1 = 0x5A17C0DE
MAGIC2 = 0x7E57B1D5
SB_BYTES = 224 * 1024
DETECT_RACES = False

ENGS = ["tensor", "vector", "scalar", "sync", "gpsimd"]
ECTR = {"tensor": "cT", "vector": "cV", "scalar": "cS", "sync": "cY", "gpsimd": "cG"}


class Prog:

    def __init__(self, nc):
        self.nc = nc
        self.items = {e: [] for e in ENGS}
        self.sems = {}
        self.cnt = {}
        self.lastwait = {e: {} for e in ENGS}

    def sem(self, name):
        if name not in self.sems:
            self.sems[name] = self.nc.alloc_semaphore(name)
            self.cnt[name] = 0
        return self.sems[name]

    def _waits(self, eng, waits):
        out = []
        for s, v in waits:
            self.sem(s)
            if v <= 0:
                continue
            if self.lastwait[eng].get(s, -1) >= v:
                continue
            self.lastwait[eng][s] = v
            out.append((s, v))
        return out

    def op(self, eng, fn, wait=None, sig=True, chain=None):
        ctr = ECTR[eng]
        self.sem(ctr)
        if chain is None:
            chain = eng in ("vector", "scalar", "gpsimd")
        waits = list(wait or [])
        if chain and self.cnt[ctr] > 0:
            waits.append((ctr, self.cnt[ctr]))
        waits = self._waits(eng, waits)
        if sig:
            self.cnt[ctr] += 1
        val = self.cnt[ctr]

        def closure(e, waits=waits, fn=fn, sig=sig, ctr=ctr):
            for s, v in waits:
                e.wait_ge(self.sems[s], v)
            ins = fn(e)
            if sig:
                if ins is None:
                    ins = e.nop()
                ins.then_inc(self.sems[ctr], 1)
            return ins
        self.items[eng].append(closure)
        return val

    def mark(self, eng):
        self.sem(ECTR[eng])
        return self.cnt[ECTR[eng]]

    def ctr(self, eng):
        return ECTR[eng]

    def now(self, semname):
        self.sem(semname)
        return self.cnt[semname]

    def dma(self, eng, out, in_, semname, wait=None, chain=None, **kw):
        self.sem(semname)
        ctr = ECTR[eng]
        if chain is None:
            chain = eng in ("vector", "scalar", "gpsimd")
        waits = list(wait or [])
        if chain and self.cnt[ctr] > 0:
            waits.append((ctr, self.cnt[ctr]))
        waits = self._waits(eng, waits)
        self.cnt[semname] += 16
        def closure(e, waits=waits, out=out, in_=in_, kw=kw, s=semname):
            for sn, v in waits:
                e.wait_ge(self.sems[sn], v)
            return e.dma_start(out=out, in_=in_, **kw).then_inc(self.sems[s], 16)
        self.items[eng].append(closure)
        return self.cnt[semname]

    def wait(self, eng, semname, value=None):
        if value is None:
            value = self.now(semname)
        ws = self._waits(eng, [(semname, value)])
        if ws:
            def closure(e, ws=ws):
                for s, v in ws:
                    e.wait_ge(self.sems[s], v)
                return None
            self.items[eng].append(closure)

    def barrier(self, tag=""):
        marks = {e: self.mark(e) for e in ENGS}
        ctrs = set(ECTR.values())
        dmas = [(s, v) for s, v in self.cnt.items() if s not in ctrs and v > 0]
        for e in ENGS:
            for e2 in ENGS:
                if e2 != e:
                    self.wait(e, ECTR[e2], marks[e2])
            for s, v in dmas:
                self.wait(e, s, v)
        for e in ENGS:
            self.op(e, lambda x: x.nop(), chain=(e in ("vector", "scalar", "gpsimd")))

    def replay(self):
        with self.nc.Block() as blk:
            for e in ENGS:
                getattr(blk, e)(lambda eng, e=e: [f(eng) for f in self.items[e]])


class PsumPool:
    def __init__(self, prog, tiles):
        self.p = prog
        self.tiles = tiles
        self.state = [None] * len(tiles)

    def acquire(self, i, eng):
        if self.state[i] is not None:
            s, v = self.state[i]
            self.p.wait(eng, s, v)
        return self.tiles[i]

    def release(self, i, eng):
        self.state[i] = (ECTR[eng], self.p.mark(eng))


def rowap(t, a, b):
    return t[0:1, a:b]


def build(stages, dev=False, s2_nwin=None):
    nc = bass.Bass("TRN2", target_bir_lowering=False,
                   detect_race_conditions=DETECT_RACES)
    P = Prog(nc)

    def inp(name, shape, dt=F32):
        return nc.dram_tensor(name, shape, dt, kind="ExternalInput")

    x_sl = inp("x_sl", [D, TC + 1])
    wq = inp("wq", [D, D]); wk = inp("wk", [D, D]); wv = inp("wv", [D, D])
    wbeta = inp("wbeta", [D, H])
    wo = inp("wo", [D, D]); w1 = inp("w1", [D, DFF]); w2 = inp("w2", [DFF, D])
    wrT = inp("wrT", [D, D]); wkT = inp("wkT", [D, D]); wvT = inp("wvT", [D, D])
    wqkv = inp("wqkv", [D, 3 * D]); wproj = inp("wproj", [D, D])
    k0u = inp("k0u", [2 * D, DFF], U8)
    k1u = inp("k1u", [DFF, DFF], U8)
    k2u = inp("k2u", [DFF, D], U8)
    vecs = inp("vecs", [128, 80])
    b1v = inp("b1v", [128, 32])
    bqkvv = inp("bqkvv", [128, 24])
    sclv = inp("sclv", [128, 4])
    bk_bc = inp("bk_bc", [128, D]); bv_bc = inp("bv_bc", [128, D])
    bqkv_k_bc = inp("bqkv_k_bc", [128, D]); bqkv_v_bc = inp("bqkv_v_bc", [128, D])
    bbeta_bc = inp("bbeta_bc", [128, H])
    i64p = inp("i64p", [128, 64])
    maskln = inp("maskln", [128, CH])
    masku = inp("masku", [128, CH])
    identb = inp("identb", [128, 128], BF16)
    onesbc = inp("onesbc", [1, 128], BF16)
    onesbf = inp("onesbf", [1, 128])
    onescb = inp("onescb", [128, 1], BF16)
    onescf = inp("onescf", [128, 1])
    sel16 = inp("sel16", [16, 8 * 128])

    outs = []

    def outp(name, shape, dt):
        outs.append(name)
        return nc.dram_tensor(name, shape, dt, kind="ExternalOutput")

    qT = nc.dram_tensor("qT", [D, TC], BF16)
    ktokN = nc.dram_tensor("ktokN", [TC, D], BF16)
    vtok = nc.dram_tensor("vtok", [TC, D], BF16)
    OT = nc.dram_tensor("OTb", [D, TC], BF16)
    ABb = nc.dram_tensor("ABb", [NCH, 8, 128, 128], BF16)
    WTb = nc.dram_tensor("WTb", [NCH, 8, 128, CH], BF16)
    statsH = nc.dram_tensor("statsH", [4, TC + 8], F32)
    stats1H = nc.dram_tensor("stats1H", [2, TC + 8], F32)
    sTh = nc.dram_tensor("sTh", [D, TC], F32)
    xaTh = nc.dram_tensor("xaTh", [D, TC], F32)
    h2Th = nc.dram_tensor("h2Th", [D, TC], BF16)
    shst = nc.dram_tensor("shst", [8, 128, 512], BF16, addr_space="Shared")
    shkv = nc.dram_tensor("shkv", [8, 128, 8 * 65], F32, addr_space="Shared")
    shfl = nc.dram_tensor("shfl", [8, 2, 16], U32, addr_space="Shared")
    q1T = nc.dram_tensor("q1T", [D, TC], BF16)
    s1Th = nc.dram_tensor("s1Th", [D, TC], F32)
    h1fT = nc.dram_tensor("h1fT", [DFF, TC], BF16)
    h2fT = nc.dram_tensor("h2fT", [DFF, TC], BF16)
    d0T = nc.dram_tensor("d0T", [D, DFF], BF16)
    w1eT = nc.dram_tensor("w1eT", [DFF, DFF], BF16)
    w2eT = nc.dram_tensor("w2eT", [DFF, D], BF16)

    full = "l1" in stages or "t3c" in stages
    if "l1" in stages and "l0" not in stages:
        x1T = inp("x1T_in", [D, TC])
        xn1T_t = inp("xn1T_in", [D, TC], BF16)
    elif dev and ("s6" in stages or "l0" in stages) and not full:
        x1T = outp("x1T", [D, TC], F32)
        xn1T_t = outp("xn1T", [D, TC], BF16)
    else:
        x1T = nc.dram_tensor("x1T", [D, TC], F32)
        xn1T_t = nc.dram_tensor("xn1T", [D, TC], BF16)
    out_sl = outp("out_sl", [D, TC], F32) if full else None

    dev_out = {}
    if dev:
        for nm in stages:
            if nm == "s1":
                dev_out["d_q"] = outp("d_q", [D, TC], BF16)
                dev_out["d_k"] = outp("d_k", [TC, D], BF16)
                dev_out["d_v"] = outp("d_v", [TC, D], BF16)
                dev_out["d_beta"] = outp("d_beta", [128, 32 * H], F32)
                dev_out["d_stats"] = outp("d_stats", [4, TC + 8], F32)
            if nm == "s2":
                dev_out["d_stl"] = outp("d_stl", [128, 8 * HD], BF16)
                dev_out["d_ot"] = outp("d_ot", [D, TC], BF16)
                dev_out["d_ab"] = outp("d_ab", [NCH, 8, 128, 128], BF16)
                dev_out["d_wt"] = outp("d_wt", [NCH, 8, 128, CH], BF16)
            if nm == "s4":
                dev_out["d_ot2"] = outp("d_ot2", [D, TC], BF16)
            if nm == "s5":
                dev_out["d_s"] = outp("d_s", [D, TC], F32)
                dev_out["d_xa"] = outp("d_xa", [D, TC], F32)
                dev_out["d_h2"] = outp("d_h2", [D, TC], BF16)

            if nm == "t1":
                dev_out["d_kv"] = outp("d_kv", [128, 8 * 65], F32)
                dev_out["d_q1"] = outp("d_q1", [D, TC], BF16)
                dev_out["d_kt"] = outp("d_kt", [128, D], BF16)
                dev_out["d_vt"] = outp("d_vt", [128, D], BF16)
                dev_out["d_ktall"] = outp("d_ktall", [32, 128, D], BF16)
            if nm == "t3a":
                dev_out["d_s1"] = outp("d_s1", [D, TC], F32)
                dev_out["d_h1f"] = outp("d_h1f", [DFF, TC], BF16)
                dev_out["d_ob"] = outp("d_ob", [128, NB * TT], BF16)
                dev_out["d_z"] = outp("d_z", [128, TT], F32)
                dev_out["d_zr"] = outp("d_zr", [128, TT], F32)
                dev_out["d_ks"] = outp("d_ks", [128, 128], BF16)
                dev_out["d_d0"] = outp("d_d0", [D, DFF], BF16)
            if nm == "t3b":
                dev_out["d_h2f"] = outp("d_h2f", [DFF, TC], BF16)

    pid0 = nc.partition_id(engines=[ET.SP, ET.Pool])
    pid = nc.snap(pid0, engines=[ET.SP, ET.Pool], min_val=0, max_val=7)
    partner = (pid // 2) * 2 + (1 - (pid % 2))

    per = Arena = None
    off = [20480]

    def salloc(name, shape, dt, at=None):
        if at is not None:
            off[0] = at
        t = nc.alloc_sbuf_tensor_at(name, list(shape), dt, offset=off[0])
        per_part = int(np.prod(shape[1:])) * mybir.dt.size(dt)
        off[0] += (per_part + 31) // 32 * 32
        assert off[0] <= SB_BYTES, f"SBUF overflow at {name}: {off[0]}"
        return t

    sb_vecs = salloc("sb_vecs", [128, 80], F32)
    sb_b1 = salloc("sb_b1", [128, 32], F32)
    sb_bqkv = salloc("sb_bqkv", [128, 24], F32)
    sb_scl = salloc("sb_scl", [128, 4], F32)
    sb_bkbc = salloc("sb_bkbc", [128, D], F32)
    sb_bvbc = salloc("sb_bvbc", [128, D], F32)
    sb_bqkbc = sb_bkbc
    sb_bqvbc = sb_bvbc
    sb_bbeta = salloc("sb_bbeta", [128, H], F32)
    sb_i64 = salloc("sb_i64", [128, 64], F32)
    sb_mln = salloc("sb_mln", [128, CH], F32)
    sb_mu = salloc("sb_mu", [128, CH], F32)
    sb_idb = salloc("sb_idb", [128, 128], BF16)
    sb_ones = salloc("sb_ones", [1, 128], BF16)
    sb_onesf = salloc("sb_onesf", [1, 128], F32)
    sb_onescb = salloc("sb_onescb", [128, 1], BF16)
    sb_onescf = salloc("sb_onescf", [128, 1], F32)
    sb_beta = salloc("sb_beta", [128, 32 * H], F32)
    sb_stl = salloc("sb_stl", [128, 8 * HD], BF16)
    sb_stc = salloc("sb_stc", [128, 8 * HD], BF16)
    sb_rows = salloc("sb_rows", [1, 2048], F32)
    sb_eps = salloc("sb_eps", [128, 8], F32)
    sb_flag = salloc("sb_flag", [1, 16], U32)
    sb_magic = salloc("sb_magic", [1, 32], U32)
    sb_lastc = salloc("sb_lastc", [128, 16], F32)
    sb_zb = salloc("sb_zb", [128, 1], BF16)
    sb_kvloc = salloc("sb_kvloc", [128, 8 * 65], F32)
    sb_kvb = salloc("sb_kvb", [128, 8 * 65], BF16)
    sb_ks = salloc("sb_ks", [128, NB * H], BF16)
    PERSIST = off[0]

    s1_wq = salloc("s1_wq", [128, NB * D], BF16, at=PERSIST)
    s1_wk = salloc("s1_wk", [128, NB * D], BF16)
    s1_wv = salloc("s1_wv", [128, NB * D], BF16)
    s1_wbeta = salloc("s1_wbeta", [128, NB * H], BF16)
    W_S1_END = off[0]
    s5_wo = salloc("s5_wo", [128, NB * D], BF16, at=PERSIST)
    W_S5A_END = off[0]
    s5_w1 = salloc("s5_w1", [128, NB * DFF], BF16, at=PERSIST)
    s5_w2 = salloc("s5_w2", [128, 32 * D], BF16)
    W_S5B_END = off[0]
    s6_wr = salloc("s6_wr", [128, NB * D], BF16, at=PERSIST)
    s6_wkk = salloc("s6_wkk", [128, NB * D], BF16)
    s6_wvv = salloc("s6_wvv", [128, NB * D], BF16)
    W_S6_END = off[0]
    t1_wqkv = salloc("t1_wqkv", [128, NB * 3 * D], BF16, at=PERSIST)
    W_T1_END = off[0]
    t3_wproj = salloc("t3_wproj", [128, NB * D], BF16, at=PERSIST)
    t3_d0 = salloc("t3_d0", [128, NB * DFF], BF16)
    W_T3A_END = off[0]
    t3_w1s = [salloc(f"t3_w1s{i}", [128, 4 * 128], BF16, at=(PERSIST if i == 0 else None))
              for i in range(64)]
    W_T3B_END = off[0]
    t3_w2 = salloc("t3_w2", [128, 32 * D], BF16, at=PERSIST)
    W_T3C_END = off[0]
    WMAX = max(W_S1_END, W_S5A_END, W_S5B_END, W_S6_END, W_T1_END,
               W_T3A_END, W_T3B_END, W_T3C_END)

    def aalloc(name, shape, dt, at=None):
        return salloc(name, shape, dt, at=at)

    s1_x = aalloc("s1_x", [128, NB * TT], F32, at=W_S1_END)
    s1_h1 = aalloc("s1_h1", [128, NB * TT], BF16)
    s1_xsq = aalloc("s1_xsq", [128, NB * TT], BF16)
    s1_kt = aalloc("s1_kt", [128, D], F32)
    s1_ksq = aalloc("s1_ksq", [128, D], BF16)
    s1_rn = aalloc("s1_rn", [128, 4 * H], F32)
    s1_st = aalloc("s1_st", [128, NB * TT], BF16)
    s1_kst = aalloc("s1_kst", [128, 2 * D], BF16)
    s1_bc = aalloc("s1_bc", [128, 2 * TT], F32)
    S1_END = off[0]
    s2_qw = aalloc("s2_qw", [128, NB * TT], BF16, at=PERSIST)
    s2_kw = aalloc("s2_kw", [128, 4 * D], BF16)
    s2_vw = aalloc("s2_vw", [128, 4 * D], BF16)
    s2_kf = aalloc("s2_kf", [128, NB * CH], BF16)
    s2_m = [aalloc(f"s2_m{i}", [128, CH], BF16) for i in range(2)]
    s2_mt = [aalloc(f"s2_mt{i}", [128, CH], BF16) for i in range(2)]
    s2_m2 = [aalloc(f"s2_m2{i}", [128, CH], BF16) for i in range(2)]
    s2_mt2 = [aalloc(f"s2_mt2{i}", [128, CH], BF16) for i in range(2)]
    s2_sacc = [aalloc(f"s2_sacc{i}", [128, CH], BF16) for i in range(2)]
    s2_mo = [aalloc(f"s2_mo{i}", [128, CH], BF16) for i in range(2)]
    s2_ab = aalloc("s2_ab", [128, 8 * 128], BF16)
    s2_wts = aalloc("s2_wts", [128, 8 * CH], BF16)
    s2_obs = aalloc("s2_obs", [128, NB * CH], BF16)
    S2_END = off[0]
    s4_ab = aalloc("s4_ab", [128, 8 * 128], BF16, at=PERSIST)
    s4_wt = aalloc("s4_wt", [128, 8 * CH], BF16)
    s4_o = aalloc("s4_o", [128, NB * CH], BF16)
    S4_END = off[0]
    s5_ot = aalloc("s5_ot", [128, NB * TT], BF16, at=W_S5A_END)
    s5_att = aalloc("s5_att", [128, NB * TT], F32)
    s5_x = aalloc("s5_x", [128, NB * TT], F32)
    s5_xa = aalloc("s5_xa", [128, NB * TT], F32)
    s5_sq = aalloc("s5_sq", [128, NB * TT], BF16)
    s5_h2 = aalloc("s5_h2", [128, NB * TT], BF16)
    s5_bc = aalloc("s5_bc", [128, 3 * TT], F32)
    S5A_END = off[0]
    s5_h2b = aalloc("s5_h2b", [128, NB * TT], BF16, at=W_S5B_END)
    s5_hf = aalloc("s5_hf", [128, 32 * TT], BF16)
    s5_gt = aalloc("s5_gt", [128, 2 * TT], F32)
    s5_xab = aalloc("s5_xab", [128, TT], F32)
    s5_s = aalloc("s5_s", [128, TT], F32)
    S5B_END = off[0]
    s6_x = aalloc("s6_x", [128, NB * TT], F32, at=W_S6_END)
    s6_xn = aalloc("s6_xn", [128, NB * (TT + 8)], BF16)
    s6_xr = aalloc("s6_xr", [128, NB * TT], BF16)
    s6_xk = aalloc("s6_xk", [128, NB * TT], BF16)
    s6_r = aalloc("s6_r", [128, NB * TT], F32)
    s6_kk = aalloc("s6_kk", [128, NB * TT], BF16)
    s6_sld = aalloc("s6_sld", [128, NB * TT], F32)
    s6_g = aalloc("s6_g", [128, 2 * TT], F32)
    s6_x1 = aalloc("s6_x1", [128, NB * TT], F32)
    s6_sq = aalloc("s6_sq", [128, NB * TT], BF16)
    s6_bc = aalloc("s6_bc", [128, 3 * TT], F32)
    s6_xn1 = aalloc("s6_xn1", [128, NB * TT], BF16)
    S6_END = off[0]
    t1_xn1 = aalloc("t1_xn1", [128, NB * TT], BF16, at=W_T1_END)
    t1_q = aalloc("t1_q", [128, NB * TT], BF16)
    t1_kt = aalloc("t1_kt", [128, D], BF16)
    t1_vt = aalloc("t1_vt", [128, D], BF16)
    t1_tmp = aalloc("t1_tmp", [128, 2 * TT], F32)
    sb_kvp = aalloc("sb_kvp", [128, 8 * 65], F32)
    T1_END = off[0]
    t3_q = aalloc("t3_q", [128, NB * TT], BF16, at=W_T3A_END)
    t3_xn1 = aalloc("t3_xn1", [128, NB * TT], BF16)
    t3_z = aalloc("t3_z", [128, TT], F32)
    t3_ob = aalloc("t3_ob", [128, NB * TT], BF16)
    t3_hf = aalloc("t3_hf", [128, 32 * TT], BF16)
    t3_tmp = aalloc("t3_tmp", [128, 2 * TT], F32)
    t3_den = aalloc("t3_den", [128, TT], F32)
    sb_sel = aalloc("sb_sel", [16, 8 * 128], F32)
    T3A_END = off[0]
    t3_h1 = aalloc("t3_h1", [128, 32 * 1024], BF16, at=W_T3B_END)
    t3_h2o = aalloc("t3_h2o", [128, 8 * 1024], BF16)
    t3b_tmp = aalloc("t3b_tmp", [128, TT], F32)
    T3B_END = off[0]
    t3_hr = aalloc("t3_hr", [128, 32 * TT], BF16, at=W_T3C_END)
    t3_g1 = aalloc("t3_g1", [128, 2 * TT], F32)
    t3_sl = aalloc("t3_sl", [128, NB * TT], F32)
    t3_x1l = aalloc("t3_x1l", [128, NB * TT], F32)
    t3_out = aalloc("t3_out", [128, NB * TT], F32)
    T3C_END = off[0]
    cv_a = aalloc("cv_a", [128, DFF], U8, at=PERSIST)
    cv_b = aalloc("cv_b", [128, DFF], U8)
    cv_o = aalloc("cv_o", [128, DFF], BF16)
    CV_END = off[0]
    for nm, v in [("S1", S1_END), ("S2", S2_END), ("S4", S4_END), ("S5A", S5A_END),
                  ("S5B", S5B_END), ("S6", S6_END), ("T1", T1_END), ("T3A", T3A_END),
                  ("T3B", T3B_END), ("T3C", T3C_END), ("CV", CV_END)]:
        assert v <= SB_BYTES, f"{nm} overflows SBUF: {v}"

    pbt = [nc.alloc_psum_tensor(f"pb{i}", [128, 512], F32) for i in range(8)]
    PS = PsumPool(P, pbt)
    PT = PS

    ctx = dict(locals())
    if s2_nwin is not None:
        ctx["S2_NWIN"] = s2_nwin

    emit_t0(ctx)
    if "s1" in stages or "l0" in stages:
        emit_s1(ctx)
    if "s2" in stages or "l0" in stages:
        emit_s2(ctx)
        if "nos34" not in stages:
            emit_s3(ctx)
            emit_s4(ctx)
    if "s5" in stages or "l0" in stages:
        emit_s5(ctx)
    if "s6" in stages or "l0" in stages:
        emit_s6(ctx)
    if "t1" in stages or "l1" in stages:
        emit_cv(ctx)
        emit_t1(ctx)
        emit_t2(ctx)
    if "t3a" in stages or "l1" in stages:
        emit_t3a(ctx)
    if "t3b" in stages or "l1" in stages:
        emit_t3b(ctx)
    if "t3c" in stages or "l1" in stages:
        emit_t3c(ctx)
    emit_dev_tail(ctx, stages, dev, dev_out)

    P.replay()
    return nc, outs


def emit_t0(c):
    P = c["P"]
    pairs = [(c["sb_vecs"], c["vecs"]), (c["sb_b1"], c["b1v"]),
             (c["sb_bqkv"], c["bqkvv"]), (c["sb_scl"], c["sclv"]),
             (c["sb_bkbc"], c["bk_bc"]), (c["sb_bvbc"], c["bv_bc"]),
             (c["sb_bbeta"], c["bbeta_bc"]), (c["sb_i64"], c["i64p"]),
             (c["sb_mln"], c["maskln"]), (c["sb_mu"], c["masku"]),
             (c["sb_idb"], c["identb"])]
    for dst, src_ in pairs:
        P.dma("sync", dst[:, :], src_[:, :], "cst")
    for dst, src_ in [(c["sb_ones"], c["onesbc"]), (c["sb_onesf"], c["onesbf"])]:
        P.dma("sync", dst[0:1, :], src_[0:1, :], "cst")
    for dst, src_ in [(c["sb_onescb"], c["onescb"]), (c["sb_onescf"], c["onescf"])]:
        P.dma("sync", dst[:, 0:1], src_[:, 0:1], "cst")
    eps = c["sb_eps"]
    for i, val in enumerate([1e-5, 1e-6, -0.5, 1.0 / D, -1.0, 1e-8, 0.0, 1.0]):
        P.op("vector", lambda v, i=i, val=val: v.memset(eps[:, i:i + 1], val))
    P.op("vector", lambda v: v.memset(c["sb_stl"][:, :], 0.0))
    P.op("vector", lambda v: v.memset(c["sb_lastc"][:, :], 0.0))
    P.op("vector", lambda v: v.memset(c["sb_zb"][:, :], 0.0))
    P.op("gpsimd", lambda g: g.memset(c["sb_magic"][0:1, 0:16], MAGIC1))
    P.op("gpsimd", lambda g: g.memset(c["sb_magic"][0:1, 16:32], MAGIC2))


def emit_s1(c):
    P, PS = c["P"], c["PS"]
    x_sl, statsH = c["x_sl"], c["statsH"]
    s1_x, s1_h1, s1_xsq = c["s1_x"], c["s1_h1"], c["s1_xsq"]
    s1_kt, s1_ksq, s1_rn, s1_st, s1_bc = (c["s1_kt"], c["s1_ksq"], c["s1_rn"],
                                          c["s1_st"], c["s1_bc"])
    s1_kst = c["s1_kst"]
    rows, eps, vecs = c["sb_rows"], c["sb_eps"], c["sb_vecs"]
    onescf, onescb, onesf = c["sb_onescf"], c["sb_onescb"], c["sb_onesf"]
    for w_h, w_s in [(c["wq"], c["s1_wq"]), (c["wk"], c["s1_wk"]),
                     (c["wv"], c["s1_wv"]), (c["wbeta"], c["s1_wbeta"])]:
        n = w_h.shape[1]
        for kb in range(NB):
            P.dma("gpsimd", w_s[:, kb * n:(kb + 1) * n],
                  w_h[kb * 128:(kb + 1) * 128, :], "wS1", chain=False)
    wS1 = P.now("wS1")

    prev_free = None
    for it in range(-1, NT):
        w = 1 if it < 0 else TT
        c0 = 0 if it < 0 else 1 + it * TT
        if prev_free is not None:
            P.wait("sync", prev_free[0], prev_free[1])
        for kb in range(NB):
            P.dma("sync", s1_x[:, kb * TT:kb * TT + w],
                  x_sl[kb * 128:(kb + 1) * 128, c0:c0 + w], "s1x",
                  **({"allow_slow_non_contiguous": True} if w == 1 else {}))
        xv = P.now("s1x")
        P.wait("scalar", "s1x", xv)
        for kb in range(NB):
            P.op("scalar", lambda e, kb=kb, w=w: e.activation(
                s1_xsq[:, kb * TT:kb * TT + w], s1_x[:, kb * TT:kb * TT + w],
                AF.Square))
        sqm = P.mark("scalar")
        P.wait("tensor", "cS", sqm)
        if it == -1:
            P.wait("tensor", "cst", P.now("cst"))
        p0 = PS.acquire(0, "tensor")
        p1 = PS.acquire(1, "tensor")
        for kb in range(NB):
            P.op("tensor", lambda e, kb=kb, w=w, p0=p0: e.matmul(
                p0[0:1, 0:w], onescf[:, 0:1],
                s1_x[:, kb * TT:kb * TT + w], start=(kb == 0), stop=(kb == NB - 1)),
                sig=(kb == NB - 1))
        for kb in range(NB):
            P.op("tensor", lambda e, kb=kb, w=w, p1=p1: e.matmul(
                p1[0:1, 0:w], onescb[:, 0:1],
                s1_xsq[:, kb * TT:kb * TT + w], start=(kb == 0), stop=(kb == NB - 1)),
                sig=(kb == NB - 1))
        mm1 = P.mark("tensor")
        r_mean = rowap(rows, 0, w)
        r_comb = rowap(rows, 512, 512 + w)
        r_rstd = rowap(rows, 1024, 1024 + w)
        r_t = rowap(rows, 1536, 1536 + w)
        P.wait("vector", "cT", mm1)
        P.wait("vector", "s1so", P.now("s1so"))
        P.op("vector", lambda v, p0=p0, r_mean=r_mean, w=w: v.tensor_scalar_mul(
            r_mean, p0[0:1, 0:w], eps[0:1, 3:4]))
        P.op("vector", lambda v, p1=p1, r_t=r_t, w=w: v.tensor_scalar_mul(
            r_t, p1[0:1, 0:w], eps[0:1, 3:4]))
        P.op("vector", lambda v, r_comb=r_comb, r_mean=r_mean: v.tensor_mul(
            r_comb, r_mean, r_mean))
        P.op("vector", lambda v, r_t=r_t, r_comb=r_comb: v.tensor_sub(
            r_t, r_t, r_comb))
        PS.release(0, "vector")
        PS.release(1, "vector")
        P.wait("scalar", "cV", P.mark("vector"))
        P.op("scalar", lambda e, r_comb=r_comb, r_t=r_t: e.activation(
            r_comb, r_t, AF.Ln, bias=eps[0:1, 0:1]))
        P.op("scalar", lambda e, r_rstd=r_rstd, r_comb=r_comb: e.activation(
            r_rstd, r_comb, AF.Exp, scale=-0.5))
        P.wait("vector", "cS", P.mark("scalar"))
        P.op("vector", lambda v, r_comb=r_comb, r_t=r_t: v.tensor_scalar_add(
            r_comb, r_t, eps[0:1, 0:1]))
        P.op("vector", lambda v, r_comb=r_comb: v.reciprocal(r_comb, r_comb))
        P.op("vector", lambda v, r_comb=r_comb, r_t=r_t: v.tensor_mul(
            r_comb, r_t, r_comb))
        P.wait("scalar", "cV", P.mark("vector"))
        P.op("scalar", lambda e, r_comb=r_comb: e.activation(
            r_comb, r_comb, AF.Ln, bias=eps[0:1, 1:2]))
        P.op("scalar", lambda e, r_comb=r_comb: e.activation(
            r_comb, r_comb, AF.Exp, scale=-0.5))
        P.wait("vector", "cS", P.mark("scalar"))
        P.op("vector", lambda v, r_comb=r_comb, r_rstd=r_rstd: v.tensor_mul(
            r_comb, r_comb, r_rstd))
        rv = P.mark("vector")
        P.wait("sync", "cV", rv)
        P.dma("sync", statsH[0:1, c0:c0 + w], rowap(rows, 0, w), "s1so",
              **({"allow_slow_non_contiguous": True} if w == 1 else {}))
        P.dma("sync", statsH[1:2, c0:c0 + w], rowap(rows, 1024, 1024 + w), "s1so",
              **({"allow_slow_non_contiguous": True} if w == 1 else {}))
        P.dma("sync", statsH[2:3, c0:c0 + w], rowap(rows, 512, 512 + w), "s1so",
              **({"allow_slow_non_contiguous": True} if w == 1 else {}))
        if it < 0:
            prev_free = ("cT", mm1)
            continue
        P.wait("tensor", "cV", rv)
        p2 = PS.acquire(2, "tensor")
        P.op("tensor", lambda e, p2=p2: e.matmul(
            p2[:, 0:TT], onesf[0:1, :], rowap(rows, 0, TT), start=True, stop=True),
            sig=False)
        p3 = PS.acquire(3, "tensor")
        P.op("tensor", lambda e, p3=p3: e.matmul(
            p3[:, 0:TT], onesf[0:1, :], rowap(rows, 512, 512 + TT),
            start=True, stop=True))
        P.wait("vector", "cT", P.mark("tensor"))
        P.op("vector", lambda v, p2=p2: v.tensor_copy(s1_bc[:, 0:TT], p2[:, 0:TT]))
        P.op("vector", lambda v, p3=p3: v.tensor_copy(s1_bc[:, TT:2 * TT], p3[:, 0:TT]))
        PS.release(2, "vector")
        PS.release(3, "vector")
        for kb in range(NB):
            P.op("vector", lambda v, kb=kb: v.tensor_sub(
                s1_x[:, kb * TT:(kb + 1) * TT], s1_x[:, kb * TT:(kb + 1) * TT],
                s1_bc[:, 0:TT]))
            P.op("vector", lambda v, kb=kb: v.scalar_tensor_tensor(
                s1_h1[:, kb * TT:(kb + 1) * TT], s1_x[:, kb * TT:(kb + 1) * TT],
                vecs[:, 40 + kb:41 + kb], s1_bc[:, TT:2 * TT], ALU.mult, ALU.mult))
        h1v = P.mark("vector")
        prev_free = ("cV", h1v)
        P.wait("tensor", "cV", h1v)
        if it == 0:
            P.wait("tensor", "wS1", wS1)
        P.wait("scalar", "s1qo", P.now("s1qo"))
        for mb in range(NB):
            pi = 4 + (mb % 2)
            pq = PS.acquire(pi, "tensor")
            for kb in range(NB):
                P.op("tensor", lambda e, kb=kb, mb=mb, pq=pq: e.matmul(
                    pq[:, 0:TT],
                    c["s1_wq"][:, kb * D + mb * 128:kb * D + (mb + 1) * 128],
                    s1_h1[:, kb * TT:(kb + 1) * TT],
                    start=(kb == 0), stop=(kb == NB - 1)), sig=(kb == NB - 1))
            P.wait("scalar", "cT", P.mark("tensor"))
            P.op("scalar", lambda e, mb=mb, pq=pq: e.activation(
                s1_st[:, mb * TT:(mb + 1) * TT], pq[:, 0:TT], AF.Identity,
                bias=vecs[:, mb:mb + 1]))
            PS.release(pi, "scalar")
        P.wait("gpsimd", "cS", P.mark("scalar"))
        for mb in range(NB):
            P.dma("gpsimd", c["qT"][mb * 128:(mb + 1) * 128, it * TT:(it + 1) * TT],
                  s1_st[:, mb * TT:(mb + 1) * TT], "s1qo", chain=False)
        qov = P.now("s1qo")
        for g in range(4):
            def lh(kb, g=g):
                return s1_h1[:, kb * TT + g * 128:kb * TT + (g + 1) * 128]
            for cc in range(2):
                pi = 4 + cc
                pk = PS.acquire(pi, "tensor")
                for kb in range(NB):
                    P.op("tensor", lambda e, kb=kb, cc=cc, pk=pk, g=g: e.matmul(
                        pk[:, 0:TT], lh(kb, g),
                        c["s1_wk"][:, kb * D + cc * TT:kb * D + (cc + 1) * TT],
                        start=(kb == 0), stop=(kb == NB - 1)), sig=(kb == NB - 1))
                P.wait("vector", "cT", P.mark("tensor"))
                P.op("vector", lambda v, cc=cc, pk=pk: v.tensor_add(
                    s1_kt[:, cc * TT:(cc + 1) * TT], pk[:, 0:TT],
                    c["sb_bkbc"][:, cc * TT:(cc + 1) * TT]))
                PS.release(pi, "vector")
            P.wait("scalar", "cV", P.mark("vector"))
            P.op("scalar", lambda e: e.activation(s1_ksq[:, :], s1_kt[:, :],
                                                  AF.Square))
            P.wait("vector", "cS", P.mark("scalar"))
            P.op("vector", lambda v: v.tensor_reduce(
                s1_rn[:, 0:H], s1_ksq[:, :].rearrange("p (h d) -> p h d", h=H),
                AX, ALU.add))
            P.wait("scalar", "cV", P.mark("vector"))
            P.op("scalar", lambda e: e.activation(s1_rn[:, H:2 * H],
                                                  s1_rn[:, 0:H], AF.Ln))
            P.op("scalar", lambda e: e.activation(s1_rn[:, H:2 * H],
                                                  s1_rn[:, H:2 * H], AF.Exp,
                                                  scale=0.5))
            P.wait("vector", "cS", P.mark("scalar"))
            P.op("vector", lambda v: v.tensor_scalar_add(
                s1_rn[:, H:2 * H], s1_rn[:, H:2 * H], eps[:, 1:2]))
            P.op("vector", lambda v: v.reciprocal(s1_rn[:, H:2 * H],
                                                  s1_rn[:, H:2 * H]))
            P.wait("vector", "s1ko", P.now("s1ko"))
            P.op("vector", lambda v: v.tensor_tensor(
                s1_kst[:, 0:D].rearrange("p (h d) -> p h d", h=H),
                s1_kt[:, :].rearrange("p (h d) -> p h d", h=H),
                s1_rn[:, H:2 * H].rearrange("p (h o) -> p h o", o=1)
                .broadcast_to((128, H, HD)), ALU.mult))
            P.wait("gpsimd", "cV", P.mark("vector"))
            P.dma("gpsimd", c["ktokN"][it * TT + g * 128:it * TT + (g + 1) * 128, :],
                  s1_kst[:, 0:D], "s1ko", chain=False)
            for cc in range(2):
                pi = 4 + cc
                pv = PS.acquire(pi, "tensor")
                for kb in range(NB):
                    P.op("tensor", lambda e, kb=kb, cc=cc, pv=pv, g=g: e.matmul(
                        pv[:, 0:TT], lh(kb, g),
                        c["s1_wv"][:, kb * D + cc * TT:kb * D + (cc + 1) * TT],
                        start=(kb == 0), stop=(kb == NB - 1)), sig=(kb == NB - 1))
                P.wait("vector", "cT", P.mark("tensor"))
                P.wait("vector", "s1ko", P.now("s1ko"))
                P.op("vector", lambda v, cc=cc, pv=pv: v.tensor_add(
                    s1_kst[:, D + cc * TT:D + (cc + 1) * TT], pv[:, 0:TT],
                    c["sb_bvbc"][:, cc * TT:(cc + 1) * TT]))
                PS.release(pi, "vector")
            P.wait("gpsimd", "cV", P.mark("vector"))
            P.dma("gpsimd", c["vtok"][it * TT + g * 128:it * TT + (g + 1) * 128, :],
                  s1_kst[:, D:2 * D], "s1ko", chain=False)
            pbeta = PS.acquire(4, "tensor")
            for kb in range(NB):
                P.op("tensor", lambda e, kb=kb, pbeta=pbeta, g=g: e.matmul(
                    pbeta[:, 0:H], lh(kb, g),
                    c["s1_wbeta"][:, kb * H:(kb + 1) * H],
                    start=(kb == 0), stop=(kb == NB - 1)), sig=(kb == NB - 1))
            P.wait("vector", "cT", P.mark("tensor"))
            gi = it * 4 + g
            P.op("vector", lambda v, pbeta=pbeta, gi=gi: v.tensor_add(
                c["sb_beta"][:, gi * H:(gi + 1) * H], pbeta[:, 0:H],
                c["sb_bbeta"][:, :]))
            PS.release(4, "vector")
            P.wait("scalar", "cV", P.mark("vector"))
            P.op("scalar", lambda e, gi=gi: e.activation(
                c["sb_beta"][:, gi * H:(gi + 1) * H],
                c["sb_beta"][:, gi * H:(gi + 1) * H], AF.Sigmoid))
    P.barrier()


def emit_s2(c):
    P, PS, PT = c["P"], c["PS"], c["PT"]
    s2_qw, s2_kw, s2_vw, s2_kf = c["s2_qw"], c["s2_kw"], c["s2_vw"], c["s2_kf"]
    s2_m, s2_mt, s2_m2, s2_mt2 = c["s2_m"], c["s2_mt"], c["s2_m2"], c["s2_mt2"]
    s2_sacc, s2_mo = c["s2_sacc"], c["s2_mo"]
    s2_ab, s2_wts, s2_obs = c["s2_ab"], c["s2_wts"], c["s2_obs"]
    beta, stl, i64, mln, mu, idb = (c["sb_beta"], c["sb_stl"], c["sb_i64"],
                                    c["sb_mln"], c["sb_mu"], c["sb_idb"])
    qT, ktokN, vtok, OT, ABb, WTb = (c["qT"], c["ktokN"], c["vtok"], c["OT"],
                                     c["ABb"], c["WTb"])
    nwin = c.get("S2_NWIN", NT)
    for wi in range(nwin):
        P.wait("sync", "cT", P.mark("tensor"))
        P.wait("sync", "s2sp", P.now("s2sp"))
        for mb in range(NB):
            P.dma("sync", s2_qw[:, mb * TT:(mb + 1) * TT],
                  qT[mb * 128:(mb + 1) * 128, wi * TT:(wi + 1) * TT], "s2w")
        for g in range(4):
            P.dma("sync", s2_kw[:, g * D:(g + 1) * D],
                  ktokN[wi * TT + g * 128:wi * TT + (g + 1) * 128, :], "s2w")
            P.dma("sync", s2_vw[:, g * D:(g + 1) * D],
                  vtok[wi * TT + g * 128:wi * TT + (g + 1) * 128, :], "s2w")
        wv_ = P.now("s2w")
        P.wait("tensor", "s2w", wv_)
        P.wait("vector", "s2w", wv_)
        for cp in range(4):
            ci = wi * 4 + cp
            P.wait("tensor", "cV", P.mark("vector"))
            for pb in range(NB):
                pti = 6 + pb % 2
                pt = PT.acquire(pti, "tensor")
                P.op("tensor", lambda e, pb=pb, cp=cp, pt=pt: e.transpose(
                    pt.bitcast(BF16)[:, 0:128],
                    s2_kw[:, cp * D + pb * 128:cp * D + (pb + 1) * 128],
                    idb[:, :]))
                P.wait("vector", "cT", P.mark("tensor"))
                P.op("vector", lambda v, pb=pb, pt=pt: v.tensor_copy(
                    s2_kf[:, pb * CH:(pb + 1) * CH], pt.bitcast(BF16)[:, 0:128]))
                PT.release(pti, "vector")
            kfv = P.mark("vector")
            P.wait("tensor", "cV", kfv)
            def unit(h, cp=cp, ci=ci):
                u = h % 2
                pb, bh = h // 2, (h % 2) * 64
                bk = [3 * u, 3 * u + 1, 3 * u + 2]
                ptb = 6 + u
                tp = (lambda r, cc: None if (r == 0 and cc == 0) else (r, cc))
                kf_h = s2_kf[bh:bh + 64, pb * CH:(pb + 1) * CH]
                q_h = s2_qw[bh:bh + 64, pb * TT + cp * 128:pb * TT + (cp + 1) * 128]
                ktok_h = s2_kw[:, cp * D + h * HD:cp * D + (h + 1) * HD]
                vtok_h = s2_vw[:, cp * D + h * HD:cp * D + (h + 1) * HD]
                bcol = beta[:, ci * H + h:ci * H + h + 1]
                m_, mt_, m2_, mt2_ = s2_m[u], s2_mt[u], s2_m2[u], s2_mt2[u]
                sacc, mo = s2_sacc[u], s2_mo[u]
                p0 = PS.acquire(bk[0], "tensor")
                P.op("tensor", lambda e, kf_h=kf_h, p0=p0, bh=bh: e.matmul(
                    p0[:, 0:CH], kf_h, kf_h, start=True, stop=True,
                    tile_position=tp(bh, 0)))
                lt = P.mark("tensor")
                yield
                P.wait("vector", "cT", lt)
                P.op("vector", lambda v, p0=p0, bcol=bcol, m_=m_:
                     v.scalar_tensor_tensor(
                         m_[:, :], p0[:, 0:CH], bcol, mln[:, :],
                         ALU.mult, ALU.mult))
                PS.release(bk[0], "vector")
                lv = P.mark("vector")
                yield
                P.wait("tensor", "cV", lv)
                pt0 = PT.acquire(ptb, "tensor")
                P.op("tensor", lambda e, pt0=pt0, m_=m_: e.transpose(
                    pt0.bitcast(BF16)[:, 0:128], m_[:, :], idb[:, :]))
                lt = P.mark("tensor")
                yield
                P.wait("vector", "cT", lt)
                P.op("vector", lambda v, pt0=pt0, mt_=mt_: v.tensor_copy(
                    mt_[:, :], pt0.bitcast(BF16)[:, 0:128]))
                PT.release(ptb, "vector")
                P.op("vector", lambda v, vtok_h=vtok_h, bcol=bcol, sacc=sacc:
                     v.tensor_scalar_mul(sacc[:, 0:64], vtok_h, bcol))
                P.op("vector", lambda v, ktok_h=ktok_h, bcol=bcol, sacc=sacc:
                     v.tensor_scalar_mul(sacc[:, 64:128], ktok_h, bcol))
                lv = P.mark("vector")
                yield
                mbufs = [(m_, mt_), (m2_, mt2_)]
                for j in range(7):
                    mj, mtj = mbufs[j % 2]
                    mj1, mtj1 = mbufs[(j + 1) % 2]
                    P.wait("tensor", "cV", lv)
                    p1 = PS.acquire(bk[1], "tensor")
                    P.op("tensor", lambda e, mtj=mtj, p1=p1, sacc=sacc: e.matmul(
                        p1[:, 0:CH], mtj[:, :], sacc[:, :],
                        start=True, stop=True))
                    if j < 6:
                        p3 = PS.acquire(bk[2], "tensor")
                        P.op("tensor", lambda e, mtj=mtj, mj=mj, p3=p3: e.matmul(
                            p3[:, 0:CH], mtj[:, :], mj[:, :],
                            start=True, stop=True))
                        p4 = PS.acquire(bk[0], "tensor")
                        P.op("tensor", lambda e, mj=mj, mtj=mtj, p4=p4: e.matmul(
                            p4[:, 0:CH], mj[:, :], mtj[:, :],
                            start=True, stop=True))
                    lt = P.mark("tensor")
                    yield
                    P.wait("vector", "cT", lt)
                    P.op("vector", lambda v, p1=p1, sacc=sacc: v.tensor_add(
                        sacc[:, :], sacc[:, :], p1[:, 0:CH]))
                    PS.release(bk[1], "vector")
                    if j < 6:
                        P.op("vector", lambda v, mj1=mj1, p3=p3: v.tensor_copy(
                            mj1[:, :], p3[:, 0:CH]))
                        PS.release(bk[2], "vector")
                        P.op("vector", lambda v, mtj1=mtj1, p4=p4: v.tensor_copy(
                            mtj1[:, :], p4[:, 0:CH]))
                        PS.release(bk[0], "vector")
                    lv = P.mark("vector")
                    yield
                P.wait("tensor", "cV", lv)
                p5 = PS.acquire(bk[1], "tensor")
                P.op("tensor", lambda e, ktok_h=ktok_h, p5=p5, bh=bh, sacc=sacc:
                     e.matmul(p5[bh:bh + 64, 0:64], sacc[:, 64:128], ktok_h,
                              start=True, stop=True, tile_position=tp(0, bh)),
                     sig=False)
                P.op("tensor", lambda e, ktok_h=ktok_h, p5=p5, bh=bh, sacc=sacc:
                     e.matmul(p5[bh:bh + 64, 64:128], ktok_h, sacc[:, 0:64],
                              start=True, stop=True, tile_position=tp(0, bh)))
                p0b = PS.acquire(bk[2], "tensor")
                P.op("tensor", lambda e, kf_h=kf_h, q_h=q_h, p0b=p0b, bh=bh:
                     e.matmul(p0b[:, 0:CH], kf_h, q_h, start=True, stop=True,
                              tile_position=tp(bh, 0)))
                lt = P.mark("tensor")
                yield
                P.wait("vector", "cT", lt)
                P.op("vector", lambda v, p5=p5, bh=bh, pb=pb: v.tensor_sub(
                    s2_ab[bh:bh + 64, pb * 128:pb * 128 + 64],
                    i64[bh:bh + 64, :], p5[bh:bh + 64, 0:64]))
                P.op("vector", lambda v, p5=p5, bh=bh, pb=pb: v.tensor_copy(
                    s2_ab[bh:bh + 64, pb * 128 + 64:(pb + 1) * 128],
                    p5[bh:bh + 64, 64:128]))
                PS.release(bk[1], "vector")
                P.op("vector", lambda v, p0b=p0b, mo=mo: v.tensor_mul(
                    mo[:, :], p0b[:, 0:CH], mu[:, :]))
                PS.release(bk[2], "vector")
                lv = P.mark("vector")
                yield
                P.wait("tensor", "cV", lv)
                p1b = PS.acquire(bk[0], "tensor")
                P.op("tensor", lambda e, p1b=p1b, bh=bh, sacc=sacc, mo=mo:
                     e.matmul(p1b[bh:bh + 64, 0:CH], sacc[:, 0:64], mo[:, :],
                              start=True, stop=True, tile_position=tp(0, bh)),
                     sig=False)
                p2b = PS.acquire(bk[2], "tensor")
                P.op("tensor", lambda e, p2b=p2b, bh=bh, sacc=sacc, mo=mo:
                     e.matmul(p2b[bh:bh + 64, 0:CH], sacc[:, 64:128], mo[:, :],
                              start=True, stop=True, tile_position=tp(0, bh)))
                a_sb = s2_ab[bh:bh + 64, pb * 128:pb * 128 + 64]
                bt_sb = s2_ab[bh:bh + 64, pb * 128 + 64:(pb + 1) * 128]
                st_sb = stl[bh:bh + 64, pb * 64:(pb + 1) * 64]
                p3b = PS.acquire(bk[1], "tensor")
                P.op("tensor", lambda e, a_sb=a_sb, st_sb=st_sb, p3b=p3b, bh=bh:
                     e.matmul(p3b[bh:bh + 64, 0:64], a_sb, st_sb,
                              start=True, stop=True, tile_position=tp(bh, bh)))
                lt = P.mark("tensor")
                yield
                P.wait("vector", "cT", lt)
                P.op("vector", lambda v, p1b=p1b, bh=bh, pb=pb: v.tensor_copy(
                    s2_obs[bh:bh + 64, pb * CH:(pb + 1) * CH],
                    p1b[bh:bh + 64, 0:CH]))
                PS.release(bk[0], "vector")
                P.op("vector", lambda v, p2b=p2b, q_h=q_h, bh=bh, pb=pb:
                     v.tensor_sub(s2_wts[bh:bh + 64, pb * CH:(pb + 1) * CH],
                                  q_h, p2b[bh:bh + 64, 0:CH]))
                PS.release(bk[2], "vector")
                P.op("vector", lambda v, p3b=p3b, bt_sb=bt_sb, st_sb=st_sb, bh=bh:
                     v.tensor_add(st_sb, p3b[bh:bh + 64, 0:64], bt_sb))
                PS.release(bk[1], "vector")

            for hp in range(H // 2):
                gens = [unit(2 * hp), unit(2 * hp + 1)]
                alive = [True, True]
                while any(alive):
                    for gi in range(2):
                        if alive[gi]:
                            try:
                                next(gens[gi])
                            except StopIteration:
                                alive[gi] = False
            sp = P.mark("vector")
            P.wait("gpsimd", "cV", sp)
            P.dma("gpsimd", ABb[ci, :, :, :].rearrange("p r c -> r p c"),
                  s2_ab[:, :].rearrange("q (p c) -> q p c", p=NB), "s2sp",
                  chain=False)
            P.dma("gpsimd", WTb[ci, :, :, :].rearrange("p r c -> r p c"),
                  s2_wts[:, :].rearrange("q (p c) -> q p c", p=NB), "s2sp",
                  chain=False)
            P.dma("gpsimd",
                  OT[:, ci * CH:(ci + 1) * CH].rearrange("(p r) c -> r p c", p=NB),
                  s2_obs[:, :].rearrange("q (p c) -> q p c", p=NB), "s2sp",
                  chain=False)
            P.wait("vector", "s2sp", P.now("s2sp"))
    P.barrier()


def emit_s3(c):
    P = c["P"]
    if "noxc" in c["stages"]:
        P.op("vector", lambda v: v.tensor_scalar_mul(
            c["sb_stc"][:, :], c["sb_stl"][:, :], c["sb_scl"][:, 3:4]))
        P.barrier()
        return
    nc, pid, partner = c["nc"], c["pid"], c["partner"]
    shst, shfl = c["shst"], c["shfl"]
    stl, stc = c["sb_stl"], c["sb_stc"]
    sb_flag, sb_magic, scl = c["sb_flag"], c["sb_magic"], c["sb_scl"]
    P.dma("gpsimd", shst[pid, :, :], stl[:, :], "s3a", chain=False)
    P.wait("gpsimd", "s3a", P.now("s3a"))
    P.dma("gpsimd", shfl[pid, 0, :], sb_magic[0:1, 0:16], "s3a", chain=False)
    P.wait("gpsimd", "s3a", P.now("s3a"))

    def poll(g):
        val = g.alloc_register("pollv")
        cnt = g.alloc_register("pollc")
        vals = bass.RegisterHandles((val,))
        cnts = bass.RegisterHandles((cnt,))
        nc.regs_mov(vals, 0)
        nc.regs_mov(cnts, P.now("s3p") + 16)
        loop, done = "s3_loop", "s3_done"
        nc.br(loop, engines=[ET.Pool])
        with nc.body(loop, valid_engines=[ET.Pool]):
            g.dma_start(out=sb_flag[:, :], in_=shfl[partner, 0, :][None, :]) \
                .then_inc(P.sems["s3p"], 16)
            g.wait_ge(P.sems["s3p"], cnt)
            nc.regs_alu(cnts, cnts, 16, op=ALU.add)
            nc.regs_load(vals, sb_flag[0:1, 0:1])
            nc.br_cmp(val, MAGIC1, on_true=done, on_false=loop,
                      comp_op="IS_EQ", engines=[ET.Pool])
        nc.switch_bb(done)
        return None
    P.sem("s3p")
    P.op("gpsimd", poll, sig=True, chain=False)
    P.dma("gpsimd", stc[:, :], shst[partner, :, :], "s3b", chain=False)
    P.wait("vector", "s3b", P.now("s3b"))
    P.op("vector", lambda v: v.tensor_scalar_mul(stc[:, :], stc[:, :],
                                                 scl[:, 3:4]))
    P.barrier()


def emit_s4(c):
    P, PS = c["P"], c["PS"]
    s4_ab, s4_wt, s4_o = c["s4_ab"], c["s4_wt"], c["s4_o"]
    stc = c["sb_stc"]
    OT, ABb, WTb = c["OT"], c["ABb"], c["WTb"]
    nch = c.get("S2_NWIN", NT) * 4
    for ci in range(nch):
        P.wait("sync", "s4o", P.now("s4o"))
        P.wait("sync", "cT", P.mark("tensor"))
        P.dma("sync", s4_ab[:, :].rearrange("q (p c) -> q p c", p=NB),
              ABb[ci, :, :, :].rearrange("p r c -> r p c"), "s4w")
        P.dma("sync", s4_wt[:, :].rearrange("q (p c) -> q p c", p=NB),
              WTb[ci, :, :, :].rearrange("p r c -> r p c"), "s4w")
        P.dma("sync", s4_o[:, :].rearrange("q (p c) -> q p c", p=NB),
              OT[:, ci * CH:(ci + 1) * CH].rearrange("(p r) c -> r p c", p=NB),
              "s4w")
        wv_ = P.now("s4w")
        P.wait("tensor", "s4w", wv_)
        P.wait("vector", "s4w", wv_)
        for h in range(H):
            pb, bh = h // 2, (h % 2) * 64
            tp = (lambda r, cc: None if (r == 0 and cc == 0) else (r, cc))
            st_sb = stc[bh:bh + 64, pb * 64:(pb + 1) * 64]
            a_sb = s4_ab[bh:bh + 64, pb * 128:pb * 128 + 64]
            bt_sb = s4_ab[bh:bh + 64, pb * 128 + 64:(pb + 1) * 128]
            wt_sb = s4_wt[bh:bh + 64, pb * CH:(pb + 1) * CH]
            p0 = PS.acquire(h % 2, "tensor")
            P.op("tensor", lambda e, st_sb=st_sb, wt_sb=wt_sb, p0=p0, bh=bh:
                 e.matmul(p0[bh:bh + 64, 0:CH], st_sb, wt_sb,
                          start=True, stop=True, tile_position=tp(bh, bh)))
            p1 = PS.acquire(2 + h % 2, "tensor")
            P.op("tensor", lambda e, a_sb=a_sb, st_sb=st_sb, p1=p1, bh=bh:
                 e.matmul(p1[bh:bh + 64, 0:64], a_sb, st_sb,
                          start=True, stop=True, tile_position=tp(bh, bh)))
            mmk = P.mark("tensor")
            P.wait("vector", "cT", mmk)
            P.op("vector", lambda v, p0=p0, bh=bh, pb=pb: v.tensor_add(
                s4_o[bh:bh + 64, pb * CH:(pb + 1) * CH],
                s4_o[bh:bh + 64, pb * CH:(pb + 1) * CH], p0[bh:bh + 64, 0:CH]))
            PS.release(h % 2, "vector")
            P.op("vector", lambda v, p1=p1, bt_sb=bt_sb, st_sb=st_sb, bh=bh:
                 v.tensor_add(st_sb, p1[bh:bh + 64, 0:64], bt_sb))
            PS.release(2 + h % 2, "vector")
        P.wait("gpsimd", "cV", P.mark("vector"))
        P.dma("gpsimd",
              OT[:, ci * CH:(ci + 1) * CH].rearrange("(p r) c -> r p c", p=NB),
              s4_o[:, :].rearrange("q (p c) -> q p c", p=NB), "s4o", chain=False)
    P.barrier()


def emit_s5(c):
    P, PS = c["P"], c["PS"]
    rows, eps, vecs = c["sb_rows"], c["sb_eps"], c["sb_vecs"]
    onescf, onescb, onesf = c["sb_onescf"], c["sb_onescb"], c["sb_onesf"]
    s5_ot, s5_att, s5_x, s5_xa = c["s5_ot"], c["s5_att"], c["s5_x"], c["s5_xa"]
    s5_sq, s5_h2, s5_bc = c["s5_sq"], c["s5_h2"], c["s5_bc"]
    OT, statsH, x_sl, xaTh, h2Th = (c["OT"], c["statsH"], c["x_sl"], c["xaTh"],
                                    c["h2Th"])
    for kb in range(NB):
        P.dma("gpsimd", c["s5_wo"][:, kb * D:(kb + 1) * D],
              c["wo"][kb * 128:(kb + 1) * 128, :], "wS5a", chain=False)
    P.wait("tensor", "wS5a", P.now("wS5a"))
    for it in range(NT):
        c0 = 1 + it * TT
        P.wait("sync", "cV", P.mark("vector"))
        P.wait("sync", "cS", P.mark("scalar"))
        for mb in range(NB):
            P.dma("sync", s5_ot[:, mb * TT:(mb + 1) * TT],
                  OT[mb * 128:(mb + 1) * 128, it * TT:(it + 1) * TT], "s5w")
            P.dma("sync", s5_x[:, mb * TT:(mb + 1) * TT],
                  x_sl[kb * 128:(kb + 1) * 128, c0:c0 + TT] if False else
                  x_sl[mb * 128:(mb + 1) * 128, c0:c0 + TT], "s5w")
        P.dma("sync", rowap(rows, 0, TT), statsH[0:1, c0:c0 + TT], "s5w")
        P.dma("sync", rowap(rows, 1024, 1024 + TT), statsH[1:2, c0:c0 + TT], "s5w")
        wv_ = P.now("s5w")
        P.wait("tensor", "s5w", wv_)
        p2 = PS.acquire(2, "tensor")
        P.op("tensor", lambda e, p2=p2: e.matmul(
            p2[:, 0:TT], onesf[0:1, :], rowap(rows, 0, TT), start=True, stop=True),
            sig=False)
        p3 = PS.acquire(3, "tensor")
        P.op("tensor", lambda e, p3=p3: e.matmul(
            p3[:, 0:TT], onesf[0:1, :], rowap(rows, 1024, 1024 + TT),
            start=True, stop=True))
        P.wait("vector", "cT", P.mark("tensor"))
        P.op("vector", lambda v, p2=p2: v.tensor_copy(s5_bc[:, 0:TT], p2[:, 0:TT]))
        P.op("vector", lambda v, p3=p3: v.tensor_copy(s5_bc[:, TT:2 * TT],
                                                      p3[:, 0:TT]))
        PS.release(2, "vector")
        PS.release(3, "vector")
        for mb in range(NB):
            pi = 4 + (mb % 2)
            pq = PS.acquire(pi, "tensor")
            for kb in range(NB):
                P.op("tensor", lambda e, kb=kb, mb=mb, pq=pq: e.matmul(
                    pq[:, 0:TT],
                    c["s5_wo"][:, kb * D + mb * 128:kb * D + (mb + 1) * 128],
                    s5_ot[:, kb * TT:(kb + 1) * TT],
                    start=(kb == 0), stop=(kb == NB - 1)), sig=(kb == NB - 1))
            P.wait("scalar", "cT", P.mark("tensor"))
            P.op("scalar", lambda e, mb=mb, pq=pq: e.activation(
                s5_att[:, mb * TT:(mb + 1) * TT], pq[:, 0:TT], AF.Identity,
                bias=vecs[:, 24 + mb:25 + mb]))
            PS.release(pi, "scalar")
        P.wait("vector", "cS", P.mark("scalar"))
        P.wait("vector", "s5o", P.now("s5o"))
        for mb in range(NB):
            sl = slice(mb * TT, (mb + 1) * TT)
            P.op("vector", lambda v, sl=sl: v.tensor_sub(
                s5_x[:, sl], s5_x[:, sl], s5_bc[:, 0:TT]))
            P.op("vector", lambda v, sl=sl: v.tensor_mul(
                s5_x[:, sl], s5_x[:, sl], s5_bc[:, TT:2 * TT]))
            P.op("vector", lambda v, sl=sl: v.tensor_add(
                s5_xa[:, sl], s5_x[:, sl], s5_att[:, sl]))
        xav = P.mark("vector")
        P.wait("gpsimd", "cV", xav)
        for mb in range(NB):
            P.dma("gpsimd", xaTh[mb * 128:(mb + 1) * 128, it * TT:(it + 1) * TT],
                  s5_xa[:, mb * TT:(mb + 1) * TT], "s5o", chain=False)
        P.wait("scalar", "cV", xav)
        for mb in range(NB):
            P.op("scalar", lambda e, mb=mb: e.activation(
                s5_sq[:, mb * TT:(mb + 1) * TT], s5_xa[:, mb * TT:(mb + 1) * TT],
                AF.Square))
        P.wait("tensor", "cS", P.mark("scalar"))
        p0 = PS.acquire(0, "tensor")
        for kb in range(NB):
            P.op("tensor", lambda e, kb=kb, p0=p0: e.matmul(
                p0[0:1, 0:TT], onescb[:, 0:1], s5_sq[:, kb * TT:(kb + 1) * TT],
                start=(kb == 0), stop=(kb == NB - 1)), sig=(kb == NB - 1))
        P.wait("vector", "cT", P.mark("tensor"))
        P.op("vector", lambda v, p0=p0: v.tensor_scalar_mul(
            rowap(rows, 1536, 1536 + TT), p0[0:1, 0:TT], eps[0:1, 3:4]))
        PS.release(0, "vector")
        P.wait("scalar", "cV", P.mark("vector"))
        P.op("scalar", lambda e: e.activation(rowap(rows, 1536, 1536 + TT),
                                              rowap(rows, 1536, 1536 + TT),
                                              AF.Ln, bias=eps[0:1, 1:2]))
        P.op("scalar", lambda e: e.activation(rowap(rows, 1536, 1536 + TT),
                                              rowap(rows, 1536, 1536 + TT),
                                              AF.Exp, scale=-0.5))
        P.wait("tensor", "cS", P.mark("scalar"))
        p1 = PS.acquire(1, "tensor")
        P.op("tensor", lambda e, p1=p1: e.matmul(
            p1[:, 0:TT], onesf[0:1, :], rowap(rows, 1536, 1536 + TT),
            start=True, stop=True))
        P.wait("vector", "cT", P.mark("tensor"))
        P.op("vector", lambda v, p1=p1: v.tensor_copy(s5_bc[:, 2 * TT:3 * TT],
                                                      p1[:, 0:TT]))
        PS.release(1, "vector")
        P.wait("vector", "s5o", P.now("s5o"))
        for mb in range(NB):
            P.op("vector", lambda v, mb=mb: v.scalar_tensor_tensor(
                s5_h2[:, mb * TT:(mb + 1) * TT], s5_xa[:, mb * TT:(mb + 1) * TT],
                vecs[:, 48 + mb:49 + mb], s5_bc[:, 2 * TT:3 * TT],
                ALU.mult, ALU.mult))
        P.wait("gpsimd", "cV", P.mark("vector"))
        for mb in range(NB):
            P.dma("gpsimd", h2Th[mb * 128:(mb + 1) * 128, it * TT:(it + 1) * TT],
                  s5_h2[:, mb * TT:(mb + 1) * TT], "s5o", chain=False)
    P.barrier()
    s5_h2b, s5_hf, s5_gt = c["s5_h2b"], c["s5_hf"], c["s5_gt"]
    s5_xab, s5_s, sb_b1 = c["s5_xab"], c["s5_s"], c["sb_b1"]
    sTh = c["sTh"]
    for kb in range(NB):
        P.dma("gpsimd", c["s5_w1"][:, kb * DFF:(kb + 1) * DFF],
              c["w1"][kb * 128:(kb + 1) * 128, :], "wS5b", chain=False)
    for ko in range(32):
        P.dma("gpsimd", c["s5_w2"][:, ko * D:(ko + 1) * D],
              c["w2"][ko * 128:(ko + 1) * 128, :], "wS5b", chain=False)
    P.wait("tensor", "wS5b", P.now("wS5b"))
    for it in range(NT):
        P.wait("sync", "cT", P.mark("tensor"))
        for mb in range(NB):
            P.dma("sync", s5_h2b[:, mb * TT:(mb + 1) * TT],
                  h2Th[mb * 128:(mb + 1) * 128, it * TT:(it + 1) * TT], "s5bw")
        wv_ = P.now("s5bw")
        P.wait("tensor", "s5bw", wv_)
        P.wait("vector", "cT", P.mark("tensor"))
        P.wait("scalar", "cT", P.mark("tensor"))
        for mo in range(32):
            pi = 4 + (mo % 2)
            pq = PS.acquire(pi, "tensor")
            for kb in range(NB):
                P.op("tensor", lambda e, kb=kb, mo=mo, pq=pq: e.matmul(
                    pq[:, 0:TT],
                    c["s5_w1"][:, kb * DFF + mo * 128:kb * DFF + (mo + 1) * 128],
                    s5_h2b[:, kb * TT:(kb + 1) * TT],
                    start=(kb == 0), stop=(kb == NB - 1)), sig=(kb == NB - 1))
            P.wait("scalar", "cT", P.mark("tensor"))
            P.wait("scalar", "cV", P.mark("vector"))
            t0 = s5_gt[:, 0:TT]
            t1 = s5_gt[:, TT:2 * TT]
            P.op("scalar", lambda e, mo=mo, pq=pq, t0=t0: e.activation(
                t0, pq[:, 0:TT], AF.Identity, bias=sb_b1[:, mo:mo + 1]))
            PS.release(pi, "scalar")
            P.op("scalar", lambda e, t0=t0, t1=t1: e.activation(t1, t0, AF.Square))
            P.wait("vector", "cS", P.mark("scalar"))
            P.op("vector", lambda v, t1=t1: v.tensor_scalar(
                t1, t1, 0.044715, 1.0, ALU.mult, ALU.add))
            P.op("vector", lambda v, t0=t0, t1=t1: v.tensor_mul(t1, t0, t1))
            P.wait("scalar", "cV", P.mark("vector"))
            P.op("scalar", lambda e, t1=t1: e.activation(
                t1, t1, AF.Tanh, scale=0.7978845608028654))
            P.wait("vector", "cS", P.mark("scalar"))
            P.op("vector", lambda v, t1=t1: v.tensor_scalar(
                t1, t1, 0.5, 0.5, ALU.mult, ALU.add))
            P.op("vector", lambda v, mo=mo, t0=t0, t1=t1: v.tensor_mul(
                s5_hf[:, mo * TT:(mo + 1) * TT], t0, t1))
        hfv = P.mark("vector")
        P.wait("tensor", "cV", hfv)
        for mb in range(NB):
            pi = 4 + (mb % 2)
            pq = PS.acquire(pi, "tensor")
            for ko in range(32):
                P.op("tensor", lambda e, ko=ko, mb=mb, pq=pq: e.matmul(
                    pq[:, 0:TT],
                    c["s5_w2"][:, ko * D + mb * 128:ko * D + (mb + 1) * 128],
                    s5_hf[:, ko * TT:(ko + 1) * TT],
                    start=(ko == 0), stop=(ko == 31)), sig=(ko == 31))
            P.dma("sync", s5_xab[:, 0:TT],
                  xaTh[mb * 128:(mb + 1) * 128, it * TT:(it + 1) * TT], "s5bx",
                  wait=[("cV", P.mark("vector"))])
            P.wait("scalar", "cT", P.mark("tensor"))
            P.wait("scalar", "s5so", P.now("s5so"))
            P.op("scalar", lambda e, mb=mb, pq=pq: e.activation(
                s5_s[:, 0:TT], pq[:, 0:TT], AF.Identity,
                bias=vecs[:, 32 + mb:33 + mb]))
            PS.release(pi, "scalar")
            P.wait("vector", "cS", P.mark("scalar"))
            P.wait("vector", "s5bx", P.now("s5bx"))
            P.op("vector", lambda v: v.tensor_add(s5_s[:, 0:TT], s5_s[:, 0:TT],
                                                  s5_xab[:, 0:TT]))
            P.wait("gpsimd", "cV", P.mark("vector"))
            P.dma("gpsimd", sTh[mb * 128:(mb + 1) * 128, it * TT:(it + 1) * TT],
                  s5_s[:, 0:TT], "s5so", chain=False)
    P.barrier()
    if "s5" in c["stages"] and c["dev"]:
        P.dma("sync", c["dev_out"]["d_s"][:, :], sTh[:, :], "dev")
        P.dma("sync", c["dev_out"]["d_xa"][:, :], xaTh[:, :], "dev")
        P.dma("sync", c["dev_out"]["d_h2"][:, :], h2Th[:, :], "dev")
        P.wait("sync", "dev", P.now("dev"))


def emit_s6(c):
    P, PS = c["P"], c["PS"]
    rows, eps, vecs = c["sb_rows"], c["sb_eps"], c["sb_vecs"]
    onescf, onescb, onesf = c["sb_onescf"], c["sb_onescb"], c["sb_onesf"]
    lastc = c["sb_lastc"]
    s6_x, s6_xn, s6_xr, s6_xk = c["s6_x"], c["s6_xn"], c["s6_xr"], c["s6_xk"]
    s6_r, s6_kk, s6_sld, s6_g = c["s6_r"], c["s6_kk"], c["s6_sld"], c["s6_g"]
    s6_x1, s6_sq, s6_bc, s6_xn1 = c["s6_x1"], c["s6_sq"], c["s6_bc"], c["s6_xn1"]
    x_sl, statsH, stats1H, sTh = c["x_sl"], c["statsH"], c["stats1H"], c["sTh"]
    x1T, xn1T = c["x1T"], c["xn1T_t"]
    WB = TT + 8
    for w_h, w_s in [(c["wrT"], c["s6_wr"]), (c["wkT"], c["s6_wkk"]),
                     (c["wvT"], c["s6_wvv"])]:
        for kb in range(NB):
            P.dma("gpsimd", w_s[:, kb * D:(kb + 1) * D],
                  w_h[kb * 128:(kb + 1) * 128, :], "wS6", chain=False)
    P.wait("tensor", "wS6", P.now("wS6"))
    P.dma("sync", rowap(rows, 0, 8), statsH[0:1, 0:8], "s6w",
          allow_slow_non_contiguous=True)
    P.dma("sync", rowap(rows, 1024, 1032), statsH[1:2, 0:8], "s6w",
          allow_slow_non_contiguous=True)
    for mb in range(NB):
        P.dma("sync", s6_x[:, mb * TT:mb * TT + 1],
              x_sl[mb * 128:(mb + 1) * 128, 0:1], "s6w",
              allow_slow_non_contiguous=True)
    P.wait("tensor", "s6w", P.now("s6w"))
    ph = PS.acquire(0, "tensor")
    P.op("tensor", lambda e, ph=ph: e.matmul(
        ph[:, 0:1], onesf[0:1, :], rowap(rows, 0, 1), start=True, stop=True),
        sig=False)
    P.op("tensor", lambda e, ph=ph: e.matmul(
        ph[:, 1:2], onesf[0:1, :], rowap(rows, 1024, 1025), start=True, stop=True))
    P.wait("vector", "cT", P.mark("tensor"))
    for mb in range(NB):
        P.op("vector", lambda v, mb=mb, ph=ph: v.tensor_sub(
            lastc[:, mb:mb + 1], s6_x[:, mb * TT:mb * TT + 1], ph[:, 0:1]))
        P.op("vector", lambda v, mb=mb, ph=ph: v.tensor_mul(
            lastc[:, mb:mb + 1], lastc[:, mb:mb + 1], ph[:, 1:2]))
    PS.release(0, "vector")
    for it in range(NT):
        c0 = 1 + it * TT
        P.wait("sync", "cV", P.mark("vector"))
        P.wait("sync", "cT", P.mark("tensor"))
        P.wait("sync", "s6so", P.now("s6so"))
        for mb in range(NB):
            P.dma("sync", s6_x[:, mb * TT:(mb + 1) * TT],
                  x_sl[mb * 128:(mb + 1) * 128, c0:c0 + TT], "s6w")
        P.dma("sync", rowap(rows, 0, TT), statsH[0:1, c0:c0 + TT], "s6w")
        P.dma("sync", rowap(rows, 1024, 1024 + TT), statsH[1:2, c0:c0 + TT], "s6w")
        wv_ = P.now("s6w")
        P.wait("tensor", "s6w", wv_)
        P.wait("vector", "cT", P.mark("tensor"))
        p2 = PS.acquire(2, "tensor")
        P.op("tensor", lambda e, p2=p2: e.matmul(
            p2[:, 0:TT], onesf[0:1, :], rowap(rows, 0, TT), start=True, stop=True),
            sig=False)
        p3 = PS.acquire(3, "tensor")
        P.op("tensor", lambda e, p3=p3: e.matmul(
            p3[:, 0:TT], onesf[0:1, :], rowap(rows, 1024, 1024 + TT),
            start=True, stop=True))
        P.wait("vector", "cT", P.mark("tensor"))
        P.op("vector", lambda v, p2=p2: v.tensor_copy(s6_bc[:, 0:TT], p2[:, 0:TT]))
        P.op("vector", lambda v, p3=p3: v.tensor_copy(s6_bc[:, TT:2 * TT],
                                                      p3[:, 0:TT]))
        PS.release(2, "vector")
        PS.release(3, "vector")
        for mb in range(NB):
            xb = slice(mb * WB + 1, mb * WB + 1 + TT)
            P.op("vector", lambda v, mb=mb: v.tensor_copy(
                s6_xn[:, mb * WB:mb * WB + 1], lastc[:, mb:mb + 1]))
            P.op("vector", lambda v, mb=mb, xb=xb: v.tensor_sub(
                s6_x[:, mb * TT:(mb + 1) * TT], s6_x[:, mb * TT:(mb + 1) * TT],
                s6_bc[:, 0:TT]))
            P.op("vector", lambda v, mb=mb, xb=xb: v.tensor_tensor(
                s6_xn[:, xb], s6_x[:, mb * TT:(mb + 1) * TT],
                s6_bc[:, TT:2 * TT], ALU.mult))
            P.op("vector", lambda v, mb=mb: v.tensor_copy(
                lastc[:, mb:mb + 1], s6_xn[:, mb * WB + TT:mb * WB + TT + 1]))
        for mb in range(NB):
            cur = s6_xn[:, mb * WB + 1:mb * WB + 1 + TT]
            prv = s6_xn[:, mb * WB:mb * WB + TT]
            dif = s6_g[:, 0:TT]
            P.op("vector", lambda v, cur=cur, prv=prv, dif=dif: v.tensor_sub(
                dif, cur, prv))
            P.op("vector", lambda v, mb=mb, dif=dif, prv=prv: v.scalar_tensor_tensor(
                s6_xr[:, mb * TT:(mb + 1) * TT], dif, vecs[:, 56 + mb:57 + mb],
                prv, ALU.mult, ALU.add))
            P.op("vector", lambda v, mb=mb, dif=dif, prv=prv: v.scalar_tensor_tensor(
                s6_xk[:, mb * TT:(mb + 1) * TT], dif, vecs[:, 64 + mb:65 + mb],
                prv, ALU.mult, ALU.add))
        mixv = P.mark("vector")
        P.wait("tensor", "cV", mixv)
        for mo in range(NB):
            pi = 4 + (mo % 2)
            pq = PS.acquire(pi, "tensor")
            for kb in range(NB):
                P.op("tensor", lambda e, kb=kb, mo=mo, pq=pq: e.matmul(
                    pq[:, 0:TT],
                    c["s6_wr"][:, kb * D + mo * 128:kb * D + (mo + 1) * 128],
                    s6_xr[:, kb * TT:(kb + 1) * TT],
                    start=(kb == 0), stop=(kb == NB - 1)), sig=(kb == NB - 1))
            P.wait("scalar", "cT", P.mark("tensor"))
            P.op("scalar", lambda e, mo=mo, pq=pq: e.activation(
                s6_r[:, mo * TT:(mo + 1) * TT], pq[:, 0:TT], AF.Sigmoid))
            PS.release(pi, "scalar")
        for mo in range(NB):
            pi = 4 + (mo % 2)
            pq = PS.acquire(pi, "tensor")
            for kb in range(NB):
                P.op("tensor", lambda e, kb=kb, mo=mo, pq=pq: e.matmul(
                    pq[:, 0:TT],
                    c["s6_wkk"][:, kb * D + mo * 128:kb * D + (mo + 1) * 128],
                    s6_xk[:, kb * TT:(kb + 1) * TT],
                    start=(kb == 0), stop=(kb == NB - 1)), sig=(kb == NB - 1))
            P.wait("scalar", "cT", P.mark("tensor"))
            P.op("scalar", lambda e, pq=pq: e.activation(
                s6_g[:, 0:TT], pq[:, 0:TT], AF.Relu))
            PS.release(pi, "scalar")
            P.op("scalar", lambda e, mo=mo: e.activation(
                s6_kk[:, mo * TT:(mo + 1) * TT], s6_g[:, 0:TT], AF.Square))
        kkv = P.mark("scalar")
        P.wait("tensor", "cS", kkv)
        P.wait("sync", "cV", P.mark("vector"))
        for mo in range(NB):
            P.dma("sync", s6_sld[:, mo * TT:(mo + 1) * TT],
                  sTh[mo * 128:(mo + 1) * 128, it * TT:(it + 1) * TT], "s6s")
        for mo in range(NB):
            pi = 4 + (mo % 2)
            pq = PS.acquire(pi, "tensor")
            for kb in range(NB):
                P.op("tensor", lambda e, kb=kb, mo=mo, pq=pq: e.matmul(
                    pq[:, 0:TT],
                    c["s6_wvv"][:, kb * D + mo * 128:kb * D + (mo + 1) * 128],
                    s6_kk[:, kb * TT:(kb + 1) * TT],
                    start=(kb == 0), stop=(kb == NB - 1)), sig=(kb == NB - 1))
            P.wait("vector", "cT", P.mark("tensor"))
            P.op("vector", lambda v, mo=mo, pq=pq: v.tensor_mul(
                s6_g[:, 0:TT], pq[:, 0:TT], s6_r[:, mo * TT:(mo + 1) * TT]))
            PS.release(pi, "vector")
            P.wait("scalar", "cV", P.mark("vector"))
            P.op("scalar", lambda e: e.activation(
                s6_g[:, TT:2 * TT], s6_g[:, 0:TT], AF.Sigmoid))
            P.wait("vector", "cS", P.mark("scalar"))
            P.op("vector", lambda v: v.tensor_mul(
                s6_g[:, 0:TT], s6_g[:, 0:TT], s6_g[:, TT:2 * TT]))
            P.wait("vector", "s6s", P.now("s6s"))
            P.op("vector", lambda v, mo=mo: v.tensor_mul(
                s6_g[:, 0:TT], s6_g[:, 0:TT], s6_sld[:, mo * TT:(mo + 1) * TT]))
            P.op("vector", lambda v, mo=mo: v.tensor_add(
                s6_g[:, 0:TT], s6_g[:, 0:TT], s6_bc[:, 0:TT]))
            P.op("vector", lambda v, mo=mo: v.tensor_add(
                s6_x1[:, mo * TT:(mo + 1) * TT], s6_x[:, mo * TT:(mo + 1) * TT],
                s6_g[:, 0:TT]))
        x1v = P.mark("vector")
        P.wait("gpsimd", "cV", x1v)
        P.wait("gpsimd", "s6o", P.now("s6o"))
        for mb in range(NB):
            P.dma("gpsimd", x1T[mb * 128:(mb + 1) * 128, it * TT:(it + 1) * TT],
                  s6_x1[:, mb * TT:(mb + 1) * TT], "s6o", chain=False)
        P.wait("scalar", "cV", x1v)
        for mb in range(NB):
            P.op("scalar", lambda e, mb=mb: e.activation(
                s6_sq[:, mb * TT:(mb + 1) * TT], s6_x1[:, mb * TT:(mb + 1) * TT],
                AF.Square))
        P.wait("tensor", "cS", P.mark("scalar"))
        P.wait("tensor", "cV", x1v)
        p0 = PS.acquire(0, "tensor")
        p1 = PS.acquire(1, "tensor")
        for kb in range(NB):
            P.op("tensor", lambda e, kb=kb, p0=p0: e.matmul(
                p0[0:1, 0:TT], onescf[:, 0:1], s6_x1[:, kb * TT:(kb + 1) * TT],
                start=(kb == 0), stop=(kb == NB - 1)), sig=(kb == NB - 1))
        for kb in range(NB):
            P.op("tensor", lambda e, kb=kb, p1=p1: e.matmul(
                p1[0:1, 0:TT], onescb[:, 0:1], s6_sq[:, kb * TT:(kb + 1) * TT],
                start=(kb == 0), stop=(kb == NB - 1)), sig=(kb == NB - 1))
        P.wait("vector", "cT", P.mark("tensor"))
        P.wait("vector", "s6so", P.now("s6so"))
        r_m = rowap(rows, 1536, 1536 + TT)
        r_v = rowap(rows, 512, 512 + TT)
        P.op("vector", lambda v, p0=p0, r_m=r_m: v.tensor_scalar_mul(
            r_m, p0[0:1, 0:TT], eps[0:1, 3:4]))
        P.op("vector", lambda v, p1=p1, r_v=r_v: v.tensor_scalar_mul(
            r_v, p1[0:1, 0:TT], eps[0:1, 3:4]))
        PS.release(0, "vector")
        PS.release(1, "vector")
        P.op("vector", lambda v, r_m=r_m: v.tensor_mul(
            rowap(rows, 0, TT), r_m, r_m))
        P.op("vector", lambda v, r_v=r_v: v.tensor_sub(
            rowap(rows, 0, TT), r_v, rowap(rows, 0, TT)))
        P.wait("scalar", "cV", P.mark("vector"))
        P.op("scalar", lambda e: e.activation(
            rowap(rows, 0, TT), rowap(rows, 0, TT), AF.Ln,
            bias=eps[0:1, 0:1]))
        P.op("scalar", lambda e: e.activation(
            rowap(rows, 0, TT), rowap(rows, 0, TT), AF.Exp,
            scale=-0.5))
        scv = P.mark("scalar")
        P.wait("sync", "cS", scv)
        P.dma("sync", stats1H[0:1, c0:c0 + TT], r_m, "s6so")
        P.dma("sync", stats1H[1:2, c0:c0 + TT], rowap(rows, 0, TT), "s6so")
        P.wait("tensor", "cS", scv)
        p2b = PS.acquire(2, "tensor")
        P.op("tensor", lambda e, p2b=p2b: e.matmul(
            p2b[:, 0:TT], onesf[0:1, :], r_m, start=True, stop=True), sig=False)
        p3b = PS.acquire(3, "tensor")
        P.op("tensor", lambda e, p3b=p3b: e.matmul(
            p3b[:, 0:TT], onesf[0:1, :], rowap(rows, 0, TT),
            start=True, stop=True))
        P.wait("vector", "cT", P.mark("tensor"))
        P.op("vector", lambda v, p2b=p2b: v.tensor_copy(s6_bc[:, 0:TT],
                                                        p2b[:, 0:TT]))
        P.op("vector", lambda v, p3b=p3b: v.tensor_copy(s6_bc[:, TT:2 * TT],
                                                        p3b[:, 0:TT]))
        PS.release(2, "vector")
        PS.release(3, "vector")
        P.wait("vector", "s6xo", P.now("s6xo"))
        P.wait("vector", "s6o", P.now("s6o"))
        for mb in range(NB):
            P.op("vector", lambda v, mb=mb: v.tensor_sub(
                s6_x1[:, mb * TT:(mb + 1) * TT], s6_x1[:, mb * TT:(mb + 1) * TT],
                s6_bc[:, 0:TT]))
            P.op("vector", lambda v, mb=mb: v.tensor_tensor(
                s6_xn1[:, mb * TT:(mb + 1) * TT], s6_x1[:, mb * TT:(mb + 1) * TT],
                s6_bc[:, TT:2 * TT], ALU.mult))
        P.wait("gpsimd", "cV", P.mark("vector"))
        for mb in range(NB):
            P.dma("gpsimd", xn1T[mb * 128:(mb + 1) * 128, it * TT:(it + 1) * TT],
                  s6_xn1[:, mb * TT:(mb + 1) * TT], "s6xo", chain=False)
    P.barrier()


def emit_cv(c):
    P = c["P"]
    cv_a, cv_b, cv_o, eps = c["cv_a"], c["cv_b"], c["cv_o"], c["sb_eps"]
    for kb in range(NB):
        P.wait("sync", "cV", P.mark("vector"))
        P.dma("sync", cv_a[:, :], c["k0u"][kb * 128:(kb + 1) * 128, :], "cvw")
        P.dma("sync", cv_b[:, :], c["k0u"][D + kb * 128:D + (kb + 1) * 128, :], "cvw")
        P.wait("vector", "cvw", P.now("cvw"))
        P.wait("vector", "cvo", P.now("cvo"))
        P.op("vector", lambda v: v.tensor_sub(cv_o[:, :], cv_a[:, :], cv_b[:, :]))
        P.wait("gpsimd", "cV", P.mark("vector"))
        P.dma("gpsimd", c["d0T"][kb * 128:(kb + 1) * 128, :], cv_o[:, :], "cvo",
              chain=False)
    for src_t, dst_t, n in [(c["k1u"], c["w1eT"], 32), (c["k2u"], c["w2eT"], 32)]:
        w = src_t.shape[1]
        for kb in range(n):
            P.wait("sync", "cS", P.mark("scalar"))
            P.wait("sync", "cV", P.mark("vector"))
            P.dma("sync", cv_a[:, 0:w], src_t[kb * 128:(kb + 1) * 128, :], "cvw")
            P.wait("scalar", "cvw", P.now("cvw"))
            P.wait("scalar", "cvo", P.now("cvo"))
            P.op("scalar", lambda e, w=w: e.activation(
                cv_o[:, 0:w], cv_a[:, 0:w], AF.Identity, bias=eps[:, 4:5],
                scale=2.0))
            P.wait("gpsimd", "cS", P.mark("scalar"))
            P.dma("gpsimd", dst_t[kb * 128:(kb + 1) * 128, :], cv_o[:, 0:w], "cvo",
                  chain=False)
    P.barrier()


def emit_t1(c):
    P, PS = c["P"], c["PS"]
    t1_xn1, t1_q, t1_kt, t1_vt, t1_tmp = (c["t1_xn1"], c["t1_q"], c["t1_kt"],
                                          c["t1_vt"], c["t1_tmp"])
    bqkv, eps = c["sb_bqkv"], c["sb_eps"]
    xn1T, q1T = c["xn1T_t"], c["q1T"]
    for kb in range(NB):
        P.dma("gpsimd", c["t1_wqkv"][:, kb * 3 * D:(kb + 1) * 3 * D],
              c["wqkv"][kb * 128:(kb + 1) * 128, :], "wT1", chain=False)
    P.dma("sync", c["sb_bqkbc"][:, :], c["bqkv_k_bc"][:, :], "wT1s")
    P.dma("sync", c["sb_bqvbc"][:, :], c["bqkv_v_bc"][:, :], "wT1s")
    P.wait("tensor", "wT1", P.now("wT1"))
    P.wait("vector", "wT1", P.now("wT1"))
    P.wait("vector", "wT1s", P.now("wT1s"))
    P.op("vector", lambda v: v.memset(c["sb_kvloc"][:, :], 0.0))
    for it in range(NT):
        P.wait("sync", "cT", P.mark("tensor"))
        P.wait("sync", "cV", P.mark("vector"))
        P.wait("sync", "cS", P.mark("scalar"))
        for mb in range(NB):
            P.dma("sync", t1_xn1[:, mb * TT:(mb + 1) * TT],
                  xn1T[mb * 128:(mb + 1) * 128, it * TT:(it + 1) * TT], "t1w")
        wv_ = P.now("t1w")
        P.wait("tensor", "t1w", wv_)
        P.wait("scalar", "t1qo", P.now("t1qo"))
        P.wait("vector", "t1qo", P.now("t1qo"))
        for mo in range(NB):
            pi = mo % 2
            pq = PS.acquire(pi, "tensor")
            for kb in range(NB):
                P.op("tensor", lambda e, kb=kb, mo=mo, pq=pq: e.matmul(
                    pq[:, 0:TT],
                    c["t1_wqkv"][:, kb * 3 * D + mo * 128:kb * 3 * D + (mo + 1) * 128],
                    t1_xn1[:, kb * TT:(kb + 1) * TT],
                    start=(kb == 0), stop=(kb == NB - 1)), sig=(kb == NB - 1))
            P.wait("scalar", "cT", P.mark("tensor"))
            P.wait("scalar", "cV", P.mark("vector"))
            u = t1_tmp[:, 0:TT]
            w2_ = t1_tmp[:, TT:2 * TT]
            P.op("scalar", lambda e, mo=mo, pq=pq, u=u: e.activation(
                u, pq[:, 0:TT], AF.Identity, bias=bqkv[:, mo:mo + 1]))
            PS.release(pi, "scalar")
            P.wait("vector", "cS", P.mark("scalar"))
            P.op("vector", lambda v, u=u, w2_=w2_: v.tensor_scalar_min(w2_, u, 0.0))
            P.wait("scalar", "cV", P.mark("vector"))
            P.op("scalar", lambda e, w2_=w2_: e.activation(w2_, w2_, AF.Exp))
            P.wait("vector", "cS", P.mark("scalar"))
            P.op("vector", lambda v, u=u: v.tensor_scalar_max(u, u, 0.0))
            P.op("vector", lambda v, mo=mo, u=u, w2_=w2_: v.tensor_add(
                t1_q[:, mo * TT:(mo + 1) * TT], u, w2_))
        P.wait("gpsimd", "cV", P.mark("vector"))
        for mo in range(NB):
            P.dma("gpsimd", q1T[mo * 128:(mo + 1) * 128, it * TT:(it + 1) * TT],
                  t1_q[:, mo * TT:(mo + 1) * TT], "t1qo", chain=False)
        for g in range(4):
            def lh(kb, g=g):
                return t1_xn1[:, kb * TT + g * 128:kb * TT + (g + 1) * 128]
            for cc in range(2):
                pi = cc
                pk = PS.acquire(pi, "tensor")
                for kb in range(NB):
                    P.op("tensor", lambda e, kb=kb, cc=cc, pk=pk, g=g: e.matmul(
                        pk[:, 0:TT], lh(kb, g),
                        c["t1_wqkv"][:, kb * 3 * D + D + cc * TT:
                                     kb * 3 * D + D + (cc + 1) * TT],
                        start=(kb == 0), stop=(kb == NB - 1)), sig=(kb == NB - 1))
                P.wait("vector", "cT", P.mark("tensor"))
                P.wait("vector", "cS", P.mark("scalar"))
                u = t1_tmp[:, 0:TT]
                w2_ = t1_tmp[:, TT:2 * TT]
                P.op("vector", lambda v, cc=cc, pk=pk, u=u: v.tensor_add(
                    u, pk[:, 0:TT], c["sb_bqkbc"][:, cc * TT:(cc + 1) * TT]))
                PS.release(pi, "vector")
                P.op("vector", lambda v, u=u, w2_=w2_: v.tensor_scalar_min(w2_, u, 0.0))
                P.wait("scalar", "cV", P.mark("vector"))
                P.op("scalar", lambda e, w2_=w2_: e.activation(w2_, w2_, AF.Exp))
                P.wait("vector", "cS", P.mark("scalar"))
                P.op("vector", lambda v, u=u: v.tensor_scalar_max(u, u, 0.0))
                P.op("vector", lambda v, cc=cc, u=u, w2_=w2_: v.tensor_tensor(
                    t1_kt[:, cc * TT:(cc + 1) * TT], u, w2_, ALU.add))
            for cc in range(2):
                pi = cc
                pv = PS.acquire(pi, "tensor")
                for kb in range(NB):
                    P.op("tensor", lambda e, kb=kb, cc=cc, pv=pv, g=g: e.matmul(
                        pv[:, 0:TT], lh(kb, g),
                        c["t1_wqkv"][:, kb * 3 * D + 2 * D + cc * TT:
                                     kb * 3 * D + 2 * D + (cc + 1) * TT],
                        start=(kb == 0), stop=(kb == NB - 1)), sig=(kb == NB - 1))
                P.wait("vector", "cT", P.mark("tensor"))
                P.op("vector", lambda v, cc=cc, pv=pv: v.tensor_add(
                    t1_vt[:, cc * TT:(cc + 1) * TT], pv[:, 0:TT],
                    c["sb_bqvbc"][:, cc * TT:(cc + 1) * TT]))
                PS.release(pi, "vector")
            kvv = P.mark("vector")
            if c["dev"] and "d_ktall" in c["dev_out"]:
                P.wait("gpsimd", "cV", kvv)
                P.dma("gpsimd", c["dev_out"]["d_ktall"][it * 4 + g, :, :],
                      c["t1_kt"][:, :], "dktg", chain=False)
                P.wait("vector", "dktg", P.now("dktg"))
            P.wait("tensor", "cV", kvv)
            pkv0 = PS.acquire(4, "tensor")
            pkv1 = PS.acquire(5, "tensor")
            for h in range(H):
                p_ = h // 2
                bh = (h % 2) * 64
                pp = pkv0 if p_ < 4 else pkv1
                co = (p_ % 4) * 65
                P.op("tensor", lambda e, h=h, bh=bh, pp=pp, co=co:
                     e.matmul(pp[bh:bh + 64, co:co + 64],
                              t1_kt[:, h * 64:(h + 1) * 64],
                              t1_vt[:, h * 64:(h + 1) * 64],
                              start=True, stop=True,
                              tile_position=(0, bh) if bh else None), sig=False)
                P.op("tensor", lambda e, h=h, bh=bh, pp=pp, co=co:
                     e.matmul(pp[bh:bh + 64, co + 64:co + 65],
                              t1_kt[:, h * 64:(h + 1) * 64],
                              c["sb_onescb"][:, 0:1],
                              start=True, stop=True,
                              tile_position=(0, bh) if bh else None),
                     sig=(h == H - 1))
            P.wait("vector", "cT", P.mark("tensor"))
            P.op("vector", lambda v, pkv0=pkv0: v.tensor_add(
                c["sb_kvloc"][:, 0:260], c["sb_kvloc"][:, 0:260], pkv0[:, 0:260]))
            P.op("vector", lambda v, pkv1=pkv1: v.tensor_add(
                c["sb_kvloc"][:, 260:520], c["sb_kvloc"][:, 260:520],
                pkv1[:, 0:260]))
            PS.release(4, "vector")
            PS.release(5, "vector")
    P.barrier()


def emit_t2(c):
    P = c["P"]
    if "noxc" in c["stages"]:
        kvl, kvb, ks = c["sb_kvloc"], c["sb_kvb"], c["sb_ks"]
        P.op("vector", lambda v: v.tensor_copy(kvb[:, :], kvl[:, :]))
        P.op("vector", lambda v: v.memset(ks[:, :], 0.0))
        for h in range(H):
            bh = (h % 2) * 64
            P.op("vector", lambda v, h=h, bh=bh: v.tensor_copy(
                ks[bh:bh + 64, (h // 2) * H + h:(h // 2) * H + h + 1],
                kvb[bh:bh + 64, (h // 2) * 65 + 64:(h // 2) * 65 + 65]))
        P.barrier()
        return
    nc, pid, partner = c["nc"], c["pid"], c["partner"]
    shkv, shfl = c["shkv"], c["shfl"]
    kvl, kvp, kvb, ks = c["sb_kvloc"], c["sb_kvp"], c["sb_kvb"], c["sb_ks"]
    sb_flag, sb_magic = c["sb_flag"], c["sb_magic"]
    P.dma("gpsimd", shkv[pid, :, :], kvl[:, :], "t2a", chain=False)
    P.wait("gpsimd", "t2a", P.now("t2a"))
    P.dma("gpsimd", shfl[pid, 1, :], sb_magic[0:1, 16:32], "t2a", chain=False)
    P.wait("gpsimd", "t2a", P.now("t2a"))

    def poll(g):
        val = g.alloc_register("pollv2")
        cnt = g.alloc_register("pollc2")
        vals = bass.RegisterHandles((val,))
        cnts = bass.RegisterHandles((cnt,))
        nc.regs_mov(vals, 0)
        nc.regs_mov(cnts, 16)
        loop, done = "t2_loop", "t2_done"
        nc.br(loop, engines=[ET.Pool])
        with nc.body(loop, valid_engines=[ET.Pool]):
            g.dma_start(out=sb_flag[:, :], in_=shfl[partner, 1, :][None, :]) \
                .then_inc(P.sems["t2p"], 16)
            g.wait_ge(P.sems["t2p"], cnt)
            nc.regs_alu(cnts, cnts, 16, op=ALU.add)
            nc.regs_load(vals, sb_flag[0:1, 0:1])
            nc.br_cmp(val, MAGIC2, on_true=done, on_false=loop,
                      comp_op="IS_EQ", engines=[ET.Pool])
        nc.switch_bb(done)
        return None
    P.sem("t2p")
    P.op("gpsimd", poll, sig=True, chain=False)
    P.dma("gpsimd", kvp[:, :], shkv[partner, :, :], "t2b", chain=False)
    P.wait("vector", "t2b", P.now("t2b"))
    P.op("vector", lambda v: v.tensor_add(kvl[:, :], kvl[:, :], kvp[:, :]))
    P.op("vector", lambda v: v.tensor_copy(kvb[:, :], kvl[:, :]))
    P.op("vector", lambda v: v.memset(ks[:, :], 0.0))
    for h in range(H):
        bh = (h % 2) * 64
        P.op("vector", lambda v, h=h, bh=bh: v.tensor_copy(
            ks[bh:bh + 64, (h // 2) * H + h:(h // 2) * H + h + 1],
            kvb[bh:bh + 64, (h // 2) * 65 + 64:(h // 2) * 65 + 65]))
    P.barrier()


def emit_t3a(c):
    P, PS = c["P"], c["PS"]
    t3_q, t3_xn1, t3_z, t3_ob = c["t3_q"], c["t3_xn1"], c["t3_z"], c["t3_ob"]
    t3_hf, t3_tmp = c["t3_hf"], c["t3_tmp"]
    kvb, ks, eps, vecs, scl, sel = (c["sb_kvb"], c["sb_ks"], c["sb_eps"],
                                    c["sb_vecs"], c["sb_scl"], c["sb_sel"])
    q1T, xn1T, s1Th, h1fT = c["q1T"], c["xn1T_t"], c["s1Th"], c["h1fT"]
    P.dma("sync", c["sb_sel"][:, :], c["sel16"][:, :], "wT3s")
    P.wait("tensor", "wT3s", P.now("wT3s"))
    for kb in range(NB):
        P.dma("gpsimd", c["t3_wproj"][:, kb * D:(kb + 1) * D],
              c["wproj"][kb * 128:(kb + 1) * 128, :], "wT3a", chain=False)
        P.dma("gpsimd", c["t3_d0"][:, kb * DFF:(kb + 1) * DFF],
              c["d0T"][kb * 128:(kb + 1) * 128, :], "wT3a", chain=False)
    P.wait("tensor", "wT3a", P.now("wT3a"))
    for it in range(NT):
        P.wait("sync", "cT", P.mark("tensor"))
        P.wait("sync", "cV", P.mark("vector"))
        for mb in range(NB):
            P.dma("sync", t3_q[:, mb * TT:(mb + 1) * TT],
                  q1T[mb * 128:(mb + 1) * 128, it * TT:(it + 1) * TT], "t3w")
            P.dma("sync", t3_xn1[:, mb * TT:(mb + 1) * TT],
                  xn1T[mb * 128:(mb + 1) * 128, it * TT:(it + 1) * TT], "t3w")
        wv_ = P.now("t3w")
        P.wait("tensor", "t3w", wv_)
        p6 = PS.acquire(6, "tensor")
        for kb in range(NB):
            P.op("tensor", lambda e, kb=kb, p6=p6: e.matmul(
                p6[0:16, 0:TT], ks[:, kb * H:(kb + 1) * H],
                t3_q[:, kb * TT:(kb + 1) * TT],
                start=(kb == 0), stop=(kb == NB - 1)), sig=(kb == NB - 1))
        P.wait("vector", "cT", P.mark("tensor"))
        P.op("vector", lambda v, p6=p6: v.tensor_scalar_add(
            c["t3_den"][0:16, 0:TT], p6[0:16, 0:TT], eps[0:16, 5:6]))
        PS.release(6, "vector")
        P.op("vector", lambda v: v.reciprocal(c["t3_den"][0:16, 0:TT],
                                              c["t3_den"][0:16, 0:TT]))
        zv = P.mark("vector")
        for pb in range(NB):
            pi = 4 + pb % 2
            po = PS.acquire(pi, "tensor")
            for sub in range(2):
                h = 2 * pb + sub
                bh = sub * 64
                P.op("tensor", lambda e, h=h, bh=bh, po=po, pb=pb: e.matmul(
                    po[bh:bh + 64, 0:TT],
                    kvb[bh:bh + 64, pb * 65:pb * 65 + 64],
                    t3_q[bh:bh + 64, pb * TT:(pb + 1) * TT],
                    start=True, stop=True,
                    tile_position=(bh, bh) if bh else None), sig=(sub == 1))
            P.wait("tensor", "cV", zv)
            p6b = PS.acquire(6, "tensor")
            P.op("tensor", lambda e, pb=pb, p6b=p6b: e.matmul(
                p6b[:, 0:TT], sel[0:16, pb * 128:(pb + 1) * 128],
                c["t3_den"][0:16, 0:TT], start=True, stop=True))
            P.wait("vector", "cT", P.mark("tensor"))
            P.op("vector", lambda v, p6b=p6b: v.tensor_copy(t3_z[:, 0:TT],
                                                            p6b[:, 0:TT]))
            PS.release(6, "vector")
            P.op("vector", lambda v, pb=pb, po=po: v.tensor_mul(
                t3_ob[:, pb * TT:(pb + 1) * TT], po[:, 0:TT], t3_z[:, 0:TT]))
            PS.release(pi, "vector")
        obv = P.mark("vector")
        P.wait("tensor", "cV", obv)
        P.wait("scalar", "t3so", P.now("t3so"))
        for mo in range(NB):
            pi = mo % 2
            pq = PS.acquire(pi, "tensor")
            for kb in range(NB):
                P.op("tensor", lambda e, kb=kb, mo=mo, pq=pq: e.matmul(
                    pq[:, 0:TT],
                    c["t3_wproj"][:, kb * D + mo * 128:kb * D + (mo + 1) * 128],
                    t3_ob[:, kb * TT:(kb + 1) * TT],
                    start=(kb == 0), stop=(kb == NB - 1)), sig=(kb == NB - 1))
            P.wait("scalar", "cT", P.mark("tensor"))
            P.op("scalar", lambda e, mo=mo, pq=pq: e.activation(
                t3_tmp[:, TT:2 * TT], pq[:, 0:TT], AF.Identity,
                bias=vecs[:, 72 + mo:73 + mo]))
            PS.release(pi, "scalar")
            P.wait("gpsimd", "cS", P.mark("scalar"))
            P.dma("gpsimd", s1Th[mo * 128:(mo + 1) * 128, it * TT:(it + 1) * TT],
                  t3_tmp[:, TT:2 * TT], "t3so", chain=False)
            P.wait("scalar", "t3so", P.now("t3so"))
        P.wait("vector", "t3ho", P.now("t3ho"))
        for mo in range(32):
            pi = 2 + mo % 2
            pq = PS.acquire(pi, "tensor")
            for kb in range(NB):
                P.op("tensor", lambda e, kb=kb, mo=mo, pq=pq: e.matmul(
                    pq[:, 0:TT],
                    c["t3_d0"][:, kb * DFF + mo * 128:kb * DFF + (mo + 1) * 128],
                    t3_xn1[:, kb * TT:(kb + 1) * TT],
                    start=(kb == 0), stop=(kb == NB - 1)), sig=(kb == NB - 1))
            P.wait("vector", "cT", P.mark("tensor"))
            P.wait("vector", "cS", P.mark("scalar"))
            P.op("vector", lambda v, pq=pq: v.tensor_scalar_mul(
                t3_tmp[:, 0:TT], pq[:, 0:TT], scl[:, 0:1]))
            PS.release(pi, "vector")
            P.wait("scalar", "cV", P.mark("vector"))
            P.op("scalar", lambda e, mo=mo: e.activation(
                t3_hf[:, mo * TT:(mo + 1) * TT], t3_tmp[:, 0:TT], AF.Relu,
                scale=4.0))
        P.wait("gpsimd", "cS", P.mark("scalar"))
        for mo in range(32):
            P.dma("gpsimd", h1fT[mo * 128:(mo + 1) * 128, it * TT:(it + 1) * TT],
                  t3_hf[:, mo * TT:(mo + 1) * TT], "t3ho", chain=False)
    P.barrier()


def emit_t3b(c):
    P, PS = c["P"], c["PS"]
    t3_h1, t3_h2o, scl = c["t3_h1"], c["t3_h2o"], c["sb_scl"]
    t3_tmp = c["t3b_tmp"]
    h1fT, h2fT, w1eT = c["h1fT"], c["h2fT"], c["w1eT"]
    w1s = c["t3_w1s"]
    NSUP = 4
    STT = 1024
    for st in range(NSUP):
        P.wait("sync", "cT", P.mark("tensor"))
        for ko in range(32):
            P.dma("sync", t3_h1[:, ko * STT:(ko + 1) * STT],
                  h1fT[ko * 128:(ko + 1) * 128, st * STT:(st + 1) * STT], "t3bw")
        hv = P.now("t3bw")
        P.wait("tensor", "t3bw", hv)
        for mog in range(8):
            bufbase = (mog % 2) * 32
            if mog >= 2 or st > 0:
                P.wait("sync", "cT", P.mark("tensor"))
            sname = f"t3bs{mog % 2}"
            for ko in range(32):
                P.dma("sync", w1s[bufbase + ko][:, :],
                      w1eT[ko * 128:(ko + 1) * 128,
                           mog * 512:(mog + 1) * 512], sname)
            sv = P.now(sname)
            P.wait("tensor", sname, sv)
            pqs = [PS.acquire(i, "tensor") for i in range(8)]
            for ko in range(32):
                for mb in range(4):
                    for nh in range(2):
                        P.op("tensor", lambda e, ko=ko, mb=mb, nh=nh, mog=mog,
                             pqs=pqs, bufbase=bufbase: e.matmul(
                                 pqs[mb * 2 + nh][:, 0:TT],
                                 w1s[bufbase + ko][:, mb * 128:(mb + 1) * 128],
                                 t3_h1[:, ko * STT + nh * TT:ko * STT + (nh + 1) * TT],
                                 start=(ko == 0), stop=(ko == 31)),
                             sig=(ko == 31 and mb == 3 and nh == 1))
            mmk = P.mark("tensor")
            P.wait("vector", "cT", mmk)
            P.wait("vector", "t3bo", P.now("t3bo"))
            P.wait("scalar", "t3bo", P.now("t3bo"))
            for mb in range(4):
                for nh in range(2):
                    P.wait("vector", "cS", P.mark("scalar"))
                    P.op("vector", lambda v, mb=mb, nh=nh, pqs=pqs: v.tensor_scalar_mul(
                        t3_tmp[:, 0:TT], pqs[mb * 2 + nh][:, 0:TT], scl[:, 1:2]))
                    P.wait("scalar", "cV", P.mark("vector"))
                    P.op("scalar", lambda e, mb=mb, nh=nh, mog=mog: e.activation(
                        t3_h2o[:, (mog % 2) * 4096 + mb * STT + nh * TT:
                               (mog % 2) * 4096 + mb * STT + (nh + 1) * TT],
                        t3_tmp[:, 0:TT], AF.Relu, scale=2.0))
            for i in range(8):
                PS.release(i, "scalar")
            P.wait("gpsimd", "cS", P.mark("scalar"))
            for mb in range(4):
                P.dma("gpsimd",
                      h2fT[mog * 512 + mb * 128:mog * 512 + (mb + 1) * 128,
                           st * STT:(st + 1) * STT],
                      t3_h2o[:, (mog % 2) * 4096 + mb * STT:
                             (mog % 2) * 4096 + (mb + 1) * STT], "t3bo",
                      chain=False)
    P.barrier()


def emit_t3c(c):
    P, PS = c["P"], c["PS"]
    t3_hr, t3_g1, t3_sl, t3_x1l, t3_out = (c["t3_hr"], c["t3_g1"], c["t3_sl"],
                                           c["t3_x1l"], c["t3_out"])
    scl = c["sb_scl"]
    h2fT, w2eT, s1Th, x1T, out_sl = (c["h2fT"], c["w2eT"], c["s1Th"], c["x1T"],
                                     c["out_sl"])
    for ko in range(32):
        P.dma("gpsimd", c["t3_w2"][:, ko * D:(ko + 1) * D],
              w2eT[ko * 128:(ko + 1) * 128, :], "wT3c", chain=False)
    P.wait("tensor", "wT3c", P.now("wT3c"))
    for it in range(NT):
        P.wait("sync", "cT", P.mark("tensor"))
        P.wait("sync", "cV", P.mark("vector"))
        for ko in range(32):
            P.dma("sync", t3_hr[:, ko * TT:(ko + 1) * TT],
                  h2fT[ko * 128:(ko + 1) * 128, it * TT:(it + 1) * TT], "t3cw")
        for mb in range(NB):
            P.dma("sync", t3_sl[:, mb * TT:(mb + 1) * TT],
                  s1Th[mb * 128:(mb + 1) * 128, it * TT:(it + 1) * TT], "t3cw")
            P.dma("sync", t3_x1l[:, mb * TT:(mb + 1) * TT],
                  x1T[mb * 128:(mb + 1) * 128, it * TT:(it + 1) * TT], "t3cw")
        wv_ = P.now("t3cw")
        P.wait("tensor", "t3cw", wv_)
        P.wait("vector", "t3cw", wv_)
        for mo in range(NB):
            pi = mo % 2
            pq = PS.acquire(pi, "tensor")
            for ko in range(32):
                P.op("tensor", lambda e, ko=ko, mo=mo, pq=pq: e.matmul(
                    pq[:, 0:TT],
                    c["t3_w2"][:, ko * D + mo * 128:ko * D + (mo + 1) * 128],
                    t3_hr[:, ko * TT:(ko + 1) * TT],
                    start=(ko == 0), stop=(ko == 31)), sig=(ko == 31))
            P.wait("vector", "cT", P.mark("tensor"))
            P.op("vector", lambda v, pq=pq: v.tensor_scalar_mul(
                t3_g1[:, 0:TT], pq[:, 0:TT], scl[:, 2:3]))
            PS.release(pi, "vector")
            P.wait("scalar", "cV", P.mark("vector"))
            P.op("scalar", lambda e: e.activation(
                t3_g1[:, TT:2 * TT], t3_g1[:, 0:TT], AF.Sigmoid))
            P.wait("vector", "cS", P.mark("scalar"))
            P.op("vector", lambda v: v.tensor_mul(
                t3_g1[:, 0:TT], t3_g1[:, 0:TT], t3_g1[:, TT:2 * TT]))
            P.op("vector", lambda v, mo=mo: v.tensor_mul(
                t3_g1[:, 0:TT], t3_g1[:, 0:TT], t3_sl[:, mo * TT:(mo + 1) * TT]))
            P.wait("vector", "t3out", P.now("t3out"))
            P.op("vector", lambda v, mo=mo: v.tensor_add(
                t3_out[:, mo * TT:(mo + 1) * TT], t3_x1l[:, mo * TT:(mo + 1) * TT],
                t3_g1[:, 0:TT]))
            P.wait("gpsimd", "cV", P.mark("vector"))
            P.dma("gpsimd", out_sl[mo * 128:(mo + 1) * 128, it * TT:(it + 1) * TT],
                  t3_out[:, mo * TT:(mo + 1) * TT], "t3out", chain=False)
    P.wait("gpsimd", "t3out", P.now("t3out"))
    P.barrier()


def emit_dev_tail(c, stages, dev, dev_out):
    if not dev:
        return
    P = c["P"]
    do = dev_out
    nch = c.get("S2_NWIN", NT) * 4
    if "s2" in stages:
        P.wait("sync", "s2sp", P.now("s2sp"))
        P.wait("sync", "cV", P.mark("vector"))
        P.dma("sync", do["d_stl"][:, :], c["sb_stl"][:, :], "dev")
        P.dma("sync", do["d_ab"][0:nch], c["ABb"][0:nch], "dev")
        P.dma("sync", do["d_wt"][0:nch], c["WTb"][0:nch], "dev")
        if "s4" not in stages:
            P.dma("sync", do["d_ot"][:, 0:nch * CH], c["OT"][:, 0:nch * CH], "dev")
    if "s4" in stages:
        P.wait("sync", "s4o", P.now("s4o"))
        P.dma("sync", do["d_ot2"][:, 0:nch * CH], c["OT"][:, 0:nch * CH], "dev")
    if "t1" in stages:
        P.wait("sync", "t1qo", P.now("t1qo"))
        P.wait("sync", "cV", P.mark("vector"))
        P.dma("sync", do["d_kv"][:, :], c["sb_kvloc"][:, :], "dev")
        P.dma("sync", do["d_q1"][:, :], c["q1T"][:, :], "dev")
        P.dma("sync", do["d_kt"][:, :], c["t1_kt"][:, :], "dev")
        P.dma("sync", do["d_vt"][:, :], c["t1_vt"][:, :], "dev")
    if "t3a" in stages:
        P.wait("sync", "t3so", P.now("t3so"))
        P.wait("sync", "t3ho", P.now("t3ho"))
        P.wait("sync", "cV", P.mark("vector"))
        P.dma("sync", do["d_s1"][:, :], c["s1Th"][:, :], "dev")
        P.dma("sync", do["d_h1f"][:, :], c["h1fT"][:, :], "dev")
        P.dma("sync", do["d_ob"][:, :], c["t3_ob"][:, :], "dev")
        P.dma("sync", do["d_z"][:, :], c["t3_z"][:, :], "dev")
        P.dma("sync", do["d_zr"][:, :], c["t3_den"][:, 0:TT], "dev")
        P.dma("sync", do["d_ks"][:, :], c["sb_ks"][:, :], "dev")
        P.dma("sync", do["d_d0"][:, :].rearrange("(kb p) c -> p kb c", p=128),
              c["t3_d0"][:, :].rearrange("p (kb c) -> p kb c", kb=NB), "dev")
    if "t3b" in stages:
        P.wait("sync", "t3bo", P.now("t3bo"))
        P.dma("sync", do["d_h2f"][:, :], c["h2fT"][:, :], "dev")
    P.wait("sync", "dev", P.now("dev"))
    if "s1" in stages:
        P.wait("sync", "s1qo", P.now("s1qo"))
        P.wait("sync", "s1ko", P.now("s1ko"))
        P.wait("sync", "s1so", P.now("s1so"))
        P.wait("sync", "cS", P.mark("scalar"))
        P.dma("sync", do["d_q"][:, :], c["qT"][:, :], "dev")
        P.dma("sync", do["d_k"][:, :], c["ktokN"][:, :], "dev")
        P.dma("sync", do["d_v"][:, :], c["vtok"][:, :], "dev")
        P.dma("sync", do["d_beta"][:, :], c["sb_beta"][:, :], "dev")
        P.dma("sync", do["d_stats"][0:3, 0:TC + 1], c["statsH"][0:3, 0:TC + 1], "dev")
        P.wait("sync", "dev", P.now("dev"))


def stage_inputs(inputs, core):
    b, hf = core // 2, core % 2
    x = np.asarray(inputs["L0_attn_x"])
    t0 = hf * TC
    sl = np.empty((TC + 1, D), np.float32)
    sl[1:] = x[b, t0:t0 + TC]
    sl[0] = 0.0 if hf == 0 else x[b, t0 - 1]
    m = {}
    m["x_sl"] = np.ascontiguousarray(sl.T)
    for nm, key in [("wq", "L0_attn_wq"), ("wk", "L0_attn_wk"), ("wv", "L0_attn_wv"),
                    ("wbeta", "L0_attn_wbeta"), ("wo", "L0_attn_wo"),
                    ("w1", "L0_attn_w1"), ("w2", "L0_attn_w2"),
                    ("wqkv", "L1_attn_w_qkv"), ("wproj", "L1_attn_w_proj")]:
        m[nm] = np.asarray(inputs[key], np.float32)
    for nm, key in [("wrT", "L0_ffn_W_r"), ("wkT", "L0_ffn_W_k"), ("wvT", "L0_ffn_W_v")]:
        m[nm] = np.ascontiguousarray(np.asarray(inputs[key], np.float32).T)
    m["k0u"] = np.asarray(inputs["L1_ffn_k0"]).view(np.uint8)
    m["k1u"] = np.asarray(inputs["L1_ffn_k1"]).view(np.uint8)
    m["k2u"] = np.asarray(inputs["L1_ffn_k2"]).view(np.uint8)

    def cols(vec, n):
        v = np.asarray(vec, np.float32).reshape(n, 128)
        return np.ascontiguousarray(v.T)
    vec_list = [cols(inputs["L0_attn_bq"], 8), cols(inputs["L0_attn_bk"], 8),
                cols(inputs["L0_attn_bv"], 8), cols(inputs["L0_attn_bo"], 8),
                cols(inputs["L0_attn_b2"], 8), cols(inputs["L0_attn_ln1_scale"], 8),
                cols(inputs["L0_attn_ln2_scale"], 8), cols(inputs["L0_ffn_mu_r"], 8),
                cols(inputs["L0_ffn_mu_k"], 8), cols(inputs["L1_attn_b_proj"], 8)]
    m["vecs"] = np.concatenate(vec_list, axis=1)
    m["b1v"] = cols(inputs["L0_attn_b1"], 32)
    m["bqkvv"] = cols(inputs["L1_attn_b_qkv"], 24)
    scl = np.zeros((128, 4), np.float32)
    scl[:, 0] = float(np.asarray(inputs["L1_ffn_s0"]))
    scl[:, 1] = float(np.asarray(inputs["L1_ffn_s1"]))
    scl[:, 2] = float(np.asarray(inputs["L1_ffn_s2"]))
    scl[:, 3] = float(hf)
    m["sclv"] = scl
    m["bk_bc"] = np.broadcast_to(np.asarray(inputs["L0_attn_bk"], np.float32),
                                 (128, D)).copy()
    m["bv_bc"] = np.broadcast_to(np.asarray(inputs["L0_attn_bv"], np.float32),
                                 (128, D)).copy()
    bq1 = np.asarray(inputs["L1_attn_b_qkv"], np.float32)
    m["bqkv_k_bc"] = np.broadcast_to(bq1[D:2 * D], (128, D)).copy()
    m["bqkv_v_bc"] = np.broadcast_to(bq1[2 * D:], (128, D)).copy()
    m["bbeta_bc"] = np.broadcast_to(np.asarray(inputs["L0_attn_bbeta"], np.float32),
                                    (128, H)).copy()
    i64 = np.zeros((128, 64), np.float32)
    i64[:64] = np.eye(64)
    i64[64:] = np.eye(64)
    m["i64p"] = i64
    m["maskln"] = -np.tril(np.ones((128, CH), np.float32), -1)
    m["masku"] = np.triu(np.ones((128, CH), np.float32), 0)
    m["identb"] = np.eye(128, dtype=NPBF16)
    m["onesbc"] = np.ones((1, 128), NPBF16)
    m["onesbf"] = np.ones((1, 128), np.float32)
    m["onescb"] = np.ones((128, 1), NPBF16)
    m["onescf"] = np.ones((128, 1), np.float32)
    sel = np.zeros((16, 8 * 128), np.float32)
    for pb in range(8):
        for sub in range(2):
            sel[2 * pb + sub, pb * 128 + sub * 64:pb * 128 + sub * 64 + 64] = 1.0
    m["sel16"] = sel
    return m


_BUILD_CACHE = {}


def kernel(**inputs):
    key = "full"
    stages = ["l0", "l1"]
    nc, outs = build(stages)
    in_maps = [stage_inputs(inputs, cc) for cc in range(8)]
    res = run_bass_kernel_spmd(nc, in_maps, core_ids=list(range(8)))
    out = np.empty((4, 2 * TC, D), np.float32)
    for cc in range(8):
        b, hf = cc // 2, cc % 2
        out[b, hf * TC:(hf + 1) * TC] = res.results[cc]["out_sl"].T
    return out


# version 03
# speedup vs baseline: 1.1972x; 1.0170x over previous
import numpy as np
import ml_dtypes
import concourse.bass as bass
import concourse.mybir as mybir
from concourse.bass_utils import run_bass_kernel_spmd

F32 = mybir.dt.float32
BF16 = mybir.dt.bfloat16
U8 = mybir.dt.uint8
U32 = mybir.dt.uint32
AF = mybir.ActivationFunctionType
ALU = mybir.AluOpType
AX = mybir.AxisListType.X
ET = mybir.EngineType
NPBF16 = ml_dtypes.bfloat16

TC = 4096
D = 1024
H = 16
HD = 64
NB = 8
TT = 512
NT = TC // TT
CH = 128
NCH = TC // CH
DFF = 4096
MAGIC1 = 0x5A17C0DE
MAGIC2 = 0x7E57B1D5
SB_BYTES = 224 * 1024
DETECT_RACES = False

ENGS = ["tensor", "vector", "scalar", "sync", "gpsimd"]
ECTR = {"tensor": "cT", "vector": "cV", "scalar": "cS", "sync": "cY", "gpsimd": "cG"}


class Prog:

    def __init__(self, nc):
        self.nc = nc
        self.items = {e: [] for e in ENGS}
        self.sems = {}
        self.cnt = {}
        self.lastwait = {e: {} for e in ENGS}

    def sem(self, name):
        if name not in self.sems:
            self.sems[name] = self.nc.alloc_semaphore(name)
            self.cnt[name] = 0
        return self.sems[name]

    def _waits(self, eng, waits):
        out = []
        for s, v in waits:
            self.sem(s)
            if v <= 0:
                continue
            if self.lastwait[eng].get(s, -1) >= v:
                continue
            self.lastwait[eng][s] = v
            out.append((s, v))
        return out

    def op(self, eng, fn, wait=None, sig=True, chain=None):
        ctr = ECTR[eng]
        self.sem(ctr)
        if chain is None:
            chain = eng in ("vector", "scalar", "gpsimd")
        waits = list(wait or [])
        if chain and self.cnt[ctr] > 0:
            waits.append((ctr, self.cnt[ctr]))
        waits = self._waits(eng, waits)
        if sig:
            self.cnt[ctr] += 1
        val = self.cnt[ctr]

        def closure(e, waits=waits, fn=fn, sig=sig, ctr=ctr):
            for s, v in waits:
                e.wait_ge(self.sems[s], v)
            ins = fn(e)
            if sig:
                if ins is None:
                    ins = e.nop()
                ins.then_inc(self.sems[ctr], 1)
            return ins
        self.items[eng].append(closure)
        return val

    def mark(self, eng):
        self.sem(ECTR[eng])
        return self.cnt[ECTR[eng]]

    def ctr(self, eng):
        return ECTR[eng]

    def now(self, semname):
        self.sem(semname)
        return self.cnt[semname]

    def dma(self, eng, out, in_, semname, wait=None, chain=None, **kw):
        self.sem(semname)
        ctr = ECTR[eng]
        if chain is None:
            chain = eng in ("vector", "scalar", "gpsimd")
        waits = list(wait or [])
        if chain and self.cnt[ctr] > 0:
            waits.append((ctr, self.cnt[ctr]))
        waits = self._waits(eng, waits)
        self.cnt[semname] += 16
        def closure(e, waits=waits, out=out, in_=in_, kw=kw, s=semname):
            for sn, v in waits:
                e.wait_ge(self.sems[sn], v)
            return e.dma_start(out=out, in_=in_, **kw).then_inc(self.sems[s], 16)
        self.items[eng].append(closure)
        return self.cnt[semname]

    def wait(self, eng, semname, value=None):
        if value is None:
            value = self.now(semname)
        ws = self._waits(eng, [(semname, value)])
        if ws:
            def closure(e, ws=ws):
                for s, v in ws:
                    e.wait_ge(self.sems[s], v)
                return None
            self.items[eng].append(closure)

    def barrier(self, tag=""):
        marks = {e: self.mark(e) for e in ENGS}
        ctrs = set(ECTR.values())
        dmas = [(s, v) for s, v in self.cnt.items() if s not in ctrs and v > 0]
        for e in ENGS:
            for e2 in ENGS:
                if e2 != e:
                    self.wait(e, ECTR[e2], marks[e2])
            for s, v in dmas:
                self.wait(e, s, v)
        for e in ENGS:
            self.op(e, lambda x: x.nop(), chain=(e in ("vector", "scalar", "gpsimd")))

    def replay(self):
        with self.nc.Block() as blk:
            for e in ENGS:
                getattr(blk, e)(lambda eng, e=e: [f(eng) for f in self.items[e]])


class PsumPool:
    def __init__(self, prog, tiles):
        self.p = prog
        self.tiles = tiles
        self.state = [None] * len(tiles)

    def acquire(self, i, eng):
        if self.state[i] is not None:
            s, v = self.state[i]
            self.p.wait(eng, s, v)
        return self.tiles[i]

    def release(self, i, eng):
        self.state[i] = (ECTR[eng], self.p.mark(eng))


def rowap(t, a, b):
    return t[0:1, a:b]


def build(stages, dev=False, s2_nwin=None):
    nc = bass.Bass("TRN2", target_bir_lowering=False,
                   detect_race_conditions=DETECT_RACES)
    P = Prog(nc)

    def inp(name, shape, dt=F32):
        return nc.dram_tensor(name, shape, dt, kind="ExternalInput")

    x_sl = inp("x_sl", [D, TC + 1])
    wq = inp("wq", [D, D]); wk = inp("wk", [D, D]); wv = inp("wv", [D, D])
    wbeta = inp("wbeta", [D, H])
    wo = inp("wo", [D, D]); w1 = inp("w1", [D, DFF]); w2 = inp("w2", [DFF, D])
    wrT = inp("wrT", [D, D]); wkT = inp("wkT", [D, D]); wvT = inp("wvT", [D, D])
    wqkv = inp("wqkv", [D, 3 * D]); wproj = inp("wproj", [D, D])
    k0u = inp("k0u", [2 * D, DFF], U8)
    k1u = inp("k1u", [DFF, DFF], U8)
    k2u = inp("k2u", [DFF, D], U8)
    vecs = inp("vecs", [128, 80])
    b1v = inp("b1v", [128, 32])
    bqkvv = inp("bqkvv", [128, 24])
    sclv = inp("sclv", [128, 4])
    bk_bc = inp("bk_bc", [128, D]); bv_bc = inp("bv_bc", [128, D])
    bqkv_k_bc = inp("bqkv_k_bc", [128, D]); bqkv_v_bc = inp("bqkv_v_bc", [128, D])
    bbeta_bc = inp("bbeta_bc", [128, H])
    i64p = inp("i64p", [128, 64])
    maskln = inp("maskln", [128, CH])
    masku = inp("masku", [128, CH])
    identb = inp("identb", [128, 128], BF16)
    onesbc = inp("onesbc", [1, 128], BF16)
    onesbf = inp("onesbf", [1, 128])
    onescb = inp("onescb", [128, 1], BF16)
    onescf = inp("onescf", [128, 1])
    sel16 = inp("sel16", [16, 8 * 128])

    outs = []

    def outp(name, shape, dt):
        outs.append(name)
        return nc.dram_tensor(name, shape, dt, kind="ExternalOutput")

    qT = nc.dram_tensor("qT", [D, TC], BF16)
    ktokN = nc.dram_tensor("ktokN", [TC, D], BF16)
    vtok = nc.dram_tensor("vtok", [TC, D], BF16)
    OT = nc.dram_tensor("OTb", [D, TC], BF16)
    ABb = nc.dram_tensor("ABb", [NCH, 8, 128, 128], BF16)
    WTb = nc.dram_tensor("WTb", [NCH, 8, 128, CH], BF16)
    statsH = nc.dram_tensor("statsH", [4, TC + 8], F32)
    stats1H = nc.dram_tensor("stats1H", [2, TC + 8], F32)
    sTh = nc.dram_tensor("sTh", [D, TC], F32)
    xaTh = nc.dram_tensor("xaTh", [D, TC], F32)
    h2Th = nc.dram_tensor("h2Th", [D, TC], BF16)
    shst = nc.dram_tensor("shst", [8, 128, 512], BF16, addr_space="Shared")
    shkv = nc.dram_tensor("shkv", [8, 128, 8 * 65], F32, addr_space="Shared")
    shfl = nc.dram_tensor("shfl", [8, 2, 16], U32, addr_space="Shared")
    q1T = nc.dram_tensor("q1T", [D, TC], BF16)
    s1Th = nc.dram_tensor("s1Th", [D, TC], F32)
    h1fT = nc.dram_tensor("h1fT", [DFF, TC], BF16)
    h2fT = nc.dram_tensor("h2fT", [DFF, TC], BF16)
    d0T = nc.dram_tensor("d0T", [D, DFF], BF16)
    w1eT = nc.dram_tensor("w1eT", [DFF, DFF], BF16)
    w2eT = nc.dram_tensor("w2eT", [DFF, D], BF16)

    full = "l1" in stages or "t3c" in stages
    if "l1" in stages and "l0" not in stages:
        x1T = inp("x1T_in", [D, TC])
        xn1T_t = inp("xn1T_in", [D, TC], BF16)
    elif dev and ("s6" in stages or "l0" in stages) and not full:
        x1T = outp("x1T", [D, TC], F32)
        xn1T_t = outp("xn1T", [D, TC], BF16)
    else:
        x1T = nc.dram_tensor("x1T", [D, TC], F32)
        xn1T_t = nc.dram_tensor("xn1T", [D, TC], BF16)
    out_sl = outp("out_sl", [D, TC], F32) if full else None

    dev_out = {}
    if dev:
        for nm in stages:
            if nm == "s1":
                dev_out["d_q"] = outp("d_q", [D, TC], BF16)
                dev_out["d_k"] = outp("d_k", [TC, D], BF16)
                dev_out["d_v"] = outp("d_v", [TC, D], BF16)
                dev_out["d_beta"] = outp("d_beta", [128, 32 * H], F32)
                dev_out["d_stats"] = outp("d_stats", [4, TC + 8], F32)
            if nm == "s2":
                dev_out["d_stl"] = outp("d_stl", [128, 8 * HD], BF16)
                dev_out["d_ot"] = outp("d_ot", [D, TC], BF16)
                dev_out["d_ab"] = outp("d_ab", [NCH, 8, 128, 128], BF16)
                dev_out["d_wt"] = outp("d_wt", [NCH, 8, 128, CH], BF16)
            if nm == "s4":
                dev_out["d_ot2"] = outp("d_ot2", [D, TC], BF16)
            if nm == "s5":
                dev_out["d_s"] = outp("d_s", [D, TC], F32)
                dev_out["d_xa"] = outp("d_xa", [D, TC], F32)
                dev_out["d_h2"] = outp("d_h2", [D, TC], BF16)

            if nm == "t1":
                dev_out["d_kv"] = outp("d_kv", [128, 8 * 65], F32)
                dev_out["d_q1"] = outp("d_q1", [D, TC], BF16)
                dev_out["d_kt"] = outp("d_kt", [128, D], BF16)
                dev_out["d_vt"] = outp("d_vt", [128, D], BF16)
                dev_out["d_ktall"] = outp("d_ktall", [32, 128, D], BF16)
            if nm == "t3a":
                dev_out["d_s1"] = outp("d_s1", [D, TC], F32)
                dev_out["d_h1f"] = outp("d_h1f", [DFF, TC], BF16)
                dev_out["d_ob"] = outp("d_ob", [128, NB * TT], BF16)
                dev_out["d_z"] = outp("d_z", [128, TT], F32)
                dev_out["d_zr"] = outp("d_zr", [128, TT], F32)
                dev_out["d_ks"] = outp("d_ks", [128, 128], BF16)
                dev_out["d_d0"] = outp("d_d0", [D, DFF], BF16)
            if nm == "t3b":
                dev_out["d_h2f"] = outp("d_h2f", [DFF, TC], BF16)

    pid0 = nc.partition_id(engines=[ET.SP, ET.Pool])
    pid = nc.snap(pid0, engines=[ET.SP, ET.Pool], min_val=0, max_val=7)
    partner = (pid // 2) * 2 + (1 - (pid % 2))

    per = Arena = None
    off = [20480]

    def salloc(name, shape, dt, at=None):
        if at is not None:
            off[0] = at
        t = nc.alloc_sbuf_tensor_at(name, list(shape), dt, offset=off[0])
        per_part = int(np.prod(shape[1:])) * mybir.dt.size(dt)
        off[0] += (per_part + 31) // 32 * 32
        assert off[0] <= SB_BYTES, f"SBUF overflow at {name}: {off[0]}"
        return t

    sb_vecs = salloc("sb_vecs", [128, 80], F32)
    sb_b1 = salloc("sb_b1", [128, 32], F32)
    sb_bqkv = salloc("sb_bqkv", [128, 24], F32)
    sb_scl = salloc("sb_scl", [128, 4], F32)
    sb_bkbc = salloc("sb_bkbc", [128, D], F32)
    sb_bvbc = salloc("sb_bvbc", [128, D], F32)
    sb_bqkbc = sb_bkbc
    sb_bqvbc = sb_bvbc
    sb_bbeta = salloc("sb_bbeta", [128, H], F32)
    sb_i64 = salloc("sb_i64", [128, 64], F32)
    sb_mln = salloc("sb_mln", [128, CH], F32)
    sb_mu = salloc("sb_mu", [128, CH], F32)
    sb_idb = salloc("sb_idb", [128, 128], BF16)
    sb_ones = salloc("sb_ones", [1, 128], BF16)
    sb_onesf = salloc("sb_onesf", [1, 128], F32)
    sb_onescb = salloc("sb_onescb", [128, 1], BF16)
    sb_onescf = salloc("sb_onescf", [128, 1], F32)
    sb_beta = salloc("sb_beta", [128, 32 * H], F32)
    sb_stl = salloc("sb_stl", [128, 8 * HD], BF16)
    sb_stc = salloc("sb_stc", [128, 8 * HD], BF16)
    sb_rows = salloc("sb_rows", [1, 2048], F32)
    sb_eps = salloc("sb_eps", [128, 8], F32)
    sb_flag = salloc("sb_flag", [1, 16], U32)
    sb_magic = salloc("sb_magic", [1, 32], U32)
    sb_lastc = salloc("sb_lastc", [128, 16], F32)
    sb_zb = salloc("sb_zb", [128, 1], BF16)
    sb_kvloc = salloc("sb_kvloc", [128, 8 * 65], F32)
    sb_kvb = salloc("sb_kvb", [128, 8 * 65], BF16)
    sb_ks = salloc("sb_ks", [128, NB * H], BF16)
    PERSIST = off[0]

    s1_wq = salloc("s1_wq", [128, NB * D], BF16, at=PERSIST)
    s1_wk = salloc("s1_wk", [128, NB * D], BF16)
    s1_wv = salloc("s1_wv", [128, NB * D], BF16)
    s1_wbeta = salloc("s1_wbeta", [128, NB * H], BF16)
    W_S1_END = off[0]
    s5_wo = salloc("s5_wo", [128, NB * D], BF16, at=PERSIST)
    W_S5A_END = off[0]
    s5_w1 = salloc("s5_w1", [128, NB * DFF], BF16, at=PERSIST)
    s5_w2 = salloc("s5_w2", [128, 32 * D], BF16)
    W_S5B_END = off[0]
    s6_wr = salloc("s6_wr", [128, NB * D], BF16, at=PERSIST)
    s6_wkk = salloc("s6_wkk", [128, NB * D], BF16)
    s6_wvv = salloc("s6_wvv", [128, NB * D], BF16)
    W_S6_END = off[0]
    t1_wqkv = salloc("t1_wqkv", [128, NB * 3 * D], BF16, at=PERSIST)
    W_T1_END = off[0]
    t3_wproj = salloc("t3_wproj", [128, NB * D], BF16, at=PERSIST)
    t3_d0 = salloc("t3_d0", [128, NB * DFF], BF16)
    W_T3A_END = off[0]
    t3_w1s = [salloc(f"t3_w1s{i}", [128, 4 * 128], BF16, at=(PERSIST if i == 0 else None))
              for i in range(64)]
    W_T3B_END = off[0]
    t3_w2 = salloc("t3_w2", [128, 32 * D], BF16, at=PERSIST)
    W_T3C_END = off[0]
    WMAX = max(W_S1_END, W_S5A_END, W_S5B_END, W_S6_END, W_T1_END,
               W_T3A_END, W_T3B_END, W_T3C_END)

    def aalloc(name, shape, dt, at=None):
        return salloc(name, shape, dt, at=at)

    s1_x = aalloc("s1_x", [128, NB * TT], F32, at=W_S1_END)
    s1_h1 = aalloc("s1_h1", [128, NB * TT], BF16)
    s1_xsq = aalloc("s1_xsq", [128, NB * TT], BF16)
    s1_kt = aalloc("s1_kt", [128, D], F32)
    s1_ksq = aalloc("s1_ksq", [128, D], BF16)
    s1_rn = aalloc("s1_rn", [128, 4 * H], F32)
    s1_st = aalloc("s1_st", [128, NB * TT], BF16)
    s1_kst = aalloc("s1_kst", [128, 2 * D], BF16)
    s1_bc = aalloc("s1_bc", [128, 2 * TT], F32)
    S1_END = off[0]
    s2_qw = aalloc("s2_qw", [128, NB * TT], BF16, at=PERSIST)
    s2_kw = aalloc("s2_kw", [128, 4 * D], BF16)
    s2_vw = aalloc("s2_vw", [128, 4 * D], BF16)
    s2_kf = aalloc("s2_kf", [128, NB * CH], BF16)
    s2_m = [aalloc(f"s2_m{i}", [128, CH], BF16) for i in range(2)]
    s2_mt = [aalloc(f"s2_mt{i}", [128, CH], BF16) for i in range(2)]
    s2_m2 = [aalloc(f"s2_m2{i}", [128, CH], BF16) for i in range(2)]
    s2_mt2 = [aalloc(f"s2_mt2{i}", [128, CH], BF16) for i in range(2)]
    s2_sacc = [aalloc(f"s2_sacc{i}", [128, CH], BF16) for i in range(2)]
    s2_mo = [aalloc(f"s2_mo{i}", [128, CH], BF16) for i in range(2)]
    s2_ab = [aalloc(f"s2_ab{i}", [128, 8 * 128], BF16) for i in range(2)]
    s2_wts = [aalloc(f"s2_wts{i}", [128, 8 * CH], BF16) for i in range(2)]
    s2_obs = [aalloc(f"s2_obs{i}", [128, NB * CH], BF16) for i in range(2)]
    S2_END = off[0]
    s4_ab = aalloc("s4_ab", [128, 8 * 128], BF16, at=PERSIST)
    s4_wt = aalloc("s4_wt", [128, 8 * CH], BF16)
    s4_o = aalloc("s4_o", [128, NB * CH], BF16)
    S4_END = off[0]
    s5_ot = aalloc("s5_ot", [128, NB * TT], BF16, at=W_S5A_END)
    s5_att = aalloc("s5_att", [128, NB * TT], F32)
    s5_x = aalloc("s5_x", [128, NB * TT], F32)
    s5_xa = aalloc("s5_xa", [128, NB * TT], F32)
    s5_sq = aalloc("s5_sq", [128, NB * TT], BF16)
    s5_h2 = aalloc("s5_h2", [128, NB * TT], BF16)
    s5_bc = aalloc("s5_bc", [128, 3 * TT], F32)
    S5A_END = off[0]
    s5_h2b = aalloc("s5_h2b", [128, NB * TT], BF16, at=W_S5B_END)
    s5_hf = aalloc("s5_hf", [128, 32 * TT], BF16)
    s5_gt = aalloc("s5_gt", [128, 2 * TT], F32)
    s5_xab = aalloc("s5_xab", [128, TT], F32)
    s5_s = aalloc("s5_s", [128, TT], F32)
    S5B_END = off[0]
    s6_x = aalloc("s6_x", [128, NB * TT], F32, at=W_S6_END)
    s6_xn = aalloc("s6_xn", [128, NB * (TT + 8)], BF16)
    s6_xr = aalloc("s6_xr", [128, NB * TT], BF16)
    s6_xk = aalloc("s6_xk", [128, NB * TT], BF16)
    s6_r = aalloc("s6_r", [128, NB * TT], F32)
    s6_kk = aalloc("s6_kk", [128, NB * TT], BF16)
    s6_sld = aalloc("s6_sld", [128, NB * TT], F32)
    s6_g = aalloc("s6_g", [128, 2 * TT], F32)
    s6_x1 = aalloc("s6_x1", [128, NB * TT], F32)
    s6_sq = aalloc("s6_sq", [128, NB * TT], BF16)
    s6_bc = aalloc("s6_bc", [128, 3 * TT], F32)
    s6_xn1 = aalloc("s6_xn1", [128, NB * TT], BF16)
    S6_END = off[0]
    t1_xn1 = aalloc("t1_xn1", [128, NB * TT], BF16, at=W_T1_END)
    t1_q = aalloc("t1_q", [128, NB * TT], BF16)
    t1_kt = aalloc("t1_kt", [128, D], BF16)
    t1_vt = aalloc("t1_vt", [128, D], BF16)
    t1_tmp = aalloc("t1_tmp", [128, 2 * TT], F32)
    sb_kvp = aalloc("sb_kvp", [128, 8 * 65], F32)
    T1_END = off[0]
    t3_q = aalloc("t3_q", [128, NB * TT], BF16, at=W_T3A_END)
    t3_xn1 = aalloc("t3_xn1", [128, NB * TT], BF16)
    t3_z = aalloc("t3_z", [128, TT], F32)
    t3_ob = aalloc("t3_ob", [128, NB * TT], BF16)
    t3_hf = aalloc("t3_hf", [128, 32 * TT], BF16)
    t3_tmp = aalloc("t3_tmp", [128, 2 * TT], F32)
    t3_den = aalloc("t3_den", [128, TT], F32)
    sb_sel = aalloc("sb_sel", [16, 8 * 128], F32)
    T3A_END = off[0]
    t3_h1 = aalloc("t3_h1", [128, 32 * 1024], BF16, at=W_T3B_END)
    t3_h2o = aalloc("t3_h2o", [128, 8 * 1024], BF16)
    t3b_tmp = aalloc("t3b_tmp", [128, TT], F32)
    T3B_END = off[0]
    t3_hr = aalloc("t3_hr", [128, 32 * TT], BF16, at=W_T3C_END)
    t3_g1 = aalloc("t3_g1", [128, 2 * TT], F32)
    t3_sl = aalloc("t3_sl", [128, NB * TT], F32)
    t3_x1l = aalloc("t3_x1l", [128, NB * TT], F32)
    t3_out = aalloc("t3_out", [128, NB * TT], F32)
    T3C_END = off[0]
    cv_a = aalloc("cv_a", [128, DFF], U8, at=PERSIST)
    cv_b = aalloc("cv_b", [128, DFF], U8)
    cv_o = aalloc("cv_o", [128, DFF], BF16)
    CV_END = off[0]
    for nm, v in [("S1", S1_END), ("S2", S2_END), ("S4", S4_END), ("S5A", S5A_END),
                  ("S5B", S5B_END), ("S6", S6_END), ("T1", T1_END), ("T3A", T3A_END),
                  ("T3B", T3B_END), ("T3C", T3C_END), ("CV", CV_END)]:
        assert v <= SB_BYTES, f"{nm} overflows SBUF: {v}"

    pbt = [nc.alloc_psum_tensor(f"pb{i}", [128, 512], F32) for i in range(8)]
    PS = PsumPool(P, pbt)
    PT = PS

    ctx = dict(locals())
    if s2_nwin is not None:
        ctx["S2_NWIN"] = s2_nwin

    emit_t0(ctx)
    if "s1" in stages or "l0" in stages:
        emit_s1(ctx)
    if "s2" in stages or "l0" in stages:
        emit_s2(ctx)
        if "nos34" not in stages:
            emit_s3(ctx)
            emit_s4(ctx)
    if "s5" in stages or "l0" in stages:
        emit_s5(ctx)
    if "s6" in stages or "l0" in stages:
        emit_s6(ctx)
    if "t1" in stages or "l1" in stages:
        emit_cv(ctx)
        emit_t1(ctx)
        emit_t2(ctx)
    if "t3a" in stages or "l1" in stages:
        emit_t3a(ctx)
    if "t3b" in stages or "l1" in stages:
        emit_t3b(ctx)
    if "t3c" in stages or "l1" in stages:
        emit_t3c(ctx)
    emit_dev_tail(ctx, stages, dev, dev_out)

    P.replay()
    return nc, outs


def emit_t0(c):
    P = c["P"]
    pairs = [(c["sb_vecs"], c["vecs"]), (c["sb_b1"], c["b1v"]),
             (c["sb_bqkv"], c["bqkvv"]), (c["sb_scl"], c["sclv"]),
             (c["sb_bkbc"], c["bk_bc"]), (c["sb_bvbc"], c["bv_bc"]),
             (c["sb_bbeta"], c["bbeta_bc"]), (c["sb_i64"], c["i64p"]),
             (c["sb_mln"], c["maskln"]), (c["sb_mu"], c["masku"]),
             (c["sb_idb"], c["identb"])]
    for dst, src_ in pairs:
        P.dma("sync", dst[:, :], src_[:, :], "cst")
    for dst, src_ in [(c["sb_ones"], c["onesbc"]), (c["sb_onesf"], c["onesbf"])]:
        P.dma("sync", dst[0:1, :], src_[0:1, :], "cst")
    for dst, src_ in [(c["sb_onescb"], c["onescb"]), (c["sb_onescf"], c["onescf"])]:
        P.dma("sync", dst[:, 0:1], src_[:, 0:1], "cst")
    eps = c["sb_eps"]
    for i, val in enumerate([1e-5, 1e-6, -0.5, 1.0 / D, -1.0, 1e-8, 0.0, 1.0]):
        P.op("vector", lambda v, i=i, val=val: v.memset(eps[:, i:i + 1], val))
    P.op("vector", lambda v: v.memset(c["sb_stl"][:, :], 0.0))
    P.op("vector", lambda v: v.memset(c["sb_lastc"][:, :], 0.0))
    P.op("vector", lambda v: v.memset(c["sb_zb"][:, :], 0.0))
    P.op("gpsimd", lambda g: g.memset(c["sb_magic"][0:1, 0:16], MAGIC1))
    P.op("gpsimd", lambda g: g.memset(c["sb_magic"][0:1, 16:32], MAGIC2))


def emit_s1(c):
    P, PS = c["P"], c["PS"]
    x_sl, statsH = c["x_sl"], c["statsH"]
    s1_x, s1_h1, s1_xsq = c["s1_x"], c["s1_h1"], c["s1_xsq"]
    s1_kt, s1_ksq, s1_rn, s1_st, s1_bc = (c["s1_kt"], c["s1_ksq"], c["s1_rn"],
                                          c["s1_st"], c["s1_bc"])
    s1_kst = c["s1_kst"]
    rows, eps, vecs = c["sb_rows"], c["sb_eps"], c["sb_vecs"]
    onescf, onescb, onesf = c["sb_onescf"], c["sb_onescb"], c["sb_onesf"]
    for w_h, w_s in [(c["wq"], c["s1_wq"]), (c["wk"], c["s1_wk"]),
                     (c["wv"], c["s1_wv"]), (c["wbeta"], c["s1_wbeta"])]:
        n = w_h.shape[1]
        for kb in range(NB):
            P.dma("gpsimd", w_s[:, kb * n:(kb + 1) * n],
                  w_h[kb * 128:(kb + 1) * 128, :], "wS1", chain=False)
    wS1 = P.now("wS1")

    prev_free = None
    for it in range(-1, NT):
        w = 1 if it < 0 else TT
        c0 = 0 if it < 0 else 1 + it * TT
        if prev_free is not None:
            P.wait("sync", prev_free[0], prev_free[1])
        for kb in range(NB):
            P.dma("sync", s1_x[:, kb * TT:kb * TT + w],
                  x_sl[kb * 128:(kb + 1) * 128, c0:c0 + w], "s1x",
                  **({"allow_slow_non_contiguous": True} if w == 1 else {}))
        xv = P.now("s1x")
        P.wait("scalar", "s1x", xv)
        for kb in range(NB):
            P.op("scalar", lambda e, kb=kb, w=w: e.activation(
                s1_xsq[:, kb * TT:kb * TT + w], s1_x[:, kb * TT:kb * TT + w],
                AF.Square))
        sqm = P.mark("scalar")
        P.wait("tensor", "cS", sqm)
        if it == -1:
            P.wait("tensor", "cst", P.now("cst"))
        p0 = PS.acquire(0, "tensor")
        p1 = PS.acquire(1, "tensor")
        for kb in range(NB):
            P.op("tensor", lambda e, kb=kb, w=w, p0=p0: e.matmul(
                p0[0:1, 0:w], onescf[:, 0:1],
                s1_x[:, kb * TT:kb * TT + w], start=(kb == 0), stop=(kb == NB - 1)),
                sig=(kb == NB - 1))
        for kb in range(NB):
            P.op("tensor", lambda e, kb=kb, w=w, p1=p1: e.matmul(
                p1[0:1, 0:w], onescb[:, 0:1],
                s1_xsq[:, kb * TT:kb * TT + w], start=(kb == 0), stop=(kb == NB - 1)),
                sig=(kb == NB - 1))
        mm1 = P.mark("tensor")
        r_mean = rowap(rows, 0, w)
        r_comb = rowap(rows, 512, 512 + w)
        r_rstd = rowap(rows, 1024, 1024 + w)
        r_t = rowap(rows, 1536, 1536 + w)
        P.wait("vector", "cT", mm1)
        P.wait("vector", "s1so", P.now("s1so"))
        P.op("vector", lambda v, p0=p0, r_mean=r_mean, w=w: v.tensor_scalar_mul(
            r_mean, p0[0:1, 0:w], eps[0:1, 3:4]))
        P.op("vector", lambda v, p1=p1, r_t=r_t, w=w: v.tensor_scalar_mul(
            r_t, p1[0:1, 0:w], eps[0:1, 3:4]))
        P.op("vector", lambda v, r_comb=r_comb, r_mean=r_mean: v.tensor_mul(
            r_comb, r_mean, r_mean))
        P.op("vector", lambda v, r_t=r_t, r_comb=r_comb: v.tensor_sub(
            r_t, r_t, r_comb))
        PS.release(0, "vector")
        PS.release(1, "vector")
        P.wait("scalar", "cV", P.mark("vector"))
        P.op("scalar", lambda e, r_comb=r_comb, r_t=r_t: e.activation(
            r_comb, r_t, AF.Ln, bias=eps[0:1, 0:1]))
        P.op("scalar", lambda e, r_rstd=r_rstd, r_comb=r_comb: e.activation(
            r_rstd, r_comb, AF.Exp, scale=-0.5))
        P.wait("vector", "cS", P.mark("scalar"))
        P.op("vector", lambda v, r_comb=r_comb, r_t=r_t: v.tensor_scalar_add(
            r_comb, r_t, eps[0:1, 0:1]))
        P.op("vector", lambda v, r_comb=r_comb: v.reciprocal(r_comb, r_comb))
        P.op("vector", lambda v, r_comb=r_comb, r_t=r_t: v.tensor_mul(
            r_comb, r_t, r_comb))
        P.wait("scalar", "cV", P.mark("vector"))
        P.op("scalar", lambda e, r_comb=r_comb: e.activation(
            r_comb, r_comb, AF.Ln, bias=eps[0:1, 1:2]))
        P.op("scalar", lambda e, r_comb=r_comb: e.activation(
            r_comb, r_comb, AF.Exp, scale=-0.5))
        P.wait("vector", "cS", P.mark("scalar"))
        P.op("vector", lambda v, r_comb=r_comb, r_rstd=r_rstd: v.tensor_mul(
            r_comb, r_comb, r_rstd))
        rv = P.mark("vector")
        P.wait("sync", "cV", rv)
        P.dma("sync", statsH[0:1, c0:c0 + w], rowap(rows, 0, w), "s1so",
              **({"allow_slow_non_contiguous": True} if w == 1 else {}))
        P.dma("sync", statsH[1:2, c0:c0 + w], rowap(rows, 1024, 1024 + w), "s1so",
              **({"allow_slow_non_contiguous": True} if w == 1 else {}))
        P.dma("sync", statsH[2:3, c0:c0 + w], rowap(rows, 512, 512 + w), "s1so",
              **({"allow_slow_non_contiguous": True} if w == 1 else {}))
        if it < 0:
            prev_free = ("cT", mm1)
            continue
        P.wait("tensor", "cV", rv)
        p2 = PS.acquire(2, "tensor")
        P.op("tensor", lambda e, p2=p2: e.matmul(
            p2[:, 0:TT], onesf[0:1, :], rowap(rows, 0, TT), start=True, stop=True),
            sig=False)
        p3 = PS.acquire(3, "tensor")
        P.op("tensor", lambda e, p3=p3: e.matmul(
            p3[:, 0:TT], onesf[0:1, :], rowap(rows, 512, 512 + TT),
            start=True, stop=True))
        P.wait("vector", "cT", P.mark("tensor"))
        P.op("vector", lambda v, p2=p2: v.tensor_copy(s1_bc[:, 0:TT], p2[:, 0:TT]))
        P.op("vector", lambda v, p3=p3: v.tensor_copy(s1_bc[:, TT:2 * TT], p3[:, 0:TT]))
        PS.release(2, "vector")
        PS.release(3, "vector")
        for kb in range(NB):
            P.op("vector", lambda v, kb=kb: v.tensor_sub(
                s1_x[:, kb * TT:(kb + 1) * TT], s1_x[:, kb * TT:(kb + 1) * TT],
                s1_bc[:, 0:TT]))
            P.op("vector", lambda v, kb=kb: v.scalar_tensor_tensor(
                s1_h1[:, kb * TT:(kb + 1) * TT], s1_x[:, kb * TT:(kb + 1) * TT],
                vecs[:, 40 + kb:41 + kb], s1_bc[:, TT:2 * TT], ALU.mult, ALU.mult))
        h1v = P.mark("vector")
        prev_free = ("cV", h1v)
        P.wait("tensor", "cV", h1v)
        if it == 0:
            P.wait("tensor", "wS1", wS1)
        P.wait("scalar", "s1qo", P.now("s1qo"))
        for mb in range(NB):
            pi = 4 + (mb % 2)
            pq = PS.acquire(pi, "tensor")
            for kb in range(NB):
                P.op("tensor", lambda e, kb=kb, mb=mb, pq=pq: e.matmul(
                    pq[:, 0:TT],
                    c["s1_wq"][:, kb * D + mb * 128:kb * D + (mb + 1) * 128],
                    s1_h1[:, kb * TT:(kb + 1) * TT],
                    start=(kb == 0), stop=(kb == NB - 1)), sig=(kb == NB - 1))
            P.wait("scalar", "cT", P.mark("tensor"))
            P.op("scalar", lambda e, mb=mb, pq=pq: e.activation(
                s1_st[:, mb * TT:(mb + 1) * TT], pq[:, 0:TT], AF.Identity,
                bias=vecs[:, mb:mb + 1]))
            PS.release(pi, "scalar")
        P.wait("gpsimd", "cS", P.mark("scalar"))
        for mb in range(NB):
            P.dma("gpsimd", c["qT"][mb * 128:(mb + 1) * 128, it * TT:(it + 1) * TT],
                  s1_st[:, mb * TT:(mb + 1) * TT], "s1qo", chain=False)
        qov = P.now("s1qo")
        for g in range(4):
            def lh(kb, g=g):
                return s1_h1[:, kb * TT + g * 128:kb * TT + (g + 1) * 128]
            for cc in range(2):
                pi = 4 + cc
                pk = PS.acquire(pi, "tensor")
                for kb in range(NB):
                    P.op("tensor", lambda e, kb=kb, cc=cc, pk=pk, g=g: e.matmul(
                        pk[:, 0:TT], lh(kb, g),
                        c["s1_wk"][:, kb * D + cc * TT:kb * D + (cc + 1) * TT],
                        start=(kb == 0), stop=(kb == NB - 1)), sig=(kb == NB - 1))
                P.wait("vector", "cT", P.mark("tensor"))
                P.op("vector", lambda v, cc=cc, pk=pk: v.tensor_add(
                    s1_kt[:, cc * TT:(cc + 1) * TT], pk[:, 0:TT],
                    c["sb_bkbc"][:, cc * TT:(cc + 1) * TT]))
                PS.release(pi, "vector")
            P.wait("scalar", "cV", P.mark("vector"))
            P.op("scalar", lambda e: e.activation(s1_ksq[:, :], s1_kt[:, :],
                                                  AF.Square))
            P.wait("vector", "cS", P.mark("scalar"))
            P.op("vector", lambda v: v.tensor_reduce(
                s1_rn[:, 0:H], s1_ksq[:, :].rearrange("p (h d) -> p h d", h=H),
                AX, ALU.add))
            P.wait("scalar", "cV", P.mark("vector"))
            P.op("scalar", lambda e: e.activation(s1_rn[:, H:2 * H],
                                                  s1_rn[:, 0:H], AF.Ln))
            P.op("scalar", lambda e: e.activation(s1_rn[:, H:2 * H],
                                                  s1_rn[:, H:2 * H], AF.Exp,
                                                  scale=0.5))
            P.wait("vector", "cS", P.mark("scalar"))
            P.op("vector", lambda v: v.tensor_scalar_add(
                s1_rn[:, H:2 * H], s1_rn[:, H:2 * H], eps[:, 1:2]))
            P.op("vector", lambda v: v.reciprocal(s1_rn[:, H:2 * H],
                                                  s1_rn[:, H:2 * H]))
            P.wait("vector", "s1ko", P.now("s1ko"))
            P.op("vector", lambda v: v.tensor_tensor(
                s1_kst[:, 0:D].rearrange("p (h d) -> p h d", h=H),
                s1_kt[:, :].rearrange("p (h d) -> p h d", h=H),
                s1_rn[:, H:2 * H].rearrange("p (h o) -> p h o", o=1)
                .broadcast_to((128, H, HD)), ALU.mult))
            P.wait("gpsimd", "cV", P.mark("vector"))
            P.dma("gpsimd", c["ktokN"][it * TT + g * 128:it * TT + (g + 1) * 128, :],
                  s1_kst[:, 0:D], "s1ko", chain=False)
            for cc in range(2):
                pi = 4 + cc
                pv = PS.acquire(pi, "tensor")
                for kb in range(NB):
                    P.op("tensor", lambda e, kb=kb, cc=cc, pv=pv, g=g: e.matmul(
                        pv[:, 0:TT], lh(kb, g),
                        c["s1_wv"][:, kb * D + cc * TT:kb * D + (cc + 1) * TT],
                        start=(kb == 0), stop=(kb == NB - 1)), sig=(kb == NB - 1))
                P.wait("vector", "cT", P.mark("tensor"))
                P.wait("vector", "s1ko", P.now("s1ko"))
                P.op("vector", lambda v, cc=cc, pv=pv: v.tensor_add(
                    s1_kst[:, D + cc * TT:D + (cc + 1) * TT], pv[:, 0:TT],
                    c["sb_bvbc"][:, cc * TT:(cc + 1) * TT]))
                PS.release(pi, "vector")
            P.wait("gpsimd", "cV", P.mark("vector"))
            P.dma("gpsimd", c["vtok"][it * TT + g * 128:it * TT + (g + 1) * 128, :],
                  s1_kst[:, D:2 * D], "s1ko", chain=False)
            pbeta = PS.acquire(4, "tensor")
            for kb in range(NB):
                P.op("tensor", lambda e, kb=kb, pbeta=pbeta, g=g: e.matmul(
                    pbeta[:, 0:H], lh(kb, g),
                    c["s1_wbeta"][:, kb * H:(kb + 1) * H],
                    start=(kb == 0), stop=(kb == NB - 1)), sig=(kb == NB - 1))
            P.wait("vector", "cT", P.mark("tensor"))
            gi = it * 4 + g
            P.op("vector", lambda v, pbeta=pbeta, gi=gi: v.tensor_add(
                c["sb_beta"][:, gi * H:(gi + 1) * H], pbeta[:, 0:H],
                c["sb_bbeta"][:, :]))
            PS.release(4, "vector")
            P.wait("scalar", "cV", P.mark("vector"))
            P.op("scalar", lambda e, gi=gi: e.activation(
                c["sb_beta"][:, gi * H:(gi + 1) * H],
                c["sb_beta"][:, gi * H:(gi + 1) * H], AF.Sigmoid))
    P.barrier()


def emit_s2(c):
    P, PS, PT = c["P"], c["PS"], c["PT"]
    s2_qw, s2_kw, s2_vw, s2_kf = c["s2_qw"], c["s2_kw"], c["s2_vw"], c["s2_kf"]
    s2_m, s2_mt, s2_m2, s2_mt2 = c["s2_m"], c["s2_mt"], c["s2_m2"], c["s2_mt2"]
    s2_sacc, s2_mo = c["s2_sacc"], c["s2_mo"]
    s2_ab, s2_wts, s2_obs = c["s2_ab"], c["s2_wts"], c["s2_obs"]
    beta, stl, i64, mln, mu, idb = (c["sb_beta"], c["sb_stl"], c["sb_i64"],
                                    c["sb_mln"], c["sb_mu"], c["sb_idb"])
    qT, ktokN, vtok, OT, ABb, WTb = (c["qT"], c["ktokN"], c["vtok"], c["OT"],
                                     c["ABb"], c["WTb"])
    nwin = c.get("S2_NWIN", NT)
    for wi in range(nwin):
        P.wait("sync", "cT", P.mark("tensor"))
        P.wait("sync", "s2sp0", P.now("s2sp0"))
        P.wait("sync", "s2sp1", P.now("s2sp1"))
        for mb in range(NB):
            P.dma("sync", s2_qw[:, mb * TT:(mb + 1) * TT],
                  qT[mb * 128:(mb + 1) * 128, wi * TT:(wi + 1) * TT], "s2w")
        for g in range(4):
            P.dma("sync", s2_kw[:, g * D:(g + 1) * D],
                  ktokN[wi * TT + g * 128:wi * TT + (g + 1) * 128, :], "s2w")
            P.dma("sync", s2_vw[:, g * D:(g + 1) * D],
                  vtok[wi * TT + g * 128:wi * TT + (g + 1) * 128, :], "s2w")
        wv_ = P.now("s2w")
        P.wait("tensor", "s2w", wv_)
        P.wait("vector", "s2w", wv_)
        for cp in range(4):
            ci = wi * 4 + cp
            sp_sem = f"s2sp{ci % 2}"
            ab_, wts_, obs_ = s2_ab[ci % 2], s2_wts[ci % 2], s2_obs[ci % 2]
            P.wait("vector", sp_sem, P.now(sp_sem))
            P.wait("tensor", "cV", P.mark("vector"))
            for pb in range(NB):
                pti = 6 + pb % 2
                pt = PT.acquire(pti, "tensor")
                P.op("tensor", lambda e, pb=pb, cp=cp, pt=pt: e.transpose(
                    pt.bitcast(BF16)[:, 0:128],
                    s2_kw[:, cp * D + pb * 128:cp * D + (pb + 1) * 128],
                    idb[:, :]))
                P.wait("vector", "cT", P.mark("tensor"))
                P.op("vector", lambda v, pb=pb, pt=pt: v.tensor_copy(
                    s2_kf[:, pb * CH:(pb + 1) * CH], pt.bitcast(BF16)[:, 0:128]))
                PT.release(pti, "vector")
            kfv = P.mark("vector")
            P.wait("tensor", "cV", kfv)
            def unit(h, cp=cp, ci=ci, ab_=ab_, wts_=wts_, obs_=obs_):
                u = h % 2
                pb, bh = h // 2, (h % 2) * 64
                bk = [3 * u, 3 * u + 1, 3 * u + 2]
                ptb = 6 + u
                tp = (lambda r, cc: None if (r == 0 and cc == 0) else (r, cc))
                kf_h = s2_kf[bh:bh + 64, pb * CH:(pb + 1) * CH]
                q_h = s2_qw[bh:bh + 64, pb * TT + cp * 128:pb * TT + (cp + 1) * 128]
                ktok_h = s2_kw[:, cp * D + h * HD:cp * D + (h + 1) * HD]
                vtok_h = s2_vw[:, cp * D + h * HD:cp * D + (h + 1) * HD]
                bcol = beta[:, ci * H + h:ci * H + h + 1]
                m_, mt_, m2_, mt2_ = s2_m[u], s2_mt[u], s2_m2[u], s2_mt2[u]
                sacc, mo = s2_sacc[u], s2_mo[u]
                p0 = PS.acquire(bk[0], "tensor")
                P.op("tensor", lambda e, kf_h=kf_h, p0=p0, bh=bh: e.matmul(
                    p0[:, 0:CH], kf_h, kf_h, start=True, stop=True,
                    tile_position=tp(bh, 0)))
                lt = P.mark("tensor")
                yield
                P.wait("vector", "cT", lt)
                P.op("vector", lambda v, p0=p0, bcol=bcol, m_=m_:
                     v.scalar_tensor_tensor(
                         m_[:, :], p0[:, 0:CH], bcol, mln[:, :],
                         ALU.mult, ALU.mult))
                PS.release(bk[0], "vector")
                lv = P.mark("vector")
                yield
                P.wait("tensor", "cV", lv)
                pt0 = PT.acquire(ptb, "tensor")
                P.op("tensor", lambda e, pt0=pt0, m_=m_: e.transpose(
                    pt0.bitcast(BF16)[:, 0:128], m_[:, :], idb[:, :]))
                lt = P.mark("tensor")
                yield
                P.wait("vector", "cT", lt)
                P.op("vector", lambda v, pt0=pt0, mt_=mt_: v.tensor_copy(
                    mt_[:, :], pt0.bitcast(BF16)[:, 0:128]))
                PT.release(ptb, "vector")
                P.op("vector", lambda v, vtok_h=vtok_h, bcol=bcol, sacc=sacc:
                     v.tensor_scalar_mul(sacc[:, 0:64], vtok_h, bcol))
                P.op("vector", lambda v, ktok_h=ktok_h, bcol=bcol, sacc=sacc:
                     v.tensor_scalar_mul(sacc[:, 64:128], ktok_h, bcol))
                lv = P.mark("vector")
                yield
                mbufs = [(m_, mt_), (m2_, mt2_)]
                for j in range(7):
                    mj, mtj = mbufs[j % 2]
                    mj1, mtj1 = mbufs[(j + 1) % 2]
                    P.wait("tensor", "cV", lv)
                    p1 = PS.acquire(bk[1], "tensor")
                    P.op("tensor", lambda e, mtj=mtj, p1=p1, sacc=sacc: e.matmul(
                        p1[:, 0:CH], mtj[:, :], sacc[:, :],
                        start=True, stop=True))
                    if j < 6:
                        p3 = PS.acquire(bk[2], "tensor")
                        P.op("tensor", lambda e, mtj=mtj, mj=mj, p3=p3: e.matmul(
                            p3[:, 0:CH], mtj[:, :], mj[:, :],
                            start=True, stop=True))
                        p4 = PS.acquire(bk[0], "tensor")
                        P.op("tensor", lambda e, mj=mj, mtj=mtj, p4=p4: e.matmul(
                            p4[:, 0:CH], mj[:, :], mtj[:, :],
                            start=True, stop=True))
                    lt = P.mark("tensor")
                    yield
                    P.wait("vector", "cT", lt)
                    P.op("vector", lambda v, p1=p1, sacc=sacc: v.tensor_add(
                        sacc[:, :], sacc[:, :], p1[:, 0:CH]))
                    PS.release(bk[1], "vector")
                    if j < 6:
                        P.op("vector", lambda v, mj1=mj1, p3=p3: v.tensor_copy(
                            mj1[:, :], p3[:, 0:CH]))
                        PS.release(bk[2], "vector")
                        P.op("vector", lambda v, mtj1=mtj1, p4=p4: v.tensor_copy(
                            mtj1[:, :], p4[:, 0:CH]))
                        PS.release(bk[0], "vector")
                    lv = P.mark("vector")
                    yield
                P.wait("tensor", "cV", lv)
                p5 = PS.acquire(bk[1], "tensor")
                P.op("tensor", lambda e, ktok_h=ktok_h, p5=p5, bh=bh, sacc=sacc:
                     e.matmul(p5[bh:bh + 64, 0:64], sacc[:, 64:128], ktok_h,
                              start=True, stop=True, tile_position=tp(0, bh)),
                     sig=False)
                P.op("tensor", lambda e, ktok_h=ktok_h, p5=p5, bh=bh, sacc=sacc:
                     e.matmul(p5[bh:bh + 64, 64:128], ktok_h, sacc[:, 0:64],
                              start=True, stop=True, tile_position=tp(0, bh)))
                p0b = PS.acquire(bk[2], "tensor")
                P.op("tensor", lambda e, kf_h=kf_h, q_h=q_h, p0b=p0b, bh=bh:
                     e.matmul(p0b[:, 0:CH], kf_h, q_h, start=True, stop=True,
                              tile_position=tp(bh, 0)))
                lt = P.mark("tensor")
                yield
                P.wait("vector", "cT", lt)
                P.op("vector", lambda v, p5=p5, bh=bh, pb=pb, ab_=ab_:
                     v.tensor_sub(ab_[bh:bh + 64, pb * 128:pb * 128 + 64],
                                  i64[bh:bh + 64, :], p5[bh:bh + 64, 0:64]))
                P.op("vector", lambda v, p5=p5, bh=bh, pb=pb, ab_=ab_:
                     v.tensor_copy(ab_[bh:bh + 64, pb * 128 + 64:(pb + 1) * 128],
                                   p5[bh:bh + 64, 64:128]))
                PS.release(bk[1], "vector")
                P.op("vector", lambda v, p0b=p0b, mo=mo: v.tensor_mul(
                    mo[:, :], p0b[:, 0:CH], mu[:, :]))
                PS.release(bk[2], "vector")
                lv = P.mark("vector")
                yield
                P.wait("tensor", "cV", lv)
                p1b = PS.acquire(bk[0], "tensor")
                P.op("tensor", lambda e, p1b=p1b, bh=bh, sacc=sacc, mo=mo:
                     e.matmul(p1b[bh:bh + 64, 0:CH], sacc[:, 0:64], mo[:, :],
                              start=True, stop=True, tile_position=tp(0, bh)),
                     sig=False)
                p2b = PS.acquire(bk[2], "tensor")
                P.op("tensor", lambda e, p2b=p2b, bh=bh, sacc=sacc, mo=mo:
                     e.matmul(p2b[bh:bh + 64, 0:CH], sacc[:, 64:128], mo[:, :],
                              start=True, stop=True, tile_position=tp(0, bh)))
                a_sb = ab_[bh:bh + 64, pb * 128:pb * 128 + 64]
                bt_sb = ab_[bh:bh + 64, pb * 128 + 64:(pb + 1) * 128]
                st_sb = stl[bh:bh + 64, pb * 64:(pb + 1) * 64]
                p3b = PS.acquire(bk[1], "tensor")
                P.op("tensor", lambda e, a_sb=a_sb, st_sb=st_sb, p3b=p3b, bh=bh:
                     e.matmul(p3b[bh:bh + 64, 0:64], a_sb, st_sb,
                              start=True, stop=True, tile_position=tp(bh, bh)))
                lt = P.mark("tensor")
                yield
                P.wait("vector", "cT", lt)
                P.op("vector", lambda v, p1b=p1b, bh=bh, pb=pb, obs_=obs_:
                     v.tensor_copy(obs_[bh:bh + 64, pb * CH:(pb + 1) * CH],
                                   p1b[bh:bh + 64, 0:CH]))
                PS.release(bk[0], "vector")
                P.op("vector", lambda v, p2b=p2b, q_h=q_h, bh=bh, pb=pb,
                     wts_=wts_:
                     v.tensor_sub(wts_[bh:bh + 64, pb * CH:(pb + 1) * CH],
                                  q_h, p2b[bh:bh + 64, 0:CH]))
                PS.release(bk[2], "vector")
                P.op("vector", lambda v, p3b=p3b, bt_sb=bt_sb, st_sb=st_sb, bh=bh:
                     v.tensor_add(st_sb, p3b[bh:bh + 64, 0:64], bt_sb))
                PS.release(bk[1], "vector")

            for hp in range(H // 2):
                gens = [unit(2 * hp), unit(2 * hp + 1)]
                alive = [True, True]
                while any(alive):
                    for gi in range(2):
                        if alive[gi]:
                            try:
                                next(gens[gi])
                            except StopIteration:
                                alive[gi] = False
            sp = P.mark("vector")
            P.wait("gpsimd", "cV", sp)
            P.dma("gpsimd", ABb[ci, :, :, :].rearrange("p r c -> r p c"),
                  ab_[:, :].rearrange("q (p c) -> q p c", p=NB), sp_sem,
                  chain=False)
            P.dma("gpsimd", WTb[ci, :, :, :].rearrange("p r c -> r p c"),
                  wts_[:, :].rearrange("q (p c) -> q p c", p=NB), sp_sem,
                  chain=False)
            P.dma("gpsimd",
                  OT[:, ci * CH:(ci + 1) * CH].rearrange("(p r) c -> r p c", p=NB),
                  obs_[:, :].rearrange("q (p c) -> q p c", p=NB), sp_sem,
                  chain=False)
    P.barrier()


def emit_s3(c):
    P = c["P"]
    if "noxc" in c["stages"]:
        P.op("vector", lambda v: v.tensor_scalar_mul(
            c["sb_stc"][:, :], c["sb_stl"][:, :], c["sb_scl"][:, 3:4]))
        P.barrier()
        return
    nc, pid, partner = c["nc"], c["pid"], c["partner"]
    shst, shfl = c["shst"], c["shfl"]
    stl, stc = c["sb_stl"], c["sb_stc"]
    sb_flag, sb_magic, scl = c["sb_flag"], c["sb_magic"], c["sb_scl"]
    P.dma("gpsimd", shst[pid, :, :], stl[:, :], "s3a", chain=False)
    P.wait("gpsimd", "s3a", P.now("s3a"))
    P.dma("gpsimd", shfl[pid, 0, :], sb_magic[0:1, 0:16], "s3a", chain=False)
    P.wait("gpsimd", "s3a", P.now("s3a"))

    def poll(g):
        val = g.alloc_register("pollv")
        cnt = g.alloc_register("pollc")
        vals = bass.RegisterHandles((val,))
        cnts = bass.RegisterHandles((cnt,))
        nc.regs_mov(vals, 0)
        nc.regs_mov(cnts, P.now("s3p") + 16)
        loop, done = "s3_loop", "s3_done"
        nc.br(loop, engines=[ET.Pool])
        with nc.body(loop, valid_engines=[ET.Pool]):
            g.dma_start(out=sb_flag[:, :], in_=shfl[partner, 0, :][None, :]) \
                .then_inc(P.sems["s3p"], 16)
            g.wait_ge(P.sems["s3p"], cnt)
            nc.regs_alu(cnts, cnts, 16, op=ALU.add)
            nc.regs_load(vals, sb_flag[0:1, 0:1])
            nc.br_cmp(val, MAGIC1, on_true=done, on_false=loop,
                      comp_op="IS_EQ", engines=[ET.Pool])
        nc.switch_bb(done)
        return None
    P.sem("s3p")
    P.op("gpsimd", poll, sig=True, chain=False)
    P.dma("gpsimd", stc[:, :], shst[partner, :, :], "s3b", chain=False)
    P.wait("vector", "s3b", P.now("s3b"))
    P.op("vector", lambda v: v.tensor_scalar_mul(stc[:, :], stc[:, :],
                                                 scl[:, 3:4]))
    P.barrier()


def emit_s4(c):
    P, PS = c["P"], c["PS"]
    s4_ab, s4_wt, s4_o = c["s4_ab"], c["s4_wt"], c["s4_o"]
    stc = c["sb_stc"]
    OT, ABb, WTb = c["OT"], c["ABb"], c["WTb"]
    nch = c.get("S2_NWIN", NT) * 4
    for ci in range(nch):
        P.wait("sync", "s4o", P.now("s4o"))
        P.wait("sync", "cT", P.mark("tensor"))
        P.dma("sync", s4_ab[:, :].rearrange("q (p c) -> q p c", p=NB),
              ABb[ci, :, :, :].rearrange("p r c -> r p c"), "s4w")
        P.dma("sync", s4_wt[:, :].rearrange("q (p c) -> q p c", p=NB),
              WTb[ci, :, :, :].rearrange("p r c -> r p c"), "s4w")
        P.dma("sync", s4_o[:, :].rearrange("q (p c) -> q p c", p=NB),
              OT[:, ci * CH:(ci + 1) * CH].rearrange("(p r) c -> r p c", p=NB),
              "s4w")
        wv_ = P.now("s4w")
        P.wait("tensor", "s4w", wv_)
        P.wait("vector", "s4w", wv_)
        for h in range(H):
            pb, bh = h // 2, (h % 2) * 64
            tp = (lambda r, cc: None if (r == 0 and cc == 0) else (r, cc))
            st_sb = stc[bh:bh + 64, pb * 64:(pb + 1) * 64]
            a_sb = s4_ab[bh:bh + 64, pb * 128:pb * 128 + 64]
            bt_sb = s4_ab[bh:bh + 64, pb * 128 + 64:(pb + 1) * 128]
            wt_sb = s4_wt[bh:bh + 64, pb * CH:(pb + 1) * CH]
            p0 = PS.acquire(h % 2, "tensor")
            P.op("tensor", lambda e, st_sb=st_sb, wt_sb=wt_sb, p0=p0, bh=bh:
                 e.matmul(p0[bh:bh + 64, 0:CH], st_sb, wt_sb,
                          start=True, stop=True, tile_position=tp(bh, bh)))
            p1 = PS.acquire(2 + h % 2, "tensor")
            P.op("tensor", lambda e, a_sb=a_sb, st_sb=st_sb, p1=p1, bh=bh:
                 e.matmul(p1[bh:bh + 64, 0:64], a_sb, st_sb,
                          start=True, stop=True, tile_position=tp(bh, bh)))
            mmk = P.mark("tensor")
            P.wait("vector", "cT", mmk)
            P.op("vector", lambda v, p0=p0, bh=bh, pb=pb: v.tensor_add(
                s4_o[bh:bh + 64, pb * CH:(pb + 1) * CH],
                s4_o[bh:bh + 64, pb * CH:(pb + 1) * CH], p0[bh:bh + 64, 0:CH]))
            PS.release(h % 2, "vector")
            P.op("vector", lambda v, p1=p1, bt_sb=bt_sb, st_sb=st_sb, bh=bh:
                 v.tensor_add(st_sb, p1[bh:bh + 64, 0:64], bt_sb))
            PS.release(2 + h % 2, "vector")
        P.wait("gpsimd", "cV", P.mark("vector"))
        P.dma("gpsimd",
              OT[:, ci * CH:(ci + 1) * CH].rearrange("(p r) c -> r p c", p=NB),
              s4_o[:, :].rearrange("q (p c) -> q p c", p=NB), "s4o", chain=False)
    P.barrier()


def emit_s5(c):
    P, PS = c["P"], c["PS"]
    rows, eps, vecs = c["sb_rows"], c["sb_eps"], c["sb_vecs"]
    onescf, onescb, onesf = c["sb_onescf"], c["sb_onescb"], c["sb_onesf"]
    s5_ot, s5_att, s5_x, s5_xa = c["s5_ot"], c["s5_att"], c["s5_x"], c["s5_xa"]
    s5_sq, s5_h2, s5_bc = c["s5_sq"], c["s5_h2"], c["s5_bc"]
    OT, statsH, x_sl, xaTh, h2Th = (c["OT"], c["statsH"], c["x_sl"], c["xaTh"],
                                    c["h2Th"])
    for kb in range(NB):
        P.dma("gpsimd", c["s5_wo"][:, kb * D:(kb + 1) * D],
              c["wo"][kb * 128:(kb + 1) * 128, :], "wS5a", chain=False)
    P.wait("tensor", "wS5a", P.now("wS5a"))
    for it in range(NT):
        c0 = 1 + it * TT
        P.wait("sync", "cV", P.mark("vector"))
        P.wait("sync", "cS", P.mark("scalar"))
        for mb in range(NB):
            P.dma("sync", s5_ot[:, mb * TT:(mb + 1) * TT],
                  OT[mb * 128:(mb + 1) * 128, it * TT:(it + 1) * TT], "s5w")
            P.dma("sync", s5_x[:, mb * TT:(mb + 1) * TT],
                  x_sl[kb * 128:(kb + 1) * 128, c0:c0 + TT] if False else
                  x_sl[mb * 128:(mb + 1) * 128, c0:c0 + TT], "s5w")
        P.dma("sync", rowap(rows, 0, TT), statsH[0:1, c0:c0 + TT], "s5w")
        P.dma("sync", rowap(rows, 1024, 1024 + TT), statsH[1:2, c0:c0 + TT], "s5w")
        wv_ = P.now("s5w")
        P.wait("tensor", "s5w", wv_)
        p2 = PS.acquire(2, "tensor")
        P.op("tensor", lambda e, p2=p2: e.matmul(
            p2[:, 0:TT], onesf[0:1, :], rowap(rows, 0, TT), start=True, stop=True),
            sig=False)
        p3 = PS.acquire(3, "tensor")
        P.op("tensor", lambda e, p3=p3: e.matmul(
            p3[:, 0:TT], onesf[0:1, :], rowap(rows, 1024, 1024 + TT),
            start=True, stop=True))
        P.wait("vector", "cT", P.mark("tensor"))
        P.op("vector", lambda v, p2=p2: v.tensor_copy(s5_bc[:, 0:TT], p2[:, 0:TT]))
        P.op("vector", lambda v, p3=p3: v.tensor_copy(s5_bc[:, TT:2 * TT],
                                                      p3[:, 0:TT]))
        PS.release(2, "vector")
        PS.release(3, "vector")
        for mb in range(NB):
            pi = 4 + (mb % 2)
            pq = PS.acquire(pi, "tensor")
            for kb in range(NB):
                P.op("tensor", lambda e, kb=kb, mb=mb, pq=pq: e.matmul(
                    pq[:, 0:TT],
                    c["s5_wo"][:, kb * D + mb * 128:kb * D + (mb + 1) * 128],
                    s5_ot[:, kb * TT:(kb + 1) * TT],
                    start=(kb == 0), stop=(kb == NB - 1)), sig=(kb == NB - 1))
            P.wait("scalar", "cT", P.mark("tensor"))
            P.op("scalar", lambda e, mb=mb, pq=pq: e.activation(
                s5_att[:, mb * TT:(mb + 1) * TT], pq[:, 0:TT], AF.Identity,
                bias=vecs[:, 24 + mb:25 + mb]))
            PS.release(pi, "scalar")
        P.wait("vector", "cS", P.mark("scalar"))
        P.wait("vector", "s5o", P.now("s5o"))
        for mb in range(NB):
            sl = slice(mb * TT, (mb + 1) * TT)
            P.op("vector", lambda v, sl=sl: v.tensor_sub(
                s5_x[:, sl], s5_x[:, sl], s5_bc[:, 0:TT]))
            P.op("vector", lambda v, sl=sl: v.tensor_mul(
                s5_x[:, sl], s5_x[:, sl], s5_bc[:, TT:2 * TT]))
            P.op("vector", lambda v, sl=sl: v.tensor_add(
                s5_xa[:, sl], s5_x[:, sl], s5_att[:, sl]))
        xav = P.mark("vector")
        P.wait("gpsimd", "cV", xav)
        for mb in range(NB):
            P.dma("gpsimd", xaTh[mb * 128:(mb + 1) * 128, it * TT:(it + 1) * TT],
                  s5_xa[:, mb * TT:(mb + 1) * TT], "s5o", chain=False)
        P.wait("scalar", "cV", xav)
        for mb in range(NB):
            P.op("scalar", lambda e, mb=mb: e.activation(
                s5_sq[:, mb * TT:(mb + 1) * TT], s5_xa[:, mb * TT:(mb + 1) * TT],
                AF.Square))
        P.wait("tensor", "cS", P.mark("scalar"))
        p0 = PS.acquire(0, "tensor")
        for kb in range(NB):
            P.op("tensor", lambda e, kb=kb, p0=p0: e.matmul(
                p0[0:1, 0:TT], onescb[:, 0:1], s5_sq[:, kb * TT:(kb + 1) * TT],
                start=(kb == 0), stop=(kb == NB - 1)), sig=(kb == NB - 1))
        P.wait("vector", "cT", P.mark("tensor"))
        P.op("vector", lambda v, p0=p0: v.tensor_scalar_mul(
            rowap(rows, 1536, 1536 + TT), p0[0:1, 0:TT], eps[0:1, 3:4]))
        PS.release(0, "vector")
        P.wait("scalar", "cV", P.mark("vector"))
        P.op("scalar", lambda e: e.activation(rowap(rows, 1536, 1536 + TT),
                                              rowap(rows, 1536, 1536 + TT),
                                              AF.Ln, bias=eps[0:1, 1:2]))
        P.op("scalar", lambda e: e.activation(rowap(rows, 1536, 1536 + TT),
                                              rowap(rows, 1536, 1536 + TT),
                                              AF.Exp, scale=-0.5))
        P.wait("tensor", "cS", P.mark("scalar"))
        p1 = PS.acquire(1, "tensor")
        P.op("tensor", lambda e, p1=p1: e.matmul(
            p1[:, 0:TT], onesf[0:1, :], rowap(rows, 1536, 1536 + TT),
            start=True, stop=True))
        P.wait("vector", "cT", P.mark("tensor"))
        P.op("vector", lambda v, p1=p1: v.tensor_copy(s5_bc[:, 2 * TT:3 * TT],
                                                      p1[:, 0:TT]))
        PS.release(1, "vector")
        P.wait("vector", "s5o", P.now("s5o"))
        for mb in range(NB):
            P.op("vector", lambda v, mb=mb: v.scalar_tensor_tensor(
                s5_h2[:, mb * TT:(mb + 1) * TT], s5_xa[:, mb * TT:(mb + 1) * TT],
                vecs[:, 48 + mb:49 + mb], s5_bc[:, 2 * TT:3 * TT],
                ALU.mult, ALU.mult))
        P.wait("gpsimd", "cV", P.mark("vector"))
        for mb in range(NB):
            P.dma("gpsimd", h2Th[mb * 128:(mb + 1) * 128, it * TT:(it + 1) * TT],
                  s5_h2[:, mb * TT:(mb + 1) * TT], "s5o", chain=False)
    P.barrier()
    s5_h2b, s5_hf, s5_gt = c["s5_h2b"], c["s5_hf"], c["s5_gt"]
    s5_xab, s5_s, sb_b1 = c["s5_xab"], c["s5_s"], c["sb_b1"]
    sTh = c["sTh"]
    for kb in range(NB):
        P.dma("gpsimd", c["s5_w1"][:, kb * DFF:(kb + 1) * DFF],
              c["w1"][kb * 128:(kb + 1) * 128, :], "wS5b", chain=False)
    for ko in range(32):
        P.dma("gpsimd", c["s5_w2"][:, ko * D:(ko + 1) * D],
              c["w2"][ko * 128:(ko + 1) * 128, :], "wS5b", chain=False)
    P.wait("tensor", "wS5b", P.now("wS5b"))
    for it in range(NT):
        P.wait("sync", "cT", P.mark("tensor"))
        for mb in range(NB):
            P.dma("sync", s5_h2b[:, mb * TT:(mb + 1) * TT],
                  h2Th[mb * 128:(mb + 1) * 128, it * TT:(it + 1) * TT], "s5bw")
        wv_ = P.now("s5bw")
        P.wait("tensor", "s5bw", wv_)
        P.wait("vector", "cT", P.mark("tensor"))
        P.wait("scalar", "cT", P.mark("tensor"))
        for mo in range(32):
            pi = 4 + (mo % 2)
            pq = PS.acquire(pi, "tensor")
            for kb in range(NB):
                P.op("tensor", lambda e, kb=kb, mo=mo, pq=pq: e.matmul(
                    pq[:, 0:TT],
                    c["s5_w1"][:, kb * DFF + mo * 128:kb * DFF + (mo + 1) * 128],
                    s5_h2b[:, kb * TT:(kb + 1) * TT],
                    start=(kb == 0), stop=(kb == NB - 1)), sig=(kb == NB - 1))
            P.wait("scalar", "cT", P.mark("tensor"))
            P.wait("scalar", "cV", P.mark("vector"))
            t0 = s5_gt[:, 0:TT]
            t1 = s5_gt[:, TT:2 * TT]
            P.op("scalar", lambda e, mo=mo, pq=pq, t0=t0: e.activation(
                t0, pq[:, 0:TT], AF.Identity, bias=sb_b1[:, mo:mo + 1]))
            PS.release(pi, "scalar")
            P.op("scalar", lambda e, t0=t0, t1=t1: e.activation(t1, t0, AF.Square))
            P.wait("vector", "cS", P.mark("scalar"))
            P.op("vector", lambda v, t1=t1: v.tensor_scalar(
                t1, t1, 0.044715, 1.0, ALU.mult, ALU.add))
            P.op("vector", lambda v, t0=t0, t1=t1: v.tensor_mul(t1, t0, t1))
            P.wait("scalar", "cV", P.mark("vector"))
            P.op("scalar", lambda e, t1=t1: e.activation(
                t1, t1, AF.Tanh, scale=0.7978845608028654))
            P.wait("vector", "cS", P.mark("scalar"))
            P.op("vector", lambda v, t1=t1: v.tensor_scalar(
                t1, t1, 0.5, 0.5, ALU.mult, ALU.add))
            P.op("vector", lambda v, mo=mo, t0=t0, t1=t1: v.tensor_mul(
                s5_hf[:, mo * TT:(mo + 1) * TT], t0, t1))
        hfv = P.mark("vector")
        P.wait("tensor", "cV", hfv)
        for mb in range(NB):
            pi = 4 + (mb % 2)
            pq = PS.acquire(pi, "tensor")
            for ko in range(32):
                P.op("tensor", lambda e, ko=ko, mb=mb, pq=pq: e.matmul(
                    pq[:, 0:TT],
                    c["s5_w2"][:, ko * D + mb * 128:ko * D + (mb + 1) * 128],
                    s5_hf[:, ko * TT:(ko + 1) * TT],
                    start=(ko == 0), stop=(ko == 31)), sig=(ko == 31))
            P.dma("sync", s5_xab[:, 0:TT],
                  xaTh[mb * 128:(mb + 1) * 128, it * TT:(it + 1) * TT], "s5bx",
                  wait=[("cV", P.mark("vector"))])
            P.wait("scalar", "cT", P.mark("tensor"))
            P.wait("scalar", "s5so", P.now("s5so"))
            P.op("scalar", lambda e, mb=mb, pq=pq: e.activation(
                s5_s[:, 0:TT], pq[:, 0:TT], AF.Identity,
                bias=vecs[:, 32 + mb:33 + mb]))
            PS.release(pi, "scalar")
            P.wait("vector", "cS", P.mark("scalar"))
            P.wait("vector", "s5bx", P.now("s5bx"))
            P.op("vector", lambda v: v.tensor_add(s5_s[:, 0:TT], s5_s[:, 0:TT],
                                                  s5_xab[:, 0:TT]))
            P.wait("gpsimd", "cV", P.mark("vector"))
            P.dma("gpsimd", sTh[mb * 128:(mb + 1) * 128, it * TT:(it + 1) * TT],
                  s5_s[:, 0:TT], "s5so", chain=False)
    P.barrier()
    if "s5" in c["stages"] and c["dev"]:
        P.dma("sync", c["dev_out"]["d_s"][:, :], sTh[:, :], "dev")
        P.dma("sync", c["dev_out"]["d_xa"][:, :], xaTh[:, :], "dev")
        P.dma("sync", c["dev_out"]["d_h2"][:, :], h2Th[:, :], "dev")
        P.wait("sync", "dev", P.now("dev"))


def emit_s6(c):
    P, PS = c["P"], c["PS"]
    rows, eps, vecs = c["sb_rows"], c["sb_eps"], c["sb_vecs"]
    onescf, onescb, onesf = c["sb_onescf"], c["sb_onescb"], c["sb_onesf"]
    lastc = c["sb_lastc"]
    s6_x, s6_xn, s6_xr, s6_xk = c["s6_x"], c["s6_xn"], c["s6_xr"], c["s6_xk"]
    s6_r, s6_kk, s6_sld, s6_g = c["s6_r"], c["s6_kk"], c["s6_sld"], c["s6_g"]
    s6_x1, s6_sq, s6_bc, s6_xn1 = c["s6_x1"], c["s6_sq"], c["s6_bc"], c["s6_xn1"]
    x_sl, statsH, stats1H, sTh = c["x_sl"], c["statsH"], c["stats1H"], c["sTh"]
    x1T, xn1T = c["x1T"], c["xn1T_t"]
    WB = TT + 8
    for w_h, w_s in [(c["wrT"], c["s6_wr"]), (c["wkT"], c["s6_wkk"]),
                     (c["wvT"], c["s6_wvv"])]:
        for kb in range(NB):
            P.dma("gpsimd", w_s[:, kb * D:(kb + 1) * D],
                  w_h[kb * 128:(kb + 1) * 128, :], "wS6", chain=False)
    P.wait("tensor", "wS6", P.now("wS6"))
    P.dma("sync", rowap(rows, 0, 8), statsH[0:1, 0:8], "s6w",
          allow_slow_non_contiguous=True)
    P.dma("sync", rowap(rows, 1024, 1032), statsH[1:2, 0:8], "s6w",
          allow_slow_non_contiguous=True)
    for mb in range(NB):
        P.dma("sync", s6_x[:, mb * TT:mb * TT + 1],
              x_sl[mb * 128:(mb + 1) * 128, 0:1], "s6w",
              allow_slow_non_contiguous=True)
    P.wait("tensor", "s6w", P.now("s6w"))
    ph = PS.acquire(0, "tensor")
    P.op("tensor", lambda e, ph=ph: e.matmul(
        ph[:, 0:1], onesf[0:1, :], rowap(rows, 0, 1), start=True, stop=True),
        sig=False)
    P.op("tensor", lambda e, ph=ph: e.matmul(
        ph[:, 1:2], onesf[0:1, :], rowap(rows, 1024, 1025), start=True, stop=True))
    P.wait("vector", "cT", P.mark("tensor"))
    for mb in range(NB):
        P.op("vector", lambda v, mb=mb, ph=ph: v.tensor_sub(
            lastc[:, mb:mb + 1], s6_x[:, mb * TT:mb * TT + 1], ph[:, 0:1]))
        P.op("vector", lambda v, mb=mb, ph=ph: v.tensor_mul(
            lastc[:, mb:mb + 1], lastc[:, mb:mb + 1], ph[:, 1:2]))
    PS.release(0, "vector")
    for it in range(NT):
        c0 = 1 + it * TT
        P.wait("sync", "cV", P.mark("vector"))
        P.wait("sync", "cT", P.mark("tensor"))
        P.wait("sync", "s6so", P.now("s6so"))
        for mb in range(NB):
            P.dma("sync", s6_x[:, mb * TT:(mb + 1) * TT],
                  x_sl[mb * 128:(mb + 1) * 128, c0:c0 + TT], "s6w")
        P.dma("sync", rowap(rows, 0, TT), statsH[0:1, c0:c0 + TT], "s6w")
        P.dma("sync", rowap(rows, 1024, 1024 + TT), statsH[1:2, c0:c0 + TT], "s6w")
        wv_ = P.now("s6w")
        P.wait("tensor", "s6w", wv_)
        P.wait("vector", "cT", P.mark("tensor"))
        p2 = PS.acquire(2, "tensor")
        P.op("tensor", lambda e, p2=p2: e.matmul(
            p2[:, 0:TT], onesf[0:1, :], rowap(rows, 0, TT), start=True, stop=True),
            sig=False)
        p3 = PS.acquire(3, "tensor")
        P.op("tensor", lambda e, p3=p3: e.matmul(
            p3[:, 0:TT], onesf[0:1, :], rowap(rows, 1024, 1024 + TT),
            start=True, stop=True))
        P.wait("vector", "cT", P.mark("tensor"))
        P.op("vector", lambda v, p2=p2: v.tensor_copy(s6_bc[:, 0:TT], p2[:, 0:TT]))
        P.op("vector", lambda v, p3=p3: v.tensor_copy(s6_bc[:, TT:2 * TT],
                                                      p3[:, 0:TT]))
        PS.release(2, "vector")
        PS.release(3, "vector")
        for mb in range(NB):
            xb = slice(mb * WB + 1, mb * WB + 1 + TT)
            P.op("vector", lambda v, mb=mb: v.tensor_copy(
                s6_xn[:, mb * WB:mb * WB + 1], lastc[:, mb:mb + 1]))
            P.op("vector", lambda v, mb=mb, xb=xb: v.tensor_sub(
                s6_x[:, mb * TT:(mb + 1) * TT], s6_x[:, mb * TT:(mb + 1) * TT],
                s6_bc[:, 0:TT]))
            P.op("vector", lambda v, mb=mb, xb=xb: v.tensor_tensor(
                s6_xn[:, xb], s6_x[:, mb * TT:(mb + 1) * TT],
                s6_bc[:, TT:2 * TT], ALU.mult))
            P.op("vector", lambda v, mb=mb: v.tensor_copy(
                lastc[:, mb:mb + 1], s6_xn[:, mb * WB + TT:mb * WB + TT + 1]))
        for mb in range(NB):
            cur = s6_xn[:, mb * WB + 1:mb * WB + 1 + TT]
            prv = s6_xn[:, mb * WB:mb * WB + TT]
            dif = s6_g[:, 0:TT]
            P.op("vector", lambda v, cur=cur, prv=prv, dif=dif: v.tensor_sub(
                dif, cur, prv))
            P.op("vector", lambda v, mb=mb, dif=dif, prv=prv: v.scalar_tensor_tensor(
                s6_xr[:, mb * TT:(mb + 1) * TT], dif, vecs[:, 56 + mb:57 + mb],
                prv, ALU.mult, ALU.add))
            P.op("vector", lambda v, mb=mb, dif=dif, prv=prv: v.scalar_tensor_tensor(
                s6_xk[:, mb * TT:(mb + 1) * TT], dif, vecs[:, 64 + mb:65 + mb],
                prv, ALU.mult, ALU.add))
        mixv = P.mark("vector")
        P.wait("tensor", "cV", mixv)
        for mo in range(NB):
            pi = 4 + (mo % 2)
            pq = PS.acquire(pi, "tensor")
            for kb in range(NB):
                P.op("tensor", lambda e, kb=kb, mo=mo, pq=pq: e.matmul(
                    pq[:, 0:TT],
                    c["s6_wr"][:, kb * D + mo * 128:kb * D + (mo + 1) * 128],
                    s6_xr[:, kb * TT:(kb + 1) * TT],
                    start=(kb == 0), stop=(kb == NB - 1)), sig=(kb == NB - 1))
            P.wait("scalar", "cT", P.mark("tensor"))
            P.op("scalar", lambda e, mo=mo, pq=pq: e.activation(
                s6_r[:, mo * TT:(mo + 1) * TT], pq[:, 0:TT], AF.Sigmoid))
            PS.release(pi, "scalar")
        for mo in range(NB):
            pi = 4 + (mo % 2)
            pq = PS.acquire(pi, "tensor")
            for kb in range(NB):
                P.op("tensor", lambda e, kb=kb, mo=mo, pq=pq: e.matmul(
                    pq[:, 0:TT],
                    c["s6_wkk"][:, kb * D + mo * 128:kb * D + (mo + 1) * 128],
                    s6_xk[:, kb * TT:(kb + 1) * TT],
                    start=(kb == 0), stop=(kb == NB - 1)), sig=(kb == NB - 1))
            P.wait("scalar", "cT", P.mark("tensor"))
            P.op("scalar", lambda e, pq=pq: e.activation(
                s6_g[:, 0:TT], pq[:, 0:TT], AF.Relu))
            PS.release(pi, "scalar")
            P.op("scalar", lambda e, mo=mo: e.activation(
                s6_kk[:, mo * TT:(mo + 1) * TT], s6_g[:, 0:TT], AF.Square))
        kkv = P.mark("scalar")
        P.wait("tensor", "cS", kkv)
        P.wait("sync", "cV", P.mark("vector"))
        for mo in range(NB):
            P.dma("sync", s6_sld[:, mo * TT:(mo + 1) * TT],
                  sTh[mo * 128:(mo + 1) * 128, it * TT:(it + 1) * TT], "s6s")
        for mo in range(NB):
            pi = 4 + (mo % 2)
            pq = PS.acquire(pi, "tensor")
            for kb in range(NB):
                P.op("tensor", lambda e, kb=kb, mo=mo, pq=pq: e.matmul(
                    pq[:, 0:TT],
                    c["s6_wvv"][:, kb * D + mo * 128:kb * D + (mo + 1) * 128],
                    s6_kk[:, kb * TT:(kb + 1) * TT],
                    start=(kb == 0), stop=(kb == NB - 1)), sig=(kb == NB - 1))
            P.wait("vector", "cT", P.mark("tensor"))
            P.op("vector", lambda v, mo=mo, pq=pq: v.tensor_mul(
                s6_g[:, 0:TT], pq[:, 0:TT], s6_r[:, mo * TT:(mo + 1) * TT]))
            PS.release(pi, "vector")
            P.wait("scalar", "cV", P.mark("vector"))
            P.op("scalar", lambda e: e.activation(
                s6_g[:, TT:2 * TT], s6_g[:, 0:TT], AF.Sigmoid))
            P.wait("vector", "cS", P.mark("scalar"))
            P.op("vector", lambda v: v.tensor_mul(
                s6_g[:, 0:TT], s6_g[:, 0:TT], s6_g[:, TT:2 * TT]))
            P.wait("vector", "s6s", P.now("s6s"))
            P.op("vector", lambda v, mo=mo: v.tensor_mul(
                s6_g[:, 0:TT], s6_g[:, 0:TT], s6_sld[:, mo * TT:(mo + 1) * TT]))
            P.op("vector", lambda v, mo=mo: v.tensor_add(
                s6_g[:, 0:TT], s6_g[:, 0:TT], s6_bc[:, 0:TT]))
            P.op("vector", lambda v, mo=mo: v.tensor_add(
                s6_x1[:, mo * TT:(mo + 1) * TT], s6_x[:, mo * TT:(mo + 1) * TT],
                s6_g[:, 0:TT]))
        x1v = P.mark("vector")
        P.wait("gpsimd", "cV", x1v)
        P.wait("gpsimd", "s6o", P.now("s6o"))
        for mb in range(NB):
            P.dma("gpsimd", x1T[mb * 128:(mb + 1) * 128, it * TT:(it + 1) * TT],
                  s6_x1[:, mb * TT:(mb + 1) * TT], "s6o", chain=False)
        P.wait("scalar", "cV", x1v)
        for mb in range(NB):
            P.op("scalar", lambda e, mb=mb: e.activation(
                s6_sq[:, mb * TT:(mb + 1) * TT], s6_x1[:, mb * TT:(mb + 1) * TT],
                AF.Square))
        P.wait("tensor", "cS", P.mark("scalar"))
        P.wait("tensor", "cV", x1v)
        p0 = PS.acquire(0, "tensor")
        p1 = PS.acquire(1, "tensor")
        for kb in range(NB):
            P.op("tensor", lambda e, kb=kb, p0=p0: e.matmul(
                p0[0:1, 0:TT], onescf[:, 0:1], s6_x1[:, kb * TT:(kb + 1) * TT],
                start=(kb == 0), stop=(kb == NB - 1)), sig=(kb == NB - 1))
        for kb in range(NB):
            P.op("tensor", lambda e, kb=kb, p1=p1: e.matmul(
                p1[0:1, 0:TT], onescb[:, 0:1], s6_sq[:, kb * TT:(kb + 1) * TT],
                start=(kb == 0), stop=(kb == NB - 1)), sig=(kb == NB - 1))
        P.wait("vector", "cT", P.mark("tensor"))
        P.wait("vector", "s6so", P.now("s6so"))
        r_m = rowap(rows, 1536, 1536 + TT)
        r_v = rowap(rows, 512, 512 + TT)
        P.op("vector", lambda v, p0=p0, r_m=r_m: v.tensor_scalar_mul(
            r_m, p0[0:1, 0:TT], eps[0:1, 3:4]))
        P.op("vector", lambda v, p1=p1, r_v=r_v: v.tensor_scalar_mul(
            r_v, p1[0:1, 0:TT], eps[0:1, 3:4]))
        PS.release(0, "vector")
        PS.release(1, "vector")
        P.op("vector", lambda v, r_m=r_m: v.tensor_mul(
            rowap(rows, 0, TT), r_m, r_m))
        P.op("vector", lambda v, r_v=r_v: v.tensor_sub(
            rowap(rows, 0, TT), r_v, rowap(rows, 0, TT)))
        P.wait("scalar", "cV", P.mark("vector"))
        P.op("scalar", lambda e: e.activation(
            rowap(rows, 0, TT), rowap(rows, 0, TT), AF.Ln,
            bias=eps[0:1, 0:1]))
        P.op("scalar", lambda e: e.activation(
            rowap(rows, 0, TT), rowap(rows, 0, TT), AF.Exp,
            scale=-0.5))
        scv = P.mark("scalar")
        P.wait("sync", "cS", scv)
        P.dma("sync", stats1H[0:1, c0:c0 + TT], r_m, "s6so")
        P.dma("sync", stats1H[1:2, c0:c0 + TT], rowap(rows, 0, TT), "s6so")
        P.wait("tensor", "cS", scv)
        p2b = PS.acquire(2, "tensor")
        P.op("tensor", lambda e, p2b=p2b: e.matmul(
            p2b[:, 0:TT], onesf[0:1, :], r_m, start=True, stop=True), sig=False)
        p3b = PS.acquire(3, "tensor")
        P.op("tensor", lambda e, p3b=p3b: e.matmul(
            p3b[:, 0:TT], onesf[0:1, :], rowap(rows, 0, TT),
            start=True, stop=True))
        P.wait("vector", "cT", P.mark("tensor"))
        P.op("vector", lambda v, p2b=p2b: v.tensor_copy(s6_bc[:, 0:TT],
                                                        p2b[:, 0:TT]))
        P.op("vector", lambda v, p3b=p3b: v.tensor_copy(s6_bc[:, TT:2 * TT],
                                                        p3b[:, 0:TT]))
        PS.release(2, "vector")
        PS.release(3, "vector")
        P.wait("vector", "s6xo", P.now("s6xo"))
        P.wait("vector", "s6o", P.now("s6o"))
        for mb in range(NB):
            P.op("vector", lambda v, mb=mb: v.tensor_sub(
                s6_x1[:, mb * TT:(mb + 1) * TT], s6_x1[:, mb * TT:(mb + 1) * TT],
                s6_bc[:, 0:TT]))
            P.op("vector", lambda v, mb=mb: v.tensor_tensor(
                s6_xn1[:, mb * TT:(mb + 1) * TT], s6_x1[:, mb * TT:(mb + 1) * TT],
                s6_bc[:, TT:2 * TT], ALU.mult))
        P.wait("gpsimd", "cV", P.mark("vector"))
        for mb in range(NB):
            P.dma("gpsimd", xn1T[mb * 128:(mb + 1) * 128, it * TT:(it + 1) * TT],
                  s6_xn1[:, mb * TT:(mb + 1) * TT], "s6xo", chain=False)
    P.barrier()


def emit_cv(c):
    P = c["P"]
    cv_a, cv_b, cv_o, eps = c["cv_a"], c["cv_b"], c["cv_o"], c["sb_eps"]
    for kb in range(NB):
        P.wait("sync", "cV", P.mark("vector"))
        P.dma("sync", cv_a[:, :], c["k0u"][kb * 128:(kb + 1) * 128, :], "cvw")
        P.dma("sync", cv_b[:, :], c["k0u"][D + kb * 128:D + (kb + 1) * 128, :], "cvw")
        P.wait("vector", "cvw", P.now("cvw"))
        P.wait("vector", "cvo", P.now("cvo"))
        P.op("vector", lambda v: v.tensor_sub(cv_o[:, :], cv_a[:, :], cv_b[:, :]))
        P.wait("gpsimd", "cV", P.mark("vector"))
        P.dma("gpsimd", c["d0T"][kb * 128:(kb + 1) * 128, :], cv_o[:, :], "cvo",
              chain=False)
    for src_t, dst_t, n in [(c["k1u"], c["w1eT"], 32), (c["k2u"], c["w2eT"], 32)]:
        w = src_t.shape[1]
        for kb in range(n):
            P.wait("sync", "cS", P.mark("scalar"))
            P.wait("sync", "cV", P.mark("vector"))
            P.dma("sync", cv_a[:, 0:w], src_t[kb * 128:(kb + 1) * 128, :], "cvw")
            P.wait("scalar", "cvw", P.now("cvw"))
            P.wait("scalar", "cvo", P.now("cvo"))
            P.op("scalar", lambda e, w=w: e.activation(
                cv_o[:, 0:w], cv_a[:, 0:w], AF.Identity, bias=eps[:, 4:5],
                scale=2.0))
            P.wait("gpsimd", "cS", P.mark("scalar"))
            P.dma("gpsimd", dst_t[kb * 128:(kb + 1) * 128, :], cv_o[:, 0:w], "cvo",
                  chain=False)
    P.barrier()


def emit_t1(c):
    P, PS = c["P"], c["PS"]
    t1_xn1, t1_q, t1_kt, t1_vt, t1_tmp = (c["t1_xn1"], c["t1_q"], c["t1_kt"],
                                          c["t1_vt"], c["t1_tmp"])
    bqkv, eps = c["sb_bqkv"], c["sb_eps"]
    xn1T, q1T = c["xn1T_t"], c["q1T"]
    for kb in range(NB):
        P.dma("gpsimd", c["t1_wqkv"][:, kb * 3 * D:(kb + 1) * 3 * D],
              c["wqkv"][kb * 128:(kb + 1) * 128, :], "wT1", chain=False)
    P.dma("sync", c["sb_bqkbc"][:, :], c["bqkv_k_bc"][:, :], "wT1s")
    P.dma("sync", c["sb_bqvbc"][:, :], c["bqkv_v_bc"][:, :], "wT1s")
    P.wait("tensor", "wT1", P.now("wT1"))
    P.wait("vector", "wT1", P.now("wT1"))
    P.wait("vector", "wT1s", P.now("wT1s"))
    P.op("vector", lambda v: v.memset(c["sb_kvloc"][:, :], 0.0))
    for it in range(NT):
        P.wait("sync", "cT", P.mark("tensor"))
        P.wait("sync", "cV", P.mark("vector"))
        P.wait("sync", "cS", P.mark("scalar"))
        for mb in range(NB):
            P.dma("sync", t1_xn1[:, mb * TT:(mb + 1) * TT],
                  xn1T[mb * 128:(mb + 1) * 128, it * TT:(it + 1) * TT], "t1w")
        wv_ = P.now("t1w")
        P.wait("tensor", "t1w", wv_)
        P.wait("scalar", "t1qo", P.now("t1qo"))
        P.wait("vector", "t1qo", P.now("t1qo"))
        for mo in range(NB):
            pi = mo % 2
            pq = PS.acquire(pi, "tensor")
            for kb in range(NB):
                P.op("tensor", lambda e, kb=kb, mo=mo, pq=pq: e.matmul(
                    pq[:, 0:TT],
                    c["t1_wqkv"][:, kb * 3 * D + mo * 128:kb * 3 * D + (mo + 1) * 128],
                    t1_xn1[:, kb * TT:(kb + 1) * TT],
                    start=(kb == 0), stop=(kb == NB - 1)), sig=(kb == NB - 1))
            P.wait("scalar", "cT", P.mark("tensor"))
            P.wait("scalar", "cV", P.mark("vector"))
            u = t1_tmp[:, 0:TT]
            w2_ = t1_tmp[:, TT:2 * TT]
            P.op("scalar", lambda e, mo=mo, pq=pq, u=u: e.activation(
                u, pq[:, 0:TT], AF.Identity, bias=bqkv[:, mo:mo + 1]))
            PS.release(pi, "scalar")
            P.wait("vector", "cS", P.mark("scalar"))
            P.op("vector", lambda v, u=u, w2_=w2_: v.tensor_scalar_min(w2_, u, 0.0))
            P.wait("scalar", "cV", P.mark("vector"))
            P.op("scalar", lambda e, w2_=w2_: e.activation(w2_, w2_, AF.Exp))
            P.wait("vector", "cS", P.mark("scalar"))
            P.op("vector", lambda v, u=u: v.tensor_scalar_max(u, u, 0.0))
            P.op("vector", lambda v, mo=mo, u=u, w2_=w2_: v.tensor_add(
                t1_q[:, mo * TT:(mo + 1) * TT], u, w2_))
        P.wait("gpsimd", "cV", P.mark("vector"))
        for mo in range(NB):
            P.dma("gpsimd", q1T[mo * 128:(mo + 1) * 128, it * TT:(it + 1) * TT],
                  t1_q[:, mo * TT:(mo + 1) * TT], "t1qo", chain=False)
        for g in range(4):
            def lh(kb, g=g):
                return t1_xn1[:, kb * TT + g * 128:kb * TT + (g + 1) * 128]
            for cc in range(2):
                pi = cc
                pk = PS.acquire(pi, "tensor")
                for kb in range(NB):
                    P.op("tensor", lambda e, kb=kb, cc=cc, pk=pk, g=g: e.matmul(
                        pk[:, 0:TT], lh(kb, g),
                        c["t1_wqkv"][:, kb * 3 * D + D + cc * TT:
                                     kb * 3 * D + D + (cc + 1) * TT],
                        start=(kb == 0), stop=(kb == NB - 1)), sig=(kb == NB - 1))
                P.wait("vector", "cT", P.mark("tensor"))
                P.wait("vector", "cS", P.mark("scalar"))
                u = t1_tmp[:, 0:TT]
                w2_ = t1_tmp[:, TT:2 * TT]
                P.op("vector", lambda v, cc=cc, pk=pk, u=u: v.tensor_add(
                    u, pk[:, 0:TT], c["sb_bqkbc"][:, cc * TT:(cc + 1) * TT]))
                PS.release(pi, "vector")
                P.op("vector", lambda v, u=u, w2_=w2_: v.tensor_scalar_min(w2_, u, 0.0))
                P.wait("scalar", "cV", P.mark("vector"))
                P.op("scalar", lambda e, w2_=w2_: e.activation(w2_, w2_, AF.Exp))
                P.wait("vector", "cS", P.mark("scalar"))
                P.op("vector", lambda v, u=u: v.tensor_scalar_max(u, u, 0.0))
                P.op("vector", lambda v, cc=cc, u=u, w2_=w2_: v.tensor_tensor(
                    t1_kt[:, cc * TT:(cc + 1) * TT], u, w2_, ALU.add))
            for cc in range(2):
                pi = cc
                pv = PS.acquire(pi, "tensor")
                for kb in range(NB):
                    P.op("tensor", lambda e, kb=kb, cc=cc, pv=pv, g=g: e.matmul(
                        pv[:, 0:TT], lh(kb, g),
                        c["t1_wqkv"][:, kb * 3 * D + 2 * D + cc * TT:
                                     kb * 3 * D + 2 * D + (cc + 1) * TT],
                        start=(kb == 0), stop=(kb == NB - 1)), sig=(kb == NB - 1))
                P.wait("vector", "cT", P.mark("tensor"))
                P.op("vector", lambda v, cc=cc, pv=pv: v.tensor_add(
                    t1_vt[:, cc * TT:(cc + 1) * TT], pv[:, 0:TT],
                    c["sb_bqvbc"][:, cc * TT:(cc + 1) * TT]))
                PS.release(pi, "vector")
            kvv = P.mark("vector")
            if c["dev"] and "d_ktall" in c["dev_out"]:
                P.wait("gpsimd", "cV", kvv)
                P.dma("gpsimd", c["dev_out"]["d_ktall"][it * 4 + g, :, :],
                      c["t1_kt"][:, :], "dktg", chain=False)
                P.wait("vector", "dktg", P.now("dktg"))
            P.wait("tensor", "cV", kvv)
            pkv0 = PS.acquire(4, "tensor")
            pkv1 = PS.acquire(5, "tensor")
            for h in range(H):
                p_ = h // 2
                bh = (h % 2) * 64
                pp = pkv0 if p_ < 4 else pkv1
                co = (p_ % 4) * 65
                P.op("tensor", lambda e, h=h, bh=bh, pp=pp, co=co:
                     e.matmul(pp[bh:bh + 64, co:co + 64],
                              t1_kt[:, h * 64:(h + 1) * 64],
                              t1_vt[:, h * 64:(h + 1) * 64],
                              start=True, stop=True,
                              tile_position=(0, bh) if bh else None), sig=False)
                P.op("tensor", lambda e, h=h, bh=bh, pp=pp, co=co:
                     e.matmul(pp[bh:bh + 64, co + 64:co + 65],
                              t1_kt[:, h * 64:(h + 1) * 64],
                              c["sb_onescb"][:, 0:1],
                              start=True, stop=True,
                              tile_position=(0, bh) if bh else None),
                     sig=(h == H - 1))
            P.wait("vector", "cT", P.mark("tensor"))
            P.op("vector", lambda v, pkv0=pkv0: v.tensor_add(
                c["sb_kvloc"][:, 0:260], c["sb_kvloc"][:, 0:260], pkv0[:, 0:260]))
            P.op("vector", lambda v, pkv1=pkv1: v.tensor_add(
                c["sb_kvloc"][:, 260:520], c["sb_kvloc"][:, 260:520],
                pkv1[:, 0:260]))
            PS.release(4, "vector")
            PS.release(5, "vector")
    P.barrier()


def emit_t2(c):
    P = c["P"]
    if "noxc" in c["stages"]:
        kvl, kvb, ks = c["sb_kvloc"], c["sb_kvb"], c["sb_ks"]
        P.op("vector", lambda v: v.tensor_copy(kvb[:, :], kvl[:, :]))
        P.op("vector", lambda v: v.memset(ks[:, :], 0.0))
        for h in range(H):
            bh = (h % 2) * 64
            P.op("vector", lambda v, h=h, bh=bh: v.tensor_copy(
                ks[bh:bh + 64, (h // 2) * H + h:(h // 2) * H + h + 1],
                kvb[bh:bh + 64, (h // 2) * 65 + 64:(h // 2) * 65 + 65]))
        P.barrier()
        return
    nc, pid, partner = c["nc"], c["pid"], c["partner"]
    shkv, shfl = c["shkv"], c["shfl"]
    kvl, kvp, kvb, ks = c["sb_kvloc"], c["sb_kvp"], c["sb_kvb"], c["sb_ks"]
    sb_flag, sb_magic = c["sb_flag"], c["sb_magic"]
    P.dma("gpsimd", shkv[pid, :, :], kvl[:, :], "t2a", chain=False)
    P.wait("gpsimd", "t2a", P.now("t2a"))
    P.dma("gpsimd", shfl[pid, 1, :], sb_magic[0:1, 16:32], "t2a", chain=False)
    P.wait("gpsimd", "t2a", P.now("t2a"))

    def poll(g):
        val = g.alloc_register("pollv2")
        cnt = g.alloc_register("pollc2")
        vals = bass.RegisterHandles((val,))
        cnts = bass.RegisterHandles((cnt,))
        nc.regs_mov(vals, 0)
        nc.regs_mov(cnts, 16)
        loop, done = "t2_loop", "t2_done"
        nc.br(loop, engines=[ET.Pool])
        with nc.body(loop, valid_engines=[ET.Pool]):
            g.dma_start(out=sb_flag[:, :], in_=shfl[partner, 1, :][None, :]) \
                .then_inc(P.sems["t2p"], 16)
            g.wait_ge(P.sems["t2p"], cnt)
            nc.regs_alu(cnts, cnts, 16, op=ALU.add)
            nc.regs_load(vals, sb_flag[0:1, 0:1])
            nc.br_cmp(val, MAGIC2, on_true=done, on_false=loop,
                      comp_op="IS_EQ", engines=[ET.Pool])
        nc.switch_bb(done)
        return None
    P.sem("t2p")
    P.op("gpsimd", poll, sig=True, chain=False)
    P.dma("gpsimd", kvp[:, :], shkv[partner, :, :], "t2b", chain=False)
    P.wait("vector", "t2b", P.now("t2b"))
    P.op("vector", lambda v: v.tensor_add(kvl[:, :], kvl[:, :], kvp[:, :]))
    P.op("vector", lambda v: v.tensor_copy(kvb[:, :], kvl[:, :]))
    P.op("vector", lambda v: v.memset(ks[:, :], 0.0))
    for h in range(H):
        bh = (h % 2) * 64
        P.op("vector", lambda v, h=h, bh=bh: v.tensor_copy(
            ks[bh:bh + 64, (h // 2) * H + h:(h // 2) * H + h + 1],
            kvb[bh:bh + 64, (h // 2) * 65 + 64:(h // 2) * 65 + 65]))
    P.barrier()


def emit_t3a(c):
    P, PS = c["P"], c["PS"]
    t3_q, t3_xn1, t3_z, t3_ob = c["t3_q"], c["t3_xn1"], c["t3_z"], c["t3_ob"]
    t3_hf, t3_tmp = c["t3_hf"], c["t3_tmp"]
    kvb, ks, eps, vecs, scl, sel = (c["sb_kvb"], c["sb_ks"], c["sb_eps"],
                                    c["sb_vecs"], c["sb_scl"], c["sb_sel"])
    q1T, xn1T, s1Th, h1fT = c["q1T"], c["xn1T_t"], c["s1Th"], c["h1fT"]
    P.dma("sync", c["sb_sel"][:, :], c["sel16"][:, :], "wT3s")
    P.wait("tensor", "wT3s", P.now("wT3s"))
    for kb in range(NB):
        P.dma("gpsimd", c["t3_wproj"][:, kb * D:(kb + 1) * D],
              c["wproj"][kb * 128:(kb + 1) * 128, :], "wT3a", chain=False)
        P.dma("gpsimd", c["t3_d0"][:, kb * DFF:(kb + 1) * DFF],
              c["d0T"][kb * 128:(kb + 1) * 128, :], "wT3a", chain=False)
    P.wait("tensor", "wT3a", P.now("wT3a"))
    for it in range(NT):
        P.wait("sync", "cT", P.mark("tensor"))
        P.wait("sync", "cV", P.mark("vector"))
        for mb in range(NB):
            P.dma("sync", t3_q[:, mb * TT:(mb + 1) * TT],
                  q1T[mb * 128:(mb + 1) * 128, it * TT:(it + 1) * TT], "t3w")
            P.dma("sync", t3_xn1[:, mb * TT:(mb + 1) * TT],
                  xn1T[mb * 128:(mb + 1) * 128, it * TT:(it + 1) * TT], "t3w")
        wv_ = P.now("t3w")
        P.wait("tensor", "t3w", wv_)
        p6 = PS.acquire(6, "tensor")
        for kb in range(NB):
            P.op("tensor", lambda e, kb=kb, p6=p6: e.matmul(
                p6[0:16, 0:TT], ks[:, kb * H:(kb + 1) * H],
                t3_q[:, kb * TT:(kb + 1) * TT],
                start=(kb == 0), stop=(kb == NB - 1)), sig=(kb == NB - 1))
        P.wait("vector", "cT", P.mark("tensor"))
        P.op("vector", lambda v, p6=p6: v.tensor_scalar_add(
            c["t3_den"][0:16, 0:TT], p6[0:16, 0:TT], eps[0:16, 5:6]))
        PS.release(6, "vector")
        P.op("vector", lambda v: v.reciprocal(c["t3_den"][0:16, 0:TT],
                                              c["t3_den"][0:16, 0:TT]))
        zv = P.mark("vector")
        for pb in range(NB):
            pi = 4 + pb % 2
            po = PS.acquire(pi, "tensor")
            for sub in range(2):
                h = 2 * pb + sub
                bh = sub * 64
                P.op("tensor", lambda e, h=h, bh=bh, po=po, pb=pb: e.matmul(
                    po[bh:bh + 64, 0:TT],
                    kvb[bh:bh + 64, pb * 65:pb * 65 + 64],
                    t3_q[bh:bh + 64, pb * TT:(pb + 1) * TT],
                    start=True, stop=True,
                    tile_position=(bh, bh) if bh else None), sig=(sub == 1))
            P.wait("tensor", "cV", zv)
            p6b = PS.acquire(6, "tensor")
            P.op("tensor", lambda e, pb=pb, p6b=p6b: e.matmul(
                p6b[:, 0:TT], sel[0:16, pb * 128:(pb + 1) * 128],
                c["t3_den"][0:16, 0:TT], start=True, stop=True))
            P.wait("vector", "cT", P.mark("tensor"))
            P.op("vector", lambda v, p6b=p6b: v.tensor_copy(t3_z[:, 0:TT],
                                                            p6b[:, 0:TT]))
            PS.release(6, "vector")
            P.op("vector", lambda v, pb=pb, po=po: v.tensor_mul(
                t3_ob[:, pb * TT:(pb + 1) * TT], po[:, 0:TT], t3_z[:, 0:TT]))
            PS.release(pi, "vector")
        obv = P.mark("vector")
        P.wait("tensor", "cV", obv)
        P.wait("scalar", "t3so", P.now("t3so"))
        for mo in range(NB):
            pi = mo % 2
            pq = PS.acquire(pi, "tensor")
            for kb in range(NB):
                P.op("tensor", lambda e, kb=kb, mo=mo, pq=pq: e.matmul(
                    pq[:, 0:TT],
                    c["t3_wproj"][:, kb * D + mo * 128:kb * D + (mo + 1) * 128],
                    t3_ob[:, kb * TT:(kb + 1) * TT],
                    start=(kb == 0), stop=(kb == NB - 1)), sig=(kb == NB - 1))
            P.wait("scalar", "cT", P.mark("tensor"))
            P.op("scalar", lambda e, mo=mo, pq=pq: e.activation(
                t3_tmp[:, TT:2 * TT], pq[:, 0:TT], AF.Identity,
                bias=vecs[:, 72 + mo:73 + mo]))
            PS.release(pi, "scalar")
            P.wait("gpsimd", "cS", P.mark("scalar"))
            P.dma("gpsimd", s1Th[mo * 128:(mo + 1) * 128, it * TT:(it + 1) * TT],
                  t3_tmp[:, TT:2 * TT], "t3so", chain=False)
            P.wait("scalar", "t3so", P.now("t3so"))
        P.wait("vector", "t3ho", P.now("t3ho"))
        for mo in range(32):
            pi = 2 + mo % 2
            pq = PS.acquire(pi, "tensor")
            for kb in range(NB):
                P.op("tensor", lambda e, kb=kb, mo=mo, pq=pq: e.matmul(
                    pq[:, 0:TT],
                    c["t3_d0"][:, kb * DFF + mo * 128:kb * DFF + (mo + 1) * 128],
                    t3_xn1[:, kb * TT:(kb + 1) * TT],
                    start=(kb == 0), stop=(kb == NB - 1)), sig=(kb == NB - 1))
            P.wait("vector", "cT", P.mark("tensor"))
            P.wait("vector", "cS", P.mark("scalar"))
            P.op("vector", lambda v, pq=pq: v.tensor_scalar_mul(
                t3_tmp[:, 0:TT], pq[:, 0:TT], scl[:, 0:1]))
            PS.release(pi, "vector")
            P.wait("scalar", "cV", P.mark("vector"))
            P.op("scalar", lambda e, mo=mo: e.activation(
                t3_hf[:, mo * TT:(mo + 1) * TT], t3_tmp[:, 0:TT], AF.Relu,
                scale=4.0))
        P.wait("gpsimd", "cS", P.mark("scalar"))
        for mo in range(32):
            P.dma("gpsimd", h1fT[mo * 128:(mo + 1) * 128, it * TT:(it + 1) * TT],
                  t3_hf[:, mo * TT:(mo + 1) * TT], "t3ho", chain=False)
    P.barrier()


def emit_t3b(c):
    P, PS = c["P"], c["PS"]
    t3_h1, t3_h2o, scl = c["t3_h1"], c["t3_h2o"], c["sb_scl"]
    t3_tmp = c["t3b_tmp"]
    h1fT, h2fT, w1eT = c["h1fT"], c["h2fT"], c["w1eT"]
    w1s = c["t3_w1s"]
    NSUP = 4
    STT = 1024
    for st in range(NSUP):
        P.wait("sync", "cT", P.mark("tensor"))
        for ko in range(32):
            P.dma("sync", t3_h1[:, ko * STT:(ko + 1) * STT],
                  h1fT[ko * 128:(ko + 1) * 128, st * STT:(st + 1) * STT], "t3bw")
        hv = P.now("t3bw")
        P.wait("tensor", "t3bw", hv)
        for mog in range(8):
            bufbase = (mog % 2) * 32
            if mog >= 2 or st > 0:
                P.wait("sync", "cT", P.mark("tensor"))
            sname = f"t3bs{mog % 2}"
            for ko in range(32):
                P.dma("sync", w1s[bufbase + ko][:, :],
                      w1eT[ko * 128:(ko + 1) * 128,
                           mog * 512:(mog + 1) * 512], sname)
            sv = P.now(sname)
            P.wait("tensor", sname, sv)
            pqs = [PS.acquire(i, "tensor") for i in range(8)]
            for ko in range(32):
                for mb in range(4):
                    for nh in range(2):
                        P.op("tensor", lambda e, ko=ko, mb=mb, nh=nh, mog=mog,
                             pqs=pqs, bufbase=bufbase: e.matmul(
                                 pqs[mb * 2 + nh][:, 0:TT],
                                 w1s[bufbase + ko][:, mb * 128:(mb + 1) * 128],
                                 t3_h1[:, ko * STT + nh * TT:ko * STT + (nh + 1) * TT],
                                 start=(ko == 0), stop=(ko == 31)),
                             sig=(ko == 31 and mb == 3 and nh == 1))
            mmk = P.mark("tensor")
            P.wait("vector", "cT", mmk)
            P.wait("vector", "t3bo", P.now("t3bo"))
            P.wait("scalar", "t3bo", P.now("t3bo"))
            for mb in range(4):
                for nh in range(2):
                    P.wait("vector", "cS", P.mark("scalar"))
                    P.op("vector", lambda v, mb=mb, nh=nh, pqs=pqs: v.tensor_scalar_mul(
                        t3_tmp[:, 0:TT], pqs[mb * 2 + nh][:, 0:TT], scl[:, 1:2]))
                    P.wait("scalar", "cV", P.mark("vector"))
                    P.op("scalar", lambda e, mb=mb, nh=nh, mog=mog: e.activation(
                        t3_h2o[:, (mog % 2) * 4096 + mb * STT + nh * TT:
                               (mog % 2) * 4096 + mb * STT + (nh + 1) * TT],
                        t3_tmp[:, 0:TT], AF.Relu, scale=2.0))
            for i in range(8):
                PS.release(i, "scalar")
            P.wait("gpsimd", "cS", P.mark("scalar"))
            for mb in range(4):
                P.dma("gpsimd",
                      h2fT[mog * 512 + mb * 128:mog * 512 + (mb + 1) * 128,
                           st * STT:(st + 1) * STT],
                      t3_h2o[:, (mog % 2) * 4096 + mb * STT:
                             (mog % 2) * 4096 + (mb + 1) * STT], "t3bo",
                      chain=False)
    P.barrier()


def emit_t3c(c):
    P, PS = c["P"], c["PS"]
    t3_hr, t3_g1, t3_sl, t3_x1l, t3_out = (c["t3_hr"], c["t3_g1"], c["t3_sl"],
                                           c["t3_x1l"], c["t3_out"])
    scl = c["sb_scl"]
    h2fT, w2eT, s1Th, x1T, out_sl = (c["h2fT"], c["w2eT"], c["s1Th"], c["x1T"],
                                     c["out_sl"])
    for ko in range(32):
        P.dma("gpsimd", c["t3_w2"][:, ko * D:(ko + 1) * D],
              w2eT[ko * 128:(ko + 1) * 128, :], "wT3c", chain=False)
    P.wait("tensor", "wT3c", P.now("wT3c"))
    for it in range(NT):
        P.wait("sync", "cT", P.mark("tensor"))
        P.wait("sync", "cV", P.mark("vector"))
        for ko in range(32):
            P.dma("sync", t3_hr[:, ko * TT:(ko + 1) * TT],
                  h2fT[ko * 128:(ko + 1) * 128, it * TT:(it + 1) * TT], "t3cw")
        for mb in range(NB):
            P.dma("sync", t3_sl[:, mb * TT:(mb + 1) * TT],
                  s1Th[mb * 128:(mb + 1) * 128, it * TT:(it + 1) * TT], "t3cw")
            P.dma("sync", t3_x1l[:, mb * TT:(mb + 1) * TT],
                  x1T[mb * 128:(mb + 1) * 128, it * TT:(it + 1) * TT], "t3cw")
        wv_ = P.now("t3cw")
        P.wait("tensor", "t3cw", wv_)
        P.wait("vector", "t3cw", wv_)
        for mo in range(NB):
            pi = mo % 2
            pq = PS.acquire(pi, "tensor")
            for ko in range(32):
                P.op("tensor", lambda e, ko=ko, mo=mo, pq=pq: e.matmul(
                    pq[:, 0:TT],
                    c["t3_w2"][:, ko * D + mo * 128:ko * D + (mo + 1) * 128],
                    t3_hr[:, ko * TT:(ko + 1) * TT],
                    start=(ko == 0), stop=(ko == 31)), sig=(ko == 31))
            P.wait("vector", "cT", P.mark("tensor"))
            P.op("vector", lambda v, pq=pq: v.tensor_scalar_mul(
                t3_g1[:, 0:TT], pq[:, 0:TT], scl[:, 2:3]))
            PS.release(pi, "vector")
            P.wait("scalar", "cV", P.mark("vector"))
            P.op("scalar", lambda e: e.activation(
                t3_g1[:, TT:2 * TT], t3_g1[:, 0:TT], AF.Sigmoid))
            P.wait("vector", "cS", P.mark("scalar"))
            P.op("vector", lambda v: v.tensor_mul(
                t3_g1[:, 0:TT], t3_g1[:, 0:TT], t3_g1[:, TT:2 * TT]))
            P.op("vector", lambda v, mo=mo: v.tensor_mul(
                t3_g1[:, 0:TT], t3_g1[:, 0:TT], t3_sl[:, mo * TT:(mo + 1) * TT]))
            P.wait("vector", "t3out", P.now("t3out"))
            P.op("vector", lambda v, mo=mo: v.tensor_add(
                t3_out[:, mo * TT:(mo + 1) * TT], t3_x1l[:, mo * TT:(mo + 1) * TT],
                t3_g1[:, 0:TT]))
            P.wait("gpsimd", "cV", P.mark("vector"))
            P.dma("gpsimd", out_sl[mo * 128:(mo + 1) * 128, it * TT:(it + 1) * TT],
                  t3_out[:, mo * TT:(mo + 1) * TT], "t3out", chain=False)
    P.wait("gpsimd", "t3out", P.now("t3out"))
    P.barrier()


def emit_dev_tail(c, stages, dev, dev_out):
    if not dev:
        return
    P = c["P"]
    do = dev_out
    nch = c.get("S2_NWIN", NT) * 4
    if "s2" in stages:
        P.wait("sync", "s2sp0", P.now("s2sp0"))
        P.wait("sync", "s2sp1", P.now("s2sp1"))
        P.wait("sync", "cV", P.mark("vector"))
        P.dma("sync", do["d_stl"][:, :], c["sb_stl"][:, :], "dev")
        P.dma("sync", do["d_ab"][0:nch], c["ABb"][0:nch], "dev")
        P.dma("sync", do["d_wt"][0:nch], c["WTb"][0:nch], "dev")
        if "s4" not in stages:
            P.dma("sync", do["d_ot"][:, 0:nch * CH], c["OT"][:, 0:nch * CH], "dev")
    if "s4" in stages:
        P.wait("sync", "s4o", P.now("s4o"))
        P.dma("sync", do["d_ot2"][:, 0:nch * CH], c["OT"][:, 0:nch * CH], "dev")
    if "t1" in stages:
        P.wait("sync", "t1qo", P.now("t1qo"))
        P.wait("sync", "cV", P.mark("vector"))
        P.dma("sync", do["d_kv"][:, :], c["sb_kvloc"][:, :], "dev")
        P.dma("sync", do["d_q1"][:, :], c["q1T"][:, :], "dev")
        P.dma("sync", do["d_kt"][:, :], c["t1_kt"][:, :], "dev")
        P.dma("sync", do["d_vt"][:, :], c["t1_vt"][:, :], "dev")
    if "t3a" in stages:
        P.wait("sync", "t3so", P.now("t3so"))
        P.wait("sync", "t3ho", P.now("t3ho"))
        P.wait("sync", "cV", P.mark("vector"))
        P.dma("sync", do["d_s1"][:, :], c["s1Th"][:, :], "dev")
        P.dma("sync", do["d_h1f"][:, :], c["h1fT"][:, :], "dev")
        P.dma("sync", do["d_ob"][:, :], c["t3_ob"][:, :], "dev")
        P.dma("sync", do["d_z"][:, :], c["t3_z"][:, :], "dev")
        P.dma("sync", do["d_zr"][:, :], c["t3_den"][:, 0:TT], "dev")
        P.dma("sync", do["d_ks"][:, :], c["sb_ks"][:, :], "dev")
        P.dma("sync", do["d_d0"][:, :].rearrange("(kb p) c -> p kb c", p=128),
              c["t3_d0"][:, :].rearrange("p (kb c) -> p kb c", kb=NB), "dev")
    if "t3b" in stages:
        P.wait("sync", "t3bo", P.now("t3bo"))
        P.dma("sync", do["d_h2f"][:, :], c["h2fT"][:, :], "dev")
    P.wait("sync", "dev", P.now("dev"))
    if "s1" in stages:
        P.wait("sync", "s1qo", P.now("s1qo"))
        P.wait("sync", "s1ko", P.now("s1ko"))
        P.wait("sync", "s1so", P.now("s1so"))
        P.wait("sync", "cS", P.mark("scalar"))
        P.dma("sync", do["d_q"][:, :], c["qT"][:, :], "dev")
        P.dma("sync", do["d_k"][:, :], c["ktokN"][:, :], "dev")
        P.dma("sync", do["d_v"][:, :], c["vtok"][:, :], "dev")
        P.dma("sync", do["d_beta"][:, :], c["sb_beta"][:, :], "dev")
        P.dma("sync", do["d_stats"][0:3, 0:TC + 1], c["statsH"][0:3, 0:TC + 1], "dev")
        P.wait("sync", "dev", P.now("dev"))


def stage_inputs(inputs, core):
    b, hf = core // 2, core % 2
    x = np.asarray(inputs["L0_attn_x"])
    t0 = hf * TC
    sl = np.empty((TC + 1, D), np.float32)
    sl[1:] = x[b, t0:t0 + TC]
    sl[0] = 0.0 if hf == 0 else x[b, t0 - 1]
    m = {}
    m["x_sl"] = np.ascontiguousarray(sl.T)
    for nm, key in [("wq", "L0_attn_wq"), ("wk", "L0_attn_wk"), ("wv", "L0_attn_wv"),
                    ("wbeta", "L0_attn_wbeta"), ("wo", "L0_attn_wo"),
                    ("w1", "L0_attn_w1"), ("w2", "L0_attn_w2"),
                    ("wqkv", "L1_attn_w_qkv"), ("wproj", "L1_attn_w_proj")]:
        m[nm] = np.asarray(inputs[key], np.float32)
    for nm, key in [("wrT", "L0_ffn_W_r"), ("wkT", "L0_ffn_W_k"), ("wvT", "L0_ffn_W_v")]:
        m[nm] = np.ascontiguousarray(np.asarray(inputs[key], np.float32).T)
    m["k0u"] = np.asarray(inputs["L1_ffn_k0"]).view(np.uint8)
    m["k1u"] = np.asarray(inputs["L1_ffn_k1"]).view(np.uint8)
    m["k2u"] = np.asarray(inputs["L1_ffn_k2"]).view(np.uint8)

    def cols(vec, n):
        v = np.asarray(vec, np.float32).reshape(n, 128)
        return np.ascontiguousarray(v.T)
    vec_list = [cols(inputs["L0_attn_bq"], 8), cols(inputs["L0_attn_bk"], 8),
                cols(inputs["L0_attn_bv"], 8), cols(inputs["L0_attn_bo"], 8),
                cols(inputs["L0_attn_b2"], 8), cols(inputs["L0_attn_ln1_scale"], 8),
                cols(inputs["L0_attn_ln2_scale"], 8), cols(inputs["L0_ffn_mu_r"], 8),
                cols(inputs["L0_ffn_mu_k"], 8), cols(inputs["L1_attn_b_proj"], 8)]
    m["vecs"] = np.concatenate(vec_list, axis=1)
    m["b1v"] = cols(inputs["L0_attn_b1"], 32)
    m["bqkvv"] = cols(inputs["L1_attn_b_qkv"], 24)
    scl = np.zeros((128, 4), np.float32)
    scl[:, 0] = float(np.asarray(inputs["L1_ffn_s0"]))
    scl[:, 1] = float(np.asarray(inputs["L1_ffn_s1"]))
    scl[:, 2] = float(np.asarray(inputs["L1_ffn_s2"]))
    scl[:, 3] = float(hf)
    m["sclv"] = scl
    m["bk_bc"] = np.broadcast_to(np.asarray(inputs["L0_attn_bk"], np.float32),
                                 (128, D)).copy()
    m["bv_bc"] = np.broadcast_to(np.asarray(inputs["L0_attn_bv"], np.float32),
                                 (128, D)).copy()
    bq1 = np.asarray(inputs["L1_attn_b_qkv"], np.float32)
    m["bqkv_k_bc"] = np.broadcast_to(bq1[D:2 * D], (128, D)).copy()
    m["bqkv_v_bc"] = np.broadcast_to(bq1[2 * D:], (128, D)).copy()
    m["bbeta_bc"] = np.broadcast_to(np.asarray(inputs["L0_attn_bbeta"], np.float32),
                                    (128, H)).copy()
    i64 = np.zeros((128, 64), np.float32)
    i64[:64] = np.eye(64)
    i64[64:] = np.eye(64)
    m["i64p"] = i64
    m["maskln"] = -np.tril(np.ones((128, CH), np.float32), -1)
    m["masku"] = np.triu(np.ones((128, CH), np.float32), 0)
    m["identb"] = np.eye(128, dtype=NPBF16)
    m["onesbc"] = np.ones((1, 128), NPBF16)
    m["onesbf"] = np.ones((1, 128), np.float32)
    m["onescb"] = np.ones((128, 1), NPBF16)
    m["onescf"] = np.ones((128, 1), np.float32)
    sel = np.zeros((16, 8 * 128), np.float32)
    for pb in range(8):
        for sub in range(2):
            sel[2 * pb + sub, pb * 128 + sub * 64:pb * 128 + sub * 64 + 64] = 1.0
    m["sel16"] = sel
    return m


_BUILD_CACHE = {}


def kernel(**inputs):
    key = "full"
    stages = ["l0", "l1"]
    nc, outs = build(stages)
    in_maps = [stage_inputs(inputs, cc) for cc in range(8)]
    res = run_bass_kernel_spmd(nc, in_maps, core_ids=list(range(8)))
    out = np.empty((4, 2 * TC, D), np.float32)
    for cc in range(8):
        b, hf = cc // 2, cc % 2
        out[b, hf * TC:(hf + 1) * TC] = res.results[cc]["out_sl"].T
    return out


# version 04 (final)
# speedup vs baseline: 1.2393x; 1.0352x over previous
import numpy as np
import ml_dtypes
import concourse.bass as bass
import concourse.mybir as mybir
from concourse.bass_utils import run_bass_kernel_spmd

F32 = mybir.dt.float32
BF16 = mybir.dt.bfloat16
U8 = mybir.dt.uint8
U32 = mybir.dt.uint32
AF = mybir.ActivationFunctionType
ALU = mybir.AluOpType
AX = mybir.AxisListType.X
ET = mybir.EngineType
NPBF16 = ml_dtypes.bfloat16

TC = 4096
D = 1024
H = 16
HD = 64
NB = 8
TT = 512
NT = TC // TT
CH = 128
NCH = TC // CH
DFF = 4096
MAGIC1 = 0x5A17C0DE
MAGIC2 = 0x7E57B1D5
SB_BYTES = 224 * 1024
DETECT_RACES = False

ENGS = ["tensor", "vector", "scalar", "sync", "gpsimd"]
ECTR = {"tensor": "cT", "vector": "cV", "scalar": "cS", "sync": "cY", "gpsimd": "cG"}


class Prog:

    def __init__(self, nc):
        self.nc = nc
        self.items = {e: [] for e in ENGS}
        self.sems = {}
        self.cnt = {}
        self.lastwait = {e: {} for e in ENGS}

    def sem(self, name):
        if name not in self.sems:
            self.sems[name] = self.nc.alloc_semaphore(name)
            self.cnt[name] = 0
        return self.sems[name]

    def _waits(self, eng, waits):
        out = []
        for s, v in waits:
            self.sem(s)
            if v <= 0:
                continue
            if self.lastwait[eng].get(s, -1) >= v:
                continue
            self.lastwait[eng][s] = v
            out.append((s, v))
        return out

    def op(self, eng, fn, wait=None, sig=True, chain=None):
        ctr = ECTR[eng]
        self.sem(ctr)
        if chain is None:
            chain = eng in ("vector", "scalar", "gpsimd")
        waits = list(wait or [])
        if chain and self.cnt[ctr] > 0:
            waits.append((ctr, self.cnt[ctr]))
        waits = self._waits(eng, waits)
        if sig:
            self.cnt[ctr] += 1
        val = self.cnt[ctr]

        def closure(e, waits=waits, fn=fn, sig=sig, ctr=ctr):
            for s, v in waits:
                e.wait_ge(self.sems[s], v)
            ins = fn(e)
            if sig:
                if ins is None:
                    ins = e.nop()
                ins.then_inc(self.sems[ctr], 1)
            return ins
        self.items[eng].append(closure)
        return val

    def mark(self, eng):
        self.sem(ECTR[eng])
        return self.cnt[ECTR[eng]]

    def ctr(self, eng):
        return ECTR[eng]

    def now(self, semname):
        self.sem(semname)
        return self.cnt[semname]

    def dma(self, eng, out, in_, semname, wait=None, chain=None, **kw):
        self.sem(semname)
        ctr = ECTR[eng]
        if chain is None:
            chain = eng in ("vector", "scalar", "gpsimd")
        waits = list(wait or [])
        if chain and self.cnt[ctr] > 0:
            waits.append((ctr, self.cnt[ctr]))
        waits = self._waits(eng, waits)
        self.cnt[semname] += 16
        def closure(e, waits=waits, out=out, in_=in_, kw=kw, s=semname):
            for sn, v in waits:
                e.wait_ge(self.sems[sn], v)
            return e.dma_start(out=out, in_=in_, **kw).then_inc(self.sems[s], 16)
        self.items[eng].append(closure)
        return self.cnt[semname]

    def wait(self, eng, semname, value=None):
        if value is None:
            value = self.now(semname)
        ws = self._waits(eng, [(semname, value)])
        if ws:
            def closure(e, ws=ws):
                for s, v in ws:
                    e.wait_ge(self.sems[s], v)
                return None
            self.items[eng].append(closure)

    def barrier(self, tag=""):
        marks = {e: self.mark(e) for e in ENGS}
        ctrs = set(ECTR.values())
        dmas = [(s, v) for s, v in self.cnt.items() if s not in ctrs and v > 0]
        for e in ENGS:
            for e2 in ENGS:
                if e2 != e:
                    self.wait(e, ECTR[e2], marks[e2])
            for s, v in dmas:
                self.wait(e, s, v)
        for e in ENGS:
            self.op(e, lambda x: x.nop(), chain=(e in ("vector", "scalar", "gpsimd")))

    def replay(self):
        with self.nc.Block() as blk:
            for e in ENGS:
                getattr(blk, e)(lambda eng, e=e: [f(eng) for f in self.items[e]])


class PsumPool:
    def __init__(self, prog, tiles):
        self.p = prog
        self.tiles = tiles
        self.state = [None] * len(tiles)

    def acquire(self, i, eng):
        if self.state[i] is not None:
            s, v = self.state[i]
            self.p.wait(eng, s, v)
        return self.tiles[i]

    def release(self, i, eng):
        self.state[i] = (ECTR[eng], self.p.mark(eng))


def rowap(t, a, b):
    return t[0:1, a:b]


def build(stages, dev=False, s2_nwin=None):
    nc = bass.Bass("TRN2", target_bir_lowering=False,
                   detect_race_conditions=DETECT_RACES)
    P = Prog(nc)

    def inp(name, shape, dt=F32):
        return nc.dram_tensor(name, shape, dt, kind="ExternalInput")

    x_sl = inp("x_sl", [D, TC + 1])
    wq = inp("wq", [D, D]); wk = inp("wk", [D, D]); wv = inp("wv", [D, D])
    wbeta = inp("wbeta", [D, H])
    wo = inp("wo", [D, D]); w1 = inp("w1", [D, DFF]); w2 = inp("w2", [DFF, D])
    wrT = inp("wrT", [D, D]); wkT = inp("wkT", [D, D]); wvT = inp("wvT", [D, D])
    wqkv = inp("wqkv", [D, 3 * D]); wproj = inp("wproj", [D, D])
    k0u = inp("k0u", [2 * D, DFF], U8)
    k1u = inp("k1u", [DFF, DFF], U8)
    k2u = inp("k2u", [DFF, D], U8)
    vecs = inp("vecs", [128, 80])
    b1v = inp("b1v", [128, 32])
    bqkvv = inp("bqkvv", [128, 24])
    sclv = inp("sclv", [128, 4])
    bk_bc = inp("bk_bc", [128, D]); bv_bc = inp("bv_bc", [128, D])
    bqkv_k_bc = inp("bqkv_k_bc", [128, D]); bqkv_v_bc = inp("bqkv_v_bc", [128, D])
    bbeta_bc = inp("bbeta_bc", [128, H])
    i64p = inp("i64p", [128, 64])
    maskln = inp("maskln", [128, CH])
    masku = inp("masku", [128, CH])
    identb = inp("identb", [128, 128], BF16)
    onesbc = inp("onesbc", [1, 128], BF16)
    onesbf = inp("onesbf", [1, 128])
    onescb = inp("onescb", [128, 1], BF16)
    onescf = inp("onescf", [128, 1])
    sel16 = inp("sel16", [16, 8 * 128])

    outs = []

    def outp(name, shape, dt):
        outs.append(name)
        return nc.dram_tensor(name, shape, dt, kind="ExternalOutput")

    qT = nc.dram_tensor("qT", [D, TC], BF16)
    ktokN = nc.dram_tensor("ktokN", [TC, D], BF16)
    vtok = nc.dram_tensor("vtok", [TC, D], BF16)
    OT = nc.dram_tensor("OTb", [D, TC], BF16)
    ABb = nc.dram_tensor("ABb", [NCH, 8, 128, 128], BF16)
    WTb = nc.dram_tensor("WTb", [NCH, 8, 128, CH], BF16)
    statsH = nc.dram_tensor("statsH", [4, TC + 8], F32)
    stats1H = nc.dram_tensor("stats1H", [2, TC + 8], F32)
    sTh = nc.dram_tensor("sTh", [D, TC], F32)
    xaTh = nc.dram_tensor("xaTh", [D, TC], F32)
    h2Th = nc.dram_tensor("h2Th", [D, TC], BF16)
    shst = nc.dram_tensor("shst", [8, 128, 512], BF16, addr_space="Shared")
    shkv = nc.dram_tensor("shkv", [8, 128, 8 * 65], F32, addr_space="Shared")
    shfl = nc.dram_tensor("shfl", [8, 2, 16], U32, addr_space="Shared")
    q1T = nc.dram_tensor("q1T", [D, TC], BF16)
    s1Th = nc.dram_tensor("s1Th", [D, TC], F32)
    h1fT = nc.dram_tensor("h1fT", [DFF, TC], BF16)
    h2fT = nc.dram_tensor("h2fT", [DFF, TC], BF16)
    d0T = nc.dram_tensor("d0T", [D, DFF], BF16)
    w1eT = nc.dram_tensor("w1eT", [DFF, DFF], BF16)
    w2eT = nc.dram_tensor("w2eT", [DFF, D], BF16)

    full = "l1" in stages or "t3c" in stages
    if "l1" in stages and "l0" not in stages:
        x1T = inp("x1T_in", [D, TC])
        xn1T_t = inp("xn1T_in", [D, TC], BF16)
    elif dev and ("s6" in stages or "l0" in stages) and not full:
        x1T = outp("x1T", [D, TC], F32)
        xn1T_t = outp("xn1T", [D, TC], BF16)
    else:
        x1T = nc.dram_tensor("x1T", [D, TC], F32)
        xn1T_t = nc.dram_tensor("xn1T", [D, TC], BF16)
    out_sl = outp("out_sl", [D, TC], F32) if full else None

    dev_out = {}
    if dev:
        for nm in stages:
            if nm == "s1":
                dev_out["d_q"] = outp("d_q", [D, TC], BF16)
                dev_out["d_k"] = outp("d_k", [TC, D], BF16)
                dev_out["d_v"] = outp("d_v", [TC, D], BF16)
                dev_out["d_beta"] = outp("d_beta", [128, 32 * H], F32)
                dev_out["d_stats"] = outp("d_stats", [4, TC + 8], F32)
            if nm == "s2":
                dev_out["d_stl"] = outp("d_stl", [128, 8 * HD], BF16)
                dev_out["d_ot"] = outp("d_ot", [D, TC], BF16)
                dev_out["d_ab"] = outp("d_ab", [NCH, 8, 128, 128], BF16)
                dev_out["d_wt"] = outp("d_wt", [NCH, 8, 128, CH], BF16)
            if nm == "s4":
                dev_out["d_ot2"] = outp("d_ot2", [D, TC], BF16)
            if nm == "s5":
                dev_out["d_s"] = outp("d_s", [D, TC], F32)
                dev_out["d_xa"] = outp("d_xa", [D, TC], F32)
                dev_out["d_h2"] = outp("d_h2", [D, TC], BF16)

            if nm == "t1":
                dev_out["d_kv"] = outp("d_kv", [128, 8 * 65], F32)
                dev_out["d_q1"] = outp("d_q1", [D, TC], BF16)
                dev_out["d_kt"] = outp("d_kt", [128, D], BF16)
                dev_out["d_vt"] = outp("d_vt", [128, D], BF16)
                dev_out["d_ktall"] = outp("d_ktall", [32, 128, D], BF16)
            if nm == "t3a":
                dev_out["d_s1"] = outp("d_s1", [D, TC], F32)
                dev_out["d_h1f"] = outp("d_h1f", [DFF, TC], BF16)
                dev_out["d_ob"] = outp("d_ob", [128, NB * TT], BF16)
                dev_out["d_z"] = outp("d_z", [128, TT], F32)
                dev_out["d_zr"] = outp("d_zr", [128, TT], F32)
                dev_out["d_ks"] = outp("d_ks", [128, 128], BF16)
                dev_out["d_d0"] = outp("d_d0", [D, DFF], BF16)
            if nm == "t3b":
                dev_out["d_h2f"] = outp("d_h2f", [DFF, TC], BF16)

    pid0 = nc.partition_id(engines=[ET.SP, ET.Pool])
    pid = nc.snap(pid0, engines=[ET.SP, ET.Pool], min_val=0, max_val=7)
    partner = (pid // 2) * 2 + (1 - (pid % 2))

    per = Arena = None
    off = [20480]

    def salloc(name, shape, dt, at=None):
        if at is not None:
            off[0] = at
        t = nc.alloc_sbuf_tensor_at(name, list(shape), dt, offset=off[0])
        per_part = int(np.prod(shape[1:])) * mybir.dt.size(dt)
        off[0] += (per_part + 31) // 32 * 32
        assert off[0] <= SB_BYTES, f"SBUF overflow at {name}: {off[0]}"
        return t

    sb_vecs = salloc("sb_vecs", [128, 80], F32)
    sb_b1 = salloc("sb_b1", [128, 32], F32)
    sb_bqkv = salloc("sb_bqkv", [128, 24], F32)
    sb_scl = salloc("sb_scl", [128, 4], F32)
    sb_bkbc = salloc("sb_bkbc", [128, D], F32)
    sb_bvbc = salloc("sb_bvbc", [128, D], F32)
    sb_bqkbc = sb_bkbc
    sb_bqvbc = sb_bvbc
    sb_bbeta = salloc("sb_bbeta", [128, H], F32)
    sb_i64 = salloc("sb_i64", [128, 64], F32)
    sb_mln = salloc("sb_mln", [128, CH], F32)
    sb_mu = salloc("sb_mu", [128, CH], F32)
    sb_idb = salloc("sb_idb", [128, 128], BF16)
    sb_ones = salloc("sb_ones", [1, 128], BF16)
    sb_onesf = salloc("sb_onesf", [1, 128], F32)
    sb_onescb = salloc("sb_onescb", [128, 1], BF16)
    sb_onescf = salloc("sb_onescf", [128, 1], F32)
    sb_beta = salloc("sb_beta", [128, 32 * H], F32)
    sb_stl = salloc("sb_stl", [128, 8 * HD], BF16)
    sb_stc = salloc("sb_stc", [128, 8 * HD], BF16)
    sb_rows = salloc("sb_rows", [1, 2048], F32)
    sb_eps = salloc("sb_eps", [128, 8], F32)
    sb_flag = salloc("sb_flag", [1, 16], U32)
    sb_magic = salloc("sb_magic", [1, 32], U32)
    sb_lastc = salloc("sb_lastc", [128, 16], F32)
    sb_zb = salloc("sb_zb", [128, 1], BF16)
    sb_kvloc = salloc("sb_kvloc", [128, 8 * 65], F32)
    sb_kvb = salloc("sb_kvb", [128, 8 * 65], BF16)
    sb_ks = salloc("sb_ks", [128, NB * H], BF16)
    PERSIST = off[0]

    s1_wq = salloc("s1_wq", [128, NB * D], BF16, at=PERSIST)
    s1_wk = salloc("s1_wk", [128, NB * D], BF16)
    s1_wv = salloc("s1_wv", [128, NB * D], BF16)
    s1_wbeta = salloc("s1_wbeta", [128, NB * H], BF16)
    W_S1_END = off[0]
    s5_wo = salloc("s5_wo", [128, NB * D], BF16, at=PERSIST)
    W_S5A_END = off[0]
    s5_w1 = salloc("s5_w1", [128, NB * DFF], BF16, at=PERSIST)
    s5_w2 = salloc("s5_w2", [128, 32 * D], BF16)
    W_S5B_END = off[0]
    s6_wr = salloc("s6_wr", [128, NB * D], BF16, at=PERSIST)
    s6_wkk = salloc("s6_wkk", [128, NB * D], BF16)
    s6_wvv = salloc("s6_wvv", [128, NB * D], BF16)
    W_S6_END = off[0]
    t1_wqkv = salloc("t1_wqkv", [128, NB * 3 * D], BF16, at=PERSIST)
    W_T1_END = off[0]
    t3_wproj = salloc("t3_wproj", [128, NB * D], BF16, at=PERSIST)
    t3_d0 = salloc("t3_d0", [128, NB * DFF], BF16)
    W_T3A_END = off[0]
    t3_w1s = [salloc(f"t3_w1s{i}", [128, 4 * 128], BF16, at=(PERSIST if i == 0 else None))
              for i in range(64)]
    W_T3B_END = off[0]
    t3_w2 = salloc("t3_w2", [128, 32 * D], BF16, at=PERSIST)
    W_T3C_END = off[0]
    WMAX = max(W_S1_END, W_S5A_END, W_S5B_END, W_S6_END, W_T1_END,
               W_T3A_END, W_T3B_END, W_T3C_END)

    def aalloc(name, shape, dt, at=None):
        return salloc(name, shape, dt, at=at)

    s1_x = aalloc("s1_x", [128, NB * TT], F32, at=W_S1_END)
    s1_h1 = aalloc("s1_h1", [128, NB * TT], BF16)
    s1_xsq = aalloc("s1_xsq", [128, NB * TT], BF16)
    s1_kt = aalloc("s1_kt", [128, D], F32)
    s1_ksq = aalloc("s1_ksq", [128, D], BF16)
    s1_rn = aalloc("s1_rn", [128, 4 * H], F32)
    s1_st = aalloc("s1_st", [128, NB * TT], BF16)
    s1_kst = aalloc("s1_kst", [128, 2 * D], BF16)
    s1_bc = aalloc("s1_bc", [128, 2 * TT], F32)
    S1_END = off[0]
    s2_qw = aalloc("s2_qw", [128, NB * TT], BF16, at=PERSIST)
    s2_kw = aalloc("s2_kw", [128, 4 * D], BF16)
    s2_vw = aalloc("s2_vw", [128, 4 * D], BF16)
    s2_kf = aalloc("s2_kf", [128, NB * CH], BF16)
    s2_m = [aalloc(f"s2_m{i}", [128, CH], BF16) for i in range(2)]
    s2_mt = [aalloc(f"s2_mt{i}", [128, CH], BF16) for i in range(2)]
    s2_m2 = [aalloc(f"s2_m2{i}", [128, CH], BF16) for i in range(2)]
    s2_mt2 = [aalloc(f"s2_mt2{i}", [128, CH], BF16) for i in range(2)]
    s2_sacc = [aalloc(f"s2_sacc{i}", [128, CH], BF16) for i in range(2)]
    s2_mo = [aalloc(f"s2_mo{i}", [128, CH], BF16) for i in range(2)]
    s2_ab = aalloc("s2_ab", [128, 8 * 128], BF16)
    s2_wts = aalloc("s2_wts", [128, 8 * CH], BF16)
    s2_obs = aalloc("s2_obs", [128, NB * CH], BF16)
    S2_END = off[0]
    s4_ab = aalloc("s4_ab", [128, 8 * 128], BF16, at=PERSIST)
    s4_wt = aalloc("s4_wt", [128, 8 * CH], BF16)
    s4_o = aalloc("s4_o", [128, NB * CH], BF16)
    S4_END = off[0]
    s5_ot = aalloc("s5_ot", [128, NB * TT], BF16, at=W_S5A_END)
    s5_att = aalloc("s5_att", [128, NB * TT], F32)
    s5_x = aalloc("s5_x", [128, NB * TT], F32)
    s5_xa = aalloc("s5_xa", [128, NB * TT], F32)
    s5_sq = aalloc("s5_sq", [128, NB * TT], BF16)
    s5_h2 = aalloc("s5_h2", [128, NB * TT], BF16)
    s5_bc = aalloc("s5_bc", [128, 3 * TT], F32)
    S5A_END = off[0]
    s5_h2b = aalloc("s5_h2b", [128, NB * TT], BF16, at=W_S5B_END)
    s5_hf = aalloc("s5_hf", [128, 32 * TT], BF16)
    s5_gt = aalloc("s5_gt", [128, 2 * TT], F32)
    s5_xab = aalloc("s5_xab", [128, TT], F32)
    s5_s = aalloc("s5_s", [128, TT], F32)
    S5B_END = off[0]
    s6_x = aalloc("s6_x", [128, NB * TT], F32, at=W_S6_END)
    s6_xn = aalloc("s6_xn", [128, NB * (TT + 8)], BF16)
    s6_xr = aalloc("s6_xr", [128, NB * TT], BF16)
    s6_xk = aalloc("s6_xk", [128, NB * TT], BF16)
    s6_r = aalloc("s6_r", [128, NB * TT], F32)
    s6_kk = aalloc("s6_kk", [128, NB * TT], BF16)
    s6_sld = aalloc("s6_sld", [128, NB * TT], F32)
    s6_g = aalloc("s6_g", [128, 2 * TT], F32)
    s6_x1 = aalloc("s6_x1", [128, NB * TT], F32)
    s6_sq = aalloc("s6_sq", [128, NB * TT], BF16)
    s6_bc = aalloc("s6_bc", [128, 3 * TT], F32)
    s6_xn1 = aalloc("s6_xn1", [128, NB * TT], BF16)
    S6_END = off[0]
    t1_xn1 = aalloc("t1_xn1", [128, NB * TT], BF16, at=W_T1_END)
    t1_q = aalloc("t1_q", [128, NB * TT], BF16)
    t1_kt = aalloc("t1_kt", [128, D], BF16)
    t1_vt = aalloc("t1_vt", [128, D], BF16)
    t1_tmp = aalloc("t1_tmp", [128, 2 * TT], F32)
    sb_kvp = aalloc("sb_kvp", [128, 8 * 65], F32)
    T1_END = off[0]
    t3_q = aalloc("t3_q", [128, NB * TT], BF16, at=W_T3A_END)
    t3_xn1 = aalloc("t3_xn1", [128, NB * TT], BF16)
    t3_z = aalloc("t3_z", [128, TT], F32)
    t3_ob = aalloc("t3_ob", [128, NB * TT], BF16)
    t3_hf = aalloc("t3_hf", [128, 32 * TT], BF16)
    t3_tmp = aalloc("t3_tmp", [128, 2 * TT], F32)
    t3_den = aalloc("t3_den", [128, TT], F32)
    sb_sel = aalloc("sb_sel", [16, 8 * 128], F32)
    T3A_END = off[0]
    t3_h1 = aalloc("t3_h1", [128, 32 * 1024], BF16, at=W_T3B_END)
    t3_h2o = aalloc("t3_h2o", [128, 8 * 1024], BF16)
    t3b_tmp = aalloc("t3b_tmp", [128, TT], F32)
    T3B_END = off[0]
    t3_hr = aalloc("t3_hr", [128, 32 * TT], BF16, at=W_T3C_END)
    t3_g1 = aalloc("t3_g1", [128, 2 * TT], F32)
    t3_sl = aalloc("t3_sl", [128, NB * TT], F32)
    t3_x1l = aalloc("t3_x1l", [128, NB * TT], F32)
    t3_out = aalloc("t3_out", [128, NB * TT], F32)
    T3C_END = off[0]
    cv_a = aalloc("cv_a", [128, DFF], U8, at=PERSIST)
    cv_b = aalloc("cv_b", [128, DFF], U8)
    cv_o = aalloc("cv_o", [128, DFF], BF16)
    CV_END = off[0]
    for nm, v in [("S1", S1_END), ("S2", S2_END), ("S4", S4_END), ("S5A", S5A_END),
                  ("S5B", S5B_END), ("S6", S6_END), ("T1", T1_END), ("T3A", T3A_END),
                  ("T3B", T3B_END), ("T3C", T3C_END), ("CV", CV_END)]:
        assert v <= SB_BYTES, f"{nm} overflows SBUF: {v}"

    pbt = [nc.alloc_psum_tensor(f"pb{i}", [128, 512], F32) for i in range(8)]
    PS = PsumPool(P, pbt)
    PT = PS

    ctx = dict(locals())
    if s2_nwin is not None:
        ctx["S2_NWIN"] = s2_nwin

    emit_t0(ctx)
    if "s1" in stages or "l0" in stages:
        emit_s1(ctx)
    if "s2" in stages or "l0" in stages:
        emit_s2(ctx)
        if "nos34" not in stages:
            emit_s3(ctx)
            emit_s4(ctx)
    if "s5" in stages or "l0" in stages:
        emit_s5(ctx)
    if "s6" in stages or "l0" in stages:
        emit_s6(ctx)
    if "t1" in stages or "l1" in stages:
        emit_cv(ctx)
        emit_t1(ctx)
        emit_t2(ctx)
    if "t3a" in stages or "l1" in stages:
        emit_t3a(ctx)
    if "t3b" in stages or "l1" in stages:
        emit_t3b(ctx)
    if "t3c" in stages or "l1" in stages:
        emit_t3c(ctx)
    emit_dev_tail(ctx, stages, dev, dev_out)

    P.replay()
    return nc, outs


def emit_t0(c):
    P = c["P"]
    pairs = [(c["sb_vecs"], c["vecs"]), (c["sb_b1"], c["b1v"]),
             (c["sb_bqkv"], c["bqkvv"]), (c["sb_scl"], c["sclv"]),
             (c["sb_bkbc"], c["bk_bc"]), (c["sb_bvbc"], c["bv_bc"]),
             (c["sb_bbeta"], c["bbeta_bc"]), (c["sb_i64"], c["i64p"]),
             (c["sb_mln"], c["maskln"]), (c["sb_mu"], c["masku"]),
             (c["sb_idb"], c["identb"])]
    for dst, src_ in pairs:
        P.dma("sync", dst[:, :], src_[:, :], "cst")
    for dst, src_ in [(c["sb_ones"], c["onesbc"]), (c["sb_onesf"], c["onesbf"])]:
        P.dma("sync", dst[0:1, :], src_[0:1, :], "cst")
    for dst, src_ in [(c["sb_onescb"], c["onescb"]), (c["sb_onescf"], c["onescf"])]:
        P.dma("sync", dst[:, 0:1], src_[:, 0:1], "cst")
    eps = c["sb_eps"]
    for i, val in enumerate([1e-5, 1e-6, -0.5, 1.0 / D, -1.0, 1e-8, 0.0, 1.0]):
        P.op("vector", lambda v, i=i, val=val: v.memset(eps[:, i:i + 1], val))
    P.op("vector", lambda v: v.memset(c["sb_stl"][:, :], 0.0))
    P.op("vector", lambda v: v.memset(c["sb_lastc"][:, :], 0.0))
    P.op("vector", lambda v: v.memset(c["sb_zb"][:, :], 0.0))
    P.op("gpsimd", lambda g: g.memset(c["sb_magic"][0:1, 0:16], MAGIC1))
    P.op("gpsimd", lambda g: g.memset(c["sb_magic"][0:1, 16:32], MAGIC2))


def emit_s1(c):
    P, PS = c["P"], c["PS"]
    x_sl, statsH = c["x_sl"], c["statsH"]
    s1_x, s1_h1, s1_xsq = c["s1_x"], c["s1_h1"], c["s1_xsq"]
    s1_kt, s1_ksq, s1_rn, s1_st, s1_bc = (c["s1_kt"], c["s1_ksq"], c["s1_rn"],
                                          c["s1_st"], c["s1_bc"])
    s1_kst = c["s1_kst"]
    rows, eps, vecs = c["sb_rows"], c["sb_eps"], c["sb_vecs"]
    onescf, onescb, onesf = c["sb_onescf"], c["sb_onescb"], c["sb_onesf"]
    for w_h, w_s in [(c["wq"], c["s1_wq"]), (c["wk"], c["s1_wk"]),
                     (c["wv"], c["s1_wv"]), (c["wbeta"], c["s1_wbeta"])]:
        n = w_h.shape[1]
        for kb in range(NB):
            P.dma("gpsimd", w_s[:, kb * n:(kb + 1) * n],
                  w_h[kb * 128:(kb + 1) * 128, :], "wS1", chain=False)
    wS1 = P.now("wS1")

    prev_free = None
    for it in range(-1, NT):
        w = 1 if it < 0 else TT
        c0 = 0 if it < 0 else 1 + it * TT
        if prev_free is not None:
            P.wait("sync", prev_free[0], prev_free[1])
        for kb in range(NB):
            P.dma("sync", s1_x[:, kb * TT:kb * TT + w],
                  x_sl[kb * 128:(kb + 1) * 128, c0:c0 + w], "s1x",
                  **({"allow_slow_non_contiguous": True} if w == 1 else {}))
        xv = P.now("s1x")
        P.wait("scalar", "s1x", xv)
        for kb in range(NB):
            P.op("scalar", lambda e, kb=kb, w=w: e.activation(
                s1_xsq[:, kb * TT:kb * TT + w], s1_x[:, kb * TT:kb * TT + w],
                AF.Square))
        sqm = P.mark("scalar")
        P.wait("tensor", "cS", sqm)
        if it == -1:
            P.wait("tensor", "cst", P.now("cst"))
        p0 = PS.acquire(0, "tensor")
        p1 = PS.acquire(1, "tensor")
        for kb in range(NB):
            P.op("tensor", lambda e, kb=kb, w=w, p0=p0: e.matmul(
                p0[0:1, 0:w], onescf[:, 0:1],
                s1_x[:, kb * TT:kb * TT + w], start=(kb == 0), stop=(kb == NB - 1)),
                sig=(kb == NB - 1))
        for kb in range(NB):
            P.op("tensor", lambda e, kb=kb, w=w, p1=p1: e.matmul(
                p1[0:1, 0:w], onescb[:, 0:1],
                s1_xsq[:, kb * TT:kb * TT + w], start=(kb == 0), stop=(kb == NB - 1)),
                sig=(kb == NB - 1))
        mm1 = P.mark("tensor")
        r_mean = rowap(rows, 0, w)
        r_comb = rowap(rows, 512, 512 + w)
        r_rstd = rowap(rows, 1024, 1024 + w)
        r_t = rowap(rows, 1536, 1536 + w)
        P.wait("vector", "cT", mm1)
        P.wait("vector", "s1so", P.now("s1so"))
        P.op("vector", lambda v, p0=p0, r_mean=r_mean, w=w: v.tensor_scalar_mul(
            r_mean, p0[0:1, 0:w], eps[0:1, 3:4]))
        P.op("vector", lambda v, p1=p1, r_t=r_t, w=w: v.tensor_scalar_mul(
            r_t, p1[0:1, 0:w], eps[0:1, 3:4]))
        P.op("vector", lambda v, r_comb=r_comb, r_mean=r_mean: v.tensor_mul(
            r_comb, r_mean, r_mean))
        P.op("vector", lambda v, r_t=r_t, r_comb=r_comb: v.tensor_sub(
            r_t, r_t, r_comb))
        PS.release(0, "vector")
        PS.release(1, "vector")
        P.wait("scalar", "cV", P.mark("vector"))
        P.op("scalar", lambda e, r_comb=r_comb, r_t=r_t: e.activation(
            r_comb, r_t, AF.Ln, bias=eps[0:1, 0:1]))
        P.op("scalar", lambda e, r_rstd=r_rstd, r_comb=r_comb: e.activation(
            r_rstd, r_comb, AF.Exp, scale=-0.5))
        P.wait("vector", "cS", P.mark("scalar"))
        P.op("vector", lambda v, r_comb=r_comb, r_t=r_t: v.tensor_scalar_add(
            r_comb, r_t, eps[0:1, 0:1]))
        P.op("vector", lambda v, r_comb=r_comb: v.reciprocal(r_comb, r_comb))
        P.op("vector", lambda v, r_comb=r_comb, r_t=r_t: v.tensor_mul(
            r_comb, r_t, r_comb))
        P.wait("scalar", "cV", P.mark("vector"))
        P.op("scalar", lambda e, r_comb=r_comb: e.activation(
            r_comb, r_comb, AF.Ln, bias=eps[0:1, 1:2]))
        P.op("scalar", lambda e, r_comb=r_comb: e.activation(
            r_comb, r_comb, AF.Exp, scale=-0.5))
        P.wait("vector", "cS", P.mark("scalar"))
        P.op("vector", lambda v, r_comb=r_comb, r_rstd=r_rstd: v.tensor_mul(
            r_comb, r_comb, r_rstd))
        rv = P.mark("vector")
        P.wait("sync", "cV", rv)
        P.dma("sync", statsH[0:1, c0:c0 + w], rowap(rows, 0, w), "s1so",
              **({"allow_slow_non_contiguous": True} if w == 1 else {}))
        P.dma("sync", statsH[1:2, c0:c0 + w], rowap(rows, 1024, 1024 + w), "s1so",
              **({"allow_slow_non_contiguous": True} if w == 1 else {}))
        P.dma("sync", statsH[2:3, c0:c0 + w], rowap(rows, 512, 512 + w), "s1so",
              **({"allow_slow_non_contiguous": True} if w == 1 else {}))
        if it < 0:
            prev_free = ("cT", mm1)
            continue
        P.wait("tensor", "cV", rv)
        p2 = PS.acquire(2, "tensor")
        P.op("tensor", lambda e, p2=p2: e.matmul(
            p2[:, 0:TT], onesf[0:1, :], rowap(rows, 0, TT), start=True, stop=True),
            sig=False)
        p3 = PS.acquire(3, "tensor")
        P.op("tensor", lambda e, p3=p3: e.matmul(
            p3[:, 0:TT], onesf[0:1, :], rowap(rows, 512, 512 + TT),
            start=True, stop=True))
        P.wait("vector", "cT", P.mark("tensor"))
        P.op("vector", lambda v, p2=p2: v.tensor_copy(s1_bc[:, 0:TT], p2[:, 0:TT]))
        P.op("vector", lambda v, p3=p3: v.tensor_copy(s1_bc[:, TT:2 * TT], p3[:, 0:TT]))
        PS.release(2, "vector")
        PS.release(3, "vector")
        for kb in range(NB):
            P.op("vector", lambda v, kb=kb: v.tensor_sub(
                s1_x[:, kb * TT:(kb + 1) * TT], s1_x[:, kb * TT:(kb + 1) * TT],
                s1_bc[:, 0:TT]))
            P.op("vector", lambda v, kb=kb: v.scalar_tensor_tensor(
                s1_h1[:, kb * TT:(kb + 1) * TT], s1_x[:, kb * TT:(kb + 1) * TT],
                vecs[:, 40 + kb:41 + kb], s1_bc[:, TT:2 * TT], ALU.mult, ALU.mult))
        h1v = P.mark("vector")
        prev_free = ("cV", h1v)
        P.wait("tensor", "cV", h1v)
        if it == 0:
            P.wait("tensor", "wS1", wS1)
        P.wait("scalar", "s1qo", P.now("s1qo"))
        for mb in range(NB):
            pi = 4 + (mb % 2)
            pq = PS.acquire(pi, "tensor")
            for kb in range(NB):
                P.op("tensor", lambda e, kb=kb, mb=mb, pq=pq: e.matmul(
                    pq[:, 0:TT],
                    c["s1_wq"][:, kb * D + mb * 128:kb * D + (mb + 1) * 128],
                    s1_h1[:, kb * TT:(kb + 1) * TT],
                    start=(kb == 0), stop=(kb == NB - 1)), sig=(kb == NB - 1))
            P.wait("scalar", "cT", P.mark("tensor"))
            P.op("scalar", lambda e, mb=mb, pq=pq: e.activation(
                s1_st[:, mb * TT:(mb + 1) * TT], pq[:, 0:TT], AF.Identity,
                bias=vecs[:, mb:mb + 1]))
            PS.release(pi, "scalar")
        P.wait("gpsimd", "cS", P.mark("scalar"))
        for mb in range(NB):
            P.dma("gpsimd", c["qT"][mb * 128:(mb + 1) * 128, it * TT:(it + 1) * TT],
                  s1_st[:, mb * TT:(mb + 1) * TT], "s1qo", chain=False)
        qov = P.now("s1qo")
        for g in range(4):
            def lh(kb, g=g):
                return s1_h1[:, kb * TT + g * 128:kb * TT + (g + 1) * 128]
            for cc in range(2):
                pi = 4 + cc
                pk = PS.acquire(pi, "tensor")
                for kb in range(NB):
                    P.op("tensor", lambda e, kb=kb, cc=cc, pk=pk, g=g: e.matmul(
                        pk[:, 0:TT], lh(kb, g),
                        c["s1_wk"][:, kb * D + cc * TT:kb * D + (cc + 1) * TT],
                        start=(kb == 0), stop=(kb == NB - 1)), sig=(kb == NB - 1))
                P.wait("vector", "cT", P.mark("tensor"))
                P.op("vector", lambda v, cc=cc, pk=pk: v.tensor_add(
                    s1_kt[:, cc * TT:(cc + 1) * TT], pk[:, 0:TT],
                    c["sb_bkbc"][:, cc * TT:(cc + 1) * TT]))
                PS.release(pi, "vector")
            P.wait("scalar", "cV", P.mark("vector"))
            P.op("scalar", lambda e: e.activation(s1_ksq[:, :], s1_kt[:, :],
                                                  AF.Square))
            P.wait("vector", "cS", P.mark("scalar"))
            P.op("vector", lambda v: v.tensor_reduce(
                s1_rn[:, 0:H], s1_ksq[:, :].rearrange("p (h d) -> p h d", h=H),
                AX, ALU.add))
            P.wait("scalar", "cV", P.mark("vector"))
            P.op("scalar", lambda e: e.activation(s1_rn[:, H:2 * H],
                                                  s1_rn[:, 0:H], AF.Ln))
            P.op("scalar", lambda e: e.activation(s1_rn[:, H:2 * H],
                                                  s1_rn[:, H:2 * H], AF.Exp,
                                                  scale=0.5))
            P.wait("vector", "cS", P.mark("scalar"))
            P.op("vector", lambda v: v.tensor_scalar_add(
                s1_rn[:, H:2 * H], s1_rn[:, H:2 * H], eps[:, 1:2]))
            P.op("vector", lambda v: v.reciprocal(s1_rn[:, H:2 * H],
                                                  s1_rn[:, H:2 * H]))
            P.wait("vector", "s1ko", P.now("s1ko"))
            P.op("vector", lambda v: v.tensor_tensor(
                s1_kst[:, 0:D].rearrange("p (h d) -> p h d", h=H),
                s1_kt[:, :].rearrange("p (h d) -> p h d", h=H),
                s1_rn[:, H:2 * H].rearrange("p (h o) -> p h o", o=1)
                .broadcast_to((128, H, HD)), ALU.mult))
            P.wait("gpsimd", "cV", P.mark("vector"))
            P.dma("gpsimd", c["ktokN"][it * TT + g * 128:it * TT + (g + 1) * 128, :],
                  s1_kst[:, 0:D], "s1ko", chain=False)
            for cc in range(2):
                pi = 4 + cc
                pv = PS.acquire(pi, "tensor")
                for kb in range(NB):
                    P.op("tensor", lambda e, kb=kb, cc=cc, pv=pv, g=g: e.matmul(
                        pv[:, 0:TT], lh(kb, g),
                        c["s1_wv"][:, kb * D + cc * TT:kb * D + (cc + 1) * TT],
                        start=(kb == 0), stop=(kb == NB - 1)), sig=(kb == NB - 1))
                P.wait("vector", "cT", P.mark("tensor"))
                P.wait("vector", "s1ko", P.now("s1ko"))
                P.op("vector", lambda v, cc=cc, pv=pv: v.tensor_add(
                    s1_kst[:, D + cc * TT:D + (cc + 1) * TT], pv[:, 0:TT],
                    c["sb_bvbc"][:, cc * TT:(cc + 1) * TT]))
                PS.release(pi, "vector")
            P.wait("gpsimd", "cV", P.mark("vector"))
            P.dma("gpsimd", c["vtok"][it * TT + g * 128:it * TT + (g + 1) * 128, :],
                  s1_kst[:, D:2 * D], "s1ko", chain=False)
            pbeta = PS.acquire(4, "tensor")
            for kb in range(NB):
                P.op("tensor", lambda e, kb=kb, pbeta=pbeta, g=g: e.matmul(
                    pbeta[:, 0:H], lh(kb, g),
                    c["s1_wbeta"][:, kb * H:(kb + 1) * H],
                    start=(kb == 0), stop=(kb == NB - 1)), sig=(kb == NB - 1))
            P.wait("vector", "cT", P.mark("tensor"))
            gi = it * 4 + g
            P.op("vector", lambda v, pbeta=pbeta, gi=gi: v.tensor_add(
                c["sb_beta"][:, gi * H:(gi + 1) * H], pbeta[:, 0:H],
                c["sb_bbeta"][:, :]))
            PS.release(4, "vector")
            P.wait("scalar", "cV", P.mark("vector"))
            P.op("scalar", lambda e, gi=gi: e.activation(
                c["sb_beta"][:, gi * H:(gi + 1) * H],
                c["sb_beta"][:, gi * H:(gi + 1) * H], AF.Sigmoid))
    P.barrier()


def emit_s2(c):
    P, PS, PT = c["P"], c["PS"], c["PT"]
    s2_qw, s2_kw, s2_vw, s2_kf = c["s2_qw"], c["s2_kw"], c["s2_vw"], c["s2_kf"]
    s2_m, s2_mt, s2_m2, s2_mt2 = c["s2_m"], c["s2_mt"], c["s2_m2"], c["s2_mt2"]
    s2_sacc, s2_mo = c["s2_sacc"], c["s2_mo"]
    s2_ab, s2_wts, s2_obs = c["s2_ab"], c["s2_wts"], c["s2_obs"]
    beta, stl, i64, mln, mu, idb = (c["sb_beta"], c["sb_stl"], c["sb_i64"],
                                    c["sb_mln"], c["sb_mu"], c["sb_idb"])
    qT, ktokN, vtok, OT, ABb, WTb = (c["qT"], c["ktokN"], c["vtok"], c["OT"],
                                     c["ABb"], c["WTb"])
    nwin = c.get("S2_NWIN", NT)
    for wi in range(nwin):
        P.wait("sync", "cT", P.mark("tensor"))
        P.wait("sync", "s2sp", P.now("s2sp"))
        for mb in range(NB):
            P.dma("sync", s2_qw[:, mb * TT:(mb + 1) * TT],
                  qT[mb * 128:(mb + 1) * 128, wi * TT:(wi + 1) * TT], "s2w")
        for g in range(4):
            P.dma("sync", s2_kw[:, g * D:(g + 1) * D],
                  ktokN[wi * TT + g * 128:wi * TT + (g + 1) * 128, :], "s2w")
            P.dma("sync", s2_vw[:, g * D:(g + 1) * D],
                  vtok[wi * TT + g * 128:wi * TT + (g + 1) * 128, :], "s2w")
        wv_ = P.now("s2w")
        P.wait("tensor", "s2w", wv_)
        P.wait("vector", "s2w", wv_)
        for cp in range(4):
            ci = wi * 4 + cp
            P.wait("tensor", "cV", P.mark("vector"))
            for pb in range(NB):
                pti = 6 + pb % 2
                pt = PT.acquire(pti, "tensor")
                P.op("tensor", lambda e, pb=pb, cp=cp, pt=pt: e.transpose(
                    pt.bitcast(BF16)[:, 0:128],
                    s2_kw[:, cp * D + pb * 128:cp * D + (pb + 1) * 128],
                    idb[:, :]))
                P.wait("vector", "cT", P.mark("tensor"))
                P.op("vector", lambda v, pb=pb, pt=pt: v.tensor_copy(
                    s2_kf[:, pb * CH:(pb + 1) * CH], pt.bitcast(BF16)[:, 0:128]))
                PT.release(pti, "vector")
            kfv = P.mark("vector")
            P.wait("tensor", "cV", kfv)
            def unit(h, cp=cp, ci=ci):
                u = h % 2
                pb, bh = h // 2, (h % 2) * 64
                bk = [3 * u, 3 * u + 1, 3 * u + 2]
                ptb = 6 + u
                tp = (lambda r, cc: None if (r == 0 and cc == 0) else (r, cc))
                kf_h = s2_kf[bh:bh + 64, pb * CH:(pb + 1) * CH]
                q_h = s2_qw[bh:bh + 64, pb * TT + cp * 128:pb * TT + (cp + 1) * 128]
                ktok_h = s2_kw[:, cp * D + h * HD:cp * D + (h + 1) * HD]
                vtok_h = s2_vw[:, cp * D + h * HD:cp * D + (h + 1) * HD]
                bcol = beta[:, ci * H + h:ci * H + h + 1]
                m_, mt_, m2_, mt2_ = s2_m[u], s2_mt[u], s2_m2[u], s2_mt2[u]
                sacc, mo = s2_sacc[u], s2_mo[u]
                p0 = PS.acquire(bk[0], "tensor")
                P.op("tensor", lambda e, kf_h=kf_h, p0=p0, bh=bh: e.matmul(
                    p0[:, 0:CH], kf_h, kf_h, start=True, stop=True,
                    tile_position=tp(bh, 0)))
                lt = P.mark("tensor")
                yield
                P.wait("vector", "cT", lt)
                P.wait("vector", "cS", P.mark("scalar"))
                P.op("vector", lambda v, p0=p0, bcol=bcol, m_=m_:
                     v.scalar_tensor_tensor(
                         m_[:, :], p0[:, 0:CH], bcol, mln[:, :],
                         ALU.mult, ALU.mult))
                PS.release(bk[0], "vector")
                lv = P.mark("vector")
                yield
                P.wait("tensor", "cV", lv)
                pt0 = PT.acquire(ptb, "tensor")
                P.op("tensor", lambda e, pt0=pt0, m_=m_: e.transpose(
                    pt0.bitcast(BF16)[:, 0:128], m_[:, :], idb[:, :]))
                lt = P.mark("tensor")
                yield
                P.wait("vector", "cT", lt)
                P.op("vector", lambda v, pt0=pt0, mt_=mt_: v.tensor_copy(
                    mt_[:, :], pt0.bitcast(BF16)[:, 0:128]))
                PT.release(ptb, "vector")
                P.op("vector", lambda v, vtok_h=vtok_h, bcol=bcol, sacc=sacc:
                     v.tensor_scalar_mul(sacc[:, 0:64], vtok_h, bcol))
                P.op("vector", lambda v, ktok_h=ktok_h, bcol=bcol, sacc=sacc:
                     v.tensor_scalar_mul(sacc[:, 64:128], ktok_h, bcol))
                lv = P.mark("vector")
                yield
                mbufs = [(m_, mt_), (m2_, mt2_)]
                for j in range(7):
                    mj, mtj = mbufs[j % 2]
                    mj1, mtj1 = mbufs[(j + 1) % 2]
                    P.wait("tensor", "cV", lv)
                    if j > 0:
                        P.wait("tensor", "cS", ls)
                    p1 = PS.acquire(bk[1], "tensor")
                    P.op("tensor", lambda e, mtj=mtj, p1=p1, sacc=sacc: e.matmul(
                        p1[:, 0:CH], mtj[:, :], sacc[:, :],
                        start=True, stop=True))
                    if j < 6:
                        p3 = PS.acquire(bk[2], "tensor")
                        P.op("tensor", lambda e, mtj=mtj, mj=mj, p3=p3: e.matmul(
                            p3[:, 0:CH], mtj[:, :], mj[:, :],
                            start=True, stop=True))
                        p4 = PS.acquire(bk[0], "tensor")
                        P.op("tensor", lambda e, mj=mj, mtj=mtj, p4=p4: e.matmul(
                            p4[:, 0:CH], mj[:, :], mtj[:, :],
                            start=True, stop=True))
                    lt = P.mark("tensor")
                    yield
                    P.wait("vector", "cT", lt)
                    P.op("vector", lambda v, p1=p1, sacc=sacc: v.tensor_add(
                        sacc[:, :], sacc[:, :], p1[:, 0:CH]))
                    PS.release(bk[1], "vector")
                    if j < 6:
                        P.wait("scalar", "cT", lt)
                        P.op("scalar", lambda s, mj1=mj1, p3=p3: s.activation(
                            mj1[:, :], p3[:, 0:CH], AF.Identity))
                        PS.release(bk[2], "scalar")
                        P.op("scalar", lambda s, mtj1=mtj1, p4=p4: s.activation(
                            mtj1[:, :], p4[:, 0:CH], AF.Identity))
                        PS.release(bk[0], "scalar")
                        ls = P.mark("scalar")
                    lv = P.mark("vector")
                    yield
                P.wait("tensor", "cV", lv)
                p5 = PS.acquire(bk[1], "tensor")
                P.op("tensor", lambda e, ktok_h=ktok_h, p5=p5, bh=bh, sacc=sacc:
                     e.matmul(p5[bh:bh + 64, 0:64], sacc[:, 64:128], ktok_h,
                              start=True, stop=True, tile_position=tp(0, bh)),
                     sig=False)
                P.op("tensor", lambda e, ktok_h=ktok_h, p5=p5, bh=bh, sacc=sacc:
                     e.matmul(p5[bh:bh + 64, 64:128], ktok_h, sacc[:, 0:64],
                              start=True, stop=True, tile_position=tp(0, bh)))
                p0b = PS.acquire(bk[2], "tensor")
                P.op("tensor", lambda e, kf_h=kf_h, q_h=q_h, p0b=p0b, bh=bh:
                     e.matmul(p0b[:, 0:CH], kf_h, q_h, start=True, stop=True,
                              tile_position=tp(bh, 0)))
                lt = P.mark("tensor")
                yield
                P.wait("vector", "cT", lt)
                P.op("vector", lambda v, p5=p5, bh=bh, pb=pb: v.tensor_sub(
                    s2_ab[bh:bh + 64, pb * 128:pb * 128 + 64],
                    i64[bh:bh + 64, :], p5[bh:bh + 64, 0:64]))
                P.op("vector", lambda v, p5=p5, bh=bh, pb=pb: v.tensor_copy(
                    s2_ab[bh:bh + 64, pb * 128 + 64:(pb + 1) * 128],
                    p5[bh:bh + 64, 64:128]))
                PS.release(bk[1], "vector")
                P.op("vector", lambda v, p0b=p0b, mo=mo: v.tensor_mul(
                    mo[:, :], p0b[:, 0:CH], mu[:, :]))
                PS.release(bk[2], "vector")
                lv = P.mark("vector")
                yield
                P.wait("tensor", "cV", lv)
                p1b = PS.acquire(bk[0], "tensor")
                P.op("tensor", lambda e, p1b=p1b, bh=bh, sacc=sacc, mo=mo:
                     e.matmul(p1b[bh:bh + 64, 0:CH], sacc[:, 0:64], mo[:, :],
                              start=True, stop=True, tile_position=tp(0, bh)),
                     sig=False)
                p2b = PS.acquire(bk[2], "tensor")
                P.op("tensor", lambda e, p2b=p2b, bh=bh, sacc=sacc, mo=mo:
                     e.matmul(p2b[bh:bh + 64, 0:CH], sacc[:, 64:128], mo[:, :],
                              start=True, stop=True, tile_position=tp(0, bh)))
                a_sb = s2_ab[bh:bh + 64, pb * 128:pb * 128 + 64]
                bt_sb = s2_ab[bh:bh + 64, pb * 128 + 64:(pb + 1) * 128]
                st_sb = stl[bh:bh + 64, pb * 64:(pb + 1) * 64]
                p3b = PS.acquire(bk[1], "tensor")
                P.op("tensor", lambda e, a_sb=a_sb, st_sb=st_sb, p3b=p3b, bh=bh:
                     e.matmul(p3b[bh:bh + 64, 0:64], a_sb, st_sb,
                              start=True, stop=True, tile_position=tp(bh, bh)))
                lt = P.mark("tensor")
                yield
                P.wait("vector", "cT", lt)
                P.op("vector", lambda v, p1b=p1b, bh=bh, pb=pb: v.tensor_copy(
                    s2_obs[bh:bh + 64, pb * CH:(pb + 1) * CH],
                    p1b[bh:bh + 64, 0:CH]))
                PS.release(bk[0], "vector")
                P.op("vector", lambda v, p2b=p2b, q_h=q_h, bh=bh, pb=pb:
                     v.tensor_sub(s2_wts[bh:bh + 64, pb * CH:(pb + 1) * CH],
                                  q_h, p2b[bh:bh + 64, 0:CH]))
                PS.release(bk[2], "vector")
                P.op("vector", lambda v, p3b=p3b, bt_sb=bt_sb, st_sb=st_sb, bh=bh:
                     v.tensor_add(st_sb, p3b[bh:bh + 64, 0:64], bt_sb))
                PS.release(bk[1], "vector")

            for hp in range(H // 2):
                gens = [unit(2 * hp), unit(2 * hp + 1)]
                alive = [True, True]
                while any(alive):
                    for gi in range(2):
                        if alive[gi]:
                            try:
                                next(gens[gi])
                            except StopIteration:
                                alive[gi] = False
            sp = P.mark("vector")
            P.wait("gpsimd", "cV", sp)
            P.dma("gpsimd", ABb[ci, :, :, :].rearrange("p r c -> r p c"),
                  s2_ab[:, :].rearrange("q (p c) -> q p c", p=NB), "s2sp",
                  chain=False)
            P.dma("gpsimd", WTb[ci, :, :, :].rearrange("p r c -> r p c"),
                  s2_wts[:, :].rearrange("q (p c) -> q p c", p=NB), "s2sp",
                  chain=False)
            P.dma("gpsimd",
                  OT[:, ci * CH:(ci + 1) * CH].rearrange("(p r) c -> r p c", p=NB),
                  s2_obs[:, :].rearrange("q (p c) -> q p c", p=NB), "s2sp",
                  chain=False)
            P.wait("vector", "s2sp", P.now("s2sp"))
    P.barrier()


def emit_s3(c):
    P = c["P"]
    if "noxc" in c["stages"]:
        P.op("vector", lambda v: v.tensor_scalar_mul(
            c["sb_stc"][:, :], c["sb_stl"][:, :], c["sb_scl"][:, 3:4]))
        P.barrier()
        return
    nc, pid, partner = c["nc"], c["pid"], c["partner"]
    shst, shfl = c["shst"], c["shfl"]
    stl, stc = c["sb_stl"], c["sb_stc"]
    sb_flag, sb_magic, scl = c["sb_flag"], c["sb_magic"], c["sb_scl"]
    P.dma("gpsimd", shst[pid, :, :], stl[:, :], "s3a", chain=False)
    P.wait("gpsimd", "s3a", P.now("s3a"))
    P.dma("gpsimd", shfl[pid, 0, :], sb_magic[0:1, 0:16], "s3a", chain=False)
    P.wait("gpsimd", "s3a", P.now("s3a"))

    def poll(g):
        val = g.alloc_register("pollv")
        cnt = g.alloc_register("pollc")
        vals = bass.RegisterHandles((val,))
        cnts = bass.RegisterHandles((cnt,))
        nc.regs_mov(vals, 0)
        nc.regs_mov(cnts, P.now("s3p") + 16)
        loop, done = "s3_loop", "s3_done"
        nc.br(loop, engines=[ET.Pool])
        with nc.body(loop, valid_engines=[ET.Pool]):
            g.dma_start(out=sb_flag[:, :], in_=shfl[partner, 0, :][None, :]) \
                .then_inc(P.sems["s3p"], 16)
            g.wait_ge(P.sems["s3p"], cnt)
            nc.regs_alu(cnts, cnts, 16, op=ALU.add)
            nc.regs_load(vals, sb_flag[0:1, 0:1])
            nc.br_cmp(val, MAGIC1, on_true=done, on_false=loop,
                      comp_op="IS_EQ", engines=[ET.Pool])
        nc.switch_bb(done)
        return None
    P.sem("s3p")
    P.op("gpsimd", poll, sig=True, chain=False)
    P.dma("gpsimd", stc[:, :], shst[partner, :, :], "s3b", chain=False)
    P.wait("vector", "s3b", P.now("s3b"))
    P.op("vector", lambda v: v.tensor_scalar_mul(stc[:, :], stc[:, :],
                                                 scl[:, 3:4]))
    P.barrier()


def emit_s4(c):
    P, PS = c["P"], c["PS"]
    s4_ab, s4_wt, s4_o = c["s4_ab"], c["s4_wt"], c["s4_o"]
    stc = c["sb_stc"]
    OT, ABb, WTb = c["OT"], c["ABb"], c["WTb"]
    nch = c.get("S2_NWIN", NT) * 4
    for ci in range(nch):
        P.wait("sync", "s4o", P.now("s4o"))
        P.wait("sync", "cT", P.mark("tensor"))
        P.dma("sync", s4_ab[:, :].rearrange("q (p c) -> q p c", p=NB),
              ABb[ci, :, :, :].rearrange("p r c -> r p c"), "s4w")
        P.dma("sync", s4_wt[:, :].rearrange("q (p c) -> q p c", p=NB),
              WTb[ci, :, :, :].rearrange("p r c -> r p c"), "s4w")
        P.dma("sync", s4_o[:, :].rearrange("q (p c) -> q p c", p=NB),
              OT[:, ci * CH:(ci + 1) * CH].rearrange("(p r) c -> r p c", p=NB),
              "s4w")
        wv_ = P.now("s4w")
        P.wait("tensor", "s4w", wv_)
        P.wait("vector", "s4w", wv_)
        for h in range(H):
            pb, bh = h // 2, (h % 2) * 64
            tp = (lambda r, cc: None if (r == 0 and cc == 0) else (r, cc))
            st_sb = stc[bh:bh + 64, pb * 64:(pb + 1) * 64]
            a_sb = s4_ab[bh:bh + 64, pb * 128:pb * 128 + 64]
            bt_sb = s4_ab[bh:bh + 64, pb * 128 + 64:(pb + 1) * 128]
            wt_sb = s4_wt[bh:bh + 64, pb * CH:(pb + 1) * CH]
            p0 = PS.acquire(h % 2, "tensor")
            P.op("tensor", lambda e, st_sb=st_sb, wt_sb=wt_sb, p0=p0, bh=bh:
                 e.matmul(p0[bh:bh + 64, 0:CH], st_sb, wt_sb,
                          start=True, stop=True, tile_position=tp(bh, bh)))
            p1 = PS.acquire(2 + h % 2, "tensor")
            P.op("tensor", lambda e, a_sb=a_sb, st_sb=st_sb, p1=p1, bh=bh:
                 e.matmul(p1[bh:bh + 64, 0:64], a_sb, st_sb,
                          start=True, stop=True, tile_position=tp(bh, bh)))
            mmk = P.mark("tensor")
            P.wait("vector", "cT", mmk)
            P.op("vector", lambda v, p0=p0, bh=bh, pb=pb: v.tensor_add(
                s4_o[bh:bh + 64, pb * CH:(pb + 1) * CH],
                s4_o[bh:bh + 64, pb * CH:(pb + 1) * CH], p0[bh:bh + 64, 0:CH]))
            PS.release(h % 2, "vector")
            P.op("vector", lambda v, p1=p1, bt_sb=bt_sb, st_sb=st_sb, bh=bh:
                 v.tensor_add(st_sb, p1[bh:bh + 64, 0:64], bt_sb))
            PS.release(2 + h % 2, "vector")
        P.wait("gpsimd", "cV", P.mark("vector"))
        P.dma("gpsimd",
              OT[:, ci * CH:(ci + 1) * CH].rearrange("(p r) c -> r p c", p=NB),
              s4_o[:, :].rearrange("q (p c) -> q p c", p=NB), "s4o", chain=False)
    P.barrier()


def emit_s5(c):
    P, PS = c["P"], c["PS"]
    rows, eps, vecs = c["sb_rows"], c["sb_eps"], c["sb_vecs"]
    onescf, onescb, onesf = c["sb_onescf"], c["sb_onescb"], c["sb_onesf"]
    s5_ot, s5_att, s5_x, s5_xa = c["s5_ot"], c["s5_att"], c["s5_x"], c["s5_xa"]
    s5_sq, s5_h2, s5_bc = c["s5_sq"], c["s5_h2"], c["s5_bc"]
    OT, statsH, x_sl, xaTh, h2Th = (c["OT"], c["statsH"], c["x_sl"], c["xaTh"],
                                    c["h2Th"])
    for kb in range(NB):
        P.dma("gpsimd", c["s5_wo"][:, kb * D:(kb + 1) * D],
              c["wo"][kb * 128:(kb + 1) * 128, :], "wS5a", chain=False)
    P.wait("tensor", "wS5a", P.now("wS5a"))
    for it in range(NT):
        c0 = 1 + it * TT
        P.wait("sync", "cV", P.mark("vector"))
        P.wait("sync", "cS", P.mark("scalar"))
        for mb in range(NB):
            P.dma("sync", s5_ot[:, mb * TT:(mb + 1) * TT],
                  OT[mb * 128:(mb + 1) * 128, it * TT:(it + 1) * TT], "s5w")
            P.dma("sync", s5_x[:, mb * TT:(mb + 1) * TT],
                  x_sl[kb * 128:(kb + 1) * 128, c0:c0 + TT] if False else
                  x_sl[mb * 128:(mb + 1) * 128, c0:c0 + TT], "s5w")
        P.dma("sync", rowap(rows, 0, TT), statsH[0:1, c0:c0 + TT], "s5w")
        P.dma("sync", rowap(rows, 1024, 1024 + TT), statsH[1:2, c0:c0 + TT], "s5w")
        wv_ = P.now("s5w")
        P.wait("tensor", "s5w", wv_)
        p2 = PS.acquire(2, "tensor")
        P.op("tensor", lambda e, p2=p2: e.matmul(
            p2[:, 0:TT], onesf[0:1, :], rowap(rows, 0, TT), start=True, stop=True),
            sig=False)
        p3 = PS.acquire(3, "tensor")
        P.op("tensor", lambda e, p3=p3: e.matmul(
            p3[:, 0:TT], onesf[0:1, :], rowap(rows, 1024, 1024 + TT),
            start=True, stop=True))
        P.wait("vector", "cT", P.mark("tensor"))
        P.op("vector", lambda v, p2=p2: v.tensor_copy(s5_bc[:, 0:TT], p2[:, 0:TT]))
        P.op("vector", lambda v, p3=p3: v.tensor_copy(s5_bc[:, TT:2 * TT],
                                                      p3[:, 0:TT]))
        PS.release(2, "vector")
        PS.release(3, "vector")
        for mb in range(NB):
            pi = 4 + (mb % 2)
            pq = PS.acquire(pi, "tensor")
            for kb in range(NB):
                P.op("tensor", lambda e, kb=kb, mb=mb, pq=pq: e.matmul(
                    pq[:, 0:TT],
                    c["s5_wo"][:, kb * D + mb * 128:kb * D + (mb + 1) * 128],
                    s5_ot[:, kb * TT:(kb + 1) * TT],
                    start=(kb == 0), stop=(kb == NB - 1)), sig=(kb == NB - 1))
            P.wait("scalar", "cT", P.mark("tensor"))
            P.op("scalar", lambda e, mb=mb, pq=pq: e.activation(
                s5_att[:, mb * TT:(mb + 1) * TT], pq[:, 0:TT], AF.Identity,
                bias=vecs[:, 24 + mb:25 + mb]))
            PS.release(pi, "scalar")
        P.wait("vector", "cS", P.mark("scalar"))
        P.wait("vector", "s5o", P.now("s5o"))
        for mb in range(NB):
            sl = slice(mb * TT, (mb + 1) * TT)
            P.op("vector", lambda v, sl=sl: v.tensor_sub(
                s5_x[:, sl], s5_x[:, sl], s5_bc[:, 0:TT]))
            P.op("vector", lambda v, sl=sl: v.tensor_mul(
                s5_x[:, sl], s5_x[:, sl], s5_bc[:, TT:2 * TT]))
            P.op("vector", lambda v, sl=sl: v.tensor_add(
                s5_xa[:, sl], s5_x[:, sl], s5_att[:, sl]))
        xav = P.mark("vector")
        P.wait("gpsimd", "cV", xav)
        for mb in range(NB):
            P.dma("gpsimd", xaTh[mb * 128:(mb + 1) * 128, it * TT:(it + 1) * TT],
                  s5_xa[:, mb * TT:(mb + 1) * TT], "s5o", chain=False)
        P.wait("scalar", "cV", xav)
        for mb in range(NB):
            P.op("scalar", lambda e, mb=mb: e.activation(
                s5_sq[:, mb * TT:(mb + 1) * TT], s5_xa[:, mb * TT:(mb + 1) * TT],
                AF.Square))
        P.wait("tensor", "cS", P.mark("scalar"))
        p0 = PS.acquire(0, "tensor")
        for kb in range(NB):
            P.op("tensor", lambda e, kb=kb, p0=p0: e.matmul(
                p0[0:1, 0:TT], onescb[:, 0:1], s5_sq[:, kb * TT:(kb + 1) * TT],
                start=(kb == 0), stop=(kb == NB - 1)), sig=(kb == NB - 1))
        P.wait("vector", "cT", P.mark("tensor"))
        P.op("vector", lambda v, p0=p0: v.tensor_scalar_mul(
            rowap(rows, 1536, 1536 + TT), p0[0:1, 0:TT], eps[0:1, 3:4]))
        PS.release(0, "vector")
        P.wait("scalar", "cV", P.mark("vector"))
        P.op("scalar", lambda e: e.activation(rowap(rows, 1536, 1536 + TT),
                                              rowap(rows, 1536, 1536 + TT),
                                              AF.Ln, bias=eps[0:1, 1:2]))
        P.op("scalar", lambda e: e.activation(rowap(rows, 1536, 1536 + TT),
                                              rowap(rows, 1536, 1536 + TT),
                                              AF.Exp, scale=-0.5))
        P.wait("tensor", "cS", P.mark("scalar"))
        p1 = PS.acquire(1, "tensor")
        P.op("tensor", lambda e, p1=p1: e.matmul(
            p1[:, 0:TT], onesf[0:1, :], rowap(rows, 1536, 1536 + TT),
            start=True, stop=True))
        P.wait("vector", "cT", P.mark("tensor"))
        P.op("vector", lambda v, p1=p1: v.tensor_copy(s5_bc[:, 2 * TT:3 * TT],
                                                      p1[:, 0:TT]))
        PS.release(1, "vector")
        P.wait("vector", "s5o", P.now("s5o"))
        for mb in range(NB):
            P.op("vector", lambda v, mb=mb: v.scalar_tensor_tensor(
                s5_h2[:, mb * TT:(mb + 1) * TT], s5_xa[:, mb * TT:(mb + 1) * TT],
                vecs[:, 48 + mb:49 + mb], s5_bc[:, 2 * TT:3 * TT],
                ALU.mult, ALU.mult))
        P.wait("gpsimd", "cV", P.mark("vector"))
        for mb in range(NB):
            P.dma("gpsimd", h2Th[mb * 128:(mb + 1) * 128, it * TT:(it + 1) * TT],
                  s5_h2[:, mb * TT:(mb + 1) * TT], "s5o", chain=False)
    P.barrier()
    s5_h2b, s5_hf, s5_gt = c["s5_h2b"], c["s5_hf"], c["s5_gt"]
    s5_xab, s5_s, sb_b1 = c["s5_xab"], c["s5_s"], c["sb_b1"]
    sTh = c["sTh"]
    for kb in range(NB):
        P.dma("gpsimd", c["s5_w1"][:, kb * DFF:(kb + 1) * DFF],
              c["w1"][kb * 128:(kb + 1) * 128, :], "wS5b", chain=False)
    for ko in range(32):
        P.dma("gpsimd", c["s5_w2"][:, ko * D:(ko + 1) * D],
              c["w2"][ko * 128:(ko + 1) * 128, :], "wS5b", chain=False)
    P.wait("tensor", "wS5b", P.now("wS5b"))
    for it in range(NT):
        P.wait("sync", "cT", P.mark("tensor"))
        for mb in range(NB):
            P.dma("sync", s5_h2b[:, mb * TT:(mb + 1) * TT],
                  h2Th[mb * 128:(mb + 1) * 128, it * TT:(it + 1) * TT], "s5bw")
        wv_ = P.now("s5bw")
        P.wait("tensor", "s5bw", wv_)
        P.wait("vector", "cT", P.mark("tensor"))
        P.wait("scalar", "cT", P.mark("tensor"))
        for mo in range(32):
            pi = 4 + (mo % 2)
            pq = PS.acquire(pi, "tensor")
            for kb in range(NB):
                P.op("tensor", lambda e, kb=kb, mo=mo, pq=pq: e.matmul(
                    pq[:, 0:TT],
                    c["s5_w1"][:, kb * DFF + mo * 128:kb * DFF + (mo + 1) * 128],
                    s5_h2b[:, kb * TT:(kb + 1) * TT],
                    start=(kb == 0), stop=(kb == NB - 1)), sig=(kb == NB - 1))
            P.wait("scalar", "cT", P.mark("tensor"))
            P.wait("scalar", "cV", P.mark("vector"))
            t0 = s5_gt[:, 0:TT]
            t1 = s5_gt[:, TT:2 * TT]
            P.op("scalar", lambda e, mo=mo, pq=pq, t0=t0: e.activation(
                t0, pq[:, 0:TT], AF.Identity, bias=sb_b1[:, mo:mo + 1]))
            PS.release(pi, "scalar")
            P.op("scalar", lambda e, t0=t0, t1=t1: e.activation(t1, t0, AF.Square))
            P.wait("vector", "cS", P.mark("scalar"))
            P.op("vector", lambda v, t1=t1: v.tensor_scalar(
                t1, t1, 0.044715, 1.0, ALU.mult, ALU.add))
            P.op("vector", lambda v, t0=t0, t1=t1: v.tensor_mul(t1, t0, t1))
            P.wait("scalar", "cV", P.mark("vector"))
            P.op("scalar", lambda e, t1=t1: e.activation(
                t1, t1, AF.Tanh, scale=0.7978845608028654))
            P.wait("vector", "cS", P.mark("scalar"))
            P.op("vector", lambda v, t1=t1: v.tensor_scalar(
                t1, t1, 0.5, 0.5, ALU.mult, ALU.add))
            P.op("vector", lambda v, mo=mo, t0=t0, t1=t1: v.tensor_mul(
                s5_hf[:, mo * TT:(mo + 1) * TT], t0, t1))
        hfv = P.mark("vector")
        P.wait("tensor", "cV", hfv)
        for mb in range(NB):
            pi = 4 + (mb % 2)
            pq = PS.acquire(pi, "tensor")
            for ko in range(32):
                P.op("tensor", lambda e, ko=ko, mb=mb, pq=pq: e.matmul(
                    pq[:, 0:TT],
                    c["s5_w2"][:, ko * D + mb * 128:ko * D + (mb + 1) * 128],
                    s5_hf[:, ko * TT:(ko + 1) * TT],
                    start=(ko == 0), stop=(ko == 31)), sig=(ko == 31))
            P.dma("sync", s5_xab[:, 0:TT],
                  xaTh[mb * 128:(mb + 1) * 128, it * TT:(it + 1) * TT], "s5bx",
                  wait=[("cV", P.mark("vector"))])
            P.wait("scalar", "cT", P.mark("tensor"))
            P.wait("scalar", "s5so", P.now("s5so"))
            P.op("scalar", lambda e, mb=mb, pq=pq: e.activation(
                s5_s[:, 0:TT], pq[:, 0:TT], AF.Identity,
                bias=vecs[:, 32 + mb:33 + mb]))
            PS.release(pi, "scalar")
            P.wait("vector", "cS", P.mark("scalar"))
            P.wait("vector", "s5bx", P.now("s5bx"))
            P.op("vector", lambda v: v.tensor_add(s5_s[:, 0:TT], s5_s[:, 0:TT],
                                                  s5_xab[:, 0:TT]))
            P.wait("gpsimd", "cV", P.mark("vector"))
            P.dma("gpsimd", sTh[mb * 128:(mb + 1) * 128, it * TT:(it + 1) * TT],
                  s5_s[:, 0:TT], "s5so", chain=False)
    P.barrier()
    if "s5" in c["stages"] and c["dev"]:
        P.dma("sync", c["dev_out"]["d_s"][:, :], sTh[:, :], "dev")
        P.dma("sync", c["dev_out"]["d_xa"][:, :], xaTh[:, :], "dev")
        P.dma("sync", c["dev_out"]["d_h2"][:, :], h2Th[:, :], "dev")
        P.wait("sync", "dev", P.now("dev"))


def emit_s6(c):
    P, PS = c["P"], c["PS"]
    rows, eps, vecs = c["sb_rows"], c["sb_eps"], c["sb_vecs"]
    onescf, onescb, onesf = c["sb_onescf"], c["sb_onescb"], c["sb_onesf"]
    lastc = c["sb_lastc"]
    s6_x, s6_xn, s6_xr, s6_xk = c["s6_x"], c["s6_xn"], c["s6_xr"], c["s6_xk"]
    s6_r, s6_kk, s6_sld, s6_g = c["s6_r"], c["s6_kk"], c["s6_sld"], c["s6_g"]
    s6_x1, s6_sq, s6_bc, s6_xn1 = c["s6_x1"], c["s6_sq"], c["s6_bc"], c["s6_xn1"]
    x_sl, statsH, stats1H, sTh = c["x_sl"], c["statsH"], c["stats1H"], c["sTh"]
    x1T, xn1T = c["x1T"], c["xn1T_t"]
    WB = TT + 8
    for w_h, w_s in [(c["wrT"], c["s6_wr"]), (c["wkT"], c["s6_wkk"]),
                     (c["wvT"], c["s6_wvv"])]:
        for kb in range(NB):
            P.dma("gpsimd", w_s[:, kb * D:(kb + 1) * D],
                  w_h[kb * 128:(kb + 1) * 128, :], "wS6", chain=False)
    P.wait("tensor", "wS6", P.now("wS6"))
    P.dma("sync", rowap(rows, 0, 8), statsH[0:1, 0:8], "s6w",
          allow_slow_non_contiguous=True)
    P.dma("sync", rowap(rows, 1024, 1032), statsH[1:2, 0:8], "s6w",
          allow_slow_non_contiguous=True)
    for mb in range(NB):
        P.dma("sync", s6_x[:, mb * TT:mb * TT + 1],
              x_sl[mb * 128:(mb + 1) * 128, 0:1], "s6w",
              allow_slow_non_contiguous=True)
    P.wait("tensor", "s6w", P.now("s6w"))
    ph = PS.acquire(0, "tensor")
    P.op("tensor", lambda e, ph=ph: e.matmul(
        ph[:, 0:1], onesf[0:1, :], rowap(rows, 0, 1), start=True, stop=True),
        sig=False)
    P.op("tensor", lambda e, ph=ph: e.matmul(
        ph[:, 1:2], onesf[0:1, :], rowap(rows, 1024, 1025), start=True, stop=True))
    P.wait("vector", "cT", P.mark("tensor"))
    for mb in range(NB):
        P.op("vector", lambda v, mb=mb, ph=ph: v.tensor_sub(
            lastc[:, mb:mb + 1], s6_x[:, mb * TT:mb * TT + 1], ph[:, 0:1]))
        P.op("vector", lambda v, mb=mb, ph=ph: v.tensor_mul(
            lastc[:, mb:mb + 1], lastc[:, mb:mb + 1], ph[:, 1:2]))
    PS.release(0, "vector")
    for it in range(NT):
        c0 = 1 + it * TT
        P.wait("sync", "cV", P.mark("vector"))
        P.wait("sync", "cT", P.mark("tensor"))
        P.wait("sync", "s6so", P.now("s6so"))
        for mb in range(NB):
            P.dma("sync", s6_x[:, mb * TT:(mb + 1) * TT],
                  x_sl[mb * 128:(mb + 1) * 128, c0:c0 + TT], "s6w")
        P.dma("sync", rowap(rows, 0, TT), statsH[0:1, c0:c0 + TT], "s6w")
        P.dma("sync", rowap(rows, 1024, 1024 + TT), statsH[1:2, c0:c0 + TT], "s6w")
        wv_ = P.now("s6w")
        P.wait("tensor", "s6w", wv_)
        P.wait("vector", "cT", P.mark("tensor"))
        p2 = PS.acquire(2, "tensor")
        P.op("tensor", lambda e, p2=p2: e.matmul(
            p2[:, 0:TT], onesf[0:1, :], rowap(rows, 0, TT), start=True, stop=True),
            sig=False)
        p3 = PS.acquire(3, "tensor")
        P.op("tensor", lambda e, p3=p3: e.matmul(
            p3[:, 0:TT], onesf[0:1, :], rowap(rows, 1024, 1024 + TT),
            start=True, stop=True))
        P.wait("vector", "cT", P.mark("tensor"))
        P.op("vector", lambda v, p2=p2: v.tensor_copy(s6_bc[:, 0:TT], p2[:, 0:TT]))
        P.op("vector", lambda v, p3=p3: v.tensor_copy(s6_bc[:, TT:2 * TT],
                                                      p3[:, 0:TT]))
        PS.release(2, "vector")
        PS.release(3, "vector")
        for mb in range(NB):
            xb = slice(mb * WB + 1, mb * WB + 1 + TT)
            P.op("vector", lambda v, mb=mb: v.tensor_copy(
                s6_xn[:, mb * WB:mb * WB + 1], lastc[:, mb:mb + 1]))
            P.op("vector", lambda v, mb=mb, xb=xb: v.tensor_sub(
                s6_x[:, mb * TT:(mb + 1) * TT], s6_x[:, mb * TT:(mb + 1) * TT],
                s6_bc[:, 0:TT]))
            P.op("vector", lambda v, mb=mb, xb=xb: v.tensor_tensor(
                s6_xn[:, xb], s6_x[:, mb * TT:(mb + 1) * TT],
                s6_bc[:, TT:2 * TT], ALU.mult))
            P.op("vector", lambda v, mb=mb: v.tensor_copy(
                lastc[:, mb:mb + 1], s6_xn[:, mb * WB + TT:mb * WB + TT + 1]))
        for mb in range(NB):
            cur = s6_xn[:, mb * WB + 1:mb * WB + 1 + TT]
            prv = s6_xn[:, mb * WB:mb * WB + TT]
            dif = s6_g[:, 0:TT]
            P.op("vector", lambda v, cur=cur, prv=prv, dif=dif: v.tensor_sub(
                dif, cur, prv))
            P.op("vector", lambda v, mb=mb, dif=dif, prv=prv: v.scalar_tensor_tensor(
                s6_xr[:, mb * TT:(mb + 1) * TT], dif, vecs[:, 56 + mb:57 + mb],
                prv, ALU.mult, ALU.add))
            P.op("vector", lambda v, mb=mb, dif=dif, prv=prv: v.scalar_tensor_tensor(
                s6_xk[:, mb * TT:(mb + 1) * TT], dif, vecs[:, 64 + mb:65 + mb],
                prv, ALU.mult, ALU.add))
        mixv = P.mark("vector")
        P.wait("tensor", "cV", mixv)
        for mo in range(NB):
            pi = 4 + (mo % 2)
            pq = PS.acquire(pi, "tensor")
            for kb in range(NB):
                P.op("tensor", lambda e, kb=kb, mo=mo, pq=pq: e.matmul(
                    pq[:, 0:TT],
                    c["s6_wr"][:, kb * D + mo * 128:kb * D + (mo + 1) * 128],
                    s6_xr[:, kb * TT:(kb + 1) * TT],
                    start=(kb == 0), stop=(kb == NB - 1)), sig=(kb == NB - 1))
            P.wait("scalar", "cT", P.mark("tensor"))
            P.op("scalar", lambda e, mo=mo, pq=pq: e.activation(
                s6_r[:, mo * TT:(mo + 1) * TT], pq[:, 0:TT], AF.Sigmoid))
            PS.release(pi, "scalar")
        for mo in range(NB):
            pi = 4 + (mo % 2)
            pq = PS.acquire(pi, "tensor")
            for kb in range(NB):
                P.op("tensor", lambda e, kb=kb, mo=mo, pq=pq: e.matmul(
                    pq[:, 0:TT],
                    c["s6_wkk"][:, kb * D + mo * 128:kb * D + (mo + 1) * 128],
                    s6_xk[:, kb * TT:(kb + 1) * TT],
                    start=(kb == 0), stop=(kb == NB - 1)), sig=(kb == NB - 1))
            P.wait("scalar", "cT", P.mark("tensor"))
            P.op("scalar", lambda e, pq=pq: e.activation(
                s6_g[:, 0:TT], pq[:, 0:TT], AF.Relu))
            PS.release(pi, "scalar")
            P.op("scalar", lambda e, mo=mo: e.activation(
                s6_kk[:, mo * TT:(mo + 1) * TT], s6_g[:, 0:TT], AF.Square))
        kkv = P.mark("scalar")
        P.wait("tensor", "cS", kkv)
        P.wait("sync", "cV", P.mark("vector"))
        for mo in range(NB):
            P.dma("sync", s6_sld[:, mo * TT:(mo + 1) * TT],
                  sTh[mo * 128:(mo + 1) * 128, it * TT:(it + 1) * TT], "s6s")
        for mo in range(NB):
            pi = 4 + (mo % 2)
            pq = PS.acquire(pi, "tensor")
            for kb in range(NB):
                P.op("tensor", lambda e, kb=kb, mo=mo, pq=pq: e.matmul(
                    pq[:, 0:TT],
                    c["s6_wvv"][:, kb * D + mo * 128:kb * D + (mo + 1) * 128],
                    s6_kk[:, kb * TT:(kb + 1) * TT],
                    start=(kb == 0), stop=(kb == NB - 1)), sig=(kb == NB - 1))
            P.wait("vector", "cT", P.mark("tensor"))
            P.op("vector", lambda v, mo=mo, pq=pq: v.tensor_mul(
                s6_g[:, 0:TT], pq[:, 0:TT], s6_r[:, mo * TT:(mo + 1) * TT]))
            PS.release(pi, "vector")
            P.wait("scalar", "cV", P.mark("vector"))
            P.op("scalar", lambda e: e.activation(
                s6_g[:, TT:2 * TT], s6_g[:, 0:TT], AF.Sigmoid))
            P.wait("vector", "cS", P.mark("scalar"))
            P.op("vector", lambda v: v.tensor_mul(
                s6_g[:, 0:TT], s6_g[:, 0:TT], s6_g[:, TT:2 * TT]))
            P.wait("vector", "s6s", P.now("s6s"))
            P.op("vector", lambda v, mo=mo: v.tensor_mul(
                s6_g[:, 0:TT], s6_g[:, 0:TT], s6_sld[:, mo * TT:(mo + 1) * TT]))
            P.op("vector", lambda v, mo=mo: v.tensor_add(
                s6_g[:, 0:TT], s6_g[:, 0:TT], s6_bc[:, 0:TT]))
            P.op("vector", lambda v, mo=mo: v.tensor_add(
                s6_x1[:, mo * TT:(mo + 1) * TT], s6_x[:, mo * TT:(mo + 1) * TT],
                s6_g[:, 0:TT]))
        x1v = P.mark("vector")
        P.wait("gpsimd", "cV", x1v)
        P.wait("gpsimd", "s6o", P.now("s6o"))
        for mb in range(NB):
            P.dma("gpsimd", x1T[mb * 128:(mb + 1) * 128, it * TT:(it + 1) * TT],
                  s6_x1[:, mb * TT:(mb + 1) * TT], "s6o", chain=False)
        P.wait("scalar", "cV", x1v)
        for mb in range(NB):
            P.op("scalar", lambda e, mb=mb: e.activation(
                s6_sq[:, mb * TT:(mb + 1) * TT], s6_x1[:, mb * TT:(mb + 1) * TT],
                AF.Square))
        P.wait("tensor", "cS", P.mark("scalar"))
        P.wait("tensor", "cV", x1v)
        p0 = PS.acquire(0, "tensor")
        p1 = PS.acquire(1, "tensor")
        for kb in range(NB):
            P.op("tensor", lambda e, kb=kb, p0=p0: e.matmul(
                p0[0:1, 0:TT], onescf[:, 0:1], s6_x1[:, kb * TT:(kb + 1) * TT],
                start=(kb == 0), stop=(kb == NB - 1)), sig=(kb == NB - 1))
        for kb in range(NB):
            P.op("tensor", lambda e, kb=kb, p1=p1: e.matmul(
                p1[0:1, 0:TT], onescb[:, 0:1], s6_sq[:, kb * TT:(kb + 1) * TT],
                start=(kb == 0), stop=(kb == NB - 1)), sig=(kb == NB - 1))
        P.wait("vector", "cT", P.mark("tensor"))
        P.wait("vector", "s6so", P.now("s6so"))
        r_m = rowap(rows, 1536, 1536 + TT)
        r_v = rowap(rows, 512, 512 + TT)
        P.op("vector", lambda v, p0=p0, r_m=r_m: v.tensor_scalar_mul(
            r_m, p0[0:1, 0:TT], eps[0:1, 3:4]))
        P.op("vector", lambda v, p1=p1, r_v=r_v: v.tensor_scalar_mul(
            r_v, p1[0:1, 0:TT], eps[0:1, 3:4]))
        PS.release(0, "vector")
        PS.release(1, "vector")
        P.op("vector", lambda v, r_m=r_m: v.tensor_mul(
            rowap(rows, 0, TT), r_m, r_m))
        P.op("vector", lambda v, r_v=r_v: v.tensor_sub(
            rowap(rows, 0, TT), r_v, rowap(rows, 0, TT)))
        P.wait("scalar", "cV", P.mark("vector"))
        P.op("scalar", lambda e: e.activation(
            rowap(rows, 0, TT), rowap(rows, 0, TT), AF.Ln,
            bias=eps[0:1, 0:1]))
        P.op("scalar", lambda e: e.activation(
            rowap(rows, 0, TT), rowap(rows, 0, TT), AF.Exp,
            scale=-0.5))
        scv = P.mark("scalar")
        P.wait("sync", "cS", scv)
        P.dma("sync", stats1H[0:1, c0:c0 + TT], r_m, "s6so")
        P.dma("sync", stats1H[1:2, c0:c0 + TT], rowap(rows, 0, TT), "s6so")
        P.wait("tensor", "cS", scv)
        p2b = PS.acquire(2, "tensor")
        P.op("tensor", lambda e, p2b=p2b: e.matmul(
            p2b[:, 0:TT], onesf[0:1, :], r_m, start=True, stop=True), sig=False)
        p3b = PS.acquire(3, "tensor")
        P.op("tensor", lambda e, p3b=p3b: e.matmul(
            p3b[:, 0:TT], onesf[0:1, :], rowap(rows, 0, TT),
            start=True, stop=True))
        P.wait("vector", "cT", P.mark("tensor"))
        P.op("vector", lambda v, p2b=p2b: v.tensor_copy(s6_bc[:, 0:TT],
                                                        p2b[:, 0:TT]))
        P.op("vector", lambda v, p3b=p3b: v.tensor_copy(s6_bc[:, TT:2 * TT],
                                                        p3b[:, 0:TT]))
        PS.release(2, "vector")
        PS.release(3, "vector")
        P.wait("vector", "s6xo", P.now("s6xo"))
        P.wait("vector", "s6o", P.now("s6o"))
        for mb in range(NB):
            P.op("vector", lambda v, mb=mb: v.tensor_sub(
                s6_x1[:, mb * TT:(mb + 1) * TT], s6_x1[:, mb * TT:(mb + 1) * TT],
                s6_bc[:, 0:TT]))
            P.op("vector", lambda v, mb=mb: v.tensor_tensor(
                s6_xn1[:, mb * TT:(mb + 1) * TT], s6_x1[:, mb * TT:(mb + 1) * TT],
                s6_bc[:, TT:2 * TT], ALU.mult))
        P.wait("gpsimd", "cV", P.mark("vector"))
        for mb in range(NB):
            P.dma("gpsimd", xn1T[mb * 128:(mb + 1) * 128, it * TT:(it + 1) * TT],
                  s6_xn1[:, mb * TT:(mb + 1) * TT], "s6xo", chain=False)
    P.barrier()


def emit_cv(c):
    P = c["P"]
    cv_a, cv_b, cv_o, eps = c["cv_a"], c["cv_b"], c["cv_o"], c["sb_eps"]
    for kb in range(NB):
        P.wait("sync", "cV", P.mark("vector"))
        P.dma("sync", cv_a[:, :], c["k0u"][kb * 128:(kb + 1) * 128, :], "cvw")
        P.dma("sync", cv_b[:, :], c["k0u"][D + kb * 128:D + (kb + 1) * 128, :], "cvw")
        P.wait("vector", "cvw", P.now("cvw"))
        P.wait("vector", "cvo", P.now("cvo"))
        P.op("vector", lambda v: v.tensor_sub(cv_o[:, :], cv_a[:, :], cv_b[:, :]))
        P.wait("gpsimd", "cV", P.mark("vector"))
        P.dma("gpsimd", c["d0T"][kb * 128:(kb + 1) * 128, :], cv_o[:, :], "cvo",
              chain=False)
    for src_t, dst_t, n in [(c["k1u"], c["w1eT"], 32), (c["k2u"], c["w2eT"], 32)]:
        w = src_t.shape[1]
        for kb in range(n):
            P.wait("sync", "cS", P.mark("scalar"))
            P.wait("sync", "cV", P.mark("vector"))
            P.dma("sync", cv_a[:, 0:w], src_t[kb * 128:(kb + 1) * 128, :], "cvw")
            P.wait("scalar", "cvw", P.now("cvw"))
            P.wait("scalar", "cvo", P.now("cvo"))
            P.op("scalar", lambda e, w=w: e.activation(
                cv_o[:, 0:w], cv_a[:, 0:w], AF.Identity, bias=eps[:, 4:5],
                scale=2.0))
            P.wait("gpsimd", "cS", P.mark("scalar"))
            P.dma("gpsimd", dst_t[kb * 128:(kb + 1) * 128, :], cv_o[:, 0:w], "cvo",
                  chain=False)
    P.barrier()


def emit_t1(c):
    P, PS = c["P"], c["PS"]
    t1_xn1, t1_q, t1_kt, t1_vt, t1_tmp = (c["t1_xn1"], c["t1_q"], c["t1_kt"],
                                          c["t1_vt"], c["t1_tmp"])
    bqkv, eps = c["sb_bqkv"], c["sb_eps"]
    xn1T, q1T = c["xn1T_t"], c["q1T"]
    for kb in range(NB):
        P.dma("gpsimd", c["t1_wqkv"][:, kb * 3 * D:(kb + 1) * 3 * D],
              c["wqkv"][kb * 128:(kb + 1) * 128, :], "wT1", chain=False)
    P.dma("sync", c["sb_bqkbc"][:, :], c["bqkv_k_bc"][:, :], "wT1s")
    P.dma("sync", c["sb_bqvbc"][:, :], c["bqkv_v_bc"][:, :], "wT1s")
    P.wait("tensor", "wT1", P.now("wT1"))
    P.wait("vector", "wT1", P.now("wT1"))
    P.wait("vector", "wT1s", P.now("wT1s"))
    P.op("vector", lambda v: v.memset(c["sb_kvloc"][:, :], 0.0))
    for it in range(NT):
        P.wait("sync", "cT", P.mark("tensor"))
        P.wait("sync", "cV", P.mark("vector"))
        P.wait("sync", "cS", P.mark("scalar"))
        for mb in range(NB):
            P.dma("sync", t1_xn1[:, mb * TT:(mb + 1) * TT],
                  xn1T[mb * 128:(mb + 1) * 128, it * TT:(it + 1) * TT], "t1w")
        wv_ = P.now("t1w")
        P.wait("tensor", "t1w", wv_)
        P.wait("scalar", "t1qo", P.now("t1qo"))
        P.wait("vector", "t1qo", P.now("t1qo"))
        for mo in range(NB):
            pi = mo % 2
            pq = PS.acquire(pi, "tensor")
            for kb in range(NB):
                P.op("tensor", lambda e, kb=kb, mo=mo, pq=pq: e.matmul(
                    pq[:, 0:TT],
                    c["t1_wqkv"][:, kb * 3 * D + mo * 128:kb * 3 * D + (mo + 1) * 128],
                    t1_xn1[:, kb * TT:(kb + 1) * TT],
                    start=(kb == 0), stop=(kb == NB - 1)), sig=(kb == NB - 1))
            P.wait("scalar", "cT", P.mark("tensor"))
            P.wait("scalar", "cV", P.mark("vector"))
            u = t1_tmp[:, 0:TT]
            w2_ = t1_tmp[:, TT:2 * TT]
            P.op("scalar", lambda e, mo=mo, pq=pq, u=u: e.activation(
                u, pq[:, 0:TT], AF.Identity, bias=bqkv[:, mo:mo + 1]))
            PS.release(pi, "scalar")
            P.wait("vector", "cS", P.mark("scalar"))
            P.op("vector", lambda v, u=u, w2_=w2_: v.tensor_scalar_min(w2_, u, 0.0))
            P.wait("scalar", "cV", P.mark("vector"))
            P.op("scalar", lambda e, w2_=w2_: e.activation(w2_, w2_, AF.Exp))
            P.wait("vector", "cS", P.mark("scalar"))
            P.op("vector", lambda v, u=u: v.tensor_scalar_max(u, u, 0.0))
            P.op("vector", lambda v, mo=mo, u=u, w2_=w2_: v.tensor_add(
                t1_q[:, mo * TT:(mo + 1) * TT], u, w2_))
        P.wait("gpsimd", "cV", P.mark("vector"))
        for mo in range(NB):
            P.dma("gpsimd", q1T[mo * 128:(mo + 1) * 128, it * TT:(it + 1) * TT],
                  t1_q[:, mo * TT:(mo + 1) * TT], "t1qo", chain=False)
        for g in range(4):
            def lh(kb, g=g):
                return t1_xn1[:, kb * TT + g * 128:kb * TT + (g + 1) * 128]
            for cc in range(2):
                pi = cc
                pk = PS.acquire(pi, "tensor")
                for kb in range(NB):
                    P.op("tensor", lambda e, kb=kb, cc=cc, pk=pk, g=g: e.matmul(
                        pk[:, 0:TT], lh(kb, g),
                        c["t1_wqkv"][:, kb * 3 * D + D + cc * TT:
                                     kb * 3 * D + D + (cc + 1) * TT],
                        start=(kb == 0), stop=(kb == NB - 1)), sig=(kb == NB - 1))
                P.wait("vector", "cT", P.mark("tensor"))
                P.wait("vector", "cS", P.mark("scalar"))
                u = t1_tmp[:, 0:TT]
                w2_ = t1_tmp[:, TT:2 * TT]
                P.op("vector", lambda v, cc=cc, pk=pk, u=u: v.tensor_add(
                    u, pk[:, 0:TT], c["sb_bqkbc"][:, cc * TT:(cc + 1) * TT]))
                PS.release(pi, "vector")
                P.op("vector", lambda v, u=u, w2_=w2_: v.tensor_scalar_min(w2_, u, 0.0))
                P.wait("scalar", "cV", P.mark("vector"))
                P.op("scalar", lambda e, w2_=w2_: e.activation(w2_, w2_, AF.Exp))
                P.wait("vector", "cS", P.mark("scalar"))
                P.op("vector", lambda v, u=u: v.tensor_scalar_max(u, u, 0.0))
                P.op("vector", lambda v, cc=cc, u=u, w2_=w2_: v.tensor_tensor(
                    t1_kt[:, cc * TT:(cc + 1) * TT], u, w2_, ALU.add))
            for cc in range(2):
                pi = cc
                pv = PS.acquire(pi, "tensor")
                for kb in range(NB):
                    P.op("tensor", lambda e, kb=kb, cc=cc, pv=pv, g=g: e.matmul(
                        pv[:, 0:TT], lh(kb, g),
                        c["t1_wqkv"][:, kb * 3 * D + 2 * D + cc * TT:
                                     kb * 3 * D + 2 * D + (cc + 1) * TT],
                        start=(kb == 0), stop=(kb == NB - 1)), sig=(kb == NB - 1))
                P.wait("vector", "cT", P.mark("tensor"))
                P.op("vector", lambda v, cc=cc, pv=pv: v.tensor_add(
                    t1_vt[:, cc * TT:(cc + 1) * TT], pv[:, 0:TT],
                    c["sb_bqvbc"][:, cc * TT:(cc + 1) * TT]))
                PS.release(pi, "vector")
            kvv = P.mark("vector")
            if c["dev"] and "d_ktall" in c["dev_out"]:
                P.wait("gpsimd", "cV", kvv)
                P.dma("gpsimd", c["dev_out"]["d_ktall"][it * 4 + g, :, :],
                      c["t1_kt"][:, :], "dktg", chain=False)
                P.wait("vector", "dktg", P.now("dktg"))
            P.wait("tensor", "cV", kvv)
            pkv0 = PS.acquire(4, "tensor")
            pkv1 = PS.acquire(5, "tensor")
            for h in range(H):
                p_ = h // 2
                bh = (h % 2) * 64
                pp = pkv0 if p_ < 4 else pkv1
                co = (p_ % 4) * 65
                P.op("tensor", lambda e, h=h, bh=bh, pp=pp, co=co:
                     e.matmul(pp[bh:bh + 64, co:co + 64],
                              t1_kt[:, h * 64:(h + 1) * 64],
                              t1_vt[:, h * 64:(h + 1) * 64],
                              start=True, stop=True,
                              tile_position=(0, bh) if bh else None), sig=False)
                P.op("tensor", lambda e, h=h, bh=bh, pp=pp, co=co:
                     e.matmul(pp[bh:bh + 64, co + 64:co + 65],
                              t1_kt[:, h * 64:(h + 1) * 64],
                              c["sb_onescb"][:, 0:1],
                              start=True, stop=True,
                              tile_position=(0, bh) if bh else None),
                     sig=(h == H - 1))
            P.wait("vector", "cT", P.mark("tensor"))
            P.op("vector", lambda v, pkv0=pkv0: v.tensor_add(
                c["sb_kvloc"][:, 0:260], c["sb_kvloc"][:, 0:260], pkv0[:, 0:260]))
            P.op("vector", lambda v, pkv1=pkv1: v.tensor_add(
                c["sb_kvloc"][:, 260:520], c["sb_kvloc"][:, 260:520],
                pkv1[:, 0:260]))
            PS.release(4, "vector")
            PS.release(5, "vector")
    P.barrier()


def emit_t2(c):
    P = c["P"]
    if "noxc" in c["stages"]:
        kvl, kvb, ks = c["sb_kvloc"], c["sb_kvb"], c["sb_ks"]
        P.op("vector", lambda v: v.tensor_copy(kvb[:, :], kvl[:, :]))
        P.op("vector", lambda v: v.memset(ks[:, :], 0.0))
        for h in range(H):
            bh = (h % 2) * 64
            P.op("vector", lambda v, h=h, bh=bh: v.tensor_copy(
                ks[bh:bh + 64, (h // 2) * H + h:(h // 2) * H + h + 1],
                kvb[bh:bh + 64, (h // 2) * 65 + 64:(h // 2) * 65 + 65]))
        P.barrier()
        return
    nc, pid, partner = c["nc"], c["pid"], c["partner"]
    shkv, shfl = c["shkv"], c["shfl"]
    kvl, kvp, kvb, ks = c["sb_kvloc"], c["sb_kvp"], c["sb_kvb"], c["sb_ks"]
    sb_flag, sb_magic = c["sb_flag"], c["sb_magic"]
    P.dma("gpsimd", shkv[pid, :, :], kvl[:, :], "t2a", chain=False)
    P.wait("gpsimd", "t2a", P.now("t2a"))
    P.dma("gpsimd", shfl[pid, 1, :], sb_magic[0:1, 16:32], "t2a", chain=False)
    P.wait("gpsimd", "t2a", P.now("t2a"))

    def poll(g):
        val = g.alloc_register("pollv2")
        cnt = g.alloc_register("pollc2")
        vals = bass.RegisterHandles((val,))
        cnts = bass.RegisterHandles((cnt,))
        nc.regs_mov(vals, 0)
        nc.regs_mov(cnts, 16)
        loop, done = "t2_loop", "t2_done"
        nc.br(loop, engines=[ET.Pool])
        with nc.body(loop, valid_engines=[ET.Pool]):
            g.dma_start(out=sb_flag[:, :], in_=shfl[partner, 1, :][None, :]) \
                .then_inc(P.sems["t2p"], 16)
            g.wait_ge(P.sems["t2p"], cnt)
            nc.regs_alu(cnts, cnts, 16, op=ALU.add)
            nc.regs_load(vals, sb_flag[0:1, 0:1])
            nc.br_cmp(val, MAGIC2, on_true=done, on_false=loop,
                      comp_op="IS_EQ", engines=[ET.Pool])
        nc.switch_bb(done)
        return None
    P.sem("t2p")
    P.op("gpsimd", poll, sig=True, chain=False)
    P.dma("gpsimd", kvp[:, :], shkv[partner, :, :], "t2b", chain=False)
    P.wait("vector", "t2b", P.now("t2b"))
    P.op("vector", lambda v: v.tensor_add(kvl[:, :], kvl[:, :], kvp[:, :]))
    P.op("vector", lambda v: v.tensor_copy(kvb[:, :], kvl[:, :]))
    P.op("vector", lambda v: v.memset(ks[:, :], 0.0))
    for h in range(H):
        bh = (h % 2) * 64
        P.op("vector", lambda v, h=h, bh=bh: v.tensor_copy(
            ks[bh:bh + 64, (h // 2) * H + h:(h // 2) * H + h + 1],
            kvb[bh:bh + 64, (h // 2) * 65 + 64:(h // 2) * 65 + 65]))
    P.barrier()


def emit_t3a(c):
    P, PS = c["P"], c["PS"]
    t3_q, t3_xn1, t3_z, t3_ob = c["t3_q"], c["t3_xn1"], c["t3_z"], c["t3_ob"]
    t3_hf, t3_tmp = c["t3_hf"], c["t3_tmp"]
    kvb, ks, eps, vecs, scl, sel = (c["sb_kvb"], c["sb_ks"], c["sb_eps"],
                                    c["sb_vecs"], c["sb_scl"], c["sb_sel"])
    q1T, xn1T, s1Th, h1fT = c["q1T"], c["xn1T_t"], c["s1Th"], c["h1fT"]
    P.dma("sync", c["sb_sel"][:, :], c["sel16"][:, :], "wT3s")
    P.wait("tensor", "wT3s", P.now("wT3s"))
    for kb in range(NB):
        P.dma("gpsimd", c["t3_wproj"][:, kb * D:(kb + 1) * D],
              c["wproj"][kb * 128:(kb + 1) * 128, :], "wT3a", chain=False)
        P.dma("gpsimd", c["t3_d0"][:, kb * DFF:(kb + 1) * DFF],
              c["d0T"][kb * 128:(kb + 1) * 128, :], "wT3a", chain=False)
    P.wait("tensor", "wT3a", P.now("wT3a"))
    for it in range(NT):
        P.wait("sync", "cT", P.mark("tensor"))
        P.wait("sync", "cV", P.mark("vector"))
        for mb in range(NB):
            P.dma("sync", t3_q[:, mb * TT:(mb + 1) * TT],
                  q1T[mb * 128:(mb + 1) * 128, it * TT:(it + 1) * TT], "t3w")
            P.dma("sync", t3_xn1[:, mb * TT:(mb + 1) * TT],
                  xn1T[mb * 128:(mb + 1) * 128, it * TT:(it + 1) * TT], "t3w")
        wv_ = P.now("t3w")
        P.wait("tensor", "t3w", wv_)
        p6 = PS.acquire(6, "tensor")
        for kb in range(NB):
            P.op("tensor", lambda e, kb=kb, p6=p6: e.matmul(
                p6[0:16, 0:TT], ks[:, kb * H:(kb + 1) * H],
                t3_q[:, kb * TT:(kb + 1) * TT],
                start=(kb == 0), stop=(kb == NB - 1)), sig=(kb == NB - 1))
        P.wait("vector", "cT", P.mark("tensor"))
        P.op("vector", lambda v, p6=p6: v.tensor_scalar_add(
            c["t3_den"][0:16, 0:TT], p6[0:16, 0:TT], eps[0:16, 5:6]))
        PS.release(6, "vector")
        P.op("vector", lambda v: v.reciprocal(c["t3_den"][0:16, 0:TT],
                                              c["t3_den"][0:16, 0:TT]))
        zv = P.mark("vector")
        for pb in range(NB):
            pi = 4 + pb % 2
            po = PS.acquire(pi, "tensor")
            for sub in range(2):
                h = 2 * pb + sub
                bh = sub * 64
                P.op("tensor", lambda e, h=h, bh=bh, po=po, pb=pb: e.matmul(
                    po[bh:bh + 64, 0:TT],
                    kvb[bh:bh + 64, pb * 65:pb * 65 + 64],
                    t3_q[bh:bh + 64, pb * TT:(pb + 1) * TT],
                    start=True, stop=True,
                    tile_position=(bh, bh) if bh else None), sig=(sub == 1))
            P.wait("tensor", "cV", zv)
            p6b = PS.acquire(6, "tensor")
            P.op("tensor", lambda e, pb=pb, p6b=p6b: e.matmul(
                p6b[:, 0:TT], sel[0:16, pb * 128:(pb + 1) * 128],
                c["t3_den"][0:16, 0:TT], start=True, stop=True))
            P.wait("vector", "cT", P.mark("tensor"))
            P.op("vector", lambda v, p6b=p6b: v.tensor_copy(t3_z[:, 0:TT],
                                                            p6b[:, 0:TT]))
            PS.release(6, "vector")
            P.op("vector", lambda v, pb=pb, po=po: v.tensor_mul(
                t3_ob[:, pb * TT:(pb + 1) * TT], po[:, 0:TT], t3_z[:, 0:TT]))
            PS.release(pi, "vector")
        obv = P.mark("vector")
        P.wait("tensor", "cV", obv)
        P.wait("scalar", "t3so", P.now("t3so"))
        for mo in range(NB):
            pi = mo % 2
            pq = PS.acquire(pi, "tensor")
            for kb in range(NB):
                P.op("tensor", lambda e, kb=kb, mo=mo, pq=pq: e.matmul(
                    pq[:, 0:TT],
                    c["t3_wproj"][:, kb * D + mo * 128:kb * D + (mo + 1) * 128],
                    t3_ob[:, kb * TT:(kb + 1) * TT],
                    start=(kb == 0), stop=(kb == NB - 1)), sig=(kb == NB - 1))
            P.wait("scalar", "cT", P.mark("tensor"))
            P.op("scalar", lambda e, mo=mo, pq=pq: e.activation(
                t3_tmp[:, TT:2 * TT], pq[:, 0:TT], AF.Identity,
                bias=vecs[:, 72 + mo:73 + mo]))
            PS.release(pi, "scalar")
            P.wait("gpsimd", "cS", P.mark("scalar"))
            P.dma("gpsimd", s1Th[mo * 128:(mo + 1) * 128, it * TT:(it + 1) * TT],
                  t3_tmp[:, TT:2 * TT], "t3so", chain=False)
            P.wait("scalar", "t3so", P.now("t3so"))
        P.wait("vector", "t3ho", P.now("t3ho"))
        for mo in range(32):
            pi = 2 + mo % 2
            pq = PS.acquire(pi, "tensor")
            for kb in range(NB):
                P.op("tensor", lambda e, kb=kb, mo=mo, pq=pq: e.matmul(
                    pq[:, 0:TT],
                    c["t3_d0"][:, kb * DFF + mo * 128:kb * DFF + (mo + 1) * 128],
                    t3_xn1[:, kb * TT:(kb + 1) * TT],
                    start=(kb == 0), stop=(kb == NB - 1)), sig=(kb == NB - 1))
            P.wait("vector", "cT", P.mark("tensor"))
            P.wait("vector", "cS", P.mark("scalar"))
            P.op("vector", lambda v, pq=pq: v.tensor_scalar_mul(
                t3_tmp[:, 0:TT], pq[:, 0:TT], scl[:, 0:1]))
            PS.release(pi, "vector")
            P.wait("scalar", "cV", P.mark("vector"))
            P.op("scalar", lambda e, mo=mo: e.activation(
                t3_hf[:, mo * TT:(mo + 1) * TT], t3_tmp[:, 0:TT], AF.Relu,
                scale=4.0))
        P.wait("gpsimd", "cS", P.mark("scalar"))
        for mo in range(32):
            P.dma("gpsimd", h1fT[mo * 128:(mo + 1) * 128, it * TT:(it + 1) * TT],
                  t3_hf[:, mo * TT:(mo + 1) * TT], "t3ho", chain=False)
    P.barrier()


def emit_t3b(c):
    P, PS = c["P"], c["PS"]
    t3_h1, t3_h2o, scl = c["t3_h1"], c["t3_h2o"], c["sb_scl"]
    t3_tmp = c["t3b_tmp"]
    h1fT, h2fT, w1eT = c["h1fT"], c["h2fT"], c["w1eT"]
    w1s = c["t3_w1s"]
    NSUP = 4
    STT = 1024
    for st in range(NSUP):
        P.wait("sync", "cT", P.mark("tensor"))
        for ko in range(32):
            P.dma("sync", t3_h1[:, ko * STT:(ko + 1) * STT],
                  h1fT[ko * 128:(ko + 1) * 128, st * STT:(st + 1) * STT], "t3bw")
        hv = P.now("t3bw")
        P.wait("tensor", "t3bw", hv)
        for mog in range(8):
            bufbase = (mog % 2) * 32
            if mog >= 2 or st > 0:
                P.wait("sync", "cT", P.mark("tensor"))
            sname = f"t3bs{mog % 2}"
            for ko in range(32):
                P.dma("sync", w1s[bufbase + ko][:, :],
                      w1eT[ko * 128:(ko + 1) * 128,
                           mog * 512:(mog + 1) * 512], sname)
            sv = P.now(sname)
            P.wait("tensor", sname, sv)
            pqs = [PS.acquire(i, "tensor") for i in range(8)]
            for ko in range(32):
                for mb in range(4):
                    for nh in range(2):
                        P.op("tensor", lambda e, ko=ko, mb=mb, nh=nh, mog=mog,
                             pqs=pqs, bufbase=bufbase: e.matmul(
                                 pqs[mb * 2 + nh][:, 0:TT],
                                 w1s[bufbase + ko][:, mb * 128:(mb + 1) * 128],
                                 t3_h1[:, ko * STT + nh * TT:ko * STT + (nh + 1) * TT],
                                 start=(ko == 0), stop=(ko == 31)),
                             sig=(ko == 31 and mb == 3 and nh == 1))
            mmk = P.mark("tensor")
            P.wait("vector", "cT", mmk)
            P.wait("vector", "t3bo", P.now("t3bo"))
            P.wait("scalar", "t3bo", P.now("t3bo"))
            for mb in range(4):
                for nh in range(2):
                    P.wait("vector", "cS", P.mark("scalar"))
                    P.op("vector", lambda v, mb=mb, nh=nh, pqs=pqs: v.tensor_scalar_mul(
                        t3_tmp[:, 0:TT], pqs[mb * 2 + nh][:, 0:TT], scl[:, 1:2]))
                    P.wait("scalar", "cV", P.mark("vector"))
                    P.op("scalar", lambda e, mb=mb, nh=nh, mog=mog: e.activation(
                        t3_h2o[:, (mog % 2) * 4096 + mb * STT + nh * TT:
                               (mog % 2) * 4096 + mb * STT + (nh + 1) * TT],
                        t3_tmp[:, 0:TT], AF.Relu, scale=2.0))
            for i in range(8):
                PS.release(i, "scalar")
            P.wait("gpsimd", "cS", P.mark("scalar"))
            for mb in range(4):
                P.dma("gpsimd",
                      h2fT[mog * 512 + mb * 128:mog * 512 + (mb + 1) * 128,
                           st * STT:(st + 1) * STT],
                      t3_h2o[:, (mog % 2) * 4096 + mb * STT:
                             (mog % 2) * 4096 + (mb + 1) * STT], "t3bo",
                      chain=False)
    P.barrier()


def emit_t3c(c):
    P, PS = c["P"], c["PS"]
    t3_hr, t3_g1, t3_sl, t3_x1l, t3_out = (c["t3_hr"], c["t3_g1"], c["t3_sl"],
                                           c["t3_x1l"], c["t3_out"])
    scl = c["sb_scl"]
    h2fT, w2eT, s1Th, x1T, out_sl = (c["h2fT"], c["w2eT"], c["s1Th"], c["x1T"],
                                     c["out_sl"])
    for ko in range(32):
        P.dma("gpsimd", c["t3_w2"][:, ko * D:(ko + 1) * D],
              w2eT[ko * 128:(ko + 1) * 128, :], "wT3c", chain=False)
    P.wait("tensor", "wT3c", P.now("wT3c"))
    for it in range(NT):
        P.wait("sync", "cT", P.mark("tensor"))
        P.wait("sync", "cV", P.mark("vector"))
        for ko in range(32):
            P.dma("sync", t3_hr[:, ko * TT:(ko + 1) * TT],
                  h2fT[ko * 128:(ko + 1) * 128, it * TT:(it + 1) * TT], "t3cw")
        for mb in range(NB):
            P.dma("sync", t3_sl[:, mb * TT:(mb + 1) * TT],
                  s1Th[mb * 128:(mb + 1) * 128, it * TT:(it + 1) * TT], "t3cw")
            P.dma("sync", t3_x1l[:, mb * TT:(mb + 1) * TT],
                  x1T[mb * 128:(mb + 1) * 128, it * TT:(it + 1) * TT], "t3cw")
        wv_ = P.now("t3cw")
        P.wait("tensor", "t3cw", wv_)
        P.wait("vector", "t3cw", wv_)
        for mo in range(NB):
            pi = mo % 2
            pq = PS.acquire(pi, "tensor")
            for ko in range(32):
                P.op("tensor", lambda e, ko=ko, mo=mo, pq=pq: e.matmul(
                    pq[:, 0:TT],
                    c["t3_w2"][:, ko * D + mo * 128:ko * D + (mo + 1) * 128],
                    t3_hr[:, ko * TT:(ko + 1) * TT],
                    start=(ko == 0), stop=(ko == 31)), sig=(ko == 31))
            P.wait("vector", "cT", P.mark("tensor"))
            P.op("vector", lambda v, pq=pq: v.tensor_scalar_mul(
                t3_g1[:, 0:TT], pq[:, 0:TT], scl[:, 2:3]))
            PS.release(pi, "vector")
            P.wait("scalar", "cV", P.mark("vector"))
            P.op("scalar", lambda e: e.activation(
                t3_g1[:, TT:2 * TT], t3_g1[:, 0:TT], AF.Sigmoid))
            P.wait("vector", "cS", P.mark("scalar"))
            P.op("vector", lambda v: v.tensor_mul(
                t3_g1[:, 0:TT], t3_g1[:, 0:TT], t3_g1[:, TT:2 * TT]))
            P.op("vector", lambda v, mo=mo: v.tensor_mul(
                t3_g1[:, 0:TT], t3_g1[:, 0:TT], t3_sl[:, mo * TT:(mo + 1) * TT]))
            P.wait("vector", "t3out", P.now("t3out"))
            P.op("vector", lambda v, mo=mo: v.tensor_add(
                t3_out[:, mo * TT:(mo + 1) * TT], t3_x1l[:, mo * TT:(mo + 1) * TT],
                t3_g1[:, 0:TT]))
            P.wait("gpsimd", "cV", P.mark("vector"))
            P.dma("gpsimd", out_sl[mo * 128:(mo + 1) * 128, it * TT:(it + 1) * TT],
                  t3_out[:, mo * TT:(mo + 1) * TT], "t3out", chain=False)
    P.wait("gpsimd", "t3out", P.now("t3out"))
    P.barrier()


def emit_dev_tail(c, stages, dev, dev_out):
    if not dev:
        return
    P = c["P"]
    do = dev_out
    nch = c.get("S2_NWIN", NT) * 4
    if "s2" in stages:
        P.wait("sync", "s2sp", P.now("s2sp"))
        P.wait("sync", "cV", P.mark("vector"))
        P.dma("sync", do["d_stl"][:, :], c["sb_stl"][:, :], "dev")
        P.dma("sync", do["d_ab"][0:nch], c["ABb"][0:nch], "dev")
        P.dma("sync", do["d_wt"][0:nch], c["WTb"][0:nch], "dev")
        if "s4" not in stages:
            P.dma("sync", do["d_ot"][:, 0:nch * CH], c["OT"][:, 0:nch * CH], "dev")
    if "s4" in stages:
        P.wait("sync", "s4o", P.now("s4o"))
        P.dma("sync", do["d_ot2"][:, 0:nch * CH], c["OT"][:, 0:nch * CH], "dev")
    if "t1" in stages:
        P.wait("sync", "t1qo", P.now("t1qo"))
        P.wait("sync", "cV", P.mark("vector"))
        P.dma("sync", do["d_kv"][:, :], c["sb_kvloc"][:, :], "dev")
        P.dma("sync", do["d_q1"][:, :], c["q1T"][:, :], "dev")
        P.dma("sync", do["d_kt"][:, :], c["t1_kt"][:, :], "dev")
        P.dma("sync", do["d_vt"][:, :], c["t1_vt"][:, :], "dev")
    if "t3a" in stages:
        P.wait("sync", "t3so", P.now("t3so"))
        P.wait("sync", "t3ho", P.now("t3ho"))
        P.wait("sync", "cV", P.mark("vector"))
        P.dma("sync", do["d_s1"][:, :], c["s1Th"][:, :], "dev")
        P.dma("sync", do["d_h1f"][:, :], c["h1fT"][:, :], "dev")
        P.dma("sync", do["d_ob"][:, :], c["t3_ob"][:, :], "dev")
        P.dma("sync", do["d_z"][:, :], c["t3_z"][:, :], "dev")
        P.dma("sync", do["d_zr"][:, :], c["t3_den"][:, 0:TT], "dev")
        P.dma("sync", do["d_ks"][:, :], c["sb_ks"][:, :], "dev")
        P.dma("sync", do["d_d0"][:, :].rearrange("(kb p) c -> p kb c", p=128),
              c["t3_d0"][:, :].rearrange("p (kb c) -> p kb c", kb=NB), "dev")
    if "t3b" in stages:
        P.wait("sync", "t3bo", P.now("t3bo"))
        P.dma("sync", do["d_h2f"][:, :], c["h2fT"][:, :], "dev")
    P.wait("sync", "dev", P.now("dev"))
    if "s1" in stages:
        P.wait("sync", "s1qo", P.now("s1qo"))
        P.wait("sync", "s1ko", P.now("s1ko"))
        P.wait("sync", "s1so", P.now("s1so"))
        P.wait("sync", "cS", P.mark("scalar"))
        P.dma("sync", do["d_q"][:, :], c["qT"][:, :], "dev")
        P.dma("sync", do["d_k"][:, :], c["ktokN"][:, :], "dev")
        P.dma("sync", do["d_v"][:, :], c["vtok"][:, :], "dev")
        P.dma("sync", do["d_beta"][:, :], c["sb_beta"][:, :], "dev")
        P.dma("sync", do["d_stats"][0:3, 0:TC + 1], c["statsH"][0:3, 0:TC + 1], "dev")
        P.wait("sync", "dev", P.now("dev"))


def stage_inputs(inputs, core):
    b, hf = core // 2, core % 2
    x = np.asarray(inputs["L0_attn_x"])
    t0 = hf * TC
    sl = np.empty((TC + 1, D), np.float32)
    sl[1:] = x[b, t0:t0 + TC]
    sl[0] = 0.0 if hf == 0 else x[b, t0 - 1]
    m = {}
    m["x_sl"] = np.ascontiguousarray(sl.T)
    for nm, key in [("wq", "L0_attn_wq"), ("wk", "L0_attn_wk"), ("wv", "L0_attn_wv"),
                    ("wbeta", "L0_attn_wbeta"), ("wo", "L0_attn_wo"),
                    ("w1", "L0_attn_w1"), ("w2", "L0_attn_w2"),
                    ("wqkv", "L1_attn_w_qkv"), ("wproj", "L1_attn_w_proj")]:
        m[nm] = np.asarray(inputs[key], np.float32)
    for nm, key in [("wrT", "L0_ffn_W_r"), ("wkT", "L0_ffn_W_k"), ("wvT", "L0_ffn_W_v")]:
        m[nm] = np.ascontiguousarray(np.asarray(inputs[key], np.float32).T)
    m["k0u"] = np.asarray(inputs["L1_ffn_k0"]).view(np.uint8)
    m["k1u"] = np.asarray(inputs["L1_ffn_k1"]).view(np.uint8)
    m["k2u"] = np.asarray(inputs["L1_ffn_k2"]).view(np.uint8)

    def cols(vec, n):
        v = np.asarray(vec, np.float32).reshape(n, 128)
        return np.ascontiguousarray(v.T)
    vec_list = [cols(inputs["L0_attn_bq"], 8), cols(inputs["L0_attn_bk"], 8),
                cols(inputs["L0_attn_bv"], 8), cols(inputs["L0_attn_bo"], 8),
                cols(inputs["L0_attn_b2"], 8), cols(inputs["L0_attn_ln1_scale"], 8),
                cols(inputs["L0_attn_ln2_scale"], 8), cols(inputs["L0_ffn_mu_r"], 8),
                cols(inputs["L0_ffn_mu_k"], 8), cols(inputs["L1_attn_b_proj"], 8)]
    m["vecs"] = np.concatenate(vec_list, axis=1)
    m["b1v"] = cols(inputs["L0_attn_b1"], 32)
    m["bqkvv"] = cols(inputs["L1_attn_b_qkv"], 24)
    scl = np.zeros((128, 4), np.float32)
    scl[:, 0] = float(np.asarray(inputs["L1_ffn_s0"]))
    scl[:, 1] = float(np.asarray(inputs["L1_ffn_s1"]))
    scl[:, 2] = float(np.asarray(inputs["L1_ffn_s2"]))
    scl[:, 3] = float(hf)
    m["sclv"] = scl
    m["bk_bc"] = np.broadcast_to(np.asarray(inputs["L0_attn_bk"], np.float32),
                                 (128, D)).copy()
    m["bv_bc"] = np.broadcast_to(np.asarray(inputs["L0_attn_bv"], np.float32),
                                 (128, D)).copy()
    bq1 = np.asarray(inputs["L1_attn_b_qkv"], np.float32)
    m["bqkv_k_bc"] = np.broadcast_to(bq1[D:2 * D], (128, D)).copy()
    m["bqkv_v_bc"] = np.broadcast_to(bq1[2 * D:], (128, D)).copy()
    m["bbeta_bc"] = np.broadcast_to(np.asarray(inputs["L0_attn_bbeta"], np.float32),
                                    (128, H)).copy()
    i64 = np.zeros((128, 64), np.float32)
    i64[:64] = np.eye(64)
    i64[64:] = np.eye(64)
    m["i64p"] = i64
    m["maskln"] = -np.tril(np.ones((128, CH), np.float32), -1)
    m["masku"] = np.triu(np.ones((128, CH), np.float32), 0)
    m["identb"] = np.eye(128, dtype=NPBF16)
    m["onesbc"] = np.ones((1, 128), NPBF16)
    m["onesbf"] = np.ones((1, 128), np.float32)
    m["onescb"] = np.ones((128, 1), NPBF16)
    m["onescf"] = np.ones((128, 1), np.float32)
    sel = np.zeros((16, 8 * 128), np.float32)
    for pb in range(8):
        for sub in range(2):
            sel[2 * pb + sub, pb * 128 + sub * 64:pb * 128 + sub * 64 + 64] = 1.0
    m["sel16"] = sel
    return m


_BUILD_CACHE = {}


def kernel(**inputs):
    key = "full"
    stages = ["l0", "l1"]
    nc, outs = build(stages)
    in_maps = [stage_inputs(inputs, cc) for cc in range(8)]
    res = run_bass_kernel_spmd(nc, in_maps, core_ids=list(range(8)))
    out = np.empty((4, 2 * TC, D), np.float32)
    for cc in range(8):
        b, hf = cc // 2, cc % 2
        out[b, hf * TC:(hf + 1) * TC] = res.results[cc]["out_sl"].T
    return out
